# version 01
# baseline (speedup 1.0000x reference)
import numpy as np
from contextlib import ExitStack
import concourse.bass as bass
import concourse.mybir as mybir
from concourse.bass_utils import run_bass_kernel_spmd
from concourse.alu_op_type import AluOpType as OP

F32 = mybir.dt.float32
BF16 = mybir.dt.bfloat16
U32 = mybir.dt.uint32
AF = mybir.ActivationFunctionType
AX = mybir.AxisListType

NCORE = 8
L = 2048
D = 1024
DI = 2048
DS = 16
DTR = 64
DFF = 4096
NE = 16
HID = 1024
NT = 4096
TL = 512
NCHUNK = NT // TL
TG = 256
NTG = NT // TG
EPS = 1e-5

ENGINES = ("sync", "tensor", "scalar", "vector", "gpsimd")
PHASES = ("A1", "A2", "B1", "B2", "C", "FIN")
TAPS = ()
LIMIT_CHUNKS = None
LIMIT_TG = None
DEBUG_SCRATCH = False
LIMIT_E = None
_last_results = None


NDQ = 4


class _Eng:
    __slots__ = ("name", "obj", "sem_c", "dq", "cnt_c", "dq_cnt", "dq_idx", "seen")

    def __init__(self, name, obj, sem_c, dq):
        self.name, self.obj = name, obj
        self.sem_c, self.dq = sem_c, dq
        self.reset()

    def reset(self):
        self.cnt_c = 0
        self.dq_cnt = [0] * NDQ
        self.dq_idx = 0
        self.seen = {}


class Ctx:
    def __init__(self, nc, sems):
        self.nc = nc
        self.engs = {n: _Eng(n, getattr(nc, n), sems[n + "_c"],
                             [sems["%s_d%d" % (n, i)] for i in range(NDQ)])
                     for n in ENGINES}
        self.cur = None
        self.taps = {}
        self.autodrain = False

    def begin(self, ename):
        self.cur = ename
        for e in self.engs.values():
            e.reset()

    def dep(self, ename, m, for_dma=False):
        if m is None:
            return
        e = self.engs[ename]
        dn, dv, kind = m
        if kind == "c" and dn == ename and not for_dma:
            return
        key = (dn, kind)
        if e.seen.get(key, 0) >= dv:
            return
        e.seen[key] = dv
        if self.cur == ename:
            if kind == "d":
                qn, slot = dn
                e.obj.wait_ge(self.engs[qn].dq[slot], dv)
            else:
                e.obj.wait_ge(self.engs[dn].sem_c, dv)

    def run(self, ename, meth, *args, deps=(), inc=0, dma=False, **kw):
        for m in deps:
            self.dep(ename, m, for_dma=dma)
        e = self.engs[ename]
        if self.autodrain and ename == "vector" and not dma and self.cur == ename:
            e.obj.drain()
        if dma:
            slot = e.dq_idx % NDQ
            e.dq_idx += 1
            if e.dq_cnt[slot]:
                self.dep(ename, ((ename, slot), e.dq_cnt[slot], "d"))
            if self.cur == ename:
                ins = getattr(e.obj, meth)(*args, **kw)
                ins.then_inc(e.dq[slot], 16)
            e.dq_cnt[slot] += 16
            return ((ename, slot), e.dq_cnt[slot], "d")
        if self.cur == ename:
            ins = getattr(e.obj, meth)(*args, **kw)
            if inc:
                if meth != "drain" and ename in ("scalar", "vector", "gpsimd"):
                    e.obj.drain().then_inc(e.sem_c, inc)
                else:
                    ins.then_inc(e.sem_c, inc)
        e.cnt_c += inc
        if inc:
            return (ename, e.cnt_c, "c")
        return None

    def drain(self, ename):
        if self.cur == ename:
            self.engs[ename].obj.drain()

    def barrier(self):
        ms = {}
        for n in ENGINES:
            e = self.engs[n]
            for slot in range(NDQ):
                if e.dq_cnt[slot]:
                    self.dep(n, ((n, slot), e.dq_cnt[slot], "d"))
            ms[n] = self.run(n, "drain", inc=1)
        for n in ENGINES:
            for o in ENGINES:
                if o != n:
                    self.dep(n, ms[o])
        return ms

    def tap(self, name, src_ap, deps=()):
        if name in self.taps:
            return self.run("gpsimd", "dma_start", out=self.taps[name], in_=src_ap,
                            deps=deps, dma=True)
        return None


def _asc(a):
    return np.ascontiguousarray(a, dtype=np.float32)


def _prep_shared(kw):
    d = {}
    d["w_in_T"] = _asc(kw["L0_attn_in_proj_w"].T)
    xp = np.zeros((DI, 128), dtype=np.float32)
    xpw = np.asarray(kw["L0_attn_x_proj_w"], dtype=np.float32)
    xp[:, 0:16] = xpw[DTR:DTR + DS].T
    xp[:, 32:48] = xpw[DTR + DS:DTR + 2 * DS].T
    xp[:, 64:128] = xpw[0:DTR].T
    d["w_xp_pad"] = _asc(xp)
    d["w_dt_T"] = _asc(kw["L0_attn_dt_proj_w"].T)
    d["w_out_T"] = _asc(kw["L0_attn_out_proj_w"].T)
    d["w1_T"] = _asc(kw["L0_attn_w1"].T)
    d["w2_T"] = _asc(kw["L0_attn_w2"].T)
    d["w3_T"] = _asc(kw["L0_attn_w3"].T)
    d["W1e"] = _asc(kw["L0_ffn_W1"])
    d["W2e"] = _asc(kw["L0_ffn_W2"])
    d["wr"] = _asc(kw["L0_ffn_Wr"])
    d["b2e"] = _asc(kw["L0_ffn_b2"])
    for nm, src in (("g1b", "L0_ffn_g1"), ("be1b", "L0_ffn_be1"), ("b1b", "L0_ffn_b1")):
        v = np.asarray(kw[src], dtype=np.float32)
        d[nm] = _asc(np.broadcast_to(v[:, None, :], (NE, 128, HID)))
    d["A_log_r"] = _asc(np.asarray(kw["L0_attn_A_log"]).reshape(16, 128, DS)
                        .transpose(1, 0, 2))
    d["convw_r"] = _asc(np.asarray(kw["L0_attn_conv_w"]).reshape(DI, 4)
                        .reshape(16, 128, 4).transpose(1, 0, 2))
    d["convb_r"] = _asc(np.asarray(kw["L0_attn_conv_b"]).reshape(16, 128).T)
    d["dtb_r"] = _asc(np.asarray(kw["L0_attn_dt_proj_b"]).reshape(16, 128).T)
    d["D_r"] = _asc(np.asarray(kw["L0_attn_D"]).reshape(16, 128).T)
    d["n1w_pp"] = _asc(np.asarray(kw["L0_attn_norm1_w"]).reshape(8, 128).T)
    d["n2w_pp"] = _asc(np.asarray(kw["L0_attn_norm2_w"]).reshape(8, 128).T)
    d["brb"] = _asc(np.broadcast_to(np.asarray(kw["L0_ffn_br"])[None, :], (128, NE)))
    d["iota16"] = _asc(np.broadcast_to(np.arange(NE, dtype=np.float32)[None, :], (128, NE)))
    d["ident"] = _asc(np.eye(128, dtype=np.float32))
    oh = np.zeros((DS, DS * 128), dtype=np.float32)
    for n in range(DS):
        oh[n, n * 128:(n + 1) * 128] = 1.0
    d["onehot"] = _asc(oh)
    return d


def _alloc(nc, es0):
    B = {}
    E0 = es0.enter_context

    def mk(scope, name, shape, dt):
        B[name] = scope(nc.sbuf_tensor(name, list(shape), dt))

    def mkp(scope, name, shape, dt=F32):
        B[name] = scope(nc.psum_tensor(name, list(shape), dt))

    mk(E0, "ident_bf", (128, 128), BF16)
    mk(E0, "iota16_sb", (128, NE), F32)
    mk(E0, "brb_sb", (128, NE), F32)
    mk(E0, "n1w_sb", (128, 8), F32)
    mk(E0, "n2w_sb", (128, 8), F32)
    mk(E0, "A_sb", (128, 16 * DS), F32)
    mk(E0, "convw_sb", (128, 16 * 4), F32)
    mk(E0, "convb_sb", (128, 16), F32)
    mk(E0, "dtb_sb", (128, 16), F32)
    mk(E0, "D_sb", (128, 16), F32)

    for nm in ("sum_s", "ssq_s", "mu0_s", "rstd0_s", "b00_s", "rs01_s", "b01_s",
               "rstd2_s", "sum3_s", "ssq3_s", "rstd3_s", "b03_s"):
        mk(E0, nm, (128, 32), F32)
    mk(E0, "sm_a", (128, 32), F32)
    mk(E0, "sm_b", (128, 32), F32)
    mk(E0, "trash_bf", (128, 1024), BF16)
    mk(E0, "state_sb", (128, 16 * DS), F32)
    mk(E0, "halo_sb", (128, 16 * 3), BF16)

    sA = ExitStack()
    EA = sA.enter_context
    sA1 = ExitStack()
    E1 = sA1.enter_context
    mk(E1, "w_in_sb", (128, 8 * DFF), BF16)
    mk(E1, "w_xp_sb", (128, 16 * 128), BF16)
    mk(E1, "x0a_sb", (128, 2 * 4 * D), F32)
    mk(E1, "x0bf_sb", (128, 4 * D), BF16)
    mk(E1, "n1T_sb", (128, 8 * TL), BF16)
    mk(E1, "ustg_sb", (128, 2 * 515), BF16)
    mk(E1, "cacc_sb", (128, 2 * TL), BF16)
    mk(E1, "uact_sb", (128, 16 * TL), BF16)
    mk(E1, "zs_sb", (128, 16 * TL), BF16)
    mk(E1, "Bst_sb", (16, TL), F32)
    mk(E1, "Cst_sb", (16, TL), F32)
    mk(E1, "dtrst_sb", (64, TL), BF16)
    mkp(E1, "psA_tr0", (128, TL))
    mkp(E1, "psA_tr1", (128, TL))
    mkp(E1, "psA_mm0", (128, TL))
    mkp(E1, "psA_mm1", (128, TL))
    mkp(E1, "psA_xp", (128, TL))
    sA1.close()
    sA2 = ExitStack()
    E2 = sA2.enter_context
    mk(E2, "onehot_bf", (16, DS * 128), BF16)
    mk(E2, "w_dt_sb", (64, DI), BF16)
    mk(E2, "w_out_sb", (128, 16 * D), BF16)
    mk(E2, "x0b_sb", (128, 4 * D), F32)
    mk(E2, "uact2_sb", (128, 16 * TL), BF16)
    mk(E2, "zs2_sb", (128, 16 * TL), BF16)
    mk(E2, "dtr2_sb", (64, TL), BF16)
    mk(E2, "Bbf_sb", (16, TL), BF16)
    mk(E2, "Cbf_sb", (16, TL), BF16)
    mk(E2, "Bb_sb", (128, 16 * TL), BF16)
    mk(E2, "Cb_sb", (128, 16 * TL), BF16)
    mk(E2, "sp1_sb", (128, TL), F32)
    mk(E2, "sp2_sb", (128, TL), F32)
    mk(E2, "dt_sb", (128, 2 * TL), F32)
    mk(E2, "du_sb", (128, 2 * TL), BF16)
    mk(E2, "aBuf_sb", (128, 2 * 2 * TL), F32)
    mk(E2, "bs_sb", (128, 2 * TL), BF16)
    mk(E2, "hb_sb", (128, 2 * TL), BF16)
    mk(E2, "yp_sb", (128, 4 * TL), BF16)
    mk(E2, "yt_sb", (128, TL), BF16)
    mk(E2, "gt_sb", (128, TL), BF16)
    mk(E2, "mT_sb", (128, 8 * TL), BF16)
    mk(E2, "h_sb", (128, 4 * D), F32)
    mk(E2, "h2bf_sb", (128, 4 * D), BF16)
    mk(E2, "n2Tst_sb", (128, 8 * TL), BF16)
    mkp(E2, "psB_bc", (128, TL))
    mkp(E2, "psB_dt", (128, TL))
    mkp(E2, "psB_op0", (128, TL))
    mkp(E2, "psB_op1", (128, TL))
    mkp(E2, "psB_tr0", (128, 1024))
    mkp(E2, "psB_tr1", (128, 1024))
    sA2.close()
    sA.close()

    sBC = ExitStack()
    EB = sBC.enter_context
    mk(EB, "u3T_sb", (128, 8 * NT), BF16)
    mk(EB, "wfull_sb", (128, 32 * NE), F32)
    mk(EB, "wroute_sb", (16, NT), BF16)
    mk(EB, "wr_sb", (128, 8 * NE), BF16)
    mk(EB, "b2_sb", (16, HID), BF16)
    sB1 = ExitStack()
    E3 = sB1.enter_context
    mk(E3, "n2Ta_sb", (128, 8 * NT), BF16)
    mk(E3, "w1s_sb", (128, 2 * 8 * 128), BF16)
    mk(E3, "w2s_sb", (128, 2 * 8 * 128), BF16)
    mk(E3, "sl_sb", (128, TL), BF16)
    mk(E3, "w2e_sb", (128, TL), BF16)
    mk(E3, "gtmp_sb", (128, 2 * TL), BF16)
    mk(E3, "gst_sb", (128, 2 * NT), BF16)
    mkp(E3, "psC_1", (128, TL))
    mkp(E3, "psC_2", (128, TL))
    mkp(E3, "psC_3", (128, TL))
    mkp(E3, "psC_4", (128, TL))
    sB1.close()
    sB2 = ExitStack()
    E4 = sB2.enter_context
    mk(E4, "w3_sb", (128, 32 * D), BF16)
    mk(E4, "gld_sb", (128, 32 * TG), BF16)
    mk(E4, "x0c_sb", (128, 2 * D), F32)
    mk(E4, "hc_sb", (128, 2 * D), F32)
    mk(E4, "x1_sb", (128, 2 * D), F32)
    mk(E4, "pT_sb", (128, 8 * TG), BF16)
    mk(E4, "u3bf_sb", (128, 2 * D), BF16)
    mk(E4, "lg_sb", (128, 2 * NE), F32)
    mk(E4, "m8_sb", (128, 2 * 8), F32)
    mk(E4, "i8_sb", (128, 2 * 8), U32)
    mk(E4, "idxf_sb", (128, 2 * 2), F32)
    mk(E4, "nm1_sb", (128, 2), F32)
    mk(E4, "t21_sb", (128, 2), F32)
    mk(E4, "den_sb", (128, 2), F32)
    mk(E4, "w1r_sb", (128, 2), F32)
    mk(E4, "w2r_sb", (128, 2), F32)
    mk(E4, "mk1_sb", (128, NE), F32)
    mk(E4, "mk2_sb", (128, NE), F32)
    mk(E4, "wfbf_sb", (128, NE), BF16)
    mkp(E4, "psD_p", (128, 8 * TG))
    mkp(E4, "psD_t", (128, 1024))
    mkp(E4, "psD_u", (128, 4 * TG))
    sB2.close()
    sC = ExitStack()
    E5 = sC.enter_context
    mk(E5, "f_sb", (128, 8 * NT), BF16)
    sC1 = ExitStack()
    E6 = sC1.enter_context
    mk(E6, "W1X_sb", (128, 4 * HID), BF16)
    mk(E6, "W1Y_sb", (128, 4 * HID), BF16)
    mk(E6, "W2X_sb", (128, 4 * D), BF16)
    mk(E6, "W2Y_sb", (128, 4 * D), BF16)
    mk(E6, "g1e_sb", (128, HID), BF16)
    mk(E6, "be1e_sb", (128, HID), BF16)
    mk(E6, "b1e_sb", (128, HID), BF16)
    mk(E6, "Sb1_sb", (128, 1), F32)
    mk(E6, "hch_sb", (128, 2 * HID), BF16)
    mk(E6, "gl_sb", (128, 2 * HID), BF16)
    mk(E6, "dg_sb", (128, 2 * 128), BF16)
    mk(E6, "glT_sb", (128, 8 * TG), BF16)
    mk(E6, "sums_sb", (128, 2), F32)
    mk(E6, "ssqE_sb", (128, 2), F32)
    mk(E6, "muE_sb", (128, 2), F32)
    mk(E6, "rstdE_sb", (128, 2), F32)
    mk(E6, "bE_sb", (128, 2), F32)
    mkp(E6, "psE_w1", (128, HID))
    mkp(E6, "psE_tr", (128, TL))
    mkp(E6, "psE_o", (128, 8 * TG))
    sC1.close()
    sC2 = ExitStack()
    E7 = sC2.enter_context
    mk(E7, "x1f_sb", (128, 2 * D), F32)
    mk(E7, "x2_sb", (128, 2 * D), F32)
    mkp(E7, "psE_f", (128, 1024))
    sC2.close()
    sC.close()
    sBC.close()
    return B


def build(core_inputs_shapes=None):
    nc = bass.Bass("TRN2", target_bir_lowering=False, detect_race_conditions=False)
    H = {}

    def din(name, shape, dt=F32):
        H[name] = nc.dram_tensor(name, list(shape), dt, kind="ExternalInput").ap()

    din("xs", (NT, D))
    din("w_in_T", (D, DFF)); din("w_xp_pad", (DI, 128)); din("w_dt_T", (DTR, DI))
    din("w_out_T", (DI, D)); din("w1_T", (D, DFF)); din("w2_T", (D, DFF))
    din("w3_T", (DFF, D)); din("W1e", (NE, D, HID)); din("W2e", (NE, HID, D))
    din("wr", (D, NE)); din("b2e", (NE, D))
    din("g1b", (NE, 128, HID)); din("be1b", (NE, 128, HID)); din("b1b", (NE, 128, HID))
    din("A_log_r", (128, 16, DS)); din("convw_r", (128, 16, 4)); din("convb_r", (128, 16))
    din("dtb_r", (128, 16)); din("D_r", (128, 16)); din("n1w_pp", (128, 8))
    din("n2w_pp", (128, 8)); din("brb", (128, NE)); din("iota16", (128, NE))
    din("ident", (128, 128)); din("onehot", (16, DS * 128))
    out_h = nc.dram_tensor("out", [NT, D], F32, kind="ExternalOutput").ap()

    def scr(name, shape, dt):
        kind = "ExternalOutput" if DEBUG_SCRATCH else "Internal"
        H[name] = nc.dram_tensor(name, list(shape), dt, kind=kind).ap()

    scr("sp_uact", (NCHUNK, 128, 16, TL), BF16)
    scr("sp_z", (NCHUNK, 128, 16, TL), BF16)
    scr("sp_dtr", (NCHUNK, 64, TL), BF16)
    scr("sp_B", (NCHUNK, 16, TL), F32)
    scr("sp_C", (NCHUNK, 16, TL), F32)
    scr("sp_h", (NCHUNK, 128, 4, D), F32)
    scr("sp_n2T", (128, 8, NT), BF16)
    scr("sp_g2", (NTG, 128, 32, TG), BF16)
    scr("sp_x1", (NTG, 128, 2, D), F32)
    if DEBUG_SCRATCH:
        scr("dbg_gld", (NTG, 128, 32 * TG), BF16)
        scr("dbg_pT", (NTG, 128, 8 * TG), BF16)

    es0 = ExitStack()
    sems = {}
    for n in ENGINES:
        sems[n + "_c"] = es0.enter_context(nc.semaphore("semc_" + n))
        for i in range(4):
            sems["%s_d%d" % (n, i)] = es0.enter_context(
                nc.semaphore("semd%d_" % i + n))
    Bf = _alloc(nc, es0)
    ctx = Ctx(nc, sems)
    for tname in TAPS:
        shape, dt = _TAPDEFS[tname]
        ctx.taps[tname] = nc.dram_tensor("tap_" + tname, list(shape), dt,
                                         kind="ExternalOutput").ap()

    with nc.Block() as block:
        for ename in ENGINES:
            fn = getattr(block, ename)

            def emit(_eng, ename=ename):
                ctx.begin(ename)
                _program(ctx, H, Bf, out_h)

            fn(emit)
    es0.close()
    return nc



def _program(ctx, H, B, out_h):
    r = ctx.run
    m_setup = _setup(ctx, H, B)
    bar = ctx.barrier()
    if "A1" in PHASES:
        _phase_a1(ctx, H, B, m_setup)
        bar = ctx.barrier()
    if "A2" in PHASES:
        _phase_a2(ctx, H, B, m_setup)
        bar = ctx.barrier()
    if "B1" in PHASES:
        _phase_b1(ctx, H, B)
        bar = ctx.barrier()
    if "B2" in PHASES:
        _phase_b2(ctx, H, B)
        bar = ctx.barrier()
    if "C" in PHASES:
        _phase_c(ctx, H, B)
        bar = ctx.barrier()
    if "FIN" in PHASES:
        _phase_fin(ctx, H, B, out_h)
    e = ctx.engs["gpsimd"]
    for slot in range(NDQ):
        if e.dq_cnt[slot]:
            for n in ENGINES:
                ctx.dep(n, (("gpsimd", slot), e.dq_cnt[slot], "d"))
    ctx.barrier()


def _setup(ctx, H, B):
    r = ctx.run
    m = {}
    for nm, dst in (("iota16", "iota16_sb"), ("brb", "brb_sb"), ("n1w_pp", "n1w_sb"),
                    ("n2w_pp", "n2w_sb"), ("convb_r", "convb_sb"), ("dtb_r", "dtb_sb"),
                    ("D_r", "D_sb")):
        m[dst] = r("sync", "dma_start", out=B[dst][:], in_=H[nm][:], dma=True)
    m["A_log"] = r("sync", "dma_start", out=B["A_sb"][:],
                   in_=H["A_log_r"].rearrange("p dc n -> p (dc n)"), dma=True)
    m["convw"] = r("sync", "dma_start", out=B["convw_sb"][:],
                   in_=H["convw_r"].rearrange("p dc t -> p (dc t)"), dma=True)
    m["ident"] = r("gpsimd", "dma_start", out=B["ident_bf"][:], in_=H["ident"][:], dma=True)
    ma1 = r("scalar", "activation", B["A_sb"][:], B["A_sb"][:], AF.Exp,
            deps=[m["A_log"]], inc=1)
    m["A"] = r("vector", "tensor_scalar", B["A_sb"][:], B["A_sb"][:], -1.0, None, OP.mult,
               deps=[ma1], inc=1)
    m["st0"] = r("gpsimd", "memset", B["state_sb"][:], 0.0, inc=1)
    m["hl0"] = r("gpsimd", "memset", B["halo_sb"][:], 0.0, inc=1)
    return m


def _phase_a1(ctx, H, B, ms):
    r = ctx.run
    mw = []
    for kg in range(8):
        mw.append(r("gpsimd", "dma_start",
                    out=B["w_in_sb"][:, kg * DFF:(kg + 1) * DFF],
                    in_=H["w_in_T"][kg * 128:(kg + 1) * 128, :], dma=True))
    m_wxp = r("gpsimd", "dma_start",
              out=B["w_xp_sb"][:].rearrange("p (dc m) -> p dc m", m=128),
              in_=H["w_xp_pad"].rearrange("(dc p) m -> p dc m", p=128), dma=True)
    m_fold = []
    for kg in range(8):
        m_fold.append(r("vector", "tensor_scalar",
                        B["w_in_sb"][:, kg * DFF:(kg + 1) * DFF],
                        B["w_in_sb"][:, kg * DFF:(kg + 1) * DFF],
                        B["n1w_sb"][:, kg:kg + 1], None, OP.mult,
                        deps=[mw[kg], ms["n1w_sb"]], inc=1))
    m_fold_all = m_fold[-1]

    prev = {}
    for c in range(LIMIT_CHUNKS or NCHUNK):
        buf = c % 2
        x0 = B["x0a_sb"][:, buf * 4 * D:(buf + 1) * 4 * D].rearrange("p (tt d) -> p tt d", d=D)
        m_ld = r("sync", "dma_start", out=x0,
                 in_=H["xs"][c * TL:(c + 1) * TL, :].rearrange("(tt p) d -> p tt d", p=128),
                 deps=[prev.get(("x0", buf))], dma=True)
        g0 = c * 4
        m_st = None
        m_stq = None
        for tt in range(4):
            g = g0 + tt
            m_st = r("vector", "tensor_reduce", B["sum_s"][:, g:g + 1], x0[:, tt, :],
                     AX.X, OP.add, deps=[m_ld], inc=(1 if tt == 3 else 0))
            m_stq = r("scalar", "activation", B["trash_bf"][:],
                      x0[:, tt, :], AF.Square,
                      accum_out=B["ssq_s"][:, g:g + 1], deps=[m_ld],
                      inc=(1 if tt == 3 else 0))
        ctx.drain("vector")
        ctx.autodrain = True
        sl = slice(g0, g0 + 4)
        r("vector", "tensor_scalar", B["mu0_s"][:, sl], B["sum_s"][:, sl], 1.0 / D, None,
          OP.mult, deps=[m_st, m_stq])
        r("vector", "tensor_scalar", B["sm_a"][:, sl], B["ssq_s"][:, sl], 1.0 / D, None, OP.mult)
        r("vector", "tensor_tensor", B["sm_b"][:, sl], B["mu0_s"][:, sl], B["mu0_s"][:, sl],
          OP.mult)
        r("vector", "tensor_tensor", B["sm_a"][:, sl], B["sm_a"][:, sl], B["sm_b"][:, sl],
          OP.subtract)
        m_v = r("vector", "tensor_scalar", B["sm_b"][:, sl], B["sm_a"][:, sl], EPS, None,
                OP.add, inc=1)
        ml1 = r("scalar", "activation", B["sm_b"][:, sl], B["sm_b"][:, sl], AF.Ln,
                deps=[m_v], inc=1)
        r("scalar", "activation", B["rstd0_s"][:, sl], B["sm_b"][:, sl], AF.Exp, scale=-0.5)
        m_s1 = r("scalar", "activation", B["sm_b"][:, sl], B["sm_b"][:, sl], AF.Exp,
                 inc=1)
        r("vector", "reciprocal", B["sm_b"][:, sl], B["sm_b"][:, sl], deps=[m_s1])
        r("vector", "tensor_tensor", B["sm_a"][:, sl], B["sm_a"][:, sl], B["sm_b"][:, sl],
          OP.mult)
        m_v2 = r("vector", "tensor_scalar", B["sm_a"][:, sl], B["sm_a"][:, sl], EPS, None,
                 OP.add, inc=1)
        r("scalar", "activation", B["sm_a"][:, sl], B["sm_a"][:, sl], AF.Ln, deps=[m_v2])
        m_s2 = r("scalar", "activation", B["sm_a"][:, sl], B["sm_a"][:, sl], AF.Exp,
                 scale=-0.5, inc=1)
        r("vector", "tensor_tensor", B["rs01_s"][:, sl], B["rstd0_s"][:, sl], B["sm_a"][:, sl],
          OP.mult, deps=[m_s2])
        r("vector", "scalar_tensor_tensor", B["b00_s"][:, sl], B["mu0_s"][:, sl],
          -1.0, B["rstd0_s"][:, sl], OP.mult, OP.mult)
        m_sm = r("vector", "scalar_tensor_tensor", B["b01_s"][:, sl], B["mu0_s"][:, sl],
                 -1.0, B["rs01_s"][:, sl], OP.mult, OP.mult, inc=1)
        ctx.autodrain = False
        m_cast = None
        for tt in range(4):
            g = g0 + tt
            m_cast = r("scalar", "activation",
                       B["x0bf_sb"][:, tt * D:(tt + 1) * D], x0[:, tt, :], AF.Identity,
                       scale=B["rs01_s"][:, g:g + 1], bias=B["b01_s"][:, g:g + 1],
                       deps=[m_sm, prev.get("n1T_done")], inc=1)
        m_ev = {}
        for dg in range(8):
            ps = B["psA_tr0"] if dg % 2 == 0 else B["psA_tr1"]
            m_mm = None
            for tt in range(4):
                m_mm = r("tensor", "matmul", ps[:, tt * 128:(tt + 1) * 128],
                         B["x0bf_sb"][:, tt * D + dg * 128: tt * D + (dg + 1) * 128],
                         B["ident_bf"][:],
                         deps=[m_cast, ms["ident"], m_ev.get(dg - 2)],
                         start=True, stop=True, inc=(1 if tt == 3 else 0))
            m_ev[dg] = r("scalar", "activation", B["n1T_sb"][:, dg * TL:(dg + 1) * TL],
                         ps[:], AF.Copy, deps=[m_mm], inc=1)
        m_n1T = m_ev[7]
        m_mm_ev = {}
        m_uact = {}
        for mt in range(32):
            ps = B["psA_mm0"] if mt % 2 == 0 else B["psA_mm1"]
            dep0 = [m_n1T, m_fold_all, m_mm_ev.get(mt - 2)]
            m_mm = None
            for kg in range(8):
                m_mm = r("tensor", "matmul", ps[:],
                         B["w_in_sb"][:, kg * DFF + mt * 128: kg * DFF + (mt + 1) * 128],
                         B["n1T_sb"][:, kg * TL:(kg + 1) * TL],
                         deps=(dep0 if kg == 0 else ()),
                         start=(kg == 0), stop=(kg == 7), inc=(1 if kg == 7 else 0))
            if mt < 16:
                dc = mt
                ub = dc % 2
                ust = B["ustg_sb"][:, ub * 515:(ub + 1) * 515]
                if c % 4 == 0:
                    r("vector", "memset", ust[:, 0:3], 0.0,
                      deps=[prev.get(("ustg", ub))])
                else:
                    r("vector", "tensor_copy", ust[:, 0:3],
                      B["halo_sb"][:, dc * 3:(dc + 1) * 3],
                      deps=[prev.get(("ustg", ub)), prev.get("halo_w%d" % dc)])
                m_evu = r("scalar", "activation", ust[:, 3:515], ps[:], AF.Copy,
                          deps=[m_mm, prev.get(("ustg", ub))], inc=1)
                m_mm_ev[mt] = m_evu
                m_tail = r("vector", "tensor_copy", B["halo_sb"][:, dc * 3:(dc + 1) * 3],
                           ust[:, 512:515], deps=[m_evu], inc=1)
                prev["halo_w%d" % dc] = m_tail
                cb = dc % 2
                cacc = B["cacc_sb"][:, cb * TL:(cb + 1) * TL]
                ctx.drain("vector")
                r("vector", "tensor_scalar", cacc, ust[:, 0:512],
                  B["convw_sb"][:, dc * 4:dc * 4 + 1], None, OP.mult,
                  deps=[m_evu, ms["convw"], prev.get(("cacc", cb))])
                m_conv = None
                for t in range(1, 4):
                    m_conv = r("vector", "scalar_tensor_tensor", cacc, ust[:, t:512 + t],
                               B["convw_sb"][:, dc * 4 + t:dc * 4 + t + 1], cacc,
                               OP.mult, OP.add, inc=(1 if t == 3 else 0))
                prev[("ustg", ub)] = m_conv
                m_ua = r("scalar", "activation", B["uact_sb"][:, dc * TL:(dc + 1) * TL],
                         cacc, AF.Silu, bias=B["convb_sb"][:, dc:dc + 1],
                         deps=[m_conv, ms["convb_sb"], prev.get("sp_uact")], inc=1)
                prev[("cacc", cb)] = m_ua
                m_uact[dc] = m_ua
            else:
                dc = mt - 16
                m_evz = r("scalar", "activation", B["zs_sb"][:, dc * TL:(dc + 1) * TL],
                          ps[:], AF.Silu, deps=[m_mm, prev.get("sp_z")], inc=1)
                m_mm_ev[mt] = m_evz
        m_xp = None
        for k in range(16):
            m_xp = r("tensor", "matmul", B["psA_xp"][:],
                     B["w_xp_sb"][:, k * 128:(k + 1) * 128],
                     B["uact_sb"][:, k * TL:(k + 1) * TL],
                     deps=[m_uact[k], m_wxp] + ([prev.get("xp_ev")] if k == 0 else []),
                     start=(k == 0), stop=(k == 15), inc=(1 if k == 15 else 0))
        m_eB = r("scalar", "activation", B["Bst_sb"][:], B["psA_xp"][0:16, :], AF.Copy,
                 deps=[m_xp, prev.get("sp_B")], inc=1)
        m_eC = r("scalar", "activation", B["Cst_sb"][:], B["psA_xp"][32:48, :], AF.Copy,
                 deps=[prev.get("sp_C")], inc=1)
        m_eD = r("scalar", "activation", B["dtrst_sb"][:], B["psA_xp"][64:128, :], AF.Copy,
                 deps=[prev.get("sp_dtr")], inc=1)
        prev["xp_ev"] = m_eD
        m_spu = r("scalar", "dma_start", out=H["sp_uact"][c],
                  in_=B["uact_sb"][:].rearrange("p (dc t) -> p dc t", t=TL),
                  deps=[m_uact[15]], dma=True)
        m_spz = r("scalar", "dma_start", out=H["sp_z"][c],
                  in_=B["zs_sb"][:].rearrange("p (dc t) -> p dc t", t=TL),
                  deps=[m_mm_ev[31]], dma=True)
        m_spB = r("scalar", "dma_start", out=H["sp_B"][c], in_=B["Bst_sb"][:],
                  deps=[m_eB], dma=True)
        m_spC = r("scalar", "dma_start", out=H["sp_C"][c], in_=B["Cst_sb"][:],
                  deps=[m_eC], dma=True)
        m_spD = r("scalar", "dma_start", out=H["sp_dtr"][c], in_=B["dtrst_sb"][:],
                  deps=[m_eD], dma=True)
        prev["sp_uact"] = m_spu
        prev["sp_z"] = m_spz
        prev["sp_B"] = m_spB
        prev["sp_C"] = m_spC
        prev["sp_dtr"] = m_spD
        prev["n1T_done"] = m_mm_ev[31]
        prev[("x0", buf)] = m_cast
        ctx.tap("a1_x0bf_c%d" % c, B["x0bf_sb"][:], deps=[m_cast])
        ctx.tap("a1_n1T_c%d" % c, B["n1T_sb"][:], deps=[m_n1T])
        ctx.tap("a1_sum_c%d" % c, B["sum_s"][:], deps=[m_sm])
        ctx.tap("a1_mu0_c%d" % c, B["mu0_s"][:], deps=[m_sm])
        ctx.tap("a1_rstd0_c%d" % c, B["rstd0_s"][:], deps=[m_sm])
        ctx.tap("a1_var_c%d" % c, B["sm_a"][:], deps=[m_sm])
        ctx.tap("a1_ssq_c%d" % c, B["ssq_s"][:], deps=[m_sm])
        ctx.tap("a1_rs01_c%d" % c, B["rs01_s"][:], deps=[m_sm])
        ctx.tap("a1_uact_c%d" % c, B["uact_sb"][:], deps=[m_uact[15]])
        ctx.tap("a1_z_c%d" % c, B["zs_sb"][:], deps=[m_mm_ev[31]])
        ctx.tap("a1_B_c%d" % c, B["Bst_sb"][:], deps=[m_eB])
        ctx.tap("a1_dtr_c%d" % c, B["dtrst_sb"][:], deps=[m_eD])


def _phase_a2(ctx, H, B, ms):
    r = ctx.run
    ms["onehot"] = r("gpsimd", "dma_start", out=B["onehot_bf"][:], in_=H["onehot"][:],
                     dma=True)
    m_wdt = r("gpsimd", "dma_start", out=B["w_dt_sb"][:], in_=H["w_dt_T"][:], dma=True)
    m_wout = []
    for kg in range(16):
        m_wout.append(r("gpsimd", "dma_start",
                        out=B["w_out_sb"][:, kg * D:(kg + 1) * D],
                        in_=H["w_out_T"][kg * 128:(kg + 1) * 128, :], dma=True))
    prev = {}
    for c in range(LIMIT_CHUNKS or NCHUNK):
        g0 = c * 4
        batch_first = (c % 4 == 0)
        m_lu = r("sync", "dma_start",
                 out=B["uact2_sb"][:].rearrange("p (dc t) -> p dc t", t=TL),
                 in_=H["sp_uact"][c], deps=[prev.get("uact_free")], dma=True)
        m_lz = r("sync", "dma_start",
                 out=B["zs2_sb"][:].rearrange("p (dc t) -> p dc t", t=TL),
                 in_=H["sp_z"][c], deps=[prev.get("z_free")], dma=True)
        m_lx = r("sync", "dma_start",
                 out=B["x0b_sb"][:].rearrange("p (tt d) -> p tt d", d=D),
                 in_=H["xs"][c * TL:(c + 1) * TL, :].rearrange("(tt p) d -> p tt d", p=128),
                 deps=[prev.get("x0_free")], dma=True)
        m_ldt = r("sync", "dma_start", out=B["dtr2_sb"][:], in_=H["sp_dtr"][c],
                  deps=[prev.get("dtr_free")], dma=True)
        m_lB = r("gpsimd", "dma_start", out=B["Bbf_sb"][:], in_=H["sp_B"][c],
                 deps=[prev.get("Bbf_free")], dma=True)
        m_lC = r("gpsimd", "dma_start", out=B["Cbf_sb"][:], in_=H["sp_C"][c],
                 deps=[prev.get("Cbf_free")], dma=True)
        if batch_first and c > 0:
            ms["st0"] = r("gpsimd", "memset", B["state_sb"][:], 0.0,
                          deps=[prev.get("scan_all")], inc=1)
        m_bb = {}
        m_cb = {}
        m_bc_ev = prev.get("bc_ev")
        for n in range(DS):
            m_mm = r("tensor", "matmul", B["psB_bc"][:],
                     B["onehot_bf"][:, n * 128:(n + 1) * 128], B["Bbf_sb"][:],
                     deps=[m_lB, ms["onehot"], m_bc_ev], start=True, stop=True, inc=1)
            m_bc_ev = r("scalar", "activation", B["Bb_sb"][:, n * TL:(n + 1) * TL],
                        B["psB_bc"][:], AF.Copy, deps=[m_mm, prev.get("Bb_free")], inc=1)
            m_bb[n] = m_bc_ev
        for n in range(DS):
            m_mm = r("tensor", "matmul", B["psB_bc"][:],
                     B["onehot_bf"][:, n * 128:(n + 1) * 128], B["Cbf_sb"][:],
                     deps=[m_lC, m_bc_ev], start=True, stop=True, inc=1)
            m_bc_ev = r("scalar", "activation", B["Cb_sb"][:, n * TL:(n + 1) * TL],
                        B["psB_bc"][:], AF.Copy, deps=[m_mm, prev.get("Cb_free")], inc=1)
            m_cb[n] = m_bc_ev
        prev["Bbf_free"] = m_bb[DS - 1]
        prev["Cbf_free"] = m_cb[DS - 1]
        prev["bc_ev"] = m_bc_ev

        m_gate = {}
        m_sp_done = None
        m_scan_last = None
        for dc in range(16):
            db = dc % 2
            m_dmm = r("tensor", "matmul", B["psB_dt"][:],
                      B["w_dt_sb"][:, dc * 128:(dc + 1) * 128], B["dtr2_sb"][:],
                      deps=[m_ldt, m_wdt, prev.get("psdt_free")], start=True, stop=True,
                      inc=1)
            r("scalar", "activation", B["sp1_sb"][:], B["psB_dt"][:], AF.Abs,
              bias=B["dtb_sb"][:, dc:dc + 1], deps=[m_dmm, ms["dtb_sb"],
                                                    prev.get(("sp", 0))])
            r("scalar", "activation", B["sp1_sb"][:], B["sp1_sb"][:], AF.Exp, scale=-1.0)
            r("scalar", "activation", B["sp1_sb"][:], B["sp1_sb"][:], AF.Ln, bias=1.0)
            m_sp = r("scalar", "activation", B["sp2_sb"][:], B["psB_dt"][:], AF.Relu,
                     bias=B["dtb_sb"][:, dc:dc + 1], deps=[prev.get(("sp", 1))], inc=1)
            prev["psdt_free"] = m_sp
            dt_t = B["dt_sb"][:, db * TL:(db + 1) * TL]
            m_dt = r("vector", "tensor_tensor", dt_t, B["sp1_sb"][:], B["sp2_sb"][:], OP.add,
                     deps=[m_sp, prev.get(("dt", db))], inc=1)
            prev[("sp", 0)] = m_dt
            prev[("sp", 1)] = m_dt
            du_t = B["du_sb"][:, db * TL:(db + 1) * TL]
            m_du = r("vector", "tensor_tensor", du_t, dt_t, B["uact2_sb"][:, dc * TL:(dc + 1) * TL],
                     OP.mult, deps=[m_lu, prev.get(("du", db))], inc=1)
            m_scan_last = None
            for grp in range(8):
                ab = grp % 2
                m_a = None
                for j in range(2):
                    n = grp * 2 + j
                    m_a = r("scalar", "activation",
                            B["aBuf_sb"][:, (ab * 2 + j) * TL:(ab * 2 + j + 1) * TL],
                            dt_t, AF.Exp, scale=B["A_sb"][:, dc * DS + n:dc * DS + n + 1],
                            deps=[m_dt, ms["A"], prev.get(("scangrp", ab))],
                            inc=(1 if j == 1 else 0))
                m_sc = None
                for j in range(2):
                    n = grp * 2 + j
                    q = n % 4
                    hb = B["hb_sb"][:, (n % 2) * TL:((n % 2) + 1) * TL]
                    bs = B["bs_sb"][:, (n % 2) * TL:((n % 2) + 1) * TL]
                    r("vector", "tensor_tensor", bs, du_t, B["Bb_sb"][:, n * TL:(n + 1) * TL],
                      OP.mult, deps=[m_du, m_bb[n], m_a])
                    r("vector", "tensor_tensor_scan", hb,
                      B["aBuf_sb"][:, (ab * 2 + j) * TL:(ab * 2 + j + 1) * TL], bs,
                      B["state_sb"][:, dc * DS + n:dc * DS + n + 1], OP.mult, OP.add,
                      deps=[ms["st0"]])
                    if n < 4:
                        m_sc = r("vector", "tensor_tensor",
                                 B["yp_sb"][:, q * TL:(q + 1) * TL], hb,
                                 B["Cb_sb"][:, n * TL:(n + 1) * TL], OP.mult,
                                 deps=[m_cb[n]], inc=(1 if j == 1 else 0))
                    else:
                        yt = B["yt_sb"][:]
                        r("vector", "tensor_tensor", yt, hb,
                          B["Cb_sb"][:, n * TL:(n + 1) * TL], OP.mult, deps=[m_cb[n]])
                        m_sc = r("vector", "tensor_tensor",
                                 B["yp_sb"][:, q * TL:(q + 1) * TL],
                                 B["yp_sb"][:, q * TL:(q + 1) * TL], yt, OP.add,
                                 inc=(1 if j == 1 else 0))
                    r("vector", "tensor_scalar",
                      B["state_sb"][:, dc * DS + n:dc * DS + n + 1],
                      hb[:, TL - 1:TL], 0.0, None, OP.add)
                prev[("scangrp", ab)] = m_sc
                m_scan_last = m_sc
            prev[("dt", db)] = m_scan_last
            prev[("du", db)] = m_scan_last
            r("vector", "tensor_tensor", B["yp_sb"][:, 0:TL], B["yp_sb"][:, 0:TL],
              B["yp_sb"][:, TL:2 * TL], OP.add)
            r("vector", "tensor_tensor", B["yp_sb"][:, 2 * TL:3 * TL],
              B["yp_sb"][:, 2 * TL:3 * TL], B["yp_sb"][:, 3 * TL:4 * TL], OP.add)
            r("vector", "tensor_tensor", B["yp_sb"][:, 0:TL], B["yp_sb"][:, 0:TL],
              B["yp_sb"][:, 2 * TL:3 * TL], OP.add)
            r("vector", "scalar_tensor_tensor", B["gt_sb"][:],
              B["uact2_sb"][:, dc * TL:(dc + 1) * TL], B["D_sb"][:, dc:dc + 1],
              B["yp_sb"][:, 0:TL], OP.mult, OP.add, deps=[ms["D_sb"]])
            m_g = r("vector", "tensor_tensor", B["uact2_sb"][:, dc * TL:(dc + 1) * TL],
                    B["gt_sb"][:], B["zs2_sb"][:, dc * TL:(dc + 1) * TL], OP.mult,
                    deps=[m_lz], inc=1)
            m_gate[dc] = m_g
        prev["scan_all"] = m_scan_last
        prev["uact_free"] = None
        prev["z_free"] = m_gate[15]
        prev["dtr_free"] = m_gate[15]
        prev["Bb_free"] = m_gate[15]
        prev["Cb_free"] = m_gate[15]
        ctx.tap("a2_gated_c%d" % c, B["uact2_sb"][:], deps=[m_gate[15]])
        m_op_ev = {}
        for mt in range(8):
            ps = B["psB_op0"] if mt % 2 == 0 else B["psB_op1"]
            m_mm = None
            for k in range(16):
                m_mm = r("tensor", "matmul", ps[:],
                         B["w_out_sb"][:, k * D + mt * 128: k * D + (mt + 1) * 128],
                         B["uact2_sb"][:, k * TL:(k + 1) * TL],
                         deps=([m_gate[k], m_wout[k]] +
                               ([m_op_ev.get(mt - 2,
                                             prev.get("psop%d_free" % (mt % 2)))]
                                if k == 0 else [])),
                         start=(k == 0), stop=(k == 15), inc=(1 if k == 15 else 0))
            m_op_ev[mt] = r("scalar", "activation", B["mT_sb"][:, mt * TL:(mt + 1) * TL],
                            ps[:], AF.Copy, deps=[m_mm, prev.get("mT_free")], inc=1)
        m_mT = m_op_ev[7]
        prev["uact_free"] = m_mT
        m_h = {}
        m_ssq2 = None
        for tt in range(4):
            ps = B["psB_tr0"] if tt % 2 == 0 else B["psB_tr1"]
            m_mm = None
            for dg in range(8):
                m_mm = r("tensor", "matmul", ps[:, dg * 128:(dg + 1) * 128],
                         B["mT_sb"][:, dg * TL + tt * 128: dg * TL + (tt + 1) * 128],
                         B["ident_bf"][:],
                         deps=[m_op_ev[dg], prev.get(("htr", tt % 2))],
                         start=True, stop=True, inc=(1 if dg == 7 else 0))
            g = g0 + tt
            r("vector", "scalar_tensor_tensor", B["h_sb"][:, tt * D:(tt + 1) * D],
              B["x0b_sb"][:, tt * D:(tt + 1) * D], B["rstd0_s"][:, g:g + 1], ps[:],
              OP.mult, OP.add, deps=[m_mm, m_lx, prev.get("h_free")])
            m_hh = r("vector", "tensor_scalar", B["h_sb"][:, tt * D:(tt + 1) * D],
                     B["h_sb"][:, tt * D:(tt + 1) * D], B["b00_s"][:, g:g + 1], None,
                     OP.add, inc=1)
            prev[("htr", tt % 2)] = m_hh
            m_h[tt] = m_hh
            g = g0 + tt
            m_ssq2 = r("scalar", "activation", B["trash_bf"][:],
                       B["h_sb"][:, tt * D:(tt + 1) * D], AF.Square,
                       accum_out=B["ssq_s"][:, g:g + 1], deps=[m_hh], inc=1)
        prev["mT_free"] = m_h[3]
        ctx.tap("a2_h_c%d" % c, B["h_sb"][:], deps=[m_h[3]])
        m_sph = r("gpsimd", "dma_start", out=H["sp_h"][c],
                  in_=B["h_sb"][:].rearrange("p (tt d) -> p tt d", d=D),
                  deps=[m_h[3]], dma=True)
        prev["h_free"] = m_sph
        sl = slice(g0, g0 + 4)
        ctx.autodrain = True
        r("vector", "tensor_scalar", B["sm_a"][:, sl], B["ssq_s"][:, sl], 1.0 / D, None,
          OP.mult, deps=[m_ssq2])
        m_v = r("vector", "tensor_scalar", B["sm_a"][:, sl], B["sm_a"][:, sl], EPS, None,
                OP.add, inc=1)
        ctx.autodrain = False
        r("scalar", "activation", B["sm_a"][:, sl], B["sm_a"][:, sl], AF.Ln, deps=[m_v])
        m_r2 = r("scalar", "activation", B["rstd2_s"][:, sl], B["sm_a"][:, sl], AF.Exp,
                 scale=-0.5, inc=1)
        m_cast = None
        for tt in range(4):
            g = g0 + tt
            m_cast = r("scalar", "activation", B["h2bf_sb"][:, tt * D:(tt + 1) * D],
                       B["h_sb"][:, tt * D:(tt + 1) * D], AF.Copy,
                       scale=B["rstd2_s"][:, g:g + 1],
                       deps=[m_r2, prev.get("h2bf_free")], inc=(1 if tt == 3 else 0))
        m_n2ev = {}
        for dg in range(8):
            ps = B["psB_op0"] if dg % 2 == 0 else B["psB_op1"]
            m_mm = None
            for tt in (3, 2, 1, 0):
                m_mm = r("tensor", "matmul", ps[:, tt * 128:(tt + 1) * 128],
                         B["h2bf_sb"][:, tt * D + dg * 128: tt * D + (dg + 1) * 128],
                         B["ident_bf"][:],
                         deps=[m_cast, m_n2ev.get(dg - 2),
                               m_op_ev[6] if dg == 0 else None,
                               m_op_ev[7] if dg == 1 else None],
                         start=True, stop=True, inc=(1 if tt == 0 else 0))
            m_n2ev[dg] = r("scalar", "activation", B["n2Tst_sb"][:, dg * TL:(dg + 1) * TL],
                           ps[:], AF.Copy, deps=[m_mm, prev.get("n2Tst_free")], inc=1)
        prev["h2bf_free"] = m_n2ev[7]
        prev["psop0_free"] = m_n2ev[6]
        prev["psop1_free"] = m_n2ev[7]
        m_spn2 = r("scalar", "dma_start",
                   out=H["sp_n2T"][:, :, c * TL:(c + 1) * TL],
                   in_=B["n2Tst_sb"][:].rearrange("p (dg t) -> p dg t", t=TL),
                   deps=[m_n2ev[7]], dma=True)
        prev["n2Tst_free"] = m_spn2
        prev["x0_free"] = m_cast


def _phase_b1(ctx, H, B):
    r = ctx.run
    m_n2 = []
    for dg in range(8):
        m_n2.append(r("sync", "dma_start", out=B["n2Ta_sb"][:, dg * NT:(dg + 1) * NT],
                      in_=H["sp_n2T"][:, dg, :], dma=True))
    prev = {}
    for mt in range(32):
        sb = mt % 2
        w1s = B["w1s_sb"][:, sb * 1024:(sb + 1) * 1024]
        w2s = B["w2s_sb"][:, sb * 1024:(sb + 1) * 1024]
        m_l1 = r("gpsimd", "dma_start",
                 out=w1s.rearrange("p (kg m) -> p kg m", m=128),
                 in_=H["w1_T"][:, mt * 128:(mt + 1) * 128].rearrange(
                     "(kg p) m -> p kg m", p=128),
                 deps=[prev.get(("w1", sb))], dma=True)
        m_l2 = r("gpsimd", "dma_start",
                 out=w2s.rearrange("p (kg m) -> p kg m", m=128),
                 in_=H["w2_T"][:, mt * 128:(mt + 1) * 128].rearrange(
                     "(kg p) m -> p kg m", p=128),
                 deps=[prev.get(("w2", sb))], dma=True)
        m_f1 = m_f2 = None
        for kg in range(8):
            m_f1 = r("vector", "tensor_scalar", w1s[:, kg * 128:(kg + 1) * 128],
                     w1s[:, kg * 128:(kg + 1) * 128], B["n2w_sb"][:, kg:kg + 1], None,
                     OP.mult, deps=[m_l1], inc=(1 if kg == 7 else 0))
            m_f2 = r("vector", "tensor_scalar", w2s[:, kg * 128:(kg + 1) * 128],
                     w2s[:, kg * 128:(kg + 1) * 128], B["n2w_sb"][:, kg:kg + 1], None,
                     OP.mult, deps=[m_l2], inc=(1 if kg == 7 else 0))
        gb = mt % 2
        gst = B["gst_sb"][:, gb * NT:(gb + 1) * NT]
        m_gd = {}
        for tg in range(8):
            ps1 = B["psC_1"] if tg % 2 == 0 else B["psC_3"]
            ps2 = B["psC_2"] if tg % 2 == 0 else B["psC_4"]
            m_mm1 = m_mm2 = None
            for kg in range(8):
                m_mm1 = r("tensor", "matmul", ps1[:],
                          w1s[:, kg * 128:(kg + 1) * 128],
                          B["n2Ta_sb"][:, kg * NT + tg * TL: kg * NT + (tg + 1) * TL],
                          deps=([m_f1, m_n2[kg]] +
                                ([prev.get(("ps", tg % 2))] if kg == 0 else [])),
                          start=(kg == 0), stop=(kg == 7), inc=(1 if kg == 7 else 0))
            for kg in range(8):
                m_mm2 = r("tensor", "matmul", ps2[:],
                          w2s[:, kg * 128:(kg + 1) * 128],
                          B["n2Ta_sb"][:, kg * NT + tg * TL: kg * NT + (tg + 1) * TL],
                          deps=([m_f2] if kg == 0 else ()),
                          start=(kg == 0), stop=(kg == 7), inc=(1 if kg == 7 else 0))
            m_s = r("scalar", "activation", B["sl_sb"][:], ps1[:], AF.Silu,
                    deps=[m_mm1, prev.get("sl_free")], inc=1)
            m_w2 = r("scalar", "activation", B["w2e_sb"][:], ps2[:], AF.Copy,
                     deps=[m_mm2, prev.get("w2e_free")], inc=1)
            gp = tg % 2
            m_g = r("vector", "tensor_tensor",
                    B["gtmp_sb"][:, gp * TL:(gp + 1) * TL],
                    B["sl_sb"][:], B["w2e_sb"][:], OP.mult,
                    deps=[m_s, m_w2, prev.get(("gtmp", gp))], inc=1)
            m_gc = r("scalar", "activation", gst[:, tg * TL:(tg + 1) * TL],
                     B["gtmp_sb"][:, gp * TL:(gp + 1) * TL], AF.Copy,
                     deps=[m_g, prev.get(("gst", gb))], inc=1)
            prev[("gtmp", gp)] = m_gc
            prev["sl_free"] = m_g
            prev["w2e_free"] = m_g
            prev[("ps", tg % 2)] = m_g
            m_gd[tg] = m_gc
        prev[("w1", sb)] = m_gd[7]
        prev[("w2", sb)] = m_gd[7]
        m_sp = None
        for tg8 in range(8):
            for half in range(2):
                m_sp = r("gpsimd", "dma_start",
                         out=H["sp_g2"][2 * tg8 + half][:, mt, :],
                         in_=gst[:, tg8 * TL + half * TG: tg8 * TL + (half + 1) * TG],
                         deps=[m_gd[tg8]], dma=True)
        prev[("gst", gb)] = m_sp


def _phase_b2(ctx, H, B):
    r = ctx.run
    m_wr = r("gpsimd", "dma_start",
             out=B["wr_sb"][:].rearrange("p (kg n) -> p kg n", n=NE),
             in_=H["wr"].rearrange("(kg p) n -> p kg n", p=128), dma=True)
    r("gpsimd", "dma_start", out=B["b2_sb"][:], in_=H["b2e"][:], dma=True)
    m_w3 = []
    for kg in range(32):
        m_w3.append(r("gpsimd", "dma_start", out=B["w3_sb"][:, kg * D:(kg + 1) * D],
                      in_=H["w3_T"][kg * 128:(kg + 1) * 128, :], dma=True))
    prev = {}
    for tg in range(LIMIT_TG or NTG):
        c = tg // 2
        tt0 = 2 * (tg % 2)
        gld = B["gld_sb"][:].rearrange("p (kg t) -> p kg t", t=TG)
        m_lg = [r("sync", "dma_start", out=gld,
                  in_=H["sp_g2"][tg], deps=[prev.get("gld_free")], dma=True)]
        m_lx = r("sync", "dma_start",
                 out=B["x0c_sb"][:].rearrange("p (tt d) -> p tt d", d=D),
                 in_=H["xs"][tg * TG:(tg + 1) * TG, :].rearrange("(tt p) d -> p tt d", p=128),
                 deps=[prev.get("x0c_free")], dma=True)
        m_lh = r("sync", "dma_start",
                 out=B["hc_sb"][:].rearrange("p (tt d) -> p tt d", d=D),
                 in_=H["sp_h"][c][:, tt0:tt0 + 2, :],
                 deps=[prev.get("hc_free")], dma=True)
        if DEBUG_SCRATCH:
            r("gpsimd", "dma_start", out=H["dbg_gld"][tg], in_=B["gld_sb"][:],
              deps=m_lg, dma=True)
        m_pev = {}
        for rnd in range(2):
            m_mmr = {}
            for mti in range(4):
                mt = 4 * rnd + mti
                dst = B["psD_p"][:, 2 * mti * TG:(2 * mti + 1) * TG]
                m_mm = None
                for kg in range(32):
                    m_mm = r("tensor", "matmul", dst,
                             B["w3_sb"][:, kg * D + mt * 128: kg * D + (mt + 1) * 128],
                             gld[:, kg, :],
                             deps=(m_lg + [m_w3[kg]] +
                                   ([prev.get("psp_free"),
                                     m_pev.get(4 * (rnd - 1) + mti)] if kg == 0 else [])),
                             start=(kg == 0), stop=(kg == 31),
                             inc=(1 if kg == 31 else 0))
                m_mmr[mti] = m_mm
            for mti in range(4):
                mt = 4 * rnd + mti
                m_pev[mt] = r("scalar", "activation",
                              B["pT_sb"][:, mt * TG:(mt + 1) * TG],
                              B["psD_p"][:, 2 * mti * TG:(2 * mti + 1) * TG], AF.Copy,
                              deps=[m_mmr[mti], prev.get("pT_free")], inc=1)
        if DEBUG_SCRATCH:
            m_dbgp = r("gpsimd", "dma_start", out=H["dbg_pT"][tg], in_=B["pT_sb"][:],
                       deps=[m_pev[mt2] for mt2 in range(8)], dma=True)
            prev["pT_free"] = m_dbgp
        m_x1 = {}
        for tt in range(2):
            g3 = 2 * tg + tt
            m_mm = None
            for dg in range(8):
                m_mm = r("tensor", "matmul", B["psD_t"][:, dg * 128:(dg + 1) * 128],
                         B["pT_sb"][:, dg * TG + tt * 128: dg * TG + (tt + 1) * 128],
                         B["ident_bf"][:],
                         deps=[m_pev[dg], prev.get("pst")],
                         start=True, stop=True, inc=(1 if dg == 7 else 0))
            r("vector", "tensor_tensor", B["x0c_sb"][:, tt * D:(tt + 1) * D],
              B["x0c_sb"][:, tt * D:(tt + 1) * D], B["hc_sb"][:, tt * D:(tt + 1) * D],
              OP.add, deps=[m_lx, m_lh])
            m_tt = r("vector", "tensor_tensor", B["x1_sb"][:, tt * D:(tt + 1) * D],
                     B["x0c_sb"][:, tt * D:(tt + 1) * D], B["psD_t"][:], OP.add,
                     deps=[m_mm, prev.get("x1_free")], inc=1)
            m_xx = r("scalar", "activation", B["trash_bf"][:],
                     B["x1_sb"][:, tt * D:(tt + 1) * D], AF.Copy,
                     accum_out=B["sum3_s"][:, g3:g3 + 1], deps=[m_tt], inc=1)
            prev["pst"] = m_xx
            ctx.tap("b2_x0h_t%d_%d" % (tg, tt), B["x0c_sb"][:, tt * D:(tt + 1) * D],
                    deps=[m_xx])
            m_x1[tt] = m_xx
            m_sq = r("scalar", "activation", B["trash_bf"][:],
                     B["x1_sb"][:, tt * D:(tt + 1) * D],
                     AF.Square, accum_out=B["ssq3_s"][:, g3:g3 + 1],
                     deps=[m_xx],
                     inc=(1 if tt == 1 else 0))
        prev["x0c_free"] = m_x1[1]
        prev["hc_free"] = m_x1[1]
        prev["pT_free"] = m_x1[1]
        ctx.tap("b2_x1_t%d" % tg, B["x1_sb"][:], deps=[m_x1[1]])
        m_spx = r("gpsimd", "dma_start", out=H["sp_x1"][tg],
                  in_=B["x1_sb"][:].rearrange("p (tt d) -> p tt d", d=D),
                  deps=[m_x1[1]], dma=True)
        ctx.drain("vector")
        ctx.autodrain = True
        sl = slice(2 * tg, 2 * tg + 2)
        r("vector", "tensor_scalar", B["sm_b"][:, sl], B["sum3_s"][:, sl], 1.0 / D, None,
          OP.mult, deps=[m_x1[1], m_sq])
        r("vector", "tensor_scalar", B["sm_a"][:, sl], B["ssq3_s"][:, sl], 1.0 / D, None,
          OP.mult)
        r("vector", "scalar_tensor_tensor", B["b03_s"][:, sl], B["sm_b"][:, sl],
          -1.0, B["sm_b"][:, sl], OP.mult, OP.mult)
        r("vector", "tensor_tensor", B["sm_a"][:, sl], B["sm_a"][:, sl], B["b03_s"][:, sl],
          OP.add)
        m_v = r("vector", "tensor_scalar", B["sm_a"][:, sl], B["sm_a"][:, sl], EPS, None,
                OP.add, inc=1)
        r("scalar", "activation", B["sm_a"][:, sl], B["sm_a"][:, sl], AF.Ln, deps=[m_v])
        m_r3 = r("scalar", "activation", B["rstd3_s"][:, sl], B["sm_a"][:, sl], AF.Exp,
                 scale=-0.5, inc=1)
        m_b3 = r("vector", "scalar_tensor_tensor", B["b03_s"][:, sl], B["sm_b"][:, sl],
                 -1.0, B["rstd3_s"][:, sl], OP.mult, OP.mult, deps=[m_r3], inc=1)
        ctx.autodrain = False
        m_cast = None
        for tt in range(2):
            g3 = 2 * tg + tt
            m_cast = r("scalar", "activation", B["u3bf_sb"][:, tt * D:(tt + 1) * D],
                       B["x1_sb"][:, tt * D:(tt + 1) * D], AF.Identity,
                       scale=B["rstd3_s"][:, g3:g3 + 1], bias=B["b03_s"][:, g3:g3 + 1],
                       deps=[m_b3, prev.get("u3bf_free")], inc=(1 if tt == 1 else 0))
        prev["x1_free"] = m_cast
        for half in range(2):
            m_mm = None
            for dgi in range(4):
                dg = half * 4 + dgi
                for tt in range(2):
                    m_mm = r("tensor", "matmul",
                             B["psD_u"][:, dgi * TG + tt * 128: dgi * TG + (tt + 1) * 128],
                             B["u3bf_sb"][:, tt * D + dg * 128: tt * D + (dg + 1) * 128],
                             B["ident_bf"][:],
                             deps=[m_cast, prev.get("psu")],
                             start=True, stop=True,
                             inc=(1 if (dgi == 3 and tt == 1) else 0))
            m_ue = None
            for dgi in range(4):
                dg = half * 4 + dgi
                m_ue = r("scalar", "activation",
                         B["u3T_sb"][:, dg * NT + tg * TG: dg * NT + (tg + 1) * TG],
                         B["psD_u"][:, dgi * TG:(dgi + 1) * TG], AF.Copy,
                         deps=[m_mm], inc=(1 if dgi == 3 else 0))
            prev["psu"] = m_ue
        prev["u3bf_free"] = prev["psu"]
        m_u3T = prev["psu"]
        m_lgmm = None
        for tt in range(2):
            for kg in range(8):
                m_lgmm = r("tensor", "matmul", B["psD_p"][:, tt * NE: tt * NE + NE],
                           B["u3T_sb"][:, kg * NT + tg * TG + tt * 128:
                                       kg * NT + tg * TG + (tt + 1) * 128],
                           B["wr_sb"][:, kg * NE:(kg + 1) * NE],
                           deps=([m_u3T, m_wr] if (tt == 0 and kg == 0) else ()),
                           start=(kg == 0), stop=(kg == 7),
                           inc=(1 if kg == 7 else 0))
        m_lgv = {}
        for tt in range(2):
            m_lgv[tt] = r("vector", "tensor_tensor", B["lg_sb"][:, tt * NE:(tt + 1) * NE],
                          B["psD_p"][:, tt * NE:tt * NE + NE], B["brb_sb"][:], OP.add,
                          deps=[m_lgmm], inc=1)
        ctx.autodrain = True
        for tt in range(2):
            g3 = 2 * tg + tt
            lg = B["lg_sb"][:, tt * NE:(tt + 1) * NE]
            r("vector", "max", B["m8_sb"][:, tt * 8:(tt + 1) * 8], lg, deps=[m_lgv[tt]])
            ctx.drain("vector")
            r("vector", "max_index", B["i8_sb"][:, tt * 8:(tt + 1) * 8],
              B["m8_sb"][:, tt * 8:(tt + 1) * 8], lg)
            ctx.drain("vector")
            r("vector", "tensor_scalar", B["idxf_sb"][:, tt * 2:(tt + 1) * 2],
              B["i8_sb"][:, tt * 8:tt * 8 + 2], 0.0, None, OP.add)
            m_vv = r("vector", "tensor_scalar", B["nm1_sb"][:, tt:tt + 1],
                     B["m8_sb"][:, tt * 8:tt * 8 + 1], -1.0, None, OP.mult, inc=1)
            m_e = r("scalar", "activation", B["t21_sb"][:, tt:tt + 1],
                    B["m8_sb"][:, tt * 8 + 1:tt * 8 + 2], AF.Exp,
                    bias=B["nm1_sb"][:, tt:tt + 1], deps=[m_vv], inc=1)
            r("vector", "tensor_scalar", B["den_sb"][:, tt:tt + 1],
              B["t21_sb"][:, tt:tt + 1], 1.0, None, OP.add, deps=[m_e])
            r("vector", "reciprocal", B["w1r_sb"][:, tt:tt + 1], B["den_sb"][:, tt:tt + 1])
            r("vector", "tensor_tensor", B["w2r_sb"][:, tt:tt + 1],
              B["t21_sb"][:, tt:tt + 1], B["w1r_sb"][:, tt:tt + 1], OP.mult)
            r("vector", "tensor_scalar", B["mk1_sb"][:], B["iota16_sb"][:],
              B["idxf_sb"][:, tt * 2:tt * 2 + 1], B["w1r_sb"][:, tt:tt + 1],
              OP.is_equal, OP.mult)
            r("vector", "tensor_scalar", B["mk2_sb"][:], B["iota16_sb"][:],
              B["idxf_sb"][:, tt * 2 + 1:tt * 2 + 2], B["w2r_sb"][:, tt:tt + 1],
              OP.is_equal, OP.mult)
            r("vector", "tensor_tensor", B["wfull_sb"][:, g3 * NE:(g3 + 1) * NE],
              B["mk1_sb"][:], B["mk2_sb"][:], OP.add)
            m_wf = r("vector", "tensor_scalar", B["wfbf_sb"][:],
                     B["wfull_sb"][:, g3 * NE:(g3 + 1) * NE], 1.0, None, OP.mult, inc=1)
            m_tr = r("tensor", "matmul", B["psD_u"][0:16, 0:128],
                     B["wfbf_sb"][:], B["ident_bf"][:],
                     deps=[m_wf, prev.get("wrt_ev"), prev.get(("psu", 0))],
                     start=True, stop=True, inc=1)
            m_wrt = r("scalar", "activation", B["wroute_sb"][:, g3 * 128:(g3 + 1) * 128],
                      B["psD_u"][0:16, 0:128], AF.Copy, deps=[m_tr], inc=1)
            prev["wrt_ev"] = m_wrt
            prev[("psu", 0)] = m_wrt
        ctx.autodrain = False
        if tg == (LIMIT_TG or NTG) - 1:
            ctx.tap("b2_u3T", B["u3T_sb"][:], deps=[prev["psu"], prev["wrt_ev"]])
        ctx.tap("b2_wfull_t%d" % tg, B["wfull_sb"][:], deps=[prev["wrt_ev"]])
        prev["gld_free"] = m_pev[7]
        prev["psp_free"] = m_lgv[1]


def _phase_c(ctx, H, B):
    r = ctx.run
    prev = {}

    def load_e(e, which):
        msd = []
        if which in ("W1", "both"):
            for half in range(2):
                dst = B["W1X_sb"] if half == 0 else B["W1Y_sb"]
                msd.append(r("gpsimd", "dma_start",
                             out=dst[:].rearrange("p (kk m) -> p kk m", m=HID),
                             in_=H["W1e"][e][half * 512:(half + 1) * 512, :].rearrange(
                                 "(kk p) m -> p kk m", p=128),
                             deps=[prev.get("W1_free")], dma=True))
        if which in ("W2", "both"):
            for half in range(2):
                dst = B["W2X_sb"] if half == 0 else B["W2Y_sb"]
                msd.append(r("gpsimd", "dma_start",
                             out=dst[:].rearrange("p (kk m) -> p kk m", m=D),
                             in_=H["W2e"][e][half * 512:(half + 1) * 512, :].rearrange(
                                 "(kk p) m -> p kk m", p=128),
                             deps=[prev.get("W2_free")], dma=True))
        return msd

    def load_vec_e(e):
        m1 = r("gpsimd", "dma_start", out=B["g1e_sb"][:], in_=H["g1b"][e],
               deps=[prev.get("g1_free")], dma=True)
        m2 = r("gpsimd", "dma_start", out=B["be1e_sb"][:], in_=H["be1b"][e],
               deps=[prev.get("be1_free")], dma=True)
        m3 = r("gpsimd", "dma_start", out=B["b1e_sb"][:], in_=H["b1b"][e],
               deps=[prev.get("b1_free")], dma=True)
        return [m1, m2, m3]

    ctx.tap("c_u3T", B["u3T_sb"][:])
    for nm in ("sum3_s", "ssq3_s", "rstd3_s", "b03_s"):
        ctx.tap("c_" + nm, B[nm][:])
    mW = load_e(0, "both")
    mV = load_vec_e(0)
    for e in range(LIMIT_E or NE):
        m_sb1 = r("vector", "tensor_reduce", B["Sb1_sb"][:], B["b1e_sb"][:], AX.X, OP.add,
                  deps=[mV[2]], inc=1)
        ctx.drain("vector")
        m_w1_last = None
        m_w2_last = None
        m_fadd_all = None
        for tg in range(LIMIT_TG or NTG):
            m_gelu = {}
            for tt in range(2):
                hb2 = tt % 2
                m_mm = None
                for kg in range(8):
                    wbuf = B["W1X_sb"] if kg < 4 else B["W1Y_sb"]
                    kk = kg % 4
                    for nh in range(2):
                        m_mm = r("tensor", "matmul", B["psE_w1"][:, nh * 512:(nh + 1) * 512],
                                 B["u3T_sb"][:, kg * NT + tg * TG + tt * 128:
                                             kg * NT + tg * TG + (tt + 1) * 128],
                                 wbuf[:, kk * HID + nh * 512: kk * HID + (nh + 1) * 512],
                                 deps=(mW + [prev.get("psw1_free")] if kg == 0 and nh == 0
                                       else ()),
                                 start=(kg == 0), stop=(kg == 7),
                                 inc=(1 if (kg == 7 and nh == 1) else 0))
                m_w1_last = m_mm
                hch = B["hch_sb"][:, hb2 * HID:(hb2 + 1) * HID]
                m_ev = r("scalar", "activation", hch, B["psE_w1"][:], AF.Copy,
                         accum_out=B["sums_sb"][:, tt:tt + 1],
                         deps=[m_mm, prev.get(("hch", hb2))], inc=1)
                prev["psw1_free"] = m_ev
                m_b1 = r("gpsimd", "tensor_tensor", hch, hch, B["b1e_sb"][:], OP.add,
                         deps=[m_ev, mV[2]], inc=1)
                m_sq = r("scalar", "activation",
                         B["trash_bf"][:], hch, AF.Square,
                         accum_out=B["ssqE_sb"][:, tt:tt + 1], deps=[m_b1], inc=1)
                ctx.autodrain = True
                ss = slice(tt, tt + 1)
                r("vector", "tensor_tensor", B["muE_sb"][:, ss], B["sums_sb"][:, ss],
                  B["Sb1_sb"][:], OP.add, deps=[m_ev, m_sb1, m_sq])
                r("vector", "tensor_scalar", B["muE_sb"][:, ss], B["muE_sb"][:, ss],
                  1.0 / HID, None, OP.mult)
                r("vector", "tensor_scalar", B["sm_a"][:, ss], B["ssqE_sb"][:, ss],
                  1.0 / HID, None, OP.mult)
                r("vector", "tensor_tensor", B["sm_b"][:, ss], B["muE_sb"][:, ss],
                  B["muE_sb"][:, ss], OP.mult)
                r("vector", "tensor_tensor", B["sm_a"][:, ss], B["sm_a"][:, ss],
                  B["sm_b"][:, ss], OP.subtract)
                m_vv = r("vector", "tensor_scalar", B["sm_a"][:, ss], B["sm_a"][:, ss],
                         EPS, None, OP.add, inc=1)
                r("scalar", "activation", B["sm_a"][:, ss], B["sm_a"][:, ss], AF.Ln,
                  deps=[m_vv])
                m_rs = r("scalar", "activation", B["rstdE_sb"][:, ss], B["sm_a"][:, ss],
                         AF.Exp, scale=-0.5, inc=1)
                m_be = r("vector", "scalar_tensor_tensor", B["bE_sb"][:, ss],
                         B["muE_sb"][:, ss], -1.0, B["rstdE_sb"][:, ss],
                         OP.mult, OP.mult, deps=[m_rs], inc=1)
                ctx.autodrain = False
                ctx.drain("vector")
                r("vector", "tensor_scalar", hch, hch, B["rstdE_sb"][:, ss],
                  B["bE_sb"][:, ss], OP.mult, OP.add, deps=[m_sq, m_b1, m_be])
                r("vector", "tensor_tensor", hch, hch, B["g1e_sb"][:], OP.mult,
                  deps=[mV[0]])
                m_t5 = r("vector", "tensor_tensor", hch, hch, B["be1e_sb"][:], OP.add,
                         deps=[mV[1]], inc=1)
                gl = B["gl_sb"][:, hb2 * HID:(hb2 + 1) * HID]
                m_gl = r("scalar", "activation", gl, hch, AF.Gelu,
                         deps=[m_t5, prev.get(("gl", hb2))], inc=1)
                prev[("hch", hb2)] = m_gl
                prev[("t5", hb2)] = m_t5
                m_gelu[tt] = m_gl
                g3 = 2 * tg + tt
                dg_t = B["dg_sb"][:, hb2 * 128:(hb2 + 1) * 128]
                m_dg = r("vector", "tensor_scalar", dg_t, B["ident_bf"][:],
                         B["wfull_sb"][:, g3 * NE + e: g3 * NE + e + 1], None, OP.mult,
                         deps=[prev.get(("dg", hb2))], inc=1)
                for half in range(2):
                    m_mm2 = None
                    for hbi in range(4):
                        hb = half * 4 + hbi
                        m_mm2 = r("tensor", "matmul",
                                  B["psE_tr"][:, hbi * 128:(hbi + 1) * 128],
                                  gl[:, hb * 128:(hb + 1) * 128], dg_t,
                                  deps=[m_gl, m_dg, prev.get("pstr_free")],
                                  start=True, stop=True, inc=(1 if hbi == 3 else 0))
                    m_tev = r("scalar", "activation",
                              B["glT_sb"][:].rearrange(
                                  "p (hb t) -> p hb t", t=TG)[:, half * 4:(half + 1) * 4,
                                                              tt * 128:(tt + 1) * 128],
                              B["psE_tr"][:].rearrange("p (hb t) -> p hb t", t=128),
                              AF.Copy, deps=[m_mm2, prev.get("glT_free")], inc=1)
                    prev["pstr_free"] = m_tev
                    prev[("glT", tt, half)] = m_tev
                prev[("dg", hb2)] = prev[("glT", tt, 1)]
            m_glT = prev[("glT", 1, 1)]
            m_mm3 = None
            first_dep = [m_glT, prev.get("pso_free")] + list(mW)
            for mt in range(8):
                if e == 0:
                    m_mm3 = r("tensor", "matmul", B["psE_o"][:, mt * TG:(mt + 1) * TG],
                              B["b2_sb"][:, mt * 128:(mt + 1) * 128],
                              B["wroute_sb"][:, tg * TG:(tg + 1) * TG],
                              deps=(first_dep if mt == 0 else ()),
                              start=True, stop=False, inc=0)
                for kg in range(8):
                    wbuf = B["W2X_sb"] if kg < 4 else B["W2Y_sb"]
                    kk = kg % 4
                    m_mm3 = r("tensor", "matmul", B["psE_o"][:, mt * TG:(mt + 1) * TG],
                              wbuf[:, kk * D + mt * 128: kk * D + (mt + 1) * 128],
                              B["glT_sb"][:, kg * TG:(kg + 1) * TG],
                              deps=(first_dep if (e > 0 and mt == 0 and kg == 0) else ()),
                              start=(kg == 0 and e != 0), stop=(kg == 7),
                              inc=(1 if (mt == 7 and kg == 7) else 0))
            m_w2_last = m_mm3
            prev[("glT_free")] = m_mm3
            m_fa = None
            for mt in range(8):
                fsl = B["f_sb"][:, mt * NT + tg * TG: mt * NT + (tg + 1) * TG]
                if e == 0:
                    m_fa = r("vector", "tensor_scalar", fsl,
                             B["psE_o"][:, mt * TG:(mt + 1) * TG], 0.0, None, OP.add,
                             deps=[m_mm3], inc=(1 if mt == 7 else 0))
                else:
                    m_fa = r("vector", "tensor_tensor", fsl,
                             B["psE_o"][:, mt * TG:(mt + 1) * TG], fsl, OP.add,
                             deps=[m_mm3], inc=(1 if mt == 7 else 0))
            prev["pso_free"] = m_fa
            m_fadd_all = m_fa
        if e + 1 < NE:
            prev["W1_free"] = m_w1_last
            prev["W2_free"] = m_w2_last
            prev["g1_free"] = prev[("t5", 1)]
            prev["be1_free"] = prev[("t5", 1)]
            prev["b1_free"] = prev[("t5", 1)]
            mW = load_e(e + 1, "both")
            mV = load_vec_e(e + 1)


def _phase_fin(ctx, H, B, out_h):
    r = ctx.run
    prev = {}
    for tg in range(LIMIT_TG or NTG):
        m_lx = r("sync", "dma_start",
                 out=B["x1f_sb"][:].rearrange("p (tt d) -> p tt d", d=D),
                 in_=H["sp_x1"][tg], deps=[prev.get("x1f_free")], dma=True)
        m_x2 = {}
        for tt in range(2):
            m_mm = None
            for og in range(8):
                m_mm = r("tensor", "matmul", B["psE_f"][:, og * 128:(og + 1) * 128],
                         B["f_sb"][:, og * NT + tg * TG + tt * 128:
                                   og * NT + tg * TG + (tt + 1) * 128],
                         B["ident_bf"][:],
                         deps=[prev.get("psf")],
                         start=True, stop=True, inc=(1 if og == 7 else 0))
            m_x2[tt] = r("vector", "tensor_tensor", B["x2_sb"][:, tt * D:(tt + 1) * D],
                         B["x1f_sb"][:, tt * D:(tt + 1) * D], B["psE_f"][:], OP.add,
                         deps=[m_mm, m_lx, prev.get("x2_free")], inc=1)
            prev["psf"] = m_x2[tt]
        m_st = r("gpsimd", "dma_start",
                 out=out_h[tg * TG:(tg + 1) * TG, :].rearrange("(tt p) d -> p tt d", p=128),
                 in_=B["x2_sb"][:].rearrange("p (tt d) -> p tt d", d=D),
                 deps=[m_x2[1]], dma=True)
        prev["x2_free"] = m_st
        prev["x1f_free"] = m_x2[1]


_TAPDEFS = {}
for _c in range(NCHUNK):
    _TAPDEFS["a1_x0bf_c%d" % _c] = ((128, 4 * D), BF16)
    _TAPDEFS["a1_n1T_c%d" % _c] = ((128, 8 * TL), BF16)
    _TAPDEFS["a1_sum_c%d" % _c] = ((128, 32), F32)
    _TAPDEFS["a1_mu0_c%d" % _c] = ((128, 32), F32)
    _TAPDEFS["a1_rstd0_c%d" % _c] = ((128, 32), F32)
    _TAPDEFS["a1_var_c%d" % _c] = ((128, 32), F32)
    _TAPDEFS["a1_ssq_c%d" % _c] = ((128, 32), F32)
    _TAPDEFS["a1_rs01_c%d" % _c] = ((128, 32), F32)
    _TAPDEFS["a1_uact_c%d" % _c] = ((128, 16 * TL), BF16)
    _TAPDEFS["a1_z_c%d" % _c] = ((128, 16 * TL), BF16)
    _TAPDEFS["a1_B_c%d" % _c] = ((16, TL), F32)
    _TAPDEFS["a1_dtr_c%d" % _c] = ((64, TL), BF16)
    _TAPDEFS["a2_gated_c%d" % _c] = ((128, 16 * TL), BF16)
    _TAPDEFS["a2_h_c%d" % _c] = ((128, 4 * D), F32)
_TAPDEFS["b2_u3T"] = ((128, 8 * NT), BF16)
_TAPDEFS["dbg_pdt_t0"] = ((128, D), BF16)
_TAPDEFS["dbg_pdt_t1"] = ((128, D), BF16)
for _n in ("sum3_s", "ssq3_s", "rstd3_s", "b03_s"):
    _TAPDEFS["c_" + _n] = ((128, 32), F32)
_TAPDEFS["c_u3T"] = ((128, 8 * NT), BF16)
for _t in range(NTG):
    _TAPDEFS["b2_pT_t%d" % _t] = ((128, 8 * TG), BF16)
    _TAPDEFS["b2_gld_t%d" % _t] = ((128, 32 * TG), BF16)
    for _u in range(2):
        _TAPDEFS["b2_x0h_t%d_%d" % (_t, _u)] = ((128, D), F32)
    _TAPDEFS["b2_x1_t%d" % _t] = ((128, 2 * D), F32)
    _TAPDEFS["b2_wfull_t%d" % _t] = ((128, 32 * NE), F32)


def kernel(**inputs):
    global _last_results
    shared = _prep_shared(inputs)
    x = np.asarray(inputs["L0_attn_x"], dtype=np.float32)
    nc = build()
    in_maps = []
    for c in range(NCORE):
        m = dict(shared)
        m["xs"] = _asc(x[2 * c:2 * c + 2].reshape(NT, D))
        in_maps.append(m)
    res = run_bass_kernel_spmd(nc, in_maps, core_ids=list(range(NCORE)))
    _last_results = res
    out = np.concatenate([np.asarray(res.results[c]["out"]).reshape(2, L, D)
                          for c in range(NCORE)], axis=0)
    return out


# version 02
# speedup vs baseline: 1.0120x; 1.0120x over previous
import numpy as np
from contextlib import ExitStack
import concourse.bass as bass
import concourse.mybir as mybir
from concourse.bass_utils import run_bass_kernel_spmd
from concourse.alu_op_type import AluOpType as OP

F32 = mybir.dt.float32
BF16 = mybir.dt.bfloat16
U32 = mybir.dt.uint32
AF = mybir.ActivationFunctionType
AX = mybir.AxisListType

NCORE = 8
L = 2048
D = 1024
DI = 2048
DS = 16
DTR = 64
DFF = 4096
NE = 16
HID = 1024
NT = 4096
TL = 512
NCHUNK = NT // TL
TG = 256
NTG = NT // TG
EPS = 1e-5

ENGINES = ("sync", "tensor", "scalar", "vector", "gpsimd")
PHASES = ("A1", "A2", "B1", "B2", "C", "FIN")
TAPS = ()
LIMIT_CHUNKS = None
LIMIT_TG = None
DEBUG_SCRATCH = False
LIMIT_E = None
_last_results = None


NDQ = 4


class _Eng:
    __slots__ = ("name", "obj", "sem_c", "dq", "cnt_c", "dq_cnt", "dq_idx", "seen")

    def __init__(self, name, obj, sem_c, dq):
        self.name, self.obj = name, obj
        self.sem_c, self.dq = sem_c, dq
        self.reset()

    def reset(self):
        self.cnt_c = 0
        self.dq_cnt = [0] * NDQ
        self.dq_idx = 0
        self.seen = {}


class Ctx:
    def __init__(self, nc, sems):
        self.nc = nc
        self.engs = {n: _Eng(n, getattr(nc, n), sems[n + "_c"],
                             [sems["%s_d%d" % (n, i)] for i in range(NDQ)])
                     for n in ENGINES}
        self.cur = None
        self.taps = {}
        self.autodrain = False

    def begin(self, ename):
        self.cur = ename
        for e in self.engs.values():
            e.reset()

    def dep(self, ename, m, for_dma=False):
        if m is None:
            return
        e = self.engs[ename]
        dn, dv, kind = m
        if kind == "c" and dn == ename and not for_dma:
            return
        key = (dn, kind)
        if e.seen.get(key, 0) >= dv:
            return
        e.seen[key] = dv
        if self.cur == ename:
            if kind == "d":
                qn, slot = dn
                e.obj.wait_ge(self.engs[qn].dq[slot], dv)
            else:
                e.obj.wait_ge(self.engs[dn].sem_c, dv)

    def run(self, ename, meth, *args, deps=(), inc=0, dma=False, war=False, **kw):
        for m in deps:
            self.dep(ename, m, for_dma=dma)
        e = self.engs[ename]
        if self.autodrain and ename == "vector" and not dma and self.cur == ename:
            e.obj.drain()
        if dma:
            slot = e.dq_idx % NDQ
            e.dq_idx += 1
            if e.dq_cnt[slot]:
                self.dep(ename, ((ename, slot), e.dq_cnt[slot], "d"))
            if self.cur == ename:
                ins = getattr(e.obj, meth)(*args, **kw)
                ins.then_inc(e.dq[slot], 16)
            e.dq_cnt[slot] += 16
            return ((ename, slot), e.dq_cnt[slot], "d")
        if self.cur == ename:
            ins = getattr(e.obj, meth)(*args, **kw)
            if inc:
                if (not war and meth != "drain"
                        and ename in ("scalar", "vector", "gpsimd")):
                    e.obj.drain().then_inc(e.sem_c, inc)
                else:
                    ins.then_inc(e.sem_c, inc)
        e.cnt_c += inc
        if inc:
            return (ename, e.cnt_c, "c")
        return None

    def drain(self, ename):
        if self.cur == ename:
            self.engs[ename].obj.drain()

    def barrier(self):
        ms = {}
        for n in ENGINES:
            e = self.engs[n]
            for slot in range(NDQ):
                if e.dq_cnt[slot]:
                    self.dep(n, ((n, slot), e.dq_cnt[slot], "d"))
            ms[n] = self.run(n, "drain", inc=1)
        for n in ENGINES:
            for o in ENGINES:
                if o != n:
                    self.dep(n, ms[o])
        return ms

    def tap(self, name, src_ap, deps=()):
        if name in self.taps:
            return self.run("gpsimd", "dma_start", out=self.taps[name], in_=src_ap,
                            deps=deps, dma=True)
        return None


def _asc(a):
    return np.ascontiguousarray(a, dtype=np.float32)


def _prep_shared(kw):
    d = {}
    d["w_in_T"] = _asc(kw["L0_attn_in_proj_w"].T)
    xp = np.zeros((DI, 128), dtype=np.float32)
    xpw = np.asarray(kw["L0_attn_x_proj_w"], dtype=np.float32)
    xp[:, 0:16] = xpw[DTR:DTR + DS].T
    xp[:, 32:48] = xpw[DTR + DS:DTR + 2 * DS].T
    xp[:, 64:128] = xpw[0:DTR].T
    d["w_xp_pad"] = _asc(xp)
    d["w_dt_T"] = _asc(kw["L0_attn_dt_proj_w"].T)
    d["w_out_T"] = _asc(kw["L0_attn_out_proj_w"].T)
    d["w1_T"] = _asc(kw["L0_attn_w1"].T)
    d["w2_T"] = _asc(kw["L0_attn_w2"].T)
    d["w3_T"] = _asc(kw["L0_attn_w3"].T)
    d["W1e"] = _asc(kw["L0_ffn_W1"])
    d["W2e"] = _asc(kw["L0_ffn_W2"])
    d["wr"] = _asc(kw["L0_ffn_Wr"])
    d["b2e"] = _asc(kw["L0_ffn_b2"])
    for nm, src in (("g1b", "L0_ffn_g1"), ("be1b", "L0_ffn_be1"), ("b1b", "L0_ffn_b1")):
        v = np.asarray(kw[src], dtype=np.float32)
        d[nm] = _asc(np.broadcast_to(v[:, None, :], (NE, 128, HID)))
    d["A_log_r"] = _asc(np.asarray(kw["L0_attn_A_log"]).reshape(16, 128, DS)
                        .transpose(1, 0, 2))
    d["convw_r"] = _asc(np.asarray(kw["L0_attn_conv_w"]).reshape(DI, 4)
                        .reshape(16, 128, 4).transpose(1, 0, 2))
    d["convb_r"] = _asc(np.asarray(kw["L0_attn_conv_b"]).reshape(16, 128).T)
    d["dtb_r"] = _asc(np.asarray(kw["L0_attn_dt_proj_b"]).reshape(16, 128).T)
    d["D_r"] = _asc(np.asarray(kw["L0_attn_D"]).reshape(16, 128).T)
    d["n1w_pp"] = _asc(np.asarray(kw["L0_attn_norm1_w"]).reshape(8, 128).T)
    d["n2w_pp"] = _asc(np.asarray(kw["L0_attn_norm2_w"]).reshape(8, 128).T)
    d["brb"] = _asc(np.broadcast_to(np.asarray(kw["L0_ffn_br"])[None, :], (128, NE)))
    d["iota16"] = _asc(np.broadcast_to(np.arange(NE, dtype=np.float32)[None, :], (128, NE)))
    d["ident"] = _asc(np.eye(128, dtype=np.float32))
    oh = np.zeros((DS, DS * 128), dtype=np.float32)
    for n in range(DS):
        oh[n, n * 128:(n + 1) * 128] = 1.0
    d["onehot"] = _asc(oh)
    return d


def _alloc(nc, es0):
    B = {}
    E0 = es0.enter_context

    def mk(scope, name, shape, dt):
        B[name] = scope(nc.sbuf_tensor(name, list(shape), dt))

    def mkp(scope, name, shape, dt=F32):
        B[name] = scope(nc.psum_tensor(name, list(shape), dt))

    mk(E0, "ident_bf", (128, 128), BF16)
    mk(E0, "iota16_sb", (128, NE), F32)
    mk(E0, "brb_sb", (128, NE), F32)
    mk(E0, "n1w_sb", (128, 8), F32)
    mk(E0, "n2w_sb", (128, 8), F32)
    mk(E0, "A_sb", (128, 16 * DS), F32)
    mk(E0, "convw_sb", (128, 16 * 4), F32)
    mk(E0, "convb_sb", (128, 16), F32)
    mk(E0, "dtb_sb", (128, 16), F32)
    mk(E0, "D_sb", (128, 16), F32)

    for nm in ("sum_s", "ssq_s", "mu0_s", "rstd0_s", "b00_s", "rs01_s", "b01_s",
               "rstd2_s", "sum3_s", "ssq3_s", "rstd3_s", "b03_s"):
        mk(E0, nm, (128, 32), F32)
    mk(E0, "sm_a", (128, 32), F32)
    mk(E0, "sm_b", (128, 32), F32)
    mk(E0, "trash_bf", (128, 1024), BF16)
    mk(E0, "state_sb", (128, 16 * DS), F32)
    mk(E0, "halo_sb", (128, 16 * 3), BF16)

    sA = ExitStack()
    EA = sA.enter_context
    sA1 = ExitStack()
    E1 = sA1.enter_context
    mk(E1, "w_in_sb", (128, 8 * DFF), BF16)
    mk(E1, "w_xp_sb", (128, 16 * 128), BF16)
    mk(E1, "x0a_sb", (128, 2 * 4 * D), F32)
    mk(E1, "x0bf_sb", (128, 4 * D), BF16)
    mk(E1, "n1T_sb", (128, 8 * TL), BF16)
    mk(E1, "ustg_sb", (128, 2 * 515), BF16)
    mk(E1, "cacc_sb", (128, 2 * TL), BF16)
    mk(E1, "uact_sb", (128, 16 * TL), BF16)
    mk(E1, "zs_sb", (128, 16 * TL), BF16)
    mk(E1, "Bst_sb", (16, TL), F32)
    mk(E1, "Cst_sb", (16, TL), F32)
    mk(E1, "dtrst_sb", (64, TL), BF16)
    mkp(E1, "psA_tr0", (128, TL))
    mkp(E1, "psA_tr1", (128, TL))
    mkp(E1, "psA_mm0", (128, TL))
    mkp(E1, "psA_mm1", (128, TL))
    mkp(E1, "psA_xp", (128, TL))
    sA1.close()
    sA2 = ExitStack()
    E2 = sA2.enter_context
    mk(E2, "onehot_bf", (16, DS * 128), BF16)
    mk(E2, "w_dt_sb", (64, DI), BF16)
    mk(E2, "w_out_sb", (128, 16 * D), BF16)
    mk(E2, "x0b_sb", (128, 4 * D), F32)
    mk(E2, "uact2_sb", (128, 16 * TL), BF16)
    mk(E2, "zs2_sb", (128, 16 * TL), BF16)
    mk(E2, "dtr2_sb", (64, TL), BF16)
    mk(E2, "Bbf_sb", (16, TL), BF16)
    mk(E2, "Cbf_sb", (16, TL), BF16)
    mk(E2, "Bb_sb", (128, 16 * TL), BF16)
    mk(E2, "Cb_sb", (128, 16 * TL), BF16)
    mk(E2, "sp1_sb", (128, TL), F32)
    mk(E2, "sp2_sb", (128, TL), F32)
    mk(E2, "dt_sb", (128, 2 * TL), F32)
    mk(E2, "du_sb", (128, 2 * TL), BF16)
    mk(E2, "aBuf_sb", (128, 2 * 2 * TL), F32)
    mk(E2, "bs_sb", (128, 2 * TL), BF16)
    mk(E2, "hb_sb", (128, 2 * TL), BF16)
    mk(E2, "yp_sb", (128, 4 * TL), BF16)
    mk(E2, "yt_sb", (128, TL), BF16)
    mk(E2, "gt_sb", (128, TL), BF16)
    mk(E2, "mT_sb", (128, 8 * TL), BF16)
    mk(E2, "h_sb", (128, 4 * D), F32)
    mk(E2, "h2bf_sb", (128, 4 * D), BF16)
    mk(E2, "n2Tst_sb", (128, 8 * TL), BF16)
    mkp(E2, "psB_bc", (128, TL))
    mkp(E2, "psB_dt", (128, TL))
    mkp(E2, "psB_op0", (128, TL))
    mkp(E2, "psB_op1", (128, TL))
    mkp(E2, "psB_tr0", (128, 1024))
    mkp(E2, "psB_tr1", (128, 1024))
    sA2.close()
    sA.close()

    sBC = ExitStack()
    EB = sBC.enter_context
    mk(EB, "u3T_sb", (128, 8 * NT), BF16)
    mk(EB, "wfull_sb", (128, 32 * NE), F32)
    mk(EB, "wroute_sb", (16, NT), BF16)
    mk(EB, "wr_sb", (128, 8 * NE), BF16)
    mk(EB, "b2_sb", (16, HID), BF16)
    sB1 = ExitStack()
    E3 = sB1.enter_context
    mk(E3, "n2Ta_sb", (128, 8 * NT), BF16)
    mk(E3, "w1s_sb", (128, 2 * 8 * 128), BF16)
    mk(E3, "w2s_sb", (128, 2 * 8 * 128), BF16)
    mk(E3, "sl_sb", (128, TL), BF16)
    mk(E3, "w2e_sb", (128, TL), BF16)
    mk(E3, "gtmp_sb", (128, 2 * TL), BF16)
    mk(E3, "gst_sb", (128, 2 * NT), BF16)
    mkp(E3, "psC_1", (128, TL))
    mkp(E3, "psC_2", (128, TL))
    mkp(E3, "psC_3", (128, TL))
    mkp(E3, "psC_4", (128, TL))
    sB1.close()
    sB2 = ExitStack()
    E4 = sB2.enter_context
    mk(E4, "w3_sb", (128, 32 * D), BF16)
    mk(E4, "gld_sb", (128, 32 * TG), BF16)
    mk(E4, "x0c_sb", (128, 2 * D), F32)
    mk(E4, "hc_sb", (128, 2 * D), F32)
    mk(E4, "x1_sb", (128, 2 * D), F32)
    mk(E4, "pT_sb", (128, 8 * TG), BF16)
    mk(E4, "u3bf_sb", (128, 2 * D), BF16)
    mk(E4, "lg_sb", (128, 2 * NE), F32)
    mk(E4, "m8_sb", (128, 2 * 8), F32)
    mk(E4, "i8_sb", (128, 2 * 8), U32)
    mk(E4, "idxf_sb", (128, 2 * 2), F32)
    mk(E4, "nm1_sb", (128, 2), F32)
    mk(E4, "t21_sb", (128, 2), F32)
    mk(E4, "den_sb", (128, 2), F32)
    mk(E4, "w1r_sb", (128, 2), F32)
    mk(E4, "w2r_sb", (128, 2), F32)
    mk(E4, "mk1_sb", (128, NE), F32)
    mk(E4, "mk2_sb", (128, NE), F32)
    mk(E4, "wfbf_sb", (128, NE), BF16)
    mkp(E4, "psD_p", (128, 8 * TG))
    mkp(E4, "psD_t", (128, 1024))
    mkp(E4, "psD_u", (128, 4 * TG))
    sB2.close()
    sC = ExitStack()
    E5 = sC.enter_context
    mk(E5, "f_sb", (128, 8 * NT), BF16)
    sC1 = ExitStack()
    E6 = sC1.enter_context
    mk(E6, "W1X_sb", (128, 4 * HID), BF16)
    mk(E6, "W1Y_sb", (128, 4 * HID), BF16)
    mk(E6, "W2X_sb", (128, 4 * D), BF16)
    mk(E6, "W2Y_sb", (128, 4 * D), BF16)
    mk(E6, "g1e_sb", (128, HID), BF16)
    mk(E6, "be1e_sb", (128, HID), BF16)
    mk(E6, "b1e_sb", (128, HID), BF16)
    mk(E6, "Sb1_sb", (128, 1), F32)
    mk(E6, "hch_sb", (128, 2 * HID), BF16)
    mk(E6, "gl_sb", (128, 2 * HID), BF16)
    mk(E6, "dg_sb", (128, 2 * 128), BF16)
    mk(E6, "glT_sb", (128, 8 * TG), BF16)
    mk(E6, "sums_sb", (128, 2), F32)
    mk(E6, "ssqE_sb", (128, 2), F32)
    mk(E6, "muE_sb", (128, 2), F32)
    mk(E6, "rstdE_sb", (128, 2), F32)
    mk(E6, "bE_sb", (128, 2), F32)
    mkp(E6, "psE_w1", (128, HID))
    mkp(E6, "psE_tr", (128, TL))
    mkp(E6, "psE_o", (128, 8 * TG))
    sC1.close()
    sC2 = ExitStack()
    E7 = sC2.enter_context
    mk(E7, "x1f_sb", (128, 2 * D), F32)
    mk(E7, "x2_sb", (128, 2 * D), F32)
    mkp(E7, "psE_f", (128, 1024))
    sC2.close()
    sC.close()
    sBC.close()
    return B


def build(core_inputs_shapes=None):
    nc = bass.Bass("TRN2", target_bir_lowering=False, detect_race_conditions=False)
    H = {}

    def din(name, shape, dt=F32):
        H[name] = nc.dram_tensor(name, list(shape), dt, kind="ExternalInput").ap()

    din("xs", (NT, D))
    din("w_in_T", (D, DFF)); din("w_xp_pad", (DI, 128)); din("w_dt_T", (DTR, DI))
    din("w_out_T", (DI, D)); din("w1_T", (D, DFF)); din("w2_T", (D, DFF))
    din("w3_T", (DFF, D)); din("W1e", (NE, D, HID)); din("W2e", (NE, HID, D))
    din("wr", (D, NE)); din("b2e", (NE, D))
    din("g1b", (NE, 128, HID)); din("be1b", (NE, 128, HID)); din("b1b", (NE, 128, HID))
    din("A_log_r", (128, 16, DS)); din("convw_r", (128, 16, 4)); din("convb_r", (128, 16))
    din("dtb_r", (128, 16)); din("D_r", (128, 16)); din("n1w_pp", (128, 8))
    din("n2w_pp", (128, 8)); din("brb", (128, NE)); din("iota16", (128, NE))
    din("ident", (128, 128)); din("onehot", (16, DS * 128))
    out_h = nc.dram_tensor("out", [NT, D], F32, kind="ExternalOutput").ap()

    def scr(name, shape, dt):
        kind = "ExternalOutput" if DEBUG_SCRATCH else "Internal"
        H[name] = nc.dram_tensor(name, list(shape), dt, kind=kind).ap()

    scr("sp_uact", (NCHUNK, 128, 16, TL), BF16)
    scr("sp_z", (NCHUNK, 128, 16, TL), BF16)
    scr("sp_dtr", (NCHUNK, 64, TL), BF16)
    scr("sp_B", (NCHUNK, 16, TL), F32)
    scr("sp_C", (NCHUNK, 16, TL), F32)
    scr("sp_h", (NCHUNK, 128, 4, D), F32)
    scr("sp_n2T", (128, 8, NT), BF16)
    scr("sp_g2", (NTG, 128, 32, TG), BF16)
    scr("sp_x1", (NTG, 128, 2, D), F32)
    if DEBUG_SCRATCH:
        scr("dbg_gld", (NTG, 128, 32 * TG), BF16)
        scr("dbg_pT", (NTG, 128, 8 * TG), BF16)

    es0 = ExitStack()
    sems = {}
    for n in ENGINES:
        sems[n + "_c"] = es0.enter_context(nc.semaphore("semc_" + n))
        for i in range(4):
            sems["%s_d%d" % (n, i)] = es0.enter_context(
                nc.semaphore("semd%d_" % i + n))
    Bf = _alloc(nc, es0)
    ctx = Ctx(nc, sems)
    for tname in TAPS:
        shape, dt = _TAPDEFS[tname]
        ctx.taps[tname] = nc.dram_tensor("tap_" + tname, list(shape), dt,
                                         kind="ExternalOutput").ap()

    with nc.Block() as block:
        for ename in ENGINES:
            fn = getattr(block, ename)

            def emit(_eng, ename=ename):
                ctx.begin(ename)
                _program(ctx, H, Bf, out_h)

            fn(emit)
    es0.close()
    return nc



def _program(ctx, H, B, out_h):
    r = ctx.run
    m_setup = _setup(ctx, H, B)
    bar = ctx.barrier()
    if "A1" in PHASES:
        _phase_a1(ctx, H, B, m_setup)
        bar = ctx.barrier()
    if "A2" in PHASES:
        _phase_a2(ctx, H, B, m_setup)
        bar = ctx.barrier()
    if "B1" in PHASES:
        _phase_b1(ctx, H, B)
        bar = ctx.barrier()
    if "B2" in PHASES:
        _phase_b2(ctx, H, B)
        bar = ctx.barrier()
    if "C" in PHASES:
        _phase_c(ctx, H, B)
        bar = ctx.barrier()
    if "FIN" in PHASES:
        _phase_fin(ctx, H, B, out_h)
    e = ctx.engs["gpsimd"]
    for slot in range(NDQ):
        if e.dq_cnt[slot]:
            for n in ENGINES:
                ctx.dep(n, (("gpsimd", slot), e.dq_cnt[slot], "d"))
    ctx.barrier()


def _setup(ctx, H, B):
    r = ctx.run
    m = {}
    for nm, dst in (("iota16", "iota16_sb"), ("brb", "brb_sb"), ("n1w_pp", "n1w_sb"),
                    ("n2w_pp", "n2w_sb"), ("convb_r", "convb_sb"), ("dtb_r", "dtb_sb"),
                    ("D_r", "D_sb")):
        m[dst] = r("sync", "dma_start", out=B[dst][:], in_=H[nm][:], dma=True)
    m["A_log"] = r("sync", "dma_start", out=B["A_sb"][:],
                   in_=H["A_log_r"].rearrange("p dc n -> p (dc n)"), dma=True)
    m["convw"] = r("sync", "dma_start", out=B["convw_sb"][:],
                   in_=H["convw_r"].rearrange("p dc t -> p (dc t)"), dma=True)
    m["ident"] = r("gpsimd", "dma_start", out=B["ident_bf"][:], in_=H["ident"][:], dma=True)
    ma1 = r("scalar", "activation", B["A_sb"][:], B["A_sb"][:], AF.Exp,
            deps=[m["A_log"]], inc=1)
    m["A"] = r("vector", "tensor_scalar", B["A_sb"][:], B["A_sb"][:], -1.0, None, OP.mult,
               deps=[ma1], inc=1)
    m["st0"] = r("gpsimd", "memset", B["state_sb"][:], 0.0, inc=1)
    m["hl0"] = r("gpsimd", "memset", B["halo_sb"][:], 0.0, inc=1)
    return m


def _phase_a1(ctx, H, B, ms):
    r = ctx.run
    mw = []
    for kg in range(8):
        mw.append(r("gpsimd", "dma_start",
                    out=B["w_in_sb"][:, kg * DFF:(kg + 1) * DFF],
                    in_=H["w_in_T"][kg * 128:(kg + 1) * 128, :], dma=True))
    m_wxp = r("gpsimd", "dma_start",
              out=B["w_xp_sb"][:].rearrange("p (dc m) -> p dc m", m=128),
              in_=H["w_xp_pad"].rearrange("(dc p) m -> p dc m", p=128), dma=True)
    m_fold = []
    for kg in range(8):
        m_fold.append(r("vector", "tensor_scalar",
                        B["w_in_sb"][:, kg * DFF:(kg + 1) * DFF],
                        B["w_in_sb"][:, kg * DFF:(kg + 1) * DFF],
                        B["n1w_sb"][:, kg:kg + 1], None, OP.mult,
                        deps=[mw[kg], ms["n1w_sb"]], inc=1))
    m_fold_all = m_fold[-1]

    prev = {}
    for c in range(LIMIT_CHUNKS or NCHUNK):
        buf = c % 2
        x0 = B["x0a_sb"][:, buf * 4 * D:(buf + 1) * 4 * D].rearrange("p (tt d) -> p tt d", d=D)
        m_ld = r("sync", "dma_start", out=x0,
                 in_=H["xs"][c * TL:(c + 1) * TL, :].rearrange("(tt p) d -> p tt d", p=128),
                 deps=[prev.get(("x0", buf))], dma=True)
        g0 = c * 4
        m_st = None
        m_stq = None
        for tt in range(4):
            g = g0 + tt
            m_st = r("vector", "tensor_reduce", B["sum_s"][:, g:g + 1], x0[:, tt, :],
                     AX.X, OP.add, deps=[m_ld], inc=(1 if tt == 3 else 0))
            m_stq = r("scalar", "activation", B["trash_bf"][:],
                      x0[:, tt, :], AF.Square,
                      accum_out=B["ssq_s"][:, g:g + 1], deps=[m_ld],
                      inc=(1 if tt == 3 else 0))
        ctx.drain("vector")
        ctx.autodrain = True
        sl = slice(g0, g0 + 4)
        r("vector", "tensor_scalar", B["mu0_s"][:, sl], B["sum_s"][:, sl], 1.0 / D, None,
          OP.mult, deps=[m_st, m_stq])
        r("vector", "tensor_scalar", B["sm_a"][:, sl], B["ssq_s"][:, sl], 1.0 / D, None, OP.mult)
        r("vector", "tensor_tensor", B["sm_b"][:, sl], B["mu0_s"][:, sl], B["mu0_s"][:, sl],
          OP.mult)
        r("vector", "tensor_tensor", B["sm_a"][:, sl], B["sm_a"][:, sl], B["sm_b"][:, sl],
          OP.subtract)
        m_v = r("vector", "tensor_scalar", B["sm_b"][:, sl], B["sm_a"][:, sl], EPS, None,
                OP.add, inc=1)
        ml1 = r("scalar", "activation", B["sm_b"][:, sl], B["sm_b"][:, sl], AF.Ln,
                deps=[m_v], inc=1)
        r("scalar", "activation", B["rstd0_s"][:, sl], B["sm_b"][:, sl], AF.Exp, scale=-0.5)
        m_s1 = r("scalar", "activation", B["sm_b"][:, sl], B["sm_b"][:, sl], AF.Exp,
                 inc=1)
        r("vector", "reciprocal", B["sm_b"][:, sl], B["sm_b"][:, sl], deps=[m_s1])
        r("vector", "tensor_tensor", B["sm_a"][:, sl], B["sm_a"][:, sl], B["sm_b"][:, sl],
          OP.mult)
        m_v2 = r("vector", "tensor_scalar", B["sm_a"][:, sl], B["sm_a"][:, sl], EPS, None,
                 OP.add, inc=1)
        r("scalar", "activation", B["sm_a"][:, sl], B["sm_a"][:, sl], AF.Ln, deps=[m_v2])
        m_s2 = r("scalar", "activation", B["sm_a"][:, sl], B["sm_a"][:, sl], AF.Exp,
                 scale=-0.5, inc=1)
        r("vector", "tensor_tensor", B["rs01_s"][:, sl], B["rstd0_s"][:, sl], B["sm_a"][:, sl],
          OP.mult, deps=[m_s2])
        r("vector", "scalar_tensor_tensor", B["b00_s"][:, sl], B["mu0_s"][:, sl],
          -1.0, B["rstd0_s"][:, sl], OP.mult, OP.mult)
        m_sm = r("vector", "scalar_tensor_tensor", B["b01_s"][:, sl], B["mu0_s"][:, sl],
                 -1.0, B["rs01_s"][:, sl], OP.mult, OP.mult, inc=1)
        ctx.autodrain = False
        m_cast = None
        for tt in range(4):
            g = g0 + tt
            m_cast = r("scalar", "activation",
                       B["x0bf_sb"][:, tt * D:(tt + 1) * D], x0[:, tt, :], AF.Identity,
                       scale=B["rs01_s"][:, g:g + 1], bias=B["b01_s"][:, g:g + 1],
                       deps=[m_sm, prev.get("n1T_done")], inc=1)
        m_ev = {}
        for dg in range(8):
            ps = B["psA_tr0"] if dg % 2 == 0 else B["psA_tr1"]
            m_mm = None
            for tt in range(4):
                m_mm = r("tensor", "matmul", ps[:, tt * 128:(tt + 1) * 128],
                         B["x0bf_sb"][:, tt * D + dg * 128: tt * D + (dg + 1) * 128],
                         B["ident_bf"][:],
                         deps=[m_cast, ms["ident"], m_ev.get(dg - 2)],
                         start=True, stop=True, inc=(1 if tt == 3 else 0))
            m_ev[dg] = r("scalar", "activation", B["n1T_sb"][:, dg * TL:(dg + 1) * TL],
                         ps[:], AF.Copy, deps=[m_mm], inc=1)
        m_n1T = m_ev[7]
        m_mm_ev = {}
        m_uact = {}
        for mt in range(32):
            ps = B["psA_mm0"] if mt % 2 == 0 else B["psA_mm1"]
            dep0 = [m_n1T, m_fold_all, m_mm_ev.get(mt - 2)]
            m_mm = None
            for kg in range(8):
                m_mm = r("tensor", "matmul", ps[:],
                         B["w_in_sb"][:, kg * DFF + mt * 128: kg * DFF + (mt + 1) * 128],
                         B["n1T_sb"][:, kg * TL:(kg + 1) * TL],
                         deps=(dep0 if kg == 0 else ()),
                         start=(kg == 0), stop=(kg == 7), inc=(1 if kg == 7 else 0))
            if mt < 16:
                dc = mt
                ub = dc % 2
                ust = B["ustg_sb"][:, ub * 515:(ub + 1) * 515]
                if c % 4 == 0:
                    r("vector", "memset", ust[:, 0:3], 0.0,
                      deps=[prev.get(("ustg", ub))])
                else:
                    r("vector", "tensor_copy", ust[:, 0:3],
                      B["halo_sb"][:, dc * 3:(dc + 1) * 3],
                      deps=[prev.get(("ustg", ub)), prev.get("halo_w%d" % dc)])
                m_evu = r("scalar", "activation", ust[:, 3:515], ps[:], AF.Copy,
                          deps=[m_mm, prev.get(("ustg", ub))], inc=1)
                m_mm_ev[mt] = m_evu
                m_tail = r("vector", "tensor_copy", B["halo_sb"][:, dc * 3:(dc + 1) * 3],
                           ust[:, 512:515], deps=[m_evu])
                prev["halo_w%d" % dc] = m_tail
                cb = dc % 2
                cacc = B["cacc_sb"][:, cb * TL:(cb + 1) * TL]
                ctx.drain("vector")
                r("vector", "tensor_scalar", cacc, ust[:, 0:512],
                  B["convw_sb"][:, dc * 4:dc * 4 + 1], None, OP.mult,
                  deps=[m_evu, ms["convw"], prev.get(("cacc", cb))])
                m_conv = None
                for t in range(1, 4):
                    m_conv = r("vector", "scalar_tensor_tensor", cacc, ust[:, t:512 + t],
                               B["convw_sb"][:, dc * 4 + t:dc * 4 + t + 1], cacc,
                               OP.mult, OP.add, inc=(1 if t == 3 else 0))
                prev[("ustg", ub)] = m_conv
                m_ua = r("scalar", "activation", B["uact_sb"][:, dc * TL:(dc + 1) * TL],
                         cacc, AF.Silu, bias=B["convb_sb"][:, dc:dc + 1],
                         deps=[m_conv, ms["convb_sb"], prev.get("sp_uact")], inc=1)
                prev[("cacc", cb)] = m_ua
                m_uact[dc] = m_ua
            else:
                dc = mt - 16
                m_evz = r("scalar", "activation", B["zs_sb"][:, dc * TL:(dc + 1) * TL],
                          ps[:], AF.Silu, deps=[m_mm, prev.get("sp_z")], inc=1)
                m_mm_ev[mt] = m_evz
        m_xp = None
        for k in range(16):
            m_xp = r("tensor", "matmul", B["psA_xp"][:],
                     B["w_xp_sb"][:, k * 128:(k + 1) * 128],
                     B["uact_sb"][:, k * TL:(k + 1) * TL],
                     deps=[m_uact[k], m_wxp] + ([prev.get("xp_ev")] if k == 0 else []),
                     start=(k == 0), stop=(k == 15), inc=(1 if k == 15 else 0))
        m_eB = r("scalar", "activation", B["Bst_sb"][:], B["psA_xp"][0:16, :], AF.Copy,
                 deps=[m_xp, prev.get("sp_B")], inc=1)
        m_eC = r("scalar", "activation", B["Cst_sb"][:], B["psA_xp"][32:48, :], AF.Copy,
                 deps=[prev.get("sp_C")], inc=1)
        m_eD = r("scalar", "activation", B["dtrst_sb"][:], B["psA_xp"][64:128, :], AF.Copy,
                 deps=[prev.get("sp_dtr")], inc=1)
        prev["xp_ev"] = m_eD
        m_spu = r("scalar", "dma_start", out=H["sp_uact"][c],
                  in_=B["uact_sb"][:].rearrange("p (dc t) -> p dc t", t=TL),
                  deps=[m_uact[15]], dma=True)
        m_spz = r("scalar", "dma_start", out=H["sp_z"][c],
                  in_=B["zs_sb"][:].rearrange("p (dc t) -> p dc t", t=TL),
                  deps=[m_mm_ev[31]], dma=True)
        m_spB = r("scalar", "dma_start", out=H["sp_B"][c], in_=B["Bst_sb"][:],
                  deps=[m_eB], dma=True)
        m_spC = r("scalar", "dma_start", out=H["sp_C"][c], in_=B["Cst_sb"][:],
                  deps=[m_eC], dma=True)
        m_spD = r("scalar", "dma_start", out=H["sp_dtr"][c], in_=B["dtrst_sb"][:],
                  deps=[m_eD], dma=True)
        prev["sp_uact"] = m_spu
        prev["sp_z"] = m_spz
        prev["sp_B"] = m_spB
        prev["sp_C"] = m_spC
        prev["sp_dtr"] = m_spD
        prev["n1T_done"] = m_mm_ev[31]
        prev[("x0", buf)] = m_cast
        ctx.tap("a1_x0bf_c%d" % c, B["x0bf_sb"][:], deps=[m_cast])
        ctx.tap("a1_n1T_c%d" % c, B["n1T_sb"][:], deps=[m_n1T])
        ctx.tap("a1_sum_c%d" % c, B["sum_s"][:], deps=[m_sm])
        ctx.tap("a1_mu0_c%d" % c, B["mu0_s"][:], deps=[m_sm])
        ctx.tap("a1_rstd0_c%d" % c, B["rstd0_s"][:], deps=[m_sm])
        ctx.tap("a1_var_c%d" % c, B["sm_a"][:], deps=[m_sm])
        ctx.tap("a1_ssq_c%d" % c, B["ssq_s"][:], deps=[m_sm])
        ctx.tap("a1_rs01_c%d" % c, B["rs01_s"][:], deps=[m_sm])
        ctx.tap("a1_uact_c%d" % c, B["uact_sb"][:], deps=[m_uact[15]])
        ctx.tap("a1_z_c%d" % c, B["zs_sb"][:], deps=[m_mm_ev[31]])
        ctx.tap("a1_B_c%d" % c, B["Bst_sb"][:], deps=[m_eB])
        ctx.tap("a1_dtr_c%d" % c, B["dtrst_sb"][:], deps=[m_eD])


def _phase_a2(ctx, H, B, ms):
    r = ctx.run
    ms["onehot"] = r("gpsimd", "dma_start", out=B["onehot_bf"][:], in_=H["onehot"][:],
                     dma=True)
    m_wdt = r("gpsimd", "dma_start", out=B["w_dt_sb"][:], in_=H["w_dt_T"][:], dma=True)
    m_wout = []
    for kg in range(16):
        m_wout.append(r("gpsimd", "dma_start",
                        out=B["w_out_sb"][:, kg * D:(kg + 1) * D],
                        in_=H["w_out_T"][kg * 128:(kg + 1) * 128, :], dma=True))
    prev = {}
    for c in range(LIMIT_CHUNKS or NCHUNK):
        g0 = c * 4
        batch_first = (c % 4 == 0)
        m_lu = r("sync", "dma_start",
                 out=B["uact2_sb"][:].rearrange("p (dc t) -> p dc t", t=TL),
                 in_=H["sp_uact"][c], deps=[prev.get("uact_free")], dma=True)
        m_lz = r("sync", "dma_start",
                 out=B["zs2_sb"][:].rearrange("p (dc t) -> p dc t", t=TL),
                 in_=H["sp_z"][c], deps=[prev.get("z_free")], dma=True)
        m_lx = r("sync", "dma_start",
                 out=B["x0b_sb"][:].rearrange("p (tt d) -> p tt d", d=D),
                 in_=H["xs"][c * TL:(c + 1) * TL, :].rearrange("(tt p) d -> p tt d", p=128),
                 deps=[prev.get("x0_free")], dma=True)
        m_ldt = r("sync", "dma_start", out=B["dtr2_sb"][:], in_=H["sp_dtr"][c],
                  deps=[prev.get("dtr_free")], dma=True)
        m_lB = r("gpsimd", "dma_start", out=B["Bbf_sb"][:], in_=H["sp_B"][c],
                 deps=[prev.get("Bbf_free")], dma=True)
        m_lC = r("gpsimd", "dma_start", out=B["Cbf_sb"][:], in_=H["sp_C"][c],
                 deps=[prev.get("Cbf_free")], dma=True)
        if batch_first and c > 0:
            ms["st0"] = r("gpsimd", "memset", B["state_sb"][:], 0.0,
                          deps=[prev.get("scan_all")], inc=1)
        m_bb = {}
        m_cb = {}
        m_bc_ev = prev.get("bc_ev")
        for n in range(DS):
            m_mm = r("tensor", "matmul", B["psB_bc"][:],
                     B["onehot_bf"][:, n * 128:(n + 1) * 128], B["Bbf_sb"][:],
                     deps=[m_lB, ms["onehot"], m_bc_ev], start=True, stop=True, inc=1)
            m_bc_ev = r("scalar", "activation", B["Bb_sb"][:, n * TL:(n + 1) * TL],
                        B["psB_bc"][:], AF.Copy, deps=[m_mm, prev.get("Bb_free")], inc=1)
            m_bb[n] = m_bc_ev
        for n in range(DS):
            m_mm = r("tensor", "matmul", B["psB_bc"][:],
                     B["onehot_bf"][:, n * 128:(n + 1) * 128], B["Cbf_sb"][:],
                     deps=[m_lC, m_bc_ev], start=True, stop=True, inc=1)
            m_bc_ev = r("scalar", "activation", B["Cb_sb"][:, n * TL:(n + 1) * TL],
                        B["psB_bc"][:], AF.Copy, deps=[m_mm, prev.get("Cb_free")], inc=1)
            m_cb[n] = m_bc_ev
        prev["Bbf_free"] = m_bb[DS - 1]
        prev["Cbf_free"] = m_cb[DS - 1]
        prev["bc_ev"] = m_bc_ev

        m_gate = {}
        m_sp_done = None
        m_scan_last = None
        for dc in range(16):
            db = dc % 2
            m_dmm = r("tensor", "matmul", B["psB_dt"][:],
                      B["w_dt_sb"][:, dc * 128:(dc + 1) * 128], B["dtr2_sb"][:],
                      deps=[m_ldt, m_wdt, prev.get("psdt_free")], start=True, stop=True,
                      inc=1)
            r("scalar", "activation", B["sp1_sb"][:], B["psB_dt"][:], AF.Abs,
              bias=B["dtb_sb"][:, dc:dc + 1], deps=[m_dmm, ms["dtb_sb"],
                                                    prev.get(("sp", 0))])
            r("scalar", "activation", B["sp1_sb"][:], B["sp1_sb"][:], AF.Exp, scale=-1.0)
            r("scalar", "activation", B["sp1_sb"][:], B["sp1_sb"][:], AF.Ln, bias=1.0)
            m_sp = r("scalar", "activation", B["sp2_sb"][:], B["psB_dt"][:], AF.Relu,
                     bias=B["dtb_sb"][:, dc:dc + 1], deps=[prev.get(("sp", 1))], inc=1)
            prev["psdt_free"] = m_sp
            dt_t = B["dt_sb"][:, db * TL:(db + 1) * TL]
            m_dt = r("vector", "tensor_tensor", dt_t, B["sp1_sb"][:], B["sp2_sb"][:], OP.add,
                     deps=[m_sp, prev.get(("dt", db))], inc=1)
            prev[("sp", 0)] = m_dt
            prev[("sp", 1)] = m_dt
            du_t = B["du_sb"][:, db * TL:(db + 1) * TL]
            m_du = r("vector", "tensor_tensor", du_t, dt_t, B["uact2_sb"][:, dc * TL:(dc + 1) * TL],
                     OP.mult, deps=[m_lu, prev.get(("du", db))])
            m_scan_last = None
            for grp in range(8):
                ab = grp % 2
                m_a = None
                for j in range(2):
                    n = grp * 2 + j
                    m_a = r("scalar", "activation",
                            B["aBuf_sb"][:, (ab * 2 + j) * TL:(ab * 2 + j + 1) * TL],
                            dt_t, AF.Exp, scale=B["A_sb"][:, dc * DS + n:dc * DS + n + 1],
                            deps=[m_dt, ms["A"], prev.get(("scangrp", ab))],
                            inc=(1 if j == 1 else 0))
                m_sc = None
                for j in range(2):
                    n = grp * 2 + j
                    q = n % 4
                    hb = B["hb_sb"][:, (n % 2) * TL:((n % 2) + 1) * TL]
                    bs = B["bs_sb"][:, (n % 2) * TL:((n % 2) + 1) * TL]
                    r("vector", "tensor_tensor", bs, du_t, B["Bb_sb"][:, n * TL:(n + 1) * TL],
                      OP.mult, deps=[m_du, m_bb[n], m_a])
                    r("vector", "tensor_tensor_scan", hb,
                      B["aBuf_sb"][:, (ab * 2 + j) * TL:(ab * 2 + j + 1) * TL], bs,
                      B["state_sb"][:, dc * DS + n:dc * DS + n + 1], OP.mult, OP.add,
                      deps=[ms["st0"]])
                    if n < 4:
                        m_sc = r("vector", "tensor_tensor",
                                 B["yp_sb"][:, q * TL:(q + 1) * TL], hb,
                                 B["Cb_sb"][:, n * TL:(n + 1) * TL], OP.mult,
                                 deps=[m_cb[n]], inc=(1 if j == 1 else 0), war=True)
                    else:
                        yt = B["yt_sb"][:]
                        r("vector", "tensor_tensor", yt, hb,
                          B["Cb_sb"][:, n * TL:(n + 1) * TL], OP.mult, deps=[m_cb[n]])
                        m_sc = r("vector", "tensor_tensor",
                                 B["yp_sb"][:, q * TL:(q + 1) * TL],
                                 B["yp_sb"][:, q * TL:(q + 1) * TL], yt, OP.add,
                                 inc=(1 if j == 1 else 0), war=True)
                    r("vector", "tensor_scalar",
                      B["state_sb"][:, dc * DS + n:dc * DS + n + 1],
                      hb[:, TL - 1:TL], 0.0, None, OP.add)
                prev[("scangrp", ab)] = m_sc
                m_scan_last = m_sc
            prev[("dt", db)] = m_scan_last
            prev[("du", db)] = m_scan_last
            r("vector", "tensor_tensor", B["yp_sb"][:, 0:TL], B["yp_sb"][:, 0:TL],
              B["yp_sb"][:, TL:2 * TL], OP.add)
            r("vector", "tensor_tensor", B["yp_sb"][:, 2 * TL:3 * TL],
              B["yp_sb"][:, 2 * TL:3 * TL], B["yp_sb"][:, 3 * TL:4 * TL], OP.add)
            r("vector", "tensor_tensor", B["yp_sb"][:, 0:TL], B["yp_sb"][:, 0:TL],
              B["yp_sb"][:, 2 * TL:3 * TL], OP.add)
            r("vector", "scalar_tensor_tensor", B["gt_sb"][:],
              B["uact2_sb"][:, dc * TL:(dc + 1) * TL], B["D_sb"][:, dc:dc + 1],
              B["yp_sb"][:, 0:TL], OP.mult, OP.add, deps=[ms["D_sb"]])
            m_g = r("vector", "tensor_tensor", B["uact2_sb"][:, dc * TL:(dc + 1) * TL],
                    B["gt_sb"][:], B["zs2_sb"][:, dc * TL:(dc + 1) * TL], OP.mult,
                    deps=[m_lz], inc=1)
            m_gate[dc] = m_g
        prev["scan_all"] = r("vector", "drain", inc=1)
        prev["uact_free"] = None
        prev["z_free"] = m_gate[15]
        prev["dtr_free"] = m_gate[15]
        prev["Bb_free"] = m_gate[15]
        prev["Cb_free"] = m_gate[15]
        ctx.tap("a2_gated_c%d" % c, B["uact2_sb"][:], deps=[m_gate[15]])
        m_op_ev = {}
        for mt in range(8):
            ps = B["psB_op0"] if mt % 2 == 0 else B["psB_op1"]
            m_mm = None
            for k in range(16):
                m_mm = r("tensor", "matmul", ps[:],
                         B["w_out_sb"][:, k * D + mt * 128: k * D + (mt + 1) * 128],
                         B["uact2_sb"][:, k * TL:(k + 1) * TL],
                         deps=([m_gate[k], m_wout[k]] +
                               ([m_op_ev.get(mt - 2,
                                             prev.get("psop%d_free" % (mt % 2)))]
                                if k == 0 else [])),
                         start=(k == 0), stop=(k == 15), inc=(1 if k == 15 else 0))
            m_op_ev[mt] = r("scalar", "activation", B["mT_sb"][:, mt * TL:(mt + 1) * TL],
                            ps[:], AF.Copy, deps=[m_mm, prev.get("mT_free")], inc=1)
        m_mT = m_op_ev[7]
        prev["uact_free"] = m_mT
        m_h = {}
        m_ssq2 = None
        for tt in range(4):
            ps = B["psB_tr0"] if tt % 2 == 0 else B["psB_tr1"]
            m_mm = None
            for dg in range(8):
                m_mm = r("tensor", "matmul", ps[:, dg * 128:(dg + 1) * 128],
                         B["mT_sb"][:, dg * TL + tt * 128: dg * TL + (tt + 1) * 128],
                         B["ident_bf"][:],
                         deps=[m_op_ev[dg], prev.get(("htr", tt % 2))],
                         start=True, stop=True, inc=(1 if dg == 7 else 0))
            g = g0 + tt
            r("vector", "scalar_tensor_tensor", B["h_sb"][:, tt * D:(tt + 1) * D],
              B["x0b_sb"][:, tt * D:(tt + 1) * D], B["rstd0_s"][:, g:g + 1], ps[:],
              OP.mult, OP.add, deps=[m_mm, m_lx, prev.get("h_free")])
            m_hh = r("vector", "tensor_scalar", B["h_sb"][:, tt * D:(tt + 1) * D],
                     B["h_sb"][:, tt * D:(tt + 1) * D], B["b00_s"][:, g:g + 1], None,
                     OP.add, inc=1)
            prev[("htr", tt % 2)] = m_hh
            m_h[tt] = m_hh
            g = g0 + tt
            m_ssq2 = r("scalar", "activation", B["trash_bf"][:],
                       B["h_sb"][:, tt * D:(tt + 1) * D], AF.Square,
                       accum_out=B["ssq_s"][:, g:g + 1], deps=[m_hh], inc=1)
        prev["mT_free"] = m_h[3]
        ctx.tap("a2_h_c%d" % c, B["h_sb"][:], deps=[m_h[3]])
        m_sph = r("gpsimd", "dma_start", out=H["sp_h"][c],
                  in_=B["h_sb"][:].rearrange("p (tt d) -> p tt d", d=D),
                  deps=[m_h[3]], dma=True)
        prev["h_free"] = m_sph
        sl = slice(g0, g0 + 4)
        ctx.autodrain = True
        r("vector", "tensor_scalar", B["sm_a"][:, sl], B["ssq_s"][:, sl], 1.0 / D, None,
          OP.mult, deps=[m_ssq2])
        m_v = r("vector", "tensor_scalar", B["sm_a"][:, sl], B["sm_a"][:, sl], EPS, None,
                OP.add, inc=1)
        ctx.autodrain = False
        r("scalar", "activation", B["sm_a"][:, sl], B["sm_a"][:, sl], AF.Ln, deps=[m_v])
        m_r2 = r("scalar", "activation", B["rstd2_s"][:, sl], B["sm_a"][:, sl], AF.Exp,
                 scale=-0.5, inc=1)
        m_cast = None
        for tt in range(4):
            g = g0 + tt
            m_cast = r("scalar", "activation", B["h2bf_sb"][:, tt * D:(tt + 1) * D],
                       B["h_sb"][:, tt * D:(tt + 1) * D], AF.Copy,
                       scale=B["rstd2_s"][:, g:g + 1],
                       deps=[m_r2, prev.get("h2bf_free")], inc=(1 if tt == 3 else 0))
        m_n2ev = {}
        for dg in range(8):
            ps = B["psB_op0"] if dg % 2 == 0 else B["psB_op1"]
            m_mm = None
            for tt in (3, 2, 1, 0):
                m_mm = r("tensor", "matmul", ps[:, tt * 128:(tt + 1) * 128],
                         B["h2bf_sb"][:, tt * D + dg * 128: tt * D + (dg + 1) * 128],
                         B["ident_bf"][:],
                         deps=[m_cast, m_n2ev.get(dg - 2),
                               m_op_ev[6] if dg == 0 else None,
                               m_op_ev[7] if dg == 1 else None],
                         start=True, stop=True, inc=(1 if tt == 0 else 0))
            m_n2ev[dg] = r("scalar", "activation", B["n2Tst_sb"][:, dg * TL:(dg + 1) * TL],
                           ps[:], AF.Copy, deps=[m_mm, prev.get("n2Tst_free")], inc=1)
        prev["h2bf_free"] = m_n2ev[7]
        prev["psop0_free"] = m_n2ev[6]
        prev["psop1_free"] = m_n2ev[7]
        m_spn2 = r("scalar", "dma_start",
                   out=H["sp_n2T"][:, :, c * TL:(c + 1) * TL],
                   in_=B["n2Tst_sb"][:].rearrange("p (dg t) -> p dg t", t=TL),
                   deps=[m_n2ev[7]], dma=True)
        prev["n2Tst_free"] = m_spn2
        prev["x0_free"] = m_cast


def _phase_b1(ctx, H, B):
    r = ctx.run
    m_n2 = []
    for dg in range(8):
        m_n2.append(r("sync", "dma_start", out=B["n2Ta_sb"][:, dg * NT:(dg + 1) * NT],
                      in_=H["sp_n2T"][:, dg, :], dma=True))
    prev = {}
    for mt in range(32):
        sb = mt % 2
        w1s = B["w1s_sb"][:, sb * 1024:(sb + 1) * 1024]
        w2s = B["w2s_sb"][:, sb * 1024:(sb + 1) * 1024]
        m_l1 = r("gpsimd", "dma_start",
                 out=w1s.rearrange("p (kg m) -> p kg m", m=128),
                 in_=H["w1_T"][:, mt * 128:(mt + 1) * 128].rearrange(
                     "(kg p) m -> p kg m", p=128),
                 deps=[prev.get(("w1", sb))], dma=True)
        m_l2 = r("gpsimd", "dma_start",
                 out=w2s.rearrange("p (kg m) -> p kg m", m=128),
                 in_=H["w2_T"][:, mt * 128:(mt + 1) * 128].rearrange(
                     "(kg p) m -> p kg m", p=128),
                 deps=[prev.get(("w2", sb))], dma=True)
        m_f1 = m_f2 = None
        for kg in range(8):
            m_f1 = r("vector", "tensor_scalar", w1s[:, kg * 128:(kg + 1) * 128],
                     w1s[:, kg * 128:(kg + 1) * 128], B["n2w_sb"][:, kg:kg + 1], None,
                     OP.mult, deps=[m_l1], inc=(1 if kg == 7 else 0))
            m_f2 = r("vector", "tensor_scalar", w2s[:, kg * 128:(kg + 1) * 128],
                     w2s[:, kg * 128:(kg + 1) * 128], B["n2w_sb"][:, kg:kg + 1], None,
                     OP.mult, deps=[m_l2], inc=(1 if kg == 7 else 0))
        gb = mt % 2
        gst = B["gst_sb"][:, gb * NT:(gb + 1) * NT]
        m_gd = {}
        for tg in range(8):
            ps1 = B["psC_1"] if tg % 2 == 0 else B["psC_3"]
            ps2 = B["psC_2"] if tg % 2 == 0 else B["psC_4"]
            m_mm1 = m_mm2 = None
            for kg in range(8):
                m_mm1 = r("tensor", "matmul", ps1[:],
                          w1s[:, kg * 128:(kg + 1) * 128],
                          B["n2Ta_sb"][:, kg * NT + tg * TL: kg * NT + (tg + 1) * TL],
                          deps=([m_f1, m_n2[kg]] +
                                ([prev.get(("ps", tg % 2))] if kg == 0 else [])),
                          start=(kg == 0), stop=(kg == 7), inc=(1 if kg == 7 else 0))
            for kg in range(8):
                m_mm2 = r("tensor", "matmul", ps2[:],
                          w2s[:, kg * 128:(kg + 1) * 128],
                          B["n2Ta_sb"][:, kg * NT + tg * TL: kg * NT + (tg + 1) * TL],
                          deps=([m_f2] if kg == 0 else ()),
                          start=(kg == 0), stop=(kg == 7), inc=(1 if kg == 7 else 0))
            m_s = r("scalar", "activation", B["sl_sb"][:], ps1[:], AF.Silu,
                    deps=[m_mm1, prev.get("sl_free")], inc=1)
            m_w2 = r("scalar", "activation", B["w2e_sb"][:], ps2[:], AF.Copy,
                     deps=[m_mm2, prev.get("w2e_free")], inc=1)
            gp = tg % 2
            m_g = r("vector", "tensor_tensor",
                    B["gtmp_sb"][:, gp * TL:(gp + 1) * TL],
                    B["sl_sb"][:], B["w2e_sb"][:], OP.mult,
                    deps=[m_s, m_w2, prev.get(("gtmp", gp))], inc=1)
            m_gc = r("scalar", "activation", gst[:, tg * TL:(tg + 1) * TL],
                     B["gtmp_sb"][:, gp * TL:(gp + 1) * TL], AF.Copy,
                     deps=[m_g, prev.get(("gst", gb))], inc=1)
            prev[("gtmp", gp)] = m_gc
            prev["sl_free"] = m_g
            prev["w2e_free"] = m_g
            prev[("ps", tg % 2)] = m_g
            m_gd[tg] = m_gc
        prev[("w1", sb)] = m_gd[7]
        prev[("w2", sb)] = m_gd[7]
        m_sp = None
        for tg8 in range(8):
            for half in range(2):
                m_sp = r("gpsimd", "dma_start",
                         out=H["sp_g2"][2 * tg8 + half][:, mt, :],
                         in_=gst[:, tg8 * TL + half * TG: tg8 * TL + (half + 1) * TG],
                         deps=[m_gd[tg8]], dma=True)
        prev[("gst", gb)] = m_sp


def _phase_b2(ctx, H, B):
    r = ctx.run
    m_wr = r("gpsimd", "dma_start",
             out=B["wr_sb"][:].rearrange("p (kg n) -> p kg n", n=NE),
             in_=H["wr"].rearrange("(kg p) n -> p kg n", p=128), dma=True)
    r("gpsimd", "dma_start", out=B["b2_sb"][:], in_=H["b2e"][:], dma=True)
    m_w3 = []
    for kg in range(32):
        m_w3.append(r("gpsimd", "dma_start", out=B["w3_sb"][:, kg * D:(kg + 1) * D],
                      in_=H["w3_T"][kg * 128:(kg + 1) * 128, :], dma=True))
    prev = {}
    for tg in range(LIMIT_TG or NTG):
        c = tg // 2
        tt0 = 2 * (tg % 2)
        gld = B["gld_sb"][:].rearrange("p (kg t) -> p kg t", t=TG)
        m_lg = [r("sync", "dma_start", out=gld,
                  in_=H["sp_g2"][tg], deps=[prev.get("gld_free")], dma=True)]
        m_lx = r("sync", "dma_start",
                 out=B["x0c_sb"][:].rearrange("p (tt d) -> p tt d", d=D),
                 in_=H["xs"][tg * TG:(tg + 1) * TG, :].rearrange("(tt p) d -> p tt d", p=128),
                 deps=[prev.get("x0c_free")], dma=True)
        m_lh = r("sync", "dma_start",
                 out=B["hc_sb"][:].rearrange("p (tt d) -> p tt d", d=D),
                 in_=H["sp_h"][c][:, tt0:tt0 + 2, :],
                 deps=[prev.get("hc_free")], dma=True)
        if DEBUG_SCRATCH:
            r("gpsimd", "dma_start", out=H["dbg_gld"][tg], in_=B["gld_sb"][:],
              deps=m_lg, dma=True)
        m_pev = {}
        for rnd in range(2):
            m_mmr = {}
            for mti in range(4):
                mt = 4 * rnd + mti
                dst = B["psD_p"][:, 2 * mti * TG:(2 * mti + 1) * TG]
                m_mm = None
                for kg in range(32):
                    m_mm = r("tensor", "matmul", dst,
                             B["w3_sb"][:, kg * D + mt * 128: kg * D + (mt + 1) * 128],
                             gld[:, kg, :],
                             deps=(m_lg + [m_w3[kg]] +
                                   ([prev.get("psp_free"),
                                     m_pev.get(4 * (rnd - 1) + mti)] if kg == 0 else [])),
                             start=(kg == 0), stop=(kg == 31),
                             inc=(1 if kg == 31 else 0))
                m_mmr[mti] = m_mm
            for mti in range(4):
                mt = 4 * rnd + mti
                m_pev[mt] = r("scalar", "activation",
                              B["pT_sb"][:, mt * TG:(mt + 1) * TG],
                              B["psD_p"][:, 2 * mti * TG:(2 * mti + 1) * TG], AF.Copy,
                              deps=[m_mmr[mti], prev.get("pT_free")], inc=1)
        if DEBUG_SCRATCH:
            m_dbgp = r("gpsimd", "dma_start", out=H["dbg_pT"][tg], in_=B["pT_sb"][:],
                       deps=[m_pev[mt2] for mt2 in range(8)], dma=True)
            prev["pT_free"] = m_dbgp
        m_x1 = {}
        for tt in range(2):
            g3 = 2 * tg + tt
            m_mm = None
            for dg in range(8):
                m_mm = r("tensor", "matmul", B["psD_t"][:, dg * 128:(dg + 1) * 128],
                         B["pT_sb"][:, dg * TG + tt * 128: dg * TG + (tt + 1) * 128],
                         B["ident_bf"][:],
                         deps=[m_pev[dg], prev.get("pst")],
                         start=True, stop=True, inc=(1 if dg == 7 else 0))
            r("vector", "tensor_tensor", B["x0c_sb"][:, tt * D:(tt + 1) * D],
              B["x0c_sb"][:, tt * D:(tt + 1) * D], B["hc_sb"][:, tt * D:(tt + 1) * D],
              OP.add, deps=[m_lx, m_lh])
            m_tt = r("vector", "tensor_tensor", B["x1_sb"][:, tt * D:(tt + 1) * D],
                     B["x0c_sb"][:, tt * D:(tt + 1) * D], B["psD_t"][:], OP.add,
                     deps=[m_mm, prev.get("x1_free")], inc=1)
            m_xx = r("scalar", "activation", B["trash_bf"][:],
                     B["x1_sb"][:, tt * D:(tt + 1) * D], AF.Copy,
                     accum_out=B["sum3_s"][:, g3:g3 + 1], deps=[m_tt], inc=1)
            prev["pst"] = m_xx
            ctx.tap("b2_x0h_t%d_%d" % (tg, tt), B["x0c_sb"][:, tt * D:(tt + 1) * D],
                    deps=[m_xx])
            m_x1[tt] = m_xx
            m_sq = r("scalar", "activation", B["trash_bf"][:],
                     B["x1_sb"][:, tt * D:(tt + 1) * D],
                     AF.Square, accum_out=B["ssq3_s"][:, g3:g3 + 1],
                     deps=[m_xx],
                     inc=(1 if tt == 1 else 0))
        prev["x0c_free"] = m_x1[1]
        prev["hc_free"] = m_x1[1]
        prev["pT_free"] = m_x1[1]
        ctx.tap("b2_x1_t%d" % tg, B["x1_sb"][:], deps=[m_x1[1]])
        m_spx = r("gpsimd", "dma_start", out=H["sp_x1"][tg],
                  in_=B["x1_sb"][:].rearrange("p (tt d) -> p tt d", d=D),
                  deps=[m_x1[1]], dma=True)
        ctx.drain("vector")
        ctx.autodrain = True
        sl = slice(2 * tg, 2 * tg + 2)
        r("vector", "tensor_scalar", B["sm_b"][:, sl], B["sum3_s"][:, sl], 1.0 / D, None,
          OP.mult, deps=[m_x1[1], m_sq])
        r("vector", "tensor_scalar", B["sm_a"][:, sl], B["ssq3_s"][:, sl], 1.0 / D, None,
          OP.mult)
        r("vector", "scalar_tensor_tensor", B["b03_s"][:, sl], B["sm_b"][:, sl],
          -1.0, B["sm_b"][:, sl], OP.mult, OP.mult)
        r("vector", "tensor_tensor", B["sm_a"][:, sl], B["sm_a"][:, sl], B["b03_s"][:, sl],
          OP.add)
        m_v = r("vector", "tensor_scalar", B["sm_a"][:, sl], B["sm_a"][:, sl], EPS, None,
                OP.add, inc=1)
        r("scalar", "activation", B["sm_a"][:, sl], B["sm_a"][:, sl], AF.Ln, deps=[m_v])
        m_r3 = r("scalar", "activation", B["rstd3_s"][:, sl], B["sm_a"][:, sl], AF.Exp,
                 scale=-0.5, inc=1)
        m_b3 = r("vector", "scalar_tensor_tensor", B["b03_s"][:, sl], B["sm_b"][:, sl],
                 -1.0, B["rstd3_s"][:, sl], OP.mult, OP.mult, deps=[m_r3], inc=1)
        ctx.autodrain = False
        m_cast = None
        for tt in range(2):
            g3 = 2 * tg + tt
            m_cast = r("scalar", "activation", B["u3bf_sb"][:, tt * D:(tt + 1) * D],
                       B["x1_sb"][:, tt * D:(tt + 1) * D], AF.Identity,
                       scale=B["rstd3_s"][:, g3:g3 + 1], bias=B["b03_s"][:, g3:g3 + 1],
                       deps=[m_b3, prev.get("u3bf_free")], inc=(1 if tt == 1 else 0))
        prev["x1_free"] = m_cast
        for half in range(2):
            m_mm = None
            for dgi in range(4):
                dg = half * 4 + dgi
                for tt in range(2):
                    m_mm = r("tensor", "matmul",
                             B["psD_u"][:, dgi * TG + tt * 128: dgi * TG + (tt + 1) * 128],
                             B["u3bf_sb"][:, tt * D + dg * 128: tt * D + (dg + 1) * 128],
                             B["ident_bf"][:],
                             deps=[m_cast, prev.get("psu")],
                             start=True, stop=True,
                             inc=(1 if (dgi == 3 and tt == 1) else 0))
            m_ue = None
            for dgi in range(4):
                dg = half * 4 + dgi
                m_ue = r("scalar", "activation",
                         B["u3T_sb"][:, dg * NT + tg * TG: dg * NT + (tg + 1) * TG],
                         B["psD_u"][:, dgi * TG:(dgi + 1) * TG], AF.Copy,
                         deps=[m_mm], inc=(1 if dgi == 3 else 0))
            prev["psu"] = m_ue
        prev["u3bf_free"] = prev["psu"]
        m_u3T = prev["psu"]
        m_lgmm = None
        for tt in range(2):
            for kg in range(8):
                m_lgmm = r("tensor", "matmul", B["psD_p"][:, tt * NE: tt * NE + NE],
                           B["u3T_sb"][:, kg * NT + tg * TG + tt * 128:
                                       kg * NT + tg * TG + (tt + 1) * 128],
                           B["wr_sb"][:, kg * NE:(kg + 1) * NE],
                           deps=([m_u3T, m_wr] if (tt == 0 and kg == 0) else ()),
                           start=(kg == 0), stop=(kg == 7),
                           inc=(1 if kg == 7 else 0))
        m_lgv = {}
        for tt in range(2):
            m_lgv[tt] = r("vector", "tensor_tensor", B["lg_sb"][:, tt * NE:(tt + 1) * NE],
                          B["psD_p"][:, tt * NE:tt * NE + NE], B["brb_sb"][:], OP.add,
                          deps=[m_lgmm], inc=1)
        ctx.autodrain = True
        for tt in range(2):
            g3 = 2 * tg + tt
            lg = B["lg_sb"][:, tt * NE:(tt + 1) * NE]
            r("vector", "max", B["m8_sb"][:, tt * 8:(tt + 1) * 8], lg, deps=[m_lgv[tt]])
            ctx.drain("vector")
            r("vector", "max_index", B["i8_sb"][:, tt * 8:(tt + 1) * 8],
              B["m8_sb"][:, tt * 8:(tt + 1) * 8], lg)
            ctx.drain("vector")
            r("vector", "tensor_scalar", B["idxf_sb"][:, tt * 2:(tt + 1) * 2],
              B["i8_sb"][:, tt * 8:tt * 8 + 2], 0.0, None, OP.add)
            m_vv = r("vector", "tensor_scalar", B["nm1_sb"][:, tt:tt + 1],
                     B["m8_sb"][:, tt * 8:tt * 8 + 1], -1.0, None, OP.mult, inc=1)
            m_e = r("scalar", "activation", B["t21_sb"][:, tt:tt + 1],
                    B["m8_sb"][:, tt * 8 + 1:tt * 8 + 2], AF.Exp,
                    bias=B["nm1_sb"][:, tt:tt + 1], deps=[m_vv], inc=1)
            r("vector", "tensor_scalar", B["den_sb"][:, tt:tt + 1],
              B["t21_sb"][:, tt:tt + 1], 1.0, None, OP.add, deps=[m_e])
            r("vector", "reciprocal", B["w1r_sb"][:, tt:tt + 1], B["den_sb"][:, tt:tt + 1])
            r("vector", "tensor_tensor", B["w2r_sb"][:, tt:tt + 1],
              B["t21_sb"][:, tt:tt + 1], B["w1r_sb"][:, tt:tt + 1], OP.mult)
            r("vector", "tensor_scalar", B["mk1_sb"][:], B["iota16_sb"][:],
              B["idxf_sb"][:, tt * 2:tt * 2 + 1], B["w1r_sb"][:, tt:tt + 1],
              OP.is_equal, OP.mult)
            r("vector", "tensor_scalar", B["mk2_sb"][:], B["iota16_sb"][:],
              B["idxf_sb"][:, tt * 2 + 1:tt * 2 + 2], B["w2r_sb"][:, tt:tt + 1],
              OP.is_equal, OP.mult)
            r("vector", "tensor_tensor", B["wfull_sb"][:, g3 * NE:(g3 + 1) * NE],
              B["mk1_sb"][:], B["mk2_sb"][:], OP.add)
            m_wf = r("vector", "tensor_scalar", B["wfbf_sb"][:],
                     B["wfull_sb"][:, g3 * NE:(g3 + 1) * NE], 1.0, None, OP.mult, inc=1)
            m_tr = r("tensor", "matmul", B["psD_u"][0:16, 0:128],
                     B["wfbf_sb"][:], B["ident_bf"][:],
                     deps=[m_wf, prev.get("wrt_ev"), prev.get(("psu", 0))],
                     start=True, stop=True, inc=1)
            m_wrt = r("scalar", "activation", B["wroute_sb"][:, g3 * 128:(g3 + 1) * 128],
                      B["psD_u"][0:16, 0:128], AF.Copy, deps=[m_tr], inc=1)
            prev["wrt_ev"] = m_wrt
            prev[("psu", 0)] = m_wrt
        ctx.autodrain = False
        if tg == (LIMIT_TG or NTG) - 1:
            ctx.tap("b2_u3T", B["u3T_sb"][:], deps=[prev["psu"], prev["wrt_ev"]])
        ctx.tap("b2_wfull_t%d" % tg, B["wfull_sb"][:], deps=[prev["wrt_ev"]])
        prev["gld_free"] = m_pev[7]
        prev["psp_free"] = m_lgv[1]


def _phase_c(ctx, H, B):
    r = ctx.run
    prev = {}

    def load_e(e, which):
        msd = []
        if which in ("W1", "both"):
            for half in range(2):
                dst = B["W1X_sb"] if half == 0 else B["W1Y_sb"]
                msd.append(r("gpsimd", "dma_start",
                             out=dst[:].rearrange("p (kk m) -> p kk m", m=HID),
                             in_=H["W1e"][e][half * 512:(half + 1) * 512, :].rearrange(
                                 "(kk p) m -> p kk m", p=128),
                             deps=[prev.get("W1_free")], dma=True))
        if which in ("W2", "both"):
            for half in range(2):
                dst = B["W2X_sb"] if half == 0 else B["W2Y_sb"]
                msd.append(r("gpsimd", "dma_start",
                             out=dst[:].rearrange("p (kk m) -> p kk m", m=D),
                             in_=H["W2e"][e][half * 512:(half + 1) * 512, :].rearrange(
                                 "(kk p) m -> p kk m", p=128),
                             deps=[prev.get("W2_free")], dma=True))
        return msd

    def load_vec_e(e):
        m1 = r("gpsimd", "dma_start", out=B["g1e_sb"][:], in_=H["g1b"][e],
               deps=[prev.get("g1_free")], dma=True)
        m2 = r("gpsimd", "dma_start", out=B["be1e_sb"][:], in_=H["be1b"][e],
               deps=[prev.get("be1_free")], dma=True)
        m3 = r("gpsimd", "dma_start", out=B["b1e_sb"][:], in_=H["b1b"][e],
               deps=[prev.get("b1_free")], dma=True)
        return [m1, m2, m3]

    ctx.tap("c_u3T", B["u3T_sb"][:])
    for nm in ("sum3_s", "ssq3_s", "rstd3_s", "b03_s"):
        ctx.tap("c_" + nm, B[nm][:])
    mW = load_e(0, "both")
    mV = load_vec_e(0)
    for e in range(LIMIT_E or NE):
        m_sb1 = r("vector", "tensor_reduce", B["Sb1_sb"][:], B["b1e_sb"][:], AX.X, OP.add,
                  deps=[mV[2]], inc=1)
        m_sb1h = r("vector", "tensor_scalar", B["Sb1_sb"][:], B["Sb1_sb"][:],
                   1.0 / HID, None, OP.mult, inc=1)
        m_w1_last = None
        m_w2_last = None
        m_fadd_all = None
        for tg in range(LIMIT_TG or NTG):
            m_gelu = {}
            for tt in range(2):
                hb2 = tt % 2
                m_mm = None
                for kg in range(8):
                    wbuf = B["W1X_sb"] if kg < 4 else B["W1Y_sb"]
                    kk = kg % 4
                    for nh in range(2):
                        m_mm = r("tensor", "matmul", B["psE_w1"][:, nh * 512:(nh + 1) * 512],
                                 B["u3T_sb"][:, kg * NT + tg * TG + tt * 128:
                                             kg * NT + tg * TG + (tt + 1) * 128],
                                 wbuf[:, kk * HID + nh * 512: kk * HID + (nh + 1) * 512],
                                 deps=(mW + [prev.get("psw1_free")] if kg == 0 and nh == 0
                                       else ()),
                                 start=(kg == 0), stop=(kg == 7),
                                 inc=(1 if (kg == 7 and nh == 1) else 0))
                m_w1_last = m_mm
                hch = B["hch_sb"][:, hb2 * HID:(hb2 + 1) * HID]
                m_ev = r("scalar", "activation", hch, B["psE_w1"][:], AF.Copy,
                         accum_out=B["sums_sb"][:, tt:tt + 1],
                         deps=[m_mm, prev.get(("hch", hb2))], inc=1)
                prev["psw1_free"] = m_ev
                m_b1 = r("gpsimd", "tensor_tensor", hch, hch, B["b1e_sb"][:], OP.add,
                         deps=[m_ev, mV[2]], inc=1)
                m_sq = r("scalar", "activation",
                         B["trash_bf"][:], hch, AF.Square,
                         accum_out=B["ssqE_sb"][:, tt:tt + 1], deps=[m_b1], inc=0)
                ss = slice(tt, tt + 1)
                r("scalar", "activation", B["muE_sb"][:, ss], B["sums_sb"][:, ss],
                  AF.Identity, scale=1.0 / HID, bias=B["Sb1_sb"][:],
                  deps=[m_sb1h])
                r("scalar", "activation", B["sm_b"][:, ss], B["muE_sb"][:, ss],
                  AF.Square)
                m_e2 = r("scalar", "activation", B["sm_a"][:, ss], B["ssqE_sb"][:, ss],
                         AF.Copy, scale=1.0 / HID, inc=1)
                m_vv = r("vector", "scalar_tensor_tensor", B["sm_a"][:, ss],
                         B["sm_a"][:, ss], EPS, B["sm_b"][:, ss],
                         OP.add, OP.subtract, deps=[m_e2], inc=1)
                r("scalar", "activation", B["sm_a"][:, ss], B["sm_a"][:, ss], AF.Ln,
                  deps=[m_vv])
                r("scalar", "activation", B["rstdE_sb"][:, ss], B["sm_a"][:, ss],
                  AF.Exp, scale=-0.5)
                r("scalar", "activation", B["sm_b"][:, ss], B["muE_sb"][:, ss],
                  AF.Identity, scale=B["rstdE_sb"][:, ss])
                r("scalar", "activation", B["bE_sb"][:, ss], B["sm_b"][:, ss],
                  AF.Identity, scale=-1.0)
                m_af = r("scalar", "activation", hch, hch, AF.Identity,
                         scale=B["rstdE_sb"][:, ss], bias=B["bE_sb"][:, ss], inc=1)
                r("vector", "tensor_tensor", hch, hch, B["g1e_sb"][:], OP.mult,
                  deps=[mV[0], m_af])
                m_t5 = r("vector", "tensor_tensor", hch, hch, B["be1e_sb"][:], OP.add,
                         deps=[mV[1]], inc=1)
                gl = B["gl_sb"][:, hb2 * HID:(hb2 + 1) * HID]
                m_gl = r("scalar", "activation", gl, hch, AF.Gelu,
                         deps=[m_t5, prev.get(("gl", hb2))], inc=1)
                prev[("hch", hb2)] = m_gl
                prev[("t5", hb2)] = m_t5
                m_gelu[tt] = m_gl
                g3 = 2 * tg + tt
                dg_t = B["dg_sb"][:, hb2 * 128:(hb2 + 1) * 128]
                m_dg = r("vector", "tensor_scalar", dg_t, B["ident_bf"][:],
                         B["wfull_sb"][:, g3 * NE + e: g3 * NE + e + 1], None, OP.mult,
                         deps=[prev.get(("dg", hb2))], inc=1)
                for half in range(2):
                    m_mm2 = None
                    for hbi in range(4):
                        hb = half * 4 + hbi
                        m_mm2 = r("tensor", "matmul",
                                  B["psE_tr"][:, hbi * 128:(hbi + 1) * 128],
                                  gl[:, hb * 128:(hb + 1) * 128], dg_t,
                                  deps=[m_gl, m_dg, prev.get("pstr_free")],
                                  start=True, stop=True, inc=(1 if hbi == 3 else 0))
                    m_tev = r("scalar", "activation",
                              B["glT_sb"][:].rearrange(
                                  "p (hb t) -> p hb t", t=TG)[:, half * 4:(half + 1) * 4,
                                                              tt * 128:(tt + 1) * 128],
                              B["psE_tr"][:].rearrange("p (hb t) -> p hb t", t=128),
                              AF.Copy, deps=[m_mm2, prev.get("glT_free")], inc=1)
                    prev["pstr_free"] = m_tev
                    prev[("glT", tt, half)] = m_tev
                prev[("dg", hb2)] = prev[("glT", tt, 1)]
            m_glT = prev[("glT", 1, 1)]
            m_mm3 = None
            first_dep = [m_glT, prev.get("pso_free")] + list(mW)
            for mt in range(8):
                if e == 0:
                    m_mm3 = r("tensor", "matmul", B["psE_o"][:, mt * TG:(mt + 1) * TG],
                              B["b2_sb"][:, mt * 128:(mt + 1) * 128],
                              B["wroute_sb"][:, tg * TG:(tg + 1) * TG],
                              deps=(first_dep if mt == 0 else ()),
                              start=True, stop=False, inc=0)
                for kg in range(8):
                    wbuf = B["W2X_sb"] if kg < 4 else B["W2Y_sb"]
                    kk = kg % 4
                    m_mm3 = r("tensor", "matmul", B["psE_o"][:, mt * TG:(mt + 1) * TG],
                              wbuf[:, kk * D + mt * 128: kk * D + (mt + 1) * 128],
                              B["glT_sb"][:, kg * TG:(kg + 1) * TG],
                              deps=(first_dep if (e > 0 and mt == 0 and kg == 0) else ()),
                              start=(kg == 0 and e != 0), stop=(kg == 7),
                              inc=(1 if (mt == 7 and kg == 7) else 0))
            m_w2_last = m_mm3
            prev[("glT_free")] = m_mm3
            m_fa = None
            for mt in range(8):
                fsl = B["f_sb"][:, mt * NT + tg * TG: mt * NT + (tg + 1) * TG]
                if e == 0:
                    m_fa = r("vector", "tensor_scalar", fsl,
                             B["psE_o"][:, mt * TG:(mt + 1) * TG], 0.0, None, OP.add,
                             deps=[m_mm3], inc=(1 if mt == 7 else 0), war=True)
                else:
                    m_fa = r("vector", "tensor_tensor", fsl,
                             B["psE_o"][:, mt * TG:(mt + 1) * TG], fsl, OP.add,
                             deps=[m_mm3], inc=(1 if mt == 7 else 0), war=True)
            prev["pso_free"] = m_fa
            m_fadd_all = m_fa
        if e + 1 < NE:
            prev["W1_free"] = m_w1_last
            prev["W2_free"] = m_w2_last
            prev["g1_free"] = prev[("t5", 1)]
            prev["be1_free"] = prev[("t5", 1)]
            prev["b1_free"] = prev[("t5", 1)]
            mW = load_e(e + 1, "both")
            mV = load_vec_e(e + 1)


def _phase_fin(ctx, H, B, out_h):
    r = ctx.run
    prev = {}
    for tg in range(LIMIT_TG or NTG):
        m_lx = r("sync", "dma_start",
                 out=B["x1f_sb"][:].rearrange("p (tt d) -> p tt d", d=D),
                 in_=H["sp_x1"][tg], deps=[prev.get("x1f_free")], dma=True)
        m_x2 = {}
        for tt in range(2):
            m_mm = None
            for og in range(8):
                m_mm = r("tensor", "matmul", B["psE_f"][:, og * 128:(og + 1) * 128],
                         B["f_sb"][:, og * NT + tg * TG + tt * 128:
                                   og * NT + tg * TG + (tt + 1) * 128],
                         B["ident_bf"][:],
                         deps=[prev.get("psf")],
                         start=True, stop=True, inc=(1 if og == 7 else 0))
            m_x2[tt] = r("vector", "tensor_tensor", B["x2_sb"][:, tt * D:(tt + 1) * D],
                         B["x1f_sb"][:, tt * D:(tt + 1) * D], B["psE_f"][:], OP.add,
                         deps=[m_mm, m_lx, prev.get("x2_free")], inc=1)
            prev["psf"] = m_x2[tt]
        m_st = r("gpsimd", "dma_start",
                 out=out_h[tg * TG:(tg + 1) * TG, :].rearrange("(tt p) d -> p tt d", p=128),
                 in_=B["x2_sb"][:].rearrange("p (tt d) -> p tt d", d=D),
                 deps=[m_x2[1]], dma=True)
        prev["x2_free"] = m_st
        prev["x1f_free"] = m_x2[1]


_TAPDEFS = {}
for _c in range(NCHUNK):
    _TAPDEFS["a1_x0bf_c%d" % _c] = ((128, 4 * D), BF16)
    _TAPDEFS["a1_n1T_c%d" % _c] = ((128, 8 * TL), BF16)
    _TAPDEFS["a1_sum_c%d" % _c] = ((128, 32), F32)
    _TAPDEFS["a1_mu0_c%d" % _c] = ((128, 32), F32)
    _TAPDEFS["a1_rstd0_c%d" % _c] = ((128, 32), F32)
    _TAPDEFS["a1_var_c%d" % _c] = ((128, 32), F32)
    _TAPDEFS["a1_ssq_c%d" % _c] = ((128, 32), F32)
    _TAPDEFS["a1_rs01_c%d" % _c] = ((128, 32), F32)
    _TAPDEFS["a1_uact_c%d" % _c] = ((128, 16 * TL), BF16)
    _TAPDEFS["a1_z_c%d" % _c] = ((128, 16 * TL), BF16)
    _TAPDEFS["a1_B_c%d" % _c] = ((16, TL), F32)
    _TAPDEFS["a1_dtr_c%d" % _c] = ((64, TL), BF16)
    _TAPDEFS["a2_gated_c%d" % _c] = ((128, 16 * TL), BF16)
    _TAPDEFS["a2_h_c%d" % _c] = ((128, 4 * D), F32)
_TAPDEFS["b2_u3T"] = ((128, 8 * NT), BF16)
_TAPDEFS["dbg_pdt_t0"] = ((128, D), BF16)
_TAPDEFS["dbg_pdt_t1"] = ((128, D), BF16)
for _n in ("sum3_s", "ssq3_s", "rstd3_s", "b03_s"):
    _TAPDEFS["c_" + _n] = ((128, 32), F32)
_TAPDEFS["c_u3T"] = ((128, 8 * NT), BF16)
for _t in range(NTG):
    _TAPDEFS["b2_pT_t%d" % _t] = ((128, 8 * TG), BF16)
    _TAPDEFS["b2_gld_t%d" % _t] = ((128, 32 * TG), BF16)
    for _u in range(2):
        _TAPDEFS["b2_x0h_t%d_%d" % (_t, _u)] = ((128, D), F32)
    _TAPDEFS["b2_x1_t%d" % _t] = ((128, 2 * D), F32)
    _TAPDEFS["b2_wfull_t%d" % _t] = ((128, 32 * NE), F32)


def kernel(**inputs):
    global _last_results
    shared = _prep_shared(inputs)
    x = np.asarray(inputs["L0_attn_x"], dtype=np.float32)
    nc = build()
    in_maps = []
    for c in range(NCORE):
        m = dict(shared)
        m["xs"] = _asc(x[2 * c:2 * c + 2].reshape(NT, D))
        in_maps.append(m)
    res = run_bass_kernel_spmd(nc, in_maps, core_ids=list(range(NCORE)))
    _last_results = res
    out = np.concatenate([np.asarray(res.results[c]["out"]).reshape(2, L, D)
                          for c in range(NCORE)], axis=0)
    return out


# version 03
# speedup vs baseline: 1.3530x; 1.3370x over previous
import numpy as np
from contextlib import ExitStack
import concourse.bass as bass
import concourse.mybir as mybir
from concourse.bass_utils import run_bass_kernel_spmd
from concourse.alu_op_type import AluOpType as OP

F32 = mybir.dt.float32
BF16 = mybir.dt.bfloat16
U32 = mybir.dt.uint32
AF = mybir.ActivationFunctionType
AX = mybir.AxisListType

NCORE = 8
L = 2048
D = 1024
DI = 2048
DS = 16
DTR = 64
DFF = 4096
NE = 16
HID = 1024
NT = 4096
TL = 512
NCHUNK = NT // TL
TG = 256
NTG = NT // TG
EPS = 1e-5

ENGINES = ("sync", "tensor", "scalar", "vector", "gpsimd")
PHASES = ("A1", "A2", "B1", "B2", "C", "FIN")
TAPS = ()
LIMIT_CHUNKS = None
LIMIT_TG = None
DEBUG_SCRATCH = False
LIMIT_E = None
_last_results = None


NDQ = 4


class _Eng:
    __slots__ = ("name", "obj", "sem_c", "dq", "cnt_c", "dq_cnt", "dq_idx", "seen")

    def __init__(self, name, obj, sem_c, dq):
        self.name, self.obj = name, obj
        self.sem_c, self.dq = sem_c, dq
        self.reset()

    def reset(self):
        self.cnt_c = 0
        self.dq_cnt = [0] * NDQ
        self.dq_idx = 0
        self.seen = {}


class Ctx:
    def __init__(self, nc, sems):
        self.nc = nc
        self.engs = {n: _Eng(n, getattr(nc, n), sems[n + "_c"],
                             [sems["%s_d%d" % (n, i)] for i in range(NDQ)])
                     for n in ENGINES}
        self.cur = None
        self.taps = {}
        self.autodrain = False

    def begin(self, ename):
        self.cur = ename
        for e in self.engs.values():
            e.reset()

    def dep(self, ename, m, for_dma=False):
        if m is None:
            return
        e = self.engs[ename]
        dn, dv, kind = m
        if kind == "c" and dn == ename and not for_dma:
            return
        key = (dn, kind)
        if e.seen.get(key, 0) >= dv:
            return
        e.seen[key] = dv
        if self.cur == ename:
            if kind == "d":
                qn, slot = dn
                e.obj.wait_ge(self.engs[qn].dq[slot], dv)
            else:
                e.obj.wait_ge(self.engs[dn].sem_c, dv)

    def run(self, ename, meth, *args, deps=(), inc=0, dma=False, war=False, **kw):
        for m in deps:
            self.dep(ename, m, for_dma=dma)
        e = self.engs[ename]
        if self.autodrain and ename == "vector" and not dma and self.cur == ename:
            e.obj.drain()
        if dma:
            slot = e.dq_idx % NDQ
            e.dq_idx += 1
            if e.dq_cnt[slot]:
                self.dep(ename, ((ename, slot), e.dq_cnt[slot], "d"))
            if self.cur == ename:
                ins = getattr(e.obj, meth)(*args, **kw)
                ins.then_inc(e.dq[slot], 16)
            e.dq_cnt[slot] += 16
            return ((ename, slot), e.dq_cnt[slot], "d")
        if self.cur == ename:
            ins = getattr(e.obj, meth)(*args, **kw)
            if inc:
                if (not war and meth != "drain"
                        and ename in ("scalar", "vector", "gpsimd")):
                    e.obj.drain().then_inc(e.sem_c, inc)
                else:
                    ins.then_inc(e.sem_c, inc)
        e.cnt_c += inc
        if inc:
            return (ename, e.cnt_c, "c")
        return None

    def drain(self, ename):
        if self.cur == ename:
            self.engs[ename].obj.drain()

    def barrier(self):
        ms = {}
        for n in ENGINES:
            e = self.engs[n]
            for slot in range(NDQ):
                if e.dq_cnt[slot]:
                    self.dep(n, ((n, slot), e.dq_cnt[slot], "d"))
            ms[n] = self.run(n, "drain", inc=1)
        for n in ENGINES:
            for o in ENGINES:
                if o != n:
                    self.dep(n, ms[o])
        return ms

    def tap(self, name, src_ap, deps=()):
        if name in self.taps:
            return self.run("gpsimd", "dma_start", out=self.taps[name], in_=src_ap,
                            deps=deps, dma=True)
        return None


def _asc(a):
    return np.ascontiguousarray(a, dtype=np.float32)


def _prep_shared(kw):
    d = {}
    d["w_in_T"] = _asc(kw["L0_attn_in_proj_w"].T)
    xp = np.zeros((DI, 128), dtype=np.float32)
    xpw = np.asarray(kw["L0_attn_x_proj_w"], dtype=np.float32)
    xp[:, 0:16] = xpw[DTR:DTR + DS].T
    xp[:, 32:48] = xpw[DTR + DS:DTR + 2 * DS].T
    xp[:, 64:128] = xpw[0:DTR].T
    d["w_xp_pad"] = _asc(xp)
    d["w_dt_T"] = _asc(kw["L0_attn_dt_proj_w"].T)
    d["w_out_T"] = _asc(kw["L0_attn_out_proj_w"].T)
    d["w1_T"] = _asc(kw["L0_attn_w1"].T)
    d["w2_T"] = _asc(kw["L0_attn_w2"].T)
    d["w3_T"] = _asc(kw["L0_attn_w3"].T)
    d["W1e"] = _asc(kw["L0_ffn_W1"])
    d["W2e"] = _asc(kw["L0_ffn_W2"])
    d["wr"] = _asc(kw["L0_ffn_Wr"])
    d["b2e"] = _asc(kw["L0_ffn_b2"])
    for nm, src in (("g1b", "L0_ffn_g1"), ("be1b", "L0_ffn_be1"), ("b1b", "L0_ffn_b1")):
        v = np.asarray(kw[src], dtype=np.float32)
        d[nm] = _asc(np.broadcast_to(v[:, None, :], (NE, 128, HID)))
    d["A_log_r"] = _asc(np.asarray(kw["L0_attn_A_log"]).reshape(16, 128, DS)
                        .transpose(1, 0, 2))
    d["convw_r"] = _asc(np.asarray(kw["L0_attn_conv_w"]).reshape(DI, 4)
                        .reshape(16, 128, 4).transpose(1, 0, 2))
    d["convb_r"] = _asc(np.asarray(kw["L0_attn_conv_b"]).reshape(16, 128).T)
    d["dtb_r"] = _asc(np.asarray(kw["L0_attn_dt_proj_b"]).reshape(16, 128).T)
    d["D_r"] = _asc(np.asarray(kw["L0_attn_D"]).reshape(16, 128).T)
    d["n1w_pp"] = _asc(np.asarray(kw["L0_attn_norm1_w"]).reshape(8, 128).T)
    d["n2w_pp"] = _asc(np.asarray(kw["L0_attn_norm2_w"]).reshape(8, 128).T)
    d["brb"] = _asc(np.broadcast_to(np.asarray(kw["L0_ffn_br"])[None, :], (128, NE)))
    d["iota16"] = _asc(np.broadcast_to(np.arange(NE, dtype=np.float32)[None, :], (128, NE)))
    d["ident"] = _asc(np.eye(128, dtype=np.float32))
    oh = np.zeros((DS, DS * 128), dtype=np.float32)
    for n in range(DS):
        oh[n, n * 128:(n + 1) * 128] = 1.0
    d["onehot"] = _asc(oh)
    return d


def _alloc(nc, es0):
    B = {}
    E0 = es0.enter_context

    def mk(scope, name, shape, dt):
        B[name] = scope(nc.sbuf_tensor(name, list(shape), dt))

    def mkp(scope, name, shape, dt=F32):
        B[name] = scope(nc.psum_tensor(name, list(shape), dt))

    mk(E0, "ident_bf", (128, 128), BF16)
    mk(E0, "iota16_sb", (128, NE), F32)
    mk(E0, "brb_sb", (128, NE), F32)
    mk(E0, "n1w_sb", (128, 8), F32)
    mk(E0, "n2w_sb", (128, 8), F32)
    mk(E0, "A_sb", (128, 16 * DS), F32)
    mk(E0, "convw_sb", (128, 16 * 4), F32)
    mk(E0, "convb_sb", (128, 16), F32)
    mk(E0, "dtb_sb", (128, 16), F32)
    mk(E0, "D_sb", (128, 16), F32)

    for nm in ("sum_s", "ssq_s", "mu0_s", "rstd0_s", "b00_s", "rs01_s", "b01_s",
               "rstd2_s", "sum3_s", "ssq3_s", "rstd3_s", "b03_s"):
        mk(E0, nm, (128, 32), F32)
    mk(E0, "sm_a", (128, 32), F32)
    mk(E0, "sm_b", (128, 32), F32)
    mk(E0, "trash_bf", (128, 1024), BF16)
    mk(E0, "state_sb", (128, 16 * DS), F32)
    mk(E0, "halo_sb", (128, 16 * 3), BF16)

    sA = ExitStack()
    EA = sA.enter_context
    sA1 = ExitStack()
    E1 = sA1.enter_context
    mk(E1, "w_in_sb", (128, 8 * DFF), BF16)
    mk(E1, "w_xp_sb", (128, 16 * 128), BF16)
    mk(E1, "x0a_sb", (128, 2 * 4 * D), F32)
    mk(E1, "x0bf_sb", (128, 4 * D), BF16)
    mk(E1, "n1T_sb", (128, 8 * TL), BF16)
    mk(E1, "ustg_sb", (128, 2 * 515), BF16)
    mk(E1, "cacc_sb", (128, 2 * TL), BF16)
    mk(E1, "uact_sb", (128, 16 * TL), BF16)
    mk(E1, "zs_sb", (128, 16 * TL), BF16)
    mk(E1, "Bst_sb", (16, TL), F32)
    mk(E1, "Cst_sb", (16, TL), F32)
    mk(E1, "dtrst_sb", (64, TL), BF16)
    mkp(E1, "psA_tr0", (128, TL))
    mkp(E1, "psA_tr1", (128, TL))
    mkp(E1, "psA_mm0", (128, TL))
    mkp(E1, "psA_mm1", (128, TL))
    mkp(E1, "psA_xp", (128, TL))
    sA1.close()
    sA2 = ExitStack()
    E2 = sA2.enter_context
    mk(E2, "onehot_bf", (16, DS * 128), BF16)
    mk(E2, "w_dt_sb", (64, DI), BF16)
    mk(E2, "w_out_sb", (128, 16 * D), BF16)
    mk(E2, "x0b_sb", (128, 4 * D), F32)
    mk(E2, "uact2_sb", (128, 16 * TL), BF16)
    mk(E2, "zs2_sb", (128, 16 * TL), BF16)
    mk(E2, "dtr2_sb", (64, TL), BF16)
    mk(E2, "Bbf_sb", (16, TL), BF16)
    mk(E2, "Cbf_sb", (16, TL), BF16)
    mk(E2, "Bb_sb", (128, 16 * TL), BF16)
    mk(E2, "Cb_sb", (128, 16 * TL), BF16)
    mk(E2, "sp1_sb", (128, TL), F32)
    mk(E2, "sp2_sb", (128, TL), F32)
    mk(E2, "dt_sb", (128, 2 * TL), F32)
    mk(E2, "du_sb", (128, 2 * TL), BF16)
    mk(E2, "aBuf_sb", (128, 2 * 2 * TL), F32)
    mk(E2, "bs_sb", (128, 2 * TL), BF16)
    mk(E2, "hb_sb", (128, 2 * TL), BF16)
    mk(E2, "yp_sb", (128, 4 * TL), BF16)
    mk(E2, "yt_sb", (128, TL), BF16)
    mk(E2, "gt_sb", (128, TL), BF16)
    mk(E2, "mT_sb", (128, 8 * TL), BF16)
    mk(E2, "h_sb", (128, 4 * D), F32)
    mk(E2, "h2bf_sb", (128, 4 * D), BF16)
    mk(E2, "n2Tst_sb", (128, 8 * TL), BF16)
    mkp(E2, "psB_bc", (128, TL))
    mkp(E2, "psB_dt", (128, TL))
    mkp(E2, "psB_op0", (128, TL))
    mkp(E2, "psB_op1", (128, TL))
    mkp(E2, "psB_tr0", (128, 1024))
    mkp(E2, "psB_tr1", (128, 1024))
    sA2.close()
    sA.close()

    sBC = ExitStack()
    EB = sBC.enter_context
    mk(EB, "u3T_sb", (128, 8 * NT), BF16)
    mk(EB, "wfull_sb", (128, 32 * NE), F32)
    mk(EB, "wroute_sb", (16, NT), BF16)
    mk(EB, "wr_sb", (128, 8 * NE), BF16)
    mk(EB, "b2_sb", (16, HID), BF16)
    sB1 = ExitStack()
    E3 = sB1.enter_context
    mk(E3, "n2Ta_sb", (128, 8 * NT), BF16)
    mk(E3, "w1s_sb", (128, 2 * 8 * 128), BF16)
    mk(E3, "w2s_sb", (128, 2 * 8 * 128), BF16)
    mk(E3, "sl_sb", (128, TL), BF16)
    mk(E3, "w2e_sb", (128, TL), BF16)
    mk(E3, "gtmp_sb", (128, 2 * TL), BF16)
    mk(E3, "gst_sb", (128, 2 * NT), BF16)
    mkp(E3, "psC_1", (128, TL))
    mkp(E3, "psC_2", (128, TL))
    mkp(E3, "psC_3", (128, TL))
    mkp(E3, "psC_4", (128, TL))
    sB1.close()
    sB2 = ExitStack()
    E4 = sB2.enter_context
    mk(E4, "w3_sb", (128, 32 * D), BF16)
    mk(E4, "gld_sb", (128, 32 * TG), BF16)
    mk(E4, "x0c_sb", (128, 2 * D), F32)
    mk(E4, "hc_sb", (128, 2 * D), F32)
    mk(E4, "x1_sb", (128, 2 * D), F32)
    mk(E4, "pT_sb", (128, 8 * TG), BF16)
    mk(E4, "u3bf_sb", (128, 2 * D), BF16)
    mk(E4, "lg_sb", (128, 2 * NE), F32)
    mk(E4, "m8_sb", (128, 2 * 8), F32)
    mk(E4, "i8_sb", (128, 2 * 8), U32)
    mk(E4, "idxf_sb", (128, 2 * 2), F32)
    mk(E4, "nm1_sb", (128, 2), F32)
    mk(E4, "t21_sb", (128, 2), F32)
    mk(E4, "den_sb", (128, 2), F32)
    mk(E4, "w1r_sb", (128, 2), F32)
    mk(E4, "w2r_sb", (128, 2), F32)
    mk(E4, "mk1_sb", (128, NE), F32)
    mk(E4, "mk2_sb", (128, NE), F32)
    mk(E4, "wfbf_sb", (128, NE), BF16)
    mkp(E4, "psD_p", (128, 8 * TG))
    mkp(E4, "psD_t", (128, 1024))
    mkp(E4, "psD_u", (128, 4 * TG))
    sB2.close()
    sC = ExitStack()
    E5 = sC.enter_context
    mk(E5, "f_sb", (128, 8 * NT), BF16)
    sC1 = ExitStack()
    E6 = sC1.enter_context
    mk(E6, "W1X_sb", (128, 4 * HID), BF16)
    mk(E6, "W1Y_sb", (128, 4 * HID), BF16)
    mk(E6, "W2X_sb", (128, 4 * D), BF16)
    mk(E6, "W2Y_sb", (128, 4 * D), BF16)
    mk(E6, "g1e_sb", (128, HID), BF16)
    mk(E6, "be1e_sb", (128, HID), BF16)
    mk(E6, "b1e_sb", (128, HID), BF16)
    mk(E6, "Sb1_sb", (128, 1), F32)
    mk(E6, "hch_sb", (128, 2 * HID), BF16)
    mk(E6, "gl_sb", (128, 2 * HID), BF16)
    mk(E6, "dg_sb", (128, 2 * 128), BF16)
    mk(E6, "glT_sb", (128, 8 * TG), BF16)
    mk(E6, "sums_sb", (128, 2), F32)
    mk(E6, "ssqE_sb", (128, 2), F32)
    mk(E6, "muE_sb", (128, 2), F32)
    mk(E6, "rstdE_sb", (128, 2), F32)
    mk(E6, "bE_sb", (128, 2), F32)
    mkp(E6, "psE_w1a", (128, HID))
    mkp(E6, "psE_w1b", (128, HID))
    mkp(E6, "psE_tr", (128, TL))
    mkp(E6, "psE_o", (128, 4 * TG))
    sC1.close()
    sC2 = ExitStack()
    E7 = sC2.enter_context
    mk(E7, "x1f_sb", (128, 2 * D), F32)
    mk(E7, "x2_sb", (128, 2 * D), F32)
    mkp(E7, "psE_f", (128, 1024))
    sC2.close()
    sC.close()
    sBC.close()
    return B


def build(core_inputs_shapes=None):
    nc = bass.Bass("TRN2", target_bir_lowering=False, detect_race_conditions=False)
    H = {}

    def din(name, shape, dt=F32):
        H[name] = nc.dram_tensor(name, list(shape), dt, kind="ExternalInput").ap()

    din("xs", (NT, D))
    din("w_in_T", (D, DFF)); din("w_xp_pad", (DI, 128)); din("w_dt_T", (DTR, DI))
    din("w_out_T", (DI, D)); din("w1_T", (D, DFF)); din("w2_T", (D, DFF))
    din("w3_T", (DFF, D)); din("W1e", (NE, D, HID)); din("W2e", (NE, HID, D))
    din("wr", (D, NE)); din("b2e", (NE, D))
    din("g1b", (NE, 128, HID)); din("be1b", (NE, 128, HID)); din("b1b", (NE, 128, HID))
    din("A_log_r", (128, 16, DS)); din("convw_r", (128, 16, 4)); din("convb_r", (128, 16))
    din("dtb_r", (128, 16)); din("D_r", (128, 16)); din("n1w_pp", (128, 8))
    din("n2w_pp", (128, 8)); din("brb", (128, NE)); din("iota16", (128, NE))
    din("ident", (128, 128)); din("onehot", (16, DS * 128))
    out_h = nc.dram_tensor("out", [NT, D], F32, kind="ExternalOutput").ap()

    def scr(name, shape, dt):
        kind = "ExternalOutput" if DEBUG_SCRATCH else "Internal"
        H[name] = nc.dram_tensor(name, list(shape), dt, kind=kind).ap()

    scr("sp_uact", (NCHUNK, 128, 16, TL), BF16)
    scr("sp_z", (NCHUNK, 128, 16, TL), BF16)
    scr("sp_dtr", (NCHUNK, 64, TL), BF16)
    scr("sp_B", (NCHUNK, 16, TL), F32)
    scr("sp_C", (NCHUNK, 16, TL), F32)
    scr("sp_h", (NCHUNK, 128, 4, D), F32)
    scr("sp_n2T", (128, 8, NT), BF16)
    scr("sp_g2", (NTG, 128, 32, TG), BF16)
    scr("sp_x1", (NTG, 128, 2, D), F32)
    if DEBUG_SCRATCH:
        scr("dbg_gld", (NTG, 128, 32 * TG), BF16)
        scr("dbg_pT", (NTG, 128, 8 * TG), BF16)

    es0 = ExitStack()
    sems = {}
    for n in ENGINES:
        sems[n + "_c"] = es0.enter_context(nc.semaphore("semc_" + n))
        for i in range(4):
            sems["%s_d%d" % (n, i)] = es0.enter_context(
                nc.semaphore("semd%d_" % i + n))
    Bf = _alloc(nc, es0)
    ctx = Ctx(nc, sems)
    for tname in TAPS:
        shape, dt = _TAPDEFS[tname]
        ctx.taps[tname] = nc.dram_tensor("tap_" + tname, list(shape), dt,
                                         kind="ExternalOutput").ap()

    with nc.Block() as block:
        for ename in ENGINES:
            fn = getattr(block, ename)

            def emit(_eng, ename=ename):
                ctx.begin(ename)
                _program(ctx, H, Bf, out_h)

            fn(emit)
    es0.close()
    return nc



def _program(ctx, H, B, out_h):
    r = ctx.run
    m_setup = _setup(ctx, H, B)
    bar = ctx.barrier()
    if "A1" in PHASES:
        _phase_a1(ctx, H, B, m_setup)
        bar = ctx.barrier()
    if "A2" in PHASES:
        _phase_a2(ctx, H, B, m_setup)
        bar = ctx.barrier()
    if "B1" in PHASES:
        _phase_b1(ctx, H, B)
        bar = ctx.barrier()
    if "B2" in PHASES:
        _phase_b2(ctx, H, B)
        bar = ctx.barrier()
    if "C" in PHASES:
        _phase_c(ctx, H, B)
        bar = ctx.barrier()
    if "FIN" in PHASES:
        _phase_fin(ctx, H, B, out_h)
    e = ctx.engs["gpsimd"]
    for slot in range(NDQ):
        if e.dq_cnt[slot]:
            for n in ENGINES:
                ctx.dep(n, (("gpsimd", slot), e.dq_cnt[slot], "d"))
    ctx.barrier()


def _setup(ctx, H, B):
    r = ctx.run
    m = {}
    for nm, dst in (("iota16", "iota16_sb"), ("brb", "brb_sb"), ("n1w_pp", "n1w_sb"),
                    ("n2w_pp", "n2w_sb"), ("convb_r", "convb_sb"), ("dtb_r", "dtb_sb"),
                    ("D_r", "D_sb")):
        m[dst] = r("sync", "dma_start", out=B[dst][:], in_=H[nm][:], dma=True)
    m["A_log"] = r("sync", "dma_start", out=B["A_sb"][:],
                   in_=H["A_log_r"].rearrange("p dc n -> p (dc n)"), dma=True)
    m["convw"] = r("sync", "dma_start", out=B["convw_sb"][:],
                   in_=H["convw_r"].rearrange("p dc t -> p (dc t)"), dma=True)
    m["ident"] = r("gpsimd", "dma_start", out=B["ident_bf"][:], in_=H["ident"][:], dma=True)
    ma1 = r("scalar", "activation", B["A_sb"][:], B["A_sb"][:], AF.Exp,
            deps=[m["A_log"]], inc=1)
    m["A"] = r("vector", "tensor_scalar", B["A_sb"][:], B["A_sb"][:], -1.0, None, OP.mult,
               deps=[ma1], inc=1)
    m["st0"] = r("gpsimd", "memset", B["state_sb"][:], 0.0, inc=1)
    m["hl0"] = r("gpsimd", "memset", B["halo_sb"][:], 0.0, inc=1)
    return m


def _phase_a1(ctx, H, B, ms):
    r = ctx.run
    mw = []
    for kg in range(8):
        mw.append(r("gpsimd", "dma_start",
                    out=B["w_in_sb"][:, kg * DFF:(kg + 1) * DFF],
                    in_=H["w_in_T"][kg * 128:(kg + 1) * 128, :], dma=True))
    m_wxp = r("gpsimd", "dma_start",
              out=B["w_xp_sb"][:].rearrange("p (dc m) -> p dc m", m=128),
              in_=H["w_xp_pad"].rearrange("(dc p) m -> p dc m", p=128), dma=True)
    m_fold = []
    for kg in range(8):
        m_fold.append(r("vector", "tensor_scalar",
                        B["w_in_sb"][:, kg * DFF:(kg + 1) * DFF],
                        B["w_in_sb"][:, kg * DFF:(kg + 1) * DFF],
                        B["n1w_sb"][:, kg:kg + 1], None, OP.mult,
                        deps=[mw[kg], ms["n1w_sb"]], inc=1))
    m_fold_all = m_fold[-1]

    prev = {}
    for c in range(LIMIT_CHUNKS or NCHUNK):
        buf = c % 2
        x0 = B["x0a_sb"][:, buf * 4 * D:(buf + 1) * 4 * D].rearrange("p (tt d) -> p tt d", d=D)
        m_ld = r("sync", "dma_start", out=x0,
                 in_=H["xs"][c * TL:(c + 1) * TL, :].rearrange("(tt p) d -> p tt d", p=128),
                 deps=[prev.get(("x0", buf))], dma=True)
        g0 = c * 4
        m_st = None
        m_stq = None
        for tt in range(4):
            g = g0 + tt
            m_st = r("vector", "tensor_reduce", B["sum_s"][:, g:g + 1], x0[:, tt, :],
                     AX.X, OP.add, deps=[m_ld], inc=(1 if tt == 3 else 0))
            m_stq = r("scalar", "activation", B["trash_bf"][:],
                      x0[:, tt, :], AF.Square,
                      accum_out=B["ssq_s"][:, g:g + 1], deps=[m_ld],
                      inc=(1 if tt == 3 else 0))
        ctx.drain("vector")
        ctx.autodrain = True
        sl = slice(g0, g0 + 4)
        r("vector", "tensor_scalar", B["mu0_s"][:, sl], B["sum_s"][:, sl], 1.0 / D, None,
          OP.mult, deps=[m_st, m_stq])
        r("vector", "tensor_scalar", B["sm_a"][:, sl], B["ssq_s"][:, sl], 1.0 / D, None, OP.mult)
        r("vector", "tensor_tensor", B["sm_b"][:, sl], B["mu0_s"][:, sl], B["mu0_s"][:, sl],
          OP.mult)
        r("vector", "tensor_tensor", B["sm_a"][:, sl], B["sm_a"][:, sl], B["sm_b"][:, sl],
          OP.subtract)
        m_v = r("vector", "tensor_scalar", B["sm_b"][:, sl], B["sm_a"][:, sl], EPS, None,
                OP.add, inc=1)
        ml1 = r("scalar", "activation", B["sm_b"][:, sl], B["sm_b"][:, sl], AF.Ln,
                deps=[m_v], inc=1)
        r("scalar", "activation", B["rstd0_s"][:, sl], B["sm_b"][:, sl], AF.Exp, scale=-0.5)
        m_s1 = r("scalar", "activation", B["sm_b"][:, sl], B["sm_b"][:, sl], AF.Exp,
                 inc=1)
        r("vector", "reciprocal", B["sm_b"][:, sl], B["sm_b"][:, sl], deps=[m_s1])
        r("vector", "tensor_tensor", B["sm_a"][:, sl], B["sm_a"][:, sl], B["sm_b"][:, sl],
          OP.mult)
        m_v2 = r("vector", "tensor_scalar", B["sm_a"][:, sl], B["sm_a"][:, sl], EPS, None,
                 OP.add, inc=1)
        r("scalar", "activation", B["sm_a"][:, sl], B["sm_a"][:, sl], AF.Ln, deps=[m_v2])
        m_s2 = r("scalar", "activation", B["sm_a"][:, sl], B["sm_a"][:, sl], AF.Exp,
                 scale=-0.5, inc=1)
        r("vector", "tensor_tensor", B["rs01_s"][:, sl], B["rstd0_s"][:, sl], B["sm_a"][:, sl],
          OP.mult, deps=[m_s2])
        r("vector", "scalar_tensor_tensor", B["b00_s"][:, sl], B["mu0_s"][:, sl],
          -1.0, B["rstd0_s"][:, sl], OP.mult, OP.mult)
        m_sm = r("vector", "scalar_tensor_tensor", B["b01_s"][:, sl], B["mu0_s"][:, sl],
                 -1.0, B["rs01_s"][:, sl], OP.mult, OP.mult, inc=1)
        ctx.autodrain = False
        m_cast = None
        for tt in range(4):
            g = g0 + tt
            m_cast = r("scalar", "activation",
                       B["x0bf_sb"][:, tt * D:(tt + 1) * D], x0[:, tt, :], AF.Identity,
                       scale=B["rs01_s"][:, g:g + 1], bias=B["b01_s"][:, g:g + 1],
                       deps=[m_sm, prev.get("n1T_done")], inc=1)
        m_ev = {}
        for dg in range(8):
            ps = B["psA_tr0"] if dg % 2 == 0 else B["psA_tr1"]
            m_mm = None
            for tt in range(4):
                m_mm = r("tensor", "matmul", ps[:, tt * 128:(tt + 1) * 128],
                         B["x0bf_sb"][:, tt * D + dg * 128: tt * D + (dg + 1) * 128],
                         B["ident_bf"][:],
                         deps=[m_cast, ms["ident"], m_ev.get(dg - 2)],
                         start=True, stop=True, inc=(1 if tt == 3 else 0))
            m_ev[dg] = r("scalar", "activation", B["n1T_sb"][:, dg * TL:(dg + 1) * TL],
                         ps[:], AF.Copy, deps=[m_mm], inc=1)
        m_n1T = m_ev[7]
        m_mm_ev = {}
        m_uact = {}
        for mt in range(32):
            ps = B["psA_mm0"] if mt % 2 == 0 else B["psA_mm1"]
            dep0 = [m_n1T, m_fold_all, m_mm_ev.get(mt - 2)]
            m_mm = None
            for kg in range(8):
                m_mm = r("tensor", "matmul", ps[:],
                         B["w_in_sb"][:, kg * DFF + mt * 128: kg * DFF + (mt + 1) * 128],
                         B["n1T_sb"][:, kg * TL:(kg + 1) * TL],
                         deps=(dep0 if kg == 0 else ()),
                         start=(kg == 0), stop=(kg == 7), inc=(1 if kg == 7 else 0))
            if mt < 16:
                dc = mt
                ub = dc % 2
                ust = B["ustg_sb"][:, ub * 515:(ub + 1) * 515]
                if c % 4 == 0:
                    r("vector", "memset", ust[:, 0:3], 0.0,
                      deps=[prev.get(("ustg", ub))])
                else:
                    r("vector", "tensor_copy", ust[:, 0:3],
                      B["halo_sb"][:, dc * 3:(dc + 1) * 3],
                      deps=[prev.get(("ustg", ub)), prev.get("halo_w%d" % dc)])
                m_evu = r("scalar", "activation", ust[:, 3:515], ps[:], AF.Copy,
                          deps=[m_mm, prev.get(("ustg", ub))], inc=1)
                m_mm_ev[mt] = m_evu
                m_tail = r("vector", "tensor_copy", B["halo_sb"][:, dc * 3:(dc + 1) * 3],
                           ust[:, 512:515], deps=[m_evu])
                prev["halo_w%d" % dc] = m_tail
                cb = dc % 2
                cacc = B["cacc_sb"][:, cb * TL:(cb + 1) * TL]
                ctx.drain("vector")
                r("vector", "tensor_scalar", cacc, ust[:, 0:512],
                  B["convw_sb"][:, dc * 4:dc * 4 + 1], None, OP.mult,
                  deps=[m_evu, ms["convw"], prev.get(("cacc", cb))])
                m_conv = None
                for t in range(1, 4):
                    m_conv = r("vector", "scalar_tensor_tensor", cacc, ust[:, t:512 + t],
                               B["convw_sb"][:, dc * 4 + t:dc * 4 + t + 1], cacc,
                               OP.mult, OP.add, inc=(1 if t == 3 else 0))
                prev[("ustg", ub)] = m_conv
                m_ua = r("scalar", "activation", B["uact_sb"][:, dc * TL:(dc + 1) * TL],
                         cacc, AF.Silu, bias=B["convb_sb"][:, dc:dc + 1],
                         deps=[m_conv, ms["convb_sb"], prev.get("sp_uact")], inc=1)
                prev[("cacc", cb)] = m_ua
                m_uact[dc] = m_ua
            else:
                dc = mt - 16
                m_evz = r("scalar", "activation", B["zs_sb"][:, dc * TL:(dc + 1) * TL],
                          ps[:], AF.Silu, deps=[m_mm, prev.get("sp_z")], inc=1)
                m_mm_ev[mt] = m_evz
        m_xp = None
        for k in range(16):
            m_xp = r("tensor", "matmul", B["psA_xp"][:],
                     B["w_xp_sb"][:, k * 128:(k + 1) * 128],
                     B["uact_sb"][:, k * TL:(k + 1) * TL],
                     deps=[m_uact[k], m_wxp] + ([prev.get("xp_ev")] if k == 0 else []),
                     start=(k == 0), stop=(k == 15), inc=(1 if k == 15 else 0))
        m_eB = r("scalar", "activation", B["Bst_sb"][:], B["psA_xp"][0:16, :], AF.Copy,
                 deps=[m_xp, prev.get("sp_B")], inc=1)
        m_eC = r("scalar", "activation", B["Cst_sb"][:], B["psA_xp"][32:48, :], AF.Copy,
                 deps=[prev.get("sp_C")], inc=1)
        m_eD = r("scalar", "activation", B["dtrst_sb"][:], B["psA_xp"][64:128, :], AF.Copy,
                 deps=[prev.get("sp_dtr")], inc=1)
        prev["xp_ev"] = m_eD
        m_spu = r("scalar", "dma_start", out=H["sp_uact"][c],
                  in_=B["uact_sb"][:].rearrange("p (dc t) -> p dc t", t=TL),
                  deps=[m_uact[15]], dma=True)
        m_spz = r("scalar", "dma_start", out=H["sp_z"][c],
                  in_=B["zs_sb"][:].rearrange("p (dc t) -> p dc t", t=TL),
                  deps=[m_mm_ev[31]], dma=True)
        m_spB = r("scalar", "dma_start", out=H["sp_B"][c], in_=B["Bst_sb"][:],
                  deps=[m_eB], dma=True)
        m_spC = r("scalar", "dma_start", out=H["sp_C"][c], in_=B["Cst_sb"][:],
                  deps=[m_eC], dma=True)
        m_spD = r("scalar", "dma_start", out=H["sp_dtr"][c], in_=B["dtrst_sb"][:],
                  deps=[m_eD], dma=True)
        prev["sp_uact"] = m_spu
        prev["sp_z"] = m_spz
        prev["sp_B"] = m_spB
        prev["sp_C"] = m_spC
        prev["sp_dtr"] = m_spD
        prev["n1T_done"] = m_mm_ev[31]
        prev[("x0", buf)] = m_cast
        ctx.tap("a1_x0bf_c%d" % c, B["x0bf_sb"][:], deps=[m_cast])
        ctx.tap("a1_n1T_c%d" % c, B["n1T_sb"][:], deps=[m_n1T])
        ctx.tap("a1_sum_c%d" % c, B["sum_s"][:], deps=[m_sm])
        ctx.tap("a1_mu0_c%d" % c, B["mu0_s"][:], deps=[m_sm])
        ctx.tap("a1_rstd0_c%d" % c, B["rstd0_s"][:], deps=[m_sm])
        ctx.tap("a1_var_c%d" % c, B["sm_a"][:], deps=[m_sm])
        ctx.tap("a1_ssq_c%d" % c, B["ssq_s"][:], deps=[m_sm])
        ctx.tap("a1_rs01_c%d" % c, B["rs01_s"][:], deps=[m_sm])
        ctx.tap("a1_uact_c%d" % c, B["uact_sb"][:], deps=[m_uact[15]])
        ctx.tap("a1_z_c%d" % c, B["zs_sb"][:], deps=[m_mm_ev[31]])
        ctx.tap("a1_B_c%d" % c, B["Bst_sb"][:], deps=[m_eB])
        ctx.tap("a1_dtr_c%d" % c, B["dtrst_sb"][:], deps=[m_eD])


def _phase_a2(ctx, H, B, ms):
    r = ctx.run
    ms["onehot"] = r("gpsimd", "dma_start", out=B["onehot_bf"][:], in_=H["onehot"][:],
                     dma=True)
    m_wdt = r("gpsimd", "dma_start", out=B["w_dt_sb"][:], in_=H["w_dt_T"][:], dma=True)
    m_wout = []
    for kg in range(16):
        m_wout.append(r("gpsimd", "dma_start",
                        out=B["w_out_sb"][:, kg * D:(kg + 1) * D],
                        in_=H["w_out_T"][kg * 128:(kg + 1) * 128, :], dma=True))
    prev = {}
    for c in range(LIMIT_CHUNKS or NCHUNK):
        g0 = c * 4
        batch_first = (c % 4 == 0)
        m_lu = r("sync", "dma_start",
                 out=B["uact2_sb"][:].rearrange("p (dc t) -> p dc t", t=TL),
                 in_=H["sp_uact"][c], deps=[prev.get("uact_free")], dma=True)
        m_lz = r("sync", "dma_start",
                 out=B["zs2_sb"][:].rearrange("p (dc t) -> p dc t", t=TL),
                 in_=H["sp_z"][c], deps=[prev.get("z_free")], dma=True)
        m_lx = r("sync", "dma_start",
                 out=B["x0b_sb"][:].rearrange("p (tt d) -> p tt d", d=D),
                 in_=H["xs"][c * TL:(c + 1) * TL, :].rearrange("(tt p) d -> p tt d", p=128),
                 deps=[prev.get("x0_free")], dma=True)
        m_ldt = r("sync", "dma_start", out=B["dtr2_sb"][:], in_=H["sp_dtr"][c],
                  deps=[prev.get("dtr_free")], dma=True)
        m_lB = r("gpsimd", "dma_start", out=B["Bbf_sb"][:], in_=H["sp_B"][c],
                 deps=[prev.get("Bbf_free")], dma=True)
        m_lC = r("gpsimd", "dma_start", out=B["Cbf_sb"][:], in_=H["sp_C"][c],
                 deps=[prev.get("Cbf_free")], dma=True)
        if batch_first and c > 0:
            ms["st0"] = r("gpsimd", "memset", B["state_sb"][:], 0.0,
                          deps=[prev.get("scan_all")], inc=1)
        m_bb = {}
        m_cb = {}
        m_bc_ev = prev.get("bc_ev")
        for n in range(DS):
            m_mm = r("tensor", "matmul", B["psB_bc"][:],
                     B["onehot_bf"][:, n * 128:(n + 1) * 128], B["Bbf_sb"][:],
                     deps=[m_lB, ms["onehot"], m_bc_ev], start=True, stop=True, inc=1)
            m_bc_ev = r("scalar", "activation", B["Bb_sb"][:, n * TL:(n + 1) * TL],
                        B["psB_bc"][:], AF.Copy, deps=[m_mm, prev.get("Bb_free")], inc=1)
            m_bb[n] = m_bc_ev
        for n in range(DS):
            m_mm = r("tensor", "matmul", B["psB_bc"][:],
                     B["onehot_bf"][:, n * 128:(n + 1) * 128], B["Cbf_sb"][:],
                     deps=[m_lC, m_bc_ev], start=True, stop=True, inc=1)
            m_bc_ev = r("scalar", "activation", B["Cb_sb"][:, n * TL:(n + 1) * TL],
                        B["psB_bc"][:], AF.Copy, deps=[m_mm, prev.get("Cb_free")], inc=1)
            m_cb[n] = m_bc_ev
        prev["Bbf_free"] = m_bb[DS - 1]
        prev["Cbf_free"] = m_cb[DS - 1]
        prev["bc_ev"] = m_bc_ev

        m_gate = {}
        m_sp_done = None
        m_scan_last = None
        for dc in range(16):
            db = dc % 2
            m_dmm = r("tensor", "matmul", B["psB_dt"][:],
                      B["w_dt_sb"][:, dc * 128:(dc + 1) * 128], B["dtr2_sb"][:],
                      deps=[m_ldt, m_wdt, prev.get("psdt_free")], start=True, stop=True,
                      inc=1)
            r("scalar", "activation", B["sp1_sb"][:], B["psB_dt"][:], AF.Abs,
              bias=B["dtb_sb"][:, dc:dc + 1], deps=[m_dmm, ms["dtb_sb"],
                                                    prev.get(("sp", 0))])
            r("scalar", "activation", B["sp1_sb"][:], B["sp1_sb"][:], AF.Exp, scale=-1.0)
            r("scalar", "activation", B["sp1_sb"][:], B["sp1_sb"][:], AF.Ln, bias=1.0)
            m_sp = r("scalar", "activation", B["sp2_sb"][:], B["psB_dt"][:], AF.Relu,
                     bias=B["dtb_sb"][:, dc:dc + 1], deps=[prev.get(("sp", 1))], inc=1)
            prev["psdt_free"] = m_sp
            dt_t = B["dt_sb"][:, db * TL:(db + 1) * TL]
            m_dt = r("vector", "tensor_tensor", dt_t, B["sp1_sb"][:], B["sp2_sb"][:], OP.add,
                     deps=[m_sp, prev.get(("dt", db))], inc=1)
            prev[("sp", 0)] = m_dt
            prev[("sp", 1)] = m_dt
            du_t = B["du_sb"][:, db * TL:(db + 1) * TL]
            m_du = r("vector", "tensor_tensor", du_t, dt_t, B["uact2_sb"][:, dc * TL:(dc + 1) * TL],
                     OP.mult, deps=[m_lu, prev.get(("du", db))])
            m_scan_last = None
            for grp in range(8):
                ab = grp % 2
                m_a = None
                for j in range(2):
                    n = grp * 2 + j
                    m_a = r("scalar", "activation",
                            B["aBuf_sb"][:, (ab * 2 + j) * TL:(ab * 2 + j + 1) * TL],
                            dt_t, AF.Exp, scale=B["A_sb"][:, dc * DS + n:dc * DS + n + 1],
                            deps=[m_dt, ms["A"], prev.get(("scangrp", ab))],
                            inc=(1 if j == 1 else 0))
                m_sc = None
                for j in range(2):
                    n = grp * 2 + j
                    q = n % 4
                    hb = B["hb_sb"][:, (n % 2) * TL:((n % 2) + 1) * TL]
                    bs = B["bs_sb"][:, (n % 2) * TL:((n % 2) + 1) * TL]
                    r("vector", "tensor_tensor", bs, du_t, B["Bb_sb"][:, n * TL:(n + 1) * TL],
                      OP.mult, deps=[m_du, m_bb[n], m_a])
                    r("vector", "tensor_tensor_scan", hb,
                      B["aBuf_sb"][:, (ab * 2 + j) * TL:(ab * 2 + j + 1) * TL], bs,
                      B["state_sb"][:, dc * DS + n:dc * DS + n + 1], OP.mult, OP.add,
                      deps=[ms["st0"]])
                    if n < 4:
                        m_sc = r("vector", "tensor_tensor",
                                 B["yp_sb"][:, q * TL:(q + 1) * TL], hb,
                                 B["Cb_sb"][:, n * TL:(n + 1) * TL], OP.mult,
                                 deps=[m_cb[n]], inc=(1 if j == 1 else 0), war=True)
                    else:
                        yt = B["yt_sb"][:]
                        r("vector", "tensor_tensor", yt, hb,
                          B["Cb_sb"][:, n * TL:(n + 1) * TL], OP.mult, deps=[m_cb[n]])
                        m_sc = r("vector", "tensor_tensor",
                                 B["yp_sb"][:, q * TL:(q + 1) * TL],
                                 B["yp_sb"][:, q * TL:(q + 1) * TL], yt, OP.add,
                                 inc=(1 if j == 1 else 0), war=True)
                    r("vector", "tensor_scalar",
                      B["state_sb"][:, dc * DS + n:dc * DS + n + 1],
                      hb[:, TL - 1:TL], 0.0, None, OP.add)
                prev[("scangrp", ab)] = m_sc
                m_scan_last = m_sc
            prev[("dt", db)] = m_scan_last
            prev[("du", db)] = m_scan_last
            r("vector", "tensor_tensor", B["yp_sb"][:, 0:TL], B["yp_sb"][:, 0:TL],
              B["yp_sb"][:, TL:2 * TL], OP.add)
            r("vector", "tensor_tensor", B["yp_sb"][:, 2 * TL:3 * TL],
              B["yp_sb"][:, 2 * TL:3 * TL], B["yp_sb"][:, 3 * TL:4 * TL], OP.add)
            r("vector", "tensor_tensor", B["yp_sb"][:, 0:TL], B["yp_sb"][:, 0:TL],
              B["yp_sb"][:, 2 * TL:3 * TL], OP.add)
            r("vector", "scalar_tensor_tensor", B["gt_sb"][:],
              B["uact2_sb"][:, dc * TL:(dc + 1) * TL], B["D_sb"][:, dc:dc + 1],
              B["yp_sb"][:, 0:TL], OP.mult, OP.add, deps=[ms["D_sb"]])
            m_g = r("vector", "tensor_tensor", B["uact2_sb"][:, dc * TL:(dc + 1) * TL],
                    B["gt_sb"][:], B["zs2_sb"][:, dc * TL:(dc + 1) * TL], OP.mult,
                    deps=[m_lz], inc=1)
            m_gate[dc] = m_g
        prev["scan_all"] = r("vector", "drain", inc=1)
        prev["uact_free"] = None
        prev["z_free"] = m_gate[15]
        prev["dtr_free"] = m_gate[15]
        prev["Bb_free"] = m_gate[15]
        prev["Cb_free"] = m_gate[15]
        ctx.tap("a2_gated_c%d" % c, B["uact2_sb"][:], deps=[m_gate[15]])
        m_op_ev = {}
        for mt in range(8):
            ps = B["psB_op0"] if mt % 2 == 0 else B["psB_op1"]
            m_mm = None
            for k in range(16):
                m_mm = r("tensor", "matmul", ps[:],
                         B["w_out_sb"][:, k * D + mt * 128: k * D + (mt + 1) * 128],
                         B["uact2_sb"][:, k * TL:(k + 1) * TL],
                         deps=([m_gate[k], m_wout[k]] +
                               ([m_op_ev.get(mt - 2,
                                             prev.get("psop%d_free" % (mt % 2)))]
                                if k == 0 else [])),
                         start=(k == 0), stop=(k == 15), inc=(1 if k == 15 else 0))
            m_op_ev[mt] = r("scalar", "activation", B["mT_sb"][:, mt * TL:(mt + 1) * TL],
                            ps[:], AF.Copy, deps=[m_mm, prev.get("mT_free")], inc=1)
        m_mT = m_op_ev[7]
        prev["uact_free"] = m_mT
        m_h = {}
        m_ssq2 = None
        for tt in range(4):
            ps = B["psB_tr0"] if tt % 2 == 0 else B["psB_tr1"]
            m_mm = None
            for dg in range(8):
                m_mm = r("tensor", "matmul", ps[:, dg * 128:(dg + 1) * 128],
                         B["mT_sb"][:, dg * TL + tt * 128: dg * TL + (tt + 1) * 128],
                         B["ident_bf"][:],
                         deps=[m_op_ev[dg], prev.get(("htr", tt % 2))],
                         start=True, stop=True, inc=(1 if dg == 7 else 0))
            g = g0 + tt
            r("vector", "scalar_tensor_tensor", B["h_sb"][:, tt * D:(tt + 1) * D],
              B["x0b_sb"][:, tt * D:(tt + 1) * D], B["rstd0_s"][:, g:g + 1], ps[:],
              OP.mult, OP.add, deps=[m_mm, m_lx, prev.get("h_free")])
            m_hh = r("vector", "tensor_scalar", B["h_sb"][:, tt * D:(tt + 1) * D],
                     B["h_sb"][:, tt * D:(tt + 1) * D], B["b00_s"][:, g:g + 1], None,
                     OP.add, inc=1)
            prev[("htr", tt % 2)] = m_hh
            m_h[tt] = m_hh
            g = g0 + tt
            m_ssq2 = r("scalar", "activation", B["trash_bf"][:],
                       B["h_sb"][:, tt * D:(tt + 1) * D], AF.Square,
                       accum_out=B["ssq_s"][:, g:g + 1], deps=[m_hh], inc=1)
        prev["mT_free"] = m_h[3]
        ctx.tap("a2_h_c%d" % c, B["h_sb"][:], deps=[m_h[3]])
        m_sph = r("gpsimd", "dma_start", out=H["sp_h"][c],
                  in_=B["h_sb"][:].rearrange("p (tt d) -> p tt d", d=D),
                  deps=[m_h[3]], dma=True)
        prev["h_free"] = m_sph
        sl = slice(g0, g0 + 4)
        ctx.autodrain = True
        r("vector", "tensor_scalar", B["sm_a"][:, sl], B["ssq_s"][:, sl], 1.0 / D, None,
          OP.mult, deps=[m_ssq2])
        m_v = r("vector", "tensor_scalar", B["sm_a"][:, sl], B["sm_a"][:, sl], EPS, None,
                OP.add, inc=1)
        ctx.autodrain = False
        r("scalar", "activation", B["sm_a"][:, sl], B["sm_a"][:, sl], AF.Ln, deps=[m_v])
        m_r2 = r("scalar", "activation", B["rstd2_s"][:, sl], B["sm_a"][:, sl], AF.Exp,
                 scale=-0.5, inc=1)
        m_cast = None
        for tt in range(4):
            g = g0 + tt
            m_cast = r("scalar", "activation", B["h2bf_sb"][:, tt * D:(tt + 1) * D],
                       B["h_sb"][:, tt * D:(tt + 1) * D], AF.Copy,
                       scale=B["rstd2_s"][:, g:g + 1],
                       deps=[m_r2, prev.get("h2bf_free")], inc=(1 if tt == 3 else 0))
        m_n2ev = {}
        for dg in range(8):
            ps = B["psB_op0"] if dg % 2 == 0 else B["psB_op1"]
            m_mm = None
            for tt in (3, 2, 1, 0):
                m_mm = r("tensor", "matmul", ps[:, tt * 128:(tt + 1) * 128],
                         B["h2bf_sb"][:, tt * D + dg * 128: tt * D + (dg + 1) * 128],
                         B["ident_bf"][:],
                         deps=[m_cast, m_n2ev.get(dg - 2),
                               m_op_ev[6] if dg == 0 else None,
                               m_op_ev[7] if dg == 1 else None],
                         start=True, stop=True, inc=(1 if tt == 0 else 0))
            m_n2ev[dg] = r("scalar", "activation", B["n2Tst_sb"][:, dg * TL:(dg + 1) * TL],
                           ps[:], AF.Copy, deps=[m_mm, prev.get("n2Tst_free")], inc=1)
        prev["h2bf_free"] = m_n2ev[7]
        prev["psop0_free"] = m_n2ev[6]
        prev["psop1_free"] = m_n2ev[7]
        m_spn2 = r("scalar", "dma_start",
                   out=H["sp_n2T"][:, :, c * TL:(c + 1) * TL],
                   in_=B["n2Tst_sb"][:].rearrange("p (dg t) -> p dg t", t=TL),
                   deps=[m_n2ev[7]], dma=True)
        prev["n2Tst_free"] = m_spn2
        prev["x0_free"] = m_cast


def _phase_b1(ctx, H, B):
    r = ctx.run
    m_n2 = []
    for dg in range(8):
        m_n2.append(r("sync", "dma_start", out=B["n2Ta_sb"][:, dg * NT:(dg + 1) * NT],
                      in_=H["sp_n2T"][:, dg, :], dma=True))
    prev = {}
    for mt in range(32):
        sb = mt % 2
        w1s = B["w1s_sb"][:, sb * 1024:(sb + 1) * 1024]
        w2s = B["w2s_sb"][:, sb * 1024:(sb + 1) * 1024]
        m_l1 = r("gpsimd", "dma_start",
                 out=w1s.rearrange("p (kg m) -> p kg m", m=128),
                 in_=H["w1_T"][:, mt * 128:(mt + 1) * 128].rearrange(
                     "(kg p) m -> p kg m", p=128),
                 deps=[prev.get(("w1", sb))], dma=True)
        m_l2 = r("gpsimd", "dma_start",
                 out=w2s.rearrange("p (kg m) -> p kg m", m=128),
                 in_=H["w2_T"][:, mt * 128:(mt + 1) * 128].rearrange(
                     "(kg p) m -> p kg m", p=128),
                 deps=[prev.get(("w2", sb))], dma=True)
        m_f1 = m_f2 = None
        for kg in range(8):
            m_f1 = r("vector", "tensor_scalar", w1s[:, kg * 128:(kg + 1) * 128],
                     w1s[:, kg * 128:(kg + 1) * 128], B["n2w_sb"][:, kg:kg + 1], None,
                     OP.mult, deps=[m_l1], inc=(1 if kg == 7 else 0))
            m_f2 = r("vector", "tensor_scalar", w2s[:, kg * 128:(kg + 1) * 128],
                     w2s[:, kg * 128:(kg + 1) * 128], B["n2w_sb"][:, kg:kg + 1], None,
                     OP.mult, deps=[m_l2], inc=(1 if kg == 7 else 0))
        gb = mt % 2
        gst = B["gst_sb"][:, gb * NT:(gb + 1) * NT]
        m_gd = {}
        for tg in range(8):
            ps1 = B["psC_1"] if tg % 2 == 0 else B["psC_3"]
            ps2 = B["psC_2"] if tg % 2 == 0 else B["psC_4"]
            m_mm1 = m_mm2 = None
            for kg in range(8):
                m_mm1 = r("tensor", "matmul", ps1[:],
                          w1s[:, kg * 128:(kg + 1) * 128],
                          B["n2Ta_sb"][:, kg * NT + tg * TL: kg * NT + (tg + 1) * TL],
                          deps=([m_f1, m_n2[kg]] +
                                ([prev.get(("ps", tg % 2))] if kg == 0 else [])),
                          start=(kg == 0), stop=(kg == 7), inc=(1 if kg == 7 else 0))
            for kg in range(8):
                m_mm2 = r("tensor", "matmul", ps2[:],
                          w2s[:, kg * 128:(kg + 1) * 128],
                          B["n2Ta_sb"][:, kg * NT + tg * TL: kg * NT + (tg + 1) * TL],
                          deps=([m_f2] if kg == 0 else ()),
                          start=(kg == 0), stop=(kg == 7), inc=(1 if kg == 7 else 0))
            m_s = r("scalar", "activation", B["sl_sb"][:], ps1[:], AF.Silu,
                    deps=[m_mm1, prev.get("sl_free")], inc=1)
            m_w2 = r("scalar", "activation", B["w2e_sb"][:], ps2[:], AF.Copy,
                     deps=[m_mm2, prev.get("w2e_free")], inc=1)
            gp = tg % 2
            m_g = r("vector", "tensor_tensor",
                    B["gtmp_sb"][:, gp * TL:(gp + 1) * TL],
                    B["sl_sb"][:], B["w2e_sb"][:], OP.mult,
                    deps=[m_s, m_w2, prev.get(("gtmp", gp))], inc=1)
            m_gc = r("scalar", "activation", gst[:, tg * TL:(tg + 1) * TL],
                     B["gtmp_sb"][:, gp * TL:(gp + 1) * TL], AF.Copy,
                     deps=[m_g, prev.get(("gst", gb))], inc=1)
            prev[("gtmp", gp)] = m_gc
            prev["sl_free"] = m_g
            prev["w2e_free"] = m_g
            prev[("ps", tg % 2)] = m_g
            m_gd[tg] = m_gc
        prev[("w1", sb)] = m_gd[7]
        prev[("w2", sb)] = m_gd[7]
        m_sp = None
        for tg8 in range(8):
            for half in range(2):
                m_sp = r("gpsimd", "dma_start",
                         out=H["sp_g2"][2 * tg8 + half][:, mt, :],
                         in_=gst[:, tg8 * TL + half * TG: tg8 * TL + (half + 1) * TG],
                         deps=[m_gd[tg8]], dma=True)
        prev[("gst", gb)] = m_sp


def _phase_b2(ctx, H, B):
    r = ctx.run
    m_wr = r("gpsimd", "dma_start",
             out=B["wr_sb"][:].rearrange("p (kg n) -> p kg n", n=NE),
             in_=H["wr"].rearrange("(kg p) n -> p kg n", p=128), dma=True)
    r("gpsimd", "dma_start", out=B["b2_sb"][:], in_=H["b2e"][:], dma=True)
    m_w3 = []
    for kg in range(32):
        m_w3.append(r("gpsimd", "dma_start", out=B["w3_sb"][:, kg * D:(kg + 1) * D],
                      in_=H["w3_T"][kg * 128:(kg + 1) * 128, :], dma=True))
    prev = {}
    for tg in range(LIMIT_TG or NTG):
        c = tg // 2
        tt0 = 2 * (tg % 2)
        gld = B["gld_sb"][:].rearrange("p (kg t) -> p kg t", t=TG)
        m_lg = [r("sync", "dma_start", out=gld,
                  in_=H["sp_g2"][tg], deps=[prev.get("gld_free")], dma=True)]
        m_lx = r("sync", "dma_start",
                 out=B["x0c_sb"][:].rearrange("p (tt d) -> p tt d", d=D),
                 in_=H["xs"][tg * TG:(tg + 1) * TG, :].rearrange("(tt p) d -> p tt d", p=128),
                 deps=[prev.get("x0c_free")], dma=True)
        m_lh = r("sync", "dma_start",
                 out=B["hc_sb"][:].rearrange("p (tt d) -> p tt d", d=D),
                 in_=H["sp_h"][c][:, tt0:tt0 + 2, :],
                 deps=[prev.get("hc_free")], dma=True)
        if DEBUG_SCRATCH:
            r("gpsimd", "dma_start", out=H["dbg_gld"][tg], in_=B["gld_sb"][:],
              deps=m_lg, dma=True)
        m_pev = {}
        for rnd in range(2):
            m_mmr = {}
            for mti in range(4):
                mt = 4 * rnd + mti
                dst = B["psD_p"][:, 2 * mti * TG:(2 * mti + 1) * TG]
                m_mm = None
                for kg in range(32):
                    m_mm = r("tensor", "matmul", dst,
                             B["w3_sb"][:, kg * D + mt * 128: kg * D + (mt + 1) * 128],
                             gld[:, kg, :],
                             deps=(m_lg + [m_w3[kg]] +
                                   ([prev.get("psp_free"),
                                     m_pev.get(4 * (rnd - 1) + mti)] if kg == 0 else [])),
                             start=(kg == 0), stop=(kg == 31),
                             inc=(1 if kg == 31 else 0))
                m_mmr[mti] = m_mm
            for mti in range(4):
                mt = 4 * rnd + mti
                m_pev[mt] = r("scalar", "activation",
                              B["pT_sb"][:, mt * TG:(mt + 1) * TG],
                              B["psD_p"][:, 2 * mti * TG:(2 * mti + 1) * TG], AF.Copy,
                              deps=[m_mmr[mti], prev.get("pT_free")], inc=1)
        if DEBUG_SCRATCH:
            m_dbgp = r("gpsimd", "dma_start", out=H["dbg_pT"][tg], in_=B["pT_sb"][:],
                       deps=[m_pev[mt2] for mt2 in range(8)], dma=True)
            prev["pT_free"] = m_dbgp
        m_x1 = {}
        for tt in range(2):
            g3 = 2 * tg + tt
            m_mm = None
            for dg in range(8):
                m_mm = r("tensor", "matmul", B["psD_t"][:, dg * 128:(dg + 1) * 128],
                         B["pT_sb"][:, dg * TG + tt * 128: dg * TG + (tt + 1) * 128],
                         B["ident_bf"][:],
                         deps=[m_pev[dg], prev.get("pst")],
                         start=True, stop=True, inc=(1 if dg == 7 else 0))
            r("vector", "tensor_tensor", B["x0c_sb"][:, tt * D:(tt + 1) * D],
              B["x0c_sb"][:, tt * D:(tt + 1) * D], B["hc_sb"][:, tt * D:(tt + 1) * D],
              OP.add, deps=[m_lx, m_lh])
            m_tt = r("vector", "tensor_tensor", B["x1_sb"][:, tt * D:(tt + 1) * D],
                     B["x0c_sb"][:, tt * D:(tt + 1) * D], B["psD_t"][:], OP.add,
                     deps=[m_mm, prev.get("x1_free")], inc=1)
            m_xx = r("scalar", "activation", B["trash_bf"][:],
                     B["x1_sb"][:, tt * D:(tt + 1) * D], AF.Copy,
                     accum_out=B["sum3_s"][:, g3:g3 + 1], deps=[m_tt], inc=1)
            prev["pst"] = m_xx
            ctx.tap("b2_x0h_t%d_%d" % (tg, tt), B["x0c_sb"][:, tt * D:(tt + 1) * D],
                    deps=[m_xx])
            m_x1[tt] = m_xx
            m_sq = r("scalar", "activation", B["trash_bf"][:],
                     B["x1_sb"][:, tt * D:(tt + 1) * D],
                     AF.Square, accum_out=B["ssq3_s"][:, g3:g3 + 1],
                     deps=[m_xx],
                     inc=(1 if tt == 1 else 0))
        prev["x0c_free"] = m_x1[1]
        prev["hc_free"] = m_x1[1]
        prev["pT_free"] = m_x1[1]
        ctx.tap("b2_x1_t%d" % tg, B["x1_sb"][:], deps=[m_x1[1]])
        m_spx = r("gpsimd", "dma_start", out=H["sp_x1"][tg],
                  in_=B["x1_sb"][:].rearrange("p (tt d) -> p tt d", d=D),
                  deps=[m_x1[1]], dma=True)
        ctx.drain("vector")
        ctx.autodrain = True
        sl = slice(2 * tg, 2 * tg + 2)
        r("vector", "tensor_scalar", B["sm_b"][:, sl], B["sum3_s"][:, sl], 1.0 / D, None,
          OP.mult, deps=[m_x1[1], m_sq])
        r("vector", "tensor_scalar", B["sm_a"][:, sl], B["ssq3_s"][:, sl], 1.0 / D, None,
          OP.mult)
        r("vector", "scalar_tensor_tensor", B["b03_s"][:, sl], B["sm_b"][:, sl],
          -1.0, B["sm_b"][:, sl], OP.mult, OP.mult)
        r("vector", "tensor_tensor", B["sm_a"][:, sl], B["sm_a"][:, sl], B["b03_s"][:, sl],
          OP.add)
        m_v = r("vector", "tensor_scalar", B["sm_a"][:, sl], B["sm_a"][:, sl], EPS, None,
                OP.add, inc=1)
        r("scalar", "activation", B["sm_a"][:, sl], B["sm_a"][:, sl], AF.Ln, deps=[m_v])
        m_r3 = r("scalar", "activation", B["rstd3_s"][:, sl], B["sm_a"][:, sl], AF.Exp,
                 scale=-0.5, inc=1)
        m_b3 = r("vector", "scalar_tensor_tensor", B["b03_s"][:, sl], B["sm_b"][:, sl],
                 -1.0, B["rstd3_s"][:, sl], OP.mult, OP.mult, deps=[m_r3], inc=1)
        ctx.autodrain = False
        m_cast = None
        for tt in range(2):
            g3 = 2 * tg + tt
            m_cast = r("scalar", "activation", B["u3bf_sb"][:, tt * D:(tt + 1) * D],
                       B["x1_sb"][:, tt * D:(tt + 1) * D], AF.Identity,
                       scale=B["rstd3_s"][:, g3:g3 + 1], bias=B["b03_s"][:, g3:g3 + 1],
                       deps=[m_b3, prev.get("u3bf_free")], inc=(1 if tt == 1 else 0))
        prev["x1_free"] = m_cast
        for half in range(2):
            m_mm = None
            for dgi in range(4):
                dg = half * 4 + dgi
                for tt in range(2):
                    m_mm = r("tensor", "matmul",
                             B["psD_u"][:, dgi * TG + tt * 128: dgi * TG + (tt + 1) * 128],
                             B["u3bf_sb"][:, tt * D + dg * 128: tt * D + (dg + 1) * 128],
                             B["ident_bf"][:],
                             deps=[m_cast, prev.get("psu")],
                             start=True, stop=True,
                             inc=(1 if (dgi == 3 and tt == 1) else 0))
            m_ue = None
            for dgi in range(4):
                dg = half * 4 + dgi
                m_ue = r("scalar", "activation",
                         B["u3T_sb"][:, dg * NT + tg * TG: dg * NT + (tg + 1) * TG],
                         B["psD_u"][:, dgi * TG:(dgi + 1) * TG], AF.Copy,
                         deps=[m_mm], inc=(1 if dgi == 3 else 0))
            prev["psu"] = m_ue
        prev["u3bf_free"] = prev["psu"]
        m_u3T = prev["psu"]
        m_lgmm = None
        for tt in range(2):
            for kg in range(8):
                m_lgmm = r("tensor", "matmul", B["psD_p"][:, tt * NE: tt * NE + NE],
                           B["u3T_sb"][:, kg * NT + tg * TG + tt * 128:
                                       kg * NT + tg * TG + (tt + 1) * 128],
                           B["wr_sb"][:, kg * NE:(kg + 1) * NE],
                           deps=([m_u3T, m_wr] if (tt == 0 and kg == 0) else ()),
                           start=(kg == 0), stop=(kg == 7),
                           inc=(1 if kg == 7 else 0))
        m_lgv = {}
        for tt in range(2):
            m_lgv[tt] = r("vector", "tensor_tensor", B["lg_sb"][:, tt * NE:(tt + 1) * NE],
                          B["psD_p"][:, tt * NE:tt * NE + NE], B["brb_sb"][:], OP.add,
                          deps=[m_lgmm], inc=1)
        ctx.autodrain = True
        for tt in range(2):
            g3 = 2 * tg + tt
            lg = B["lg_sb"][:, tt * NE:(tt + 1) * NE]
            r("vector", "max", B["m8_sb"][:, tt * 8:(tt + 1) * 8], lg, deps=[m_lgv[tt]])
            ctx.drain("vector")
            r("vector", "max_index", B["i8_sb"][:, tt * 8:(tt + 1) * 8],
              B["m8_sb"][:, tt * 8:(tt + 1) * 8], lg)
            ctx.drain("vector")
            r("vector", "tensor_scalar", B["idxf_sb"][:, tt * 2:(tt + 1) * 2],
              B["i8_sb"][:, tt * 8:tt * 8 + 2], 0.0, None, OP.add)
            m_vv = r("vector", "tensor_scalar", B["nm1_sb"][:, tt:tt + 1],
                     B["m8_sb"][:, tt * 8:tt * 8 + 1], -1.0, None, OP.mult, inc=1)
            m_e = r("scalar", "activation", B["t21_sb"][:, tt:tt + 1],
                    B["m8_sb"][:, tt * 8 + 1:tt * 8 + 2], AF.Exp,
                    bias=B["nm1_sb"][:, tt:tt + 1], deps=[m_vv], inc=1)
            r("vector", "tensor_scalar", B["den_sb"][:, tt:tt + 1],
              B["t21_sb"][:, tt:tt + 1], 1.0, None, OP.add, deps=[m_e])
            r("vector", "reciprocal", B["w1r_sb"][:, tt:tt + 1], B["den_sb"][:, tt:tt + 1])
            r("vector", "tensor_tensor", B["w2r_sb"][:, tt:tt + 1],
              B["t21_sb"][:, tt:tt + 1], B["w1r_sb"][:, tt:tt + 1], OP.mult)
            r("vector", "tensor_scalar", B["mk1_sb"][:], B["iota16_sb"][:],
              B["idxf_sb"][:, tt * 2:tt * 2 + 1], B["w1r_sb"][:, tt:tt + 1],
              OP.is_equal, OP.mult)
            r("vector", "tensor_scalar", B["mk2_sb"][:], B["iota16_sb"][:],
              B["idxf_sb"][:, tt * 2 + 1:tt * 2 + 2], B["w2r_sb"][:, tt:tt + 1],
              OP.is_equal, OP.mult)
            r("vector", "tensor_tensor", B["wfull_sb"][:, g3 * NE:(g3 + 1) * NE],
              B["mk1_sb"][:], B["mk2_sb"][:], OP.add)
            m_wf = r("vector", "tensor_scalar", B["wfbf_sb"][:],
                     B["wfull_sb"][:, g3 * NE:(g3 + 1) * NE], 1.0, None, OP.mult, inc=1)
            m_tr = r("tensor", "matmul", B["psD_u"][0:16, 0:128],
                     B["wfbf_sb"][:], B["ident_bf"][:],
                     deps=[m_wf, prev.get("wrt_ev"), prev.get(("psu", 0))],
                     start=True, stop=True, inc=1)
            m_wrt = r("scalar", "activation", B["wroute_sb"][:, g3 * 128:(g3 + 1) * 128],
                      B["psD_u"][0:16, 0:128], AF.Copy, deps=[m_tr], inc=1)
            prev["wrt_ev"] = m_wrt
            prev[("psu", 0)] = m_wrt
        ctx.autodrain = False
        if tg == (LIMIT_TG or NTG) - 1:
            ctx.tap("b2_u3T", B["u3T_sb"][:], deps=[prev["psu"], prev["wrt_ev"]])
        ctx.tap("b2_wfull_t%d" % tg, B["wfull_sb"][:], deps=[prev["wrt_ev"]])
        prev["gld_free"] = m_pev[7]
        prev["psp_free"] = m_lgv[1]


def _phase_c(ctx, H, B):
    r = ctx.run
    prev = {}

    def load_e(e, which):
        msd = []
        if which in ("W1", "both"):
            for half in range(2):
                dst = B["W1X_sb"] if half == 0 else B["W1Y_sb"]
                msd.append(r("gpsimd", "dma_start",
                             out=dst[:].rearrange("p (kk m) -> p kk m", m=HID),
                             in_=H["W1e"][e][half * 512:(half + 1) * 512, :].rearrange(
                                 "(kk p) m -> p kk m", p=128),
                             deps=[prev.get("W1_free")], dma=True))
        if which in ("W2", "both"):
            for half in range(2):
                dst = B["W2X_sb"] if half == 0 else B["W2Y_sb"]
                msd.append(r("gpsimd", "dma_start",
                             out=dst[:].rearrange("p (kk m) -> p kk m", m=D),
                             in_=H["W2e"][e][half * 512:(half + 1) * 512, :].rearrange(
                                 "(kk p) m -> p kk m", p=128),
                             deps=[prev.get("W2_free")], dma=True))
        return msd

    def load_vec_e(e):
        m1 = r("gpsimd", "dma_start", out=B["g1e_sb"][:], in_=H["g1b"][e],
               deps=[prev.get("g1_free")], dma=True)
        m2 = r("gpsimd", "dma_start", out=B["be1e_sb"][:], in_=H["be1b"][e],
               deps=[prev.get("be1_free")], dma=True)
        m3 = r("gpsimd", "dma_start", out=B["b1e_sb"][:], in_=H["b1b"][e],
               deps=[prev.get("b1_free")], dma=True)
        return [m1, m2, m3]

    def part_a(e, tg, mW):
        out = {}
        for tt in range(2):
            ps = B["psE_w1a"] if tt == 0 else B["psE_w1b"]
            m_mm = None
            for kg in range(8):
                wbuf = B["W1X_sb"] if kg < 4 else B["W1Y_sb"]
                kk = kg % 4
                for nh in range(2):
                    m_mm = r("tensor", "matmul", ps[:, nh * 512:(nh + 1) * 512],
                             B["u3T_sb"][:, kg * NT + tg * TG + tt * 128:
                                         kg * NT + tg * TG + (tt + 1) * 128],
                             wbuf[:, kk * HID + nh * 512: kk * HID + (nh + 1) * 512],
                             deps=(list(mW) + [prev.get(("psw1", tt))]
                                   if kg == 0 and nh == 0 else ()),
                             start=(kg == 0), stop=(kg == 7),
                             inc=(1 if (kg == 7 and nh == 1) else 0))
            out[tt] = m_mm
        return out

    def part_evac(e, tg, mA, mV):
        for tt in range(2):
            ps = B["psE_w1a"] if tt == 0 else B["psE_w1b"]
            hch = B["hch_sb"][:, tt * HID:(tt + 1) * HID]
            m_ev = r("scalar", "activation", hch, ps[:], AF.Copy,
                     accum_out=B["sums_sb"][:, tt:tt + 1],
                     deps=[mA[tt], prev.get(("hch", tt))], inc=1)
            prev[("psw1", tt)] = m_ev
            prev[("b1", tt)] = r("gpsimd", "tensor_tensor", hch, hch,
                                 B["b1e_sb"][:], OP.add,
                                 deps=[m_ev, mV[2]], inc=1)

    def part_b(e, tg, mW, mV, m_sb1h):
        for tt in range(2):
            hch = B["hch_sb"][:, tt * HID:(tt + 1) * HID]
            ss = slice(tt, tt + 1)
            r("scalar", "activation", B["trash_bf"][:], hch, AF.Square,
              accum_out=B["ssqE_sb"][:, ss], deps=[prev[("b1", tt)]], inc=0)
            r("scalar", "activation", B["muE_sb"][:, ss], B["sums_sb"][:, ss],
              AF.Identity, scale=1.0 / HID, bias=B["Sb1_sb"][:], deps=[m_sb1h])
            r("scalar", "activation", B["sm_b"][:, ss], B["muE_sb"][:, ss], AF.Square)
            m_e2 = r("scalar", "activation", B["sm_a"][:, ss], B["ssqE_sb"][:, ss],
                     AF.Copy, scale=1.0 / HID, inc=1)
            m_vv = r("vector", "scalar_tensor_tensor", B["sm_a"][:, ss],
                     B["sm_a"][:, ss], EPS, B["sm_b"][:, ss],
                     OP.add, OP.subtract, deps=[m_e2], inc=1)
            r("scalar", "activation", B["sm_a"][:, ss], B["sm_a"][:, ss], AF.Ln,
              deps=[m_vv])
            r("scalar", "activation", B["rstdE_sb"][:, ss], B["sm_a"][:, ss],
              AF.Exp, scale=-0.5)
            r("scalar", "activation", B["sm_b"][:, ss], B["muE_sb"][:, ss],
              AF.Identity, scale=B["rstdE_sb"][:, ss])
            r("scalar", "activation", B["bE_sb"][:, ss], B["sm_b"][:, ss],
              AF.Identity, scale=-1.0)
            m_af = r("scalar", "activation", hch, hch, AF.Identity,
                     scale=B["rstdE_sb"][:, ss], bias=B["bE_sb"][:, ss], inc=1)
            r("vector", "tensor_tensor", hch, hch, B["g1e_sb"][:], OP.mult,
              deps=[mV[0], m_af])
            prev[("t5", tt)] = r("vector", "tensor_tensor", hch, hch,
                                 B["be1e_sb"][:], OP.add, deps=[mV[1]], inc=1)
        m_gelu = {}
        for tt in range(2):
            hch = B["hch_sb"][:, tt * HID:(tt + 1) * HID]
            gl = B["gl_sb"][:, tt * HID:(tt + 1) * HID]
            m_gl = r("scalar", "activation", gl, hch, AF.Gelu,
                     deps=[prev[("t5", tt)]], inc=1)
            prev[("hch", tt)] = m_gl
            m_gelu[tt] = m_gl
        for tt in range(2):
            gl = B["gl_sb"][:, tt * HID:(tt + 1) * HID]
            g3 = 2 * tg + tt
            dg_t = B["dg_sb"][:, tt * 128:(tt + 1) * 128]
            m_dg = r("vector", "tensor_scalar", dg_t, B["ident_bf"][:],
                     B["wfull_sb"][:, g3 * NE + e: g3 * NE + e + 1], None, OP.mult,
                     deps=[prev.get(("dg", tt))], inc=1)
            for half in range(2):
                m_mm2 = None
                for hbi in range(4):
                    hb = half * 4 + hbi
                    m_mm2 = r("tensor", "matmul",
                              B["psE_tr"][:, hbi * 128:(hbi + 1) * 128],
                              gl[:, hb * 128:(hb + 1) * 128], dg_t,
                              deps=[m_gelu[tt], m_dg, prev.get("pstr_free")],
                              start=True, stop=True, inc=(1 if hbi == 3 else 0))
                m_tev = r("scalar", "activation",
                          B["glT_sb"][:].rearrange(
                              "p (hb t) -> p hb t", t=TG)[:, half * 4:(half + 1) * 4,
                                                          tt * 128:(tt + 1) * 128],
                          B["psE_tr"][:].rearrange("p (hb t) -> p hb t", t=128),
                          AF.Copy, deps=[m_mm2, prev.get("glT_free")], inc=1)
                prev["pstr_free"] = m_tev
                prev[("glT", tt, half)] = m_tev
            prev[("dg", tt)] = prev[("glT", tt, 1)]
        m_glT = prev[("glT", 1, 1)]
        m_mm3 = None
        for rnd in range(4):
            bank = rnd % 2
            for mti in range(2):
                mt = rnd * 2 + mti
                dst = B["psE_o"][:, (bank * 2 + mti) * TG:(bank * 2 + mti + 1) * TG]
                hd = ([m_glT, prev.get(("pso", bank))] + (list(mW) if rnd == 0 else [])
                      if mti == 0 else ())
                if e == 0:
                    m_mm3 = r("tensor", "matmul", dst,
                              B["b2_sb"][:, mt * 128:(mt + 1) * 128],
                              B["wroute_sb"][:, tg * TG:(tg + 1) * TG],
                              deps=hd, start=True, stop=False, inc=0)
                for kg in range(8):
                    wbuf = B["W2X_sb"] if kg < 4 else B["W2Y_sb"]
                    kk = kg % 4
                    m_mm3 = r("tensor", "matmul", dst,
                              wbuf[:, kk * D + mt * 128: kk * D + (mt + 1) * 128],
                              B["glT_sb"][:, kg * TG:(kg + 1) * TG],
                              deps=(hd if (e > 0 and kg == 0) else ()),
                              start=(kg == 0 and e != 0), stop=(kg == 7),
                              inc=(1 if (mti == 1 and kg == 7) else 0))
            m_fa = None
            for mti in range(2):
                mt = rnd * 2 + mti
                fsl = B["f_sb"][:, mt * NT + tg * TG: mt * NT + (tg + 1) * TG]
                psl = B["psE_o"][:, (bank * 2 + mti) * TG:(bank * 2 + mti + 1) * TG]
                if e == 0:
                    m_fa = r("vector", "tensor_scalar", fsl, psl, 0.0, None,
                             OP.add, deps=[m_mm3], inc=(1 if mti == 1 else 0),
                             war=True)
                else:
                    m_fa = r("vector", "tensor_tensor", fsl, psl, fsl, OP.add,
                             deps=[m_mm3], inc=(1 if mti == 1 else 0), war=True)
            prev[("pso", bank)] = m_fa
        prev["glT_free"] = m_mm3
        return m_gelu, m_mm3

    ctx.tap("c_u3T", B["u3T_sb"][:])
    for nm in ("sum3_s", "ssq3_s", "rstd3_s", "b03_s"):
        ctx.tap("c_" + nm, B[nm][:])

    eW = {0: load_e(0, "both")}
    eV = {0: load_vec_e(0)}
    eSb = {}

    def mk_sb1(e):
        r("vector", "tensor_reduce", B["Sb1_sb"][:], B["b1e_sb"][:], AX.X, OP.add,
          deps=[eV[e][2], prev.get("sb1_free")], inc=1)
        eSb[e] = r("vector", "tensor_scalar", B["Sb1_sb"][:], B["Sb1_sb"][:],
                   1.0 / HID, None, OP.mult, inc=1)

    mk_sb1(0)
    ntg = LIMIT_TG or NTG
    ne = LIMIT_E or NE
    pend = None
    for e in range(ne):
        for tg in range(ntg):
            mA = part_a(e, tg, eW[e])
            if tg == ntg - 1:
                prev["W1_free"] = mA[1]
                if e + 1 < ne:
                    eW[e + 1] = load_e(e + 1, "W1")
            if pend is not None:
                pe_, ptg = pend
                m_g, m_w2 = part_b(pe_, ptg, eW[pe_], eV[pe_], eSb[pe_])
                if ptg == ntg - 1:
                    prev["W2_free"] = m_w2
                    prev["g1_free"] = prev[("t5", 1)]
                    prev["be1_free"] = prev[("t5", 1)]
                    prev["b1_free"] = prev[("b1", 1)]
                    prev["sb1_free"] = m_g[1]
                    if pe_ + 1 < ne:
                        eW[pe_ + 1] = eW[pe_ + 1] + load_e(pe_ + 1, "W2")
                        eV[pe_ + 1] = load_vec_e(pe_ + 1)
                        mk_sb1(pe_ + 1)
            part_evac(e, tg, mA, eV[e])
            pend = (e, tg)
    part_b(pend[0], pend[1], eW[pend[0]], eV[pend[0]], eSb[pend[0]])


def _phase_fin(ctx, H, B, out_h):
    r = ctx.run
    prev = {}
    for tg in range(LIMIT_TG or NTG):
        m_lx = r("sync", "dma_start",
                 out=B["x1f_sb"][:].rearrange("p (tt d) -> p tt d", d=D),
                 in_=H["sp_x1"][tg], deps=[prev.get("x1f_free")], dma=True)
        m_x2 = {}
        for tt in range(2):
            m_mm = None
            for og in range(8):
                m_mm = r("tensor", "matmul", B["psE_f"][:, og * 128:(og + 1) * 128],
                         B["f_sb"][:, og * NT + tg * TG + tt * 128:
                                   og * NT + tg * TG + (tt + 1) * 128],
                         B["ident_bf"][:],
                         deps=[prev.get("psf")],
                         start=True, stop=True, inc=(1 if og == 7 else 0))
            m_x2[tt] = r("vector", "tensor_tensor", B["x2_sb"][:, tt * D:(tt + 1) * D],
                         B["x1f_sb"][:, tt * D:(tt + 1) * D], B["psE_f"][:], OP.add,
                         deps=[m_mm, m_lx, prev.get("x2_free")], inc=1)
            prev["psf"] = m_x2[tt]
        m_st = r("gpsimd", "dma_start",
                 out=out_h[tg * TG:(tg + 1) * TG, :].rearrange("(tt p) d -> p tt d", p=128),
                 in_=B["x2_sb"][:].rearrange("p (tt d) -> p tt d", d=D),
                 deps=[m_x2[1]], dma=True)
        prev["x2_free"] = m_st
        prev["x1f_free"] = m_x2[1]


_TAPDEFS = {}
for _c in range(NCHUNK):
    _TAPDEFS["a1_x0bf_c%d" % _c] = ((128, 4 * D), BF16)
    _TAPDEFS["a1_n1T_c%d" % _c] = ((128, 8 * TL), BF16)
    _TAPDEFS["a1_sum_c%d" % _c] = ((128, 32), F32)
    _TAPDEFS["a1_mu0_c%d" % _c] = ((128, 32), F32)
    _TAPDEFS["a1_rstd0_c%d" % _c] = ((128, 32), F32)
    _TAPDEFS["a1_var_c%d" % _c] = ((128, 32), F32)
    _TAPDEFS["a1_ssq_c%d" % _c] = ((128, 32), F32)
    _TAPDEFS["a1_rs01_c%d" % _c] = ((128, 32), F32)
    _TAPDEFS["a1_uact_c%d" % _c] = ((128, 16 * TL), BF16)
    _TAPDEFS["a1_z_c%d" % _c] = ((128, 16 * TL), BF16)
    _TAPDEFS["a1_B_c%d" % _c] = ((16, TL), F32)
    _TAPDEFS["a1_dtr_c%d" % _c] = ((64, TL), BF16)
    _TAPDEFS["a2_gated_c%d" % _c] = ((128, 16 * TL), BF16)
    _TAPDEFS["a2_h_c%d" % _c] = ((128, 4 * D), F32)
_TAPDEFS["b2_u3T"] = ((128, 8 * NT), BF16)
_TAPDEFS["dbg_pdt_t0"] = ((128, D), BF16)
_TAPDEFS["dbg_pdt_t1"] = ((128, D), BF16)
for _n in ("sum3_s", "ssq3_s", "rstd3_s", "b03_s"):
    _TAPDEFS["c_" + _n] = ((128, 32), F32)
_TAPDEFS["c_u3T"] = ((128, 8 * NT), BF16)
for _t in range(NTG):
    _TAPDEFS["b2_pT_t%d" % _t] = ((128, 8 * TG), BF16)
    _TAPDEFS["b2_gld_t%d" % _t] = ((128, 32 * TG), BF16)
    for _u in range(2):
        _TAPDEFS["b2_x0h_t%d_%d" % (_t, _u)] = ((128, D), F32)
    _TAPDEFS["b2_x1_t%d" % _t] = ((128, 2 * D), F32)
    _TAPDEFS["b2_wfull_t%d" % _t] = ((128, 32 * NE), F32)


def kernel(**inputs):
    global _last_results
    shared = _prep_shared(inputs)
    x = np.asarray(inputs["L0_attn_x"], dtype=np.float32)
    nc = build()
    in_maps = []
    for c in range(NCORE):
        m = dict(shared)
        m["xs"] = _asc(x[2 * c:2 * c + 2].reshape(NT, D))
        in_maps.append(m)
    res = run_bass_kernel_spmd(nc, in_maps, core_ids=list(range(NCORE)))
    _last_results = res
    out = np.concatenate([np.asarray(res.results[c]["out"]).reshape(2, L, D)
                          for c in range(NCORE)], axis=0)
    return out


# version 04
# speedup vs baseline: 1.4441x; 1.0673x over previous
import numpy as np
from contextlib import ExitStack
import concourse.bass as bass
import concourse.mybir as mybir
from concourse.bass_utils import run_bass_kernel_spmd
from concourse.alu_op_type import AluOpType as OP

F32 = mybir.dt.float32
BF16 = mybir.dt.bfloat16
U32 = mybir.dt.uint32
AF = mybir.ActivationFunctionType
AX = mybir.AxisListType

NCORE = 8
L = 2048
D = 1024
DI = 2048
DS = 16
DTR = 64
DFF = 4096
NE = 16
HID = 1024
NT = 4096
TL = 512
NCHUNK = NT // TL
TG = 256
NTG = NT // TG
EPS = 1e-5

ENGINES = ("sync", "tensor", "scalar", "vector", "gpsimd")
PHASES = ("A1", "A2", "B1", "B2", "C", "FIN")
TAPS = ()
LIMIT_CHUNKS = None
LIMIT_TG = None
DEBUG_SCRATCH = False
LIMIT_E = None
_last_results = None


NDQ = 4


class _Eng:
    __slots__ = ("name", "obj", "sem_c", "dq", "cnt_c", "dq_cnt", "dq_idx", "seen")

    def __init__(self, name, obj, sem_c, dq):
        self.name, self.obj = name, obj
        self.sem_c, self.dq = sem_c, dq
        self.reset()

    def reset(self):
        self.cnt_c = 0
        self.dq_cnt = [0] * NDQ
        self.dq_idx = 0
        self.seen = {}


class Ctx:
    def __init__(self, nc, sems):
        self.nc = nc
        self.engs = {n: _Eng(n, getattr(nc, n), sems[n + "_c"],
                             [sems["%s_d%d" % (n, i)] for i in range(NDQ)])
                     for n in ENGINES}
        self.cur = None
        self.taps = {}
        self.autodrain = False

    def begin(self, ename):
        self.cur = ename
        for e in self.engs.values():
            e.reset()

    def dep(self, ename, m, for_dma=False):
        if m is None:
            return
        e = self.engs[ename]
        dn, dv, kind = m
        if kind == "c" and dn == ename and not for_dma:
            return
        key = (dn, kind)
        if e.seen.get(key, 0) >= dv:
            return
        e.seen[key] = dv
        if self.cur == ename:
            if kind == "d":
                qn, slot = dn
                e.obj.wait_ge(self.engs[qn].dq[slot], dv)
            else:
                e.obj.wait_ge(self.engs[dn].sem_c, dv)

    def run(self, ename, meth, *args, deps=(), inc=0, dma=False, war=False, **kw):
        for m in deps:
            self.dep(ename, m, for_dma=dma)
        e = self.engs[ename]
        if self.autodrain and ename == "vector" and not dma and self.cur == ename:
            e.obj.drain()
        if dma:
            slot = e.dq_idx % NDQ
            e.dq_idx += 1
            if e.dq_cnt[slot]:
                self.dep(ename, ((ename, slot), e.dq_cnt[slot], "d"))
            if self.cur == ename:
                ins = getattr(e.obj, meth)(*args, **kw)
                ins.then_inc(e.dq[slot], 16)
            e.dq_cnt[slot] += 16
            return ((ename, slot), e.dq_cnt[slot], "d")
        if self.cur == ename:
            ins = getattr(e.obj, meth)(*args, **kw)
            if inc:
                if (not war and meth != "drain"
                        and ename in ("scalar", "vector", "gpsimd")):
                    e.obj.drain().then_inc(e.sem_c, inc)
                else:
                    ins.then_inc(e.sem_c, inc)
        e.cnt_c += inc
        if inc:
            return (ename, e.cnt_c, "c")
        return None

    def drain(self, ename):
        if self.cur == ename:
            self.engs[ename].obj.drain()

    def barrier(self):
        ms = {}
        for n in ENGINES:
            e = self.engs[n]
            for slot in range(NDQ):
                if e.dq_cnt[slot]:
                    self.dep(n, ((n, slot), e.dq_cnt[slot], "d"))
            ms[n] = self.run(n, "drain", inc=1)
        for n in ENGINES:
            for o in ENGINES:
                if o != n:
                    self.dep(n, ms[o])
        return ms

    def tap(self, name, src_ap, deps=()):
        if name in self.taps:
            return self.run("gpsimd", "dma_start", out=self.taps[name], in_=src_ap,
                            deps=deps, dma=True)
        return None


def _asc(a):
    return np.ascontiguousarray(a, dtype=np.float32)


def _prep_shared(kw):
    d = {}
    d["w_in_T"] = _asc(kw["L0_attn_in_proj_w"].T)
    xp = np.zeros((DI, 128), dtype=np.float32)
    xpw = np.asarray(kw["L0_attn_x_proj_w"], dtype=np.float32)
    xp[:, 0:16] = xpw[DTR:DTR + DS].T
    xp[:, 32:48] = xpw[DTR + DS:DTR + 2 * DS].T
    xp[:, 64:128] = xpw[0:DTR].T
    d["w_xp_pad"] = _asc(xp)
    d["w_dt_T"] = _asc(kw["L0_attn_dt_proj_w"].T)
    d["w_out_T"] = _asc(kw["L0_attn_out_proj_w"].T)
    d["w1_T"] = _asc(kw["L0_attn_w1"].T)
    d["w2_T"] = _asc(kw["L0_attn_w2"].T)
    d["w3_T"] = _asc(kw["L0_attn_w3"].T)
    d["W1e"] = _asc(kw["L0_ffn_W1"])
    d["W2e"] = _asc(kw["L0_ffn_W2"])
    d["wr"] = _asc(kw["L0_ffn_Wr"])
    d["b2e"] = _asc(kw["L0_ffn_b2"])
    for nm, src in (("g1b", "L0_ffn_g1"), ("be1b", "L0_ffn_be1"), ("b1b", "L0_ffn_b1")):
        v = np.asarray(kw[src], dtype=np.float32)
        d[nm] = _asc(np.broadcast_to(v[:, None, :], (NE, 128, HID)))
    d["A_log_r"] = _asc(np.asarray(kw["L0_attn_A_log"]).reshape(16, 128, DS)
                        .transpose(1, 0, 2))
    d["convw_r"] = _asc(np.asarray(kw["L0_attn_conv_w"]).reshape(DI, 4)
                        .reshape(16, 128, 4).transpose(1, 0, 2))
    d["convb_r"] = _asc(np.asarray(kw["L0_attn_conv_b"]).reshape(16, 128).T)
    d["dtb_r"] = _asc(np.asarray(kw["L0_attn_dt_proj_b"]).reshape(16, 128).T)
    d["D_r"] = _asc(np.asarray(kw["L0_attn_D"]).reshape(16, 128).T)
    d["n1w_pp"] = _asc(np.asarray(kw["L0_attn_norm1_w"]).reshape(8, 128).T)
    d["n2w_pp"] = _asc(np.asarray(kw["L0_attn_norm2_w"]).reshape(8, 128).T)
    d["brb"] = _asc(np.broadcast_to(np.asarray(kw["L0_ffn_br"])[None, :], (128, NE)))
    d["iota16"] = _asc(np.broadcast_to(np.arange(NE, dtype=np.float32)[None, :], (128, NE)))
    d["ident"] = _asc(np.eye(128, dtype=np.float32))
    oh = np.zeros((DS, DS * 128), dtype=np.float32)
    for n in range(DS):
        oh[n, n * 128:(n + 1) * 128] = 1.0
    d["onehot"] = _asc(oh)
    return d


def _alloc(nc, es0):
    B = {}
    E0 = es0.enter_context

    def mk(scope, name, shape, dt):
        B[name] = scope(nc.sbuf_tensor(name, list(shape), dt))

    def mkp(scope, name, shape, dt=F32):
        B[name] = scope(nc.psum_tensor(name, list(shape), dt))

    mk(E0, "ident_bf", (128, 128), BF16)
    mk(E0, "ones_sb", (1, 128), BF16)
    mk(E0, "iota16_sb", (128, NE), F32)
    mk(E0, "brb_sb", (128, NE), F32)
    mk(E0, "n1w_sb", (128, 8), F32)
    mk(E0, "n2w_sb", (128, 8), F32)
    mk(E0, "A_sb", (128, 16 * DS), F32)
    mk(E0, "convw_sb", (128, 16 * 4), F32)
    mk(E0, "convb_sb", (128, 16), F32)
    mk(E0, "dtb_sb", (128, 16), F32)
    mk(E0, "D_sb", (128, 16), F32)

    for nm in ("sum_s", "ssq_s", "mu0_s", "rstd0_s", "b00_s", "rs01_s", "b01_s",
               "rstd2_s", "sum3_s", "ssq3_s", "rstd3_s", "b03_s"):
        mk(E0, nm, (128, 32), F32)
    mk(E0, "sm_a", (128, 32), F32)
    mk(E0, "sm_b", (128, 32), F32)
    mk(E0, "trash_bf", (128, 1024), BF16)
    mk(E0, "state_sb", (128, 16 * DS), F32)
    mk(E0, "halo_sb", (128, 16 * 3), BF16)

    sA = ExitStack()
    EA = sA.enter_context
    sA1 = ExitStack()
    E1 = sA1.enter_context
    mk(E1, "w_in_sb", (128, 8 * DFF), BF16)
    mk(E1, "w_xp_sb", (128, 16 * 128), BF16)
    mk(E1, "x0a_sb", (128, 2 * 4 * D), F32)
    mk(E1, "x0bf_sb", (128, 4 * D), BF16)
    mk(E1, "n1T_sb", (128, 8 * TL), BF16)
    mk(E1, "ustg_sb", (128, 2 * 515), BF16)
    mk(E1, "cacc_sb", (128, 2 * TL), BF16)
    mk(E1, "uact_sb", (128, 16 * TL), BF16)
    mk(E1, "zs_sb", (128, 16 * TL), BF16)
    mk(E1, "Bst_sb", (16, TL), F32)
    mk(E1, "Cst_sb", (16, TL), F32)
    mk(E1, "dtrst_sb", (64, TL), BF16)
    mkp(E1, "psA_tr0", (128, TL))
    mkp(E1, "psA_tr1", (128, TL))
    mkp(E1, "psA_mm0", (128, TL))
    mkp(E1, "psA_mm1", (128, TL))
    mkp(E1, "psA_xp", (128, TL))
    sA1.close()
    sA2 = ExitStack()
    E2 = sA2.enter_context
    mk(E2, "onehot_bf", (16, DS * 128), BF16)
    mk(E2, "w_dt_sb", (64, DI), BF16)
    mk(E2, "w_out_sb", (128, 16 * D), BF16)
    mk(E2, "x0b_sb", (128, 4 * D), F32)
    mk(E2, "uact2_sb", (128, 16 * TL), BF16)
    mk(E2, "zs2_sb", (128, 16 * TL), BF16)
    mk(E2, "dtr2_sb", (64, TL), BF16)
    mk(E2, "Bbf_sb", (16, TL), BF16)
    mk(E2, "Cbf_sb", (16, TL), BF16)
    mk(E2, "Bb_sb", (128, 16 * TL), BF16)
    mk(E2, "Cb_sb", (128, 16 * TL), BF16)
    mk(E2, "sp1_sb", (128, TL), F32)
    mk(E2, "sp2_sb", (128, TL), F32)
    mk(E2, "dt_sb", (128, 2 * TL), F32)
    mk(E2, "du_sb", (128, 2 * TL), BF16)
    mk(E2, "aBuf_sb", (128, 2 * 2 * TL), F32)
    mk(E2, "bs_sb", (128, 2 * TL), BF16)
    mk(E2, "hb_sb", (128, 2 * TL), BF16)
    mk(E2, "yp_sb", (128, 4 * TL), BF16)
    mk(E2, "yt_sb", (128, TL), BF16)
    mk(E2, "gt_sb", (128, TL), BF16)
    mk(E2, "mT_sb", (128, 8 * TL), BF16)
    mk(E2, "h_sb", (128, 4 * D), F32)
    mk(E2, "h2bf_sb", (128, 4 * D), BF16)
    mk(E2, "n2Tst_sb", (128, 8 * TL), BF16)
    mkp(E2, "psB_bc", (128, TL))
    mkp(E2, "psB_dt", (128, TL))
    mkp(E2, "psB_op0", (128, TL))
    mkp(E2, "psB_op1", (128, TL))
    mkp(E2, "psB_tr0", (128, 1024))
    mkp(E2, "psB_tr1", (128, 1024))
    sA2.close()
    sA.close()

    sBC = ExitStack()
    EB = sBC.enter_context
    mk(EB, "u3T_sb", (128, 8 * NT), BF16)
    mk(EB, "wfull_sb", (128, 32 * NE), F32)
    mk(EB, "wroute_sb", (16, NT), BF16)
    mk(EB, "wr_sb", (128, 8 * NE), BF16)
    mk(EB, "b2_sb", (16, HID), BF16)
    sB1 = ExitStack()
    E3 = sB1.enter_context
    mk(E3, "n2Ta_sb", (128, 8 * NT), BF16)
    mk(E3, "w1s_sb", (128, 2 * 8 * 128), BF16)
    mk(E3, "w2s_sb", (128, 2 * 8 * 128), BF16)
    mk(E3, "sl_sb", (128, TL), BF16)
    mk(E3, "w2e_sb", (128, TL), BF16)
    mk(E3, "gtmp_sb", (128, 2 * TL), BF16)
    mk(E3, "gst_sb", (128, 2 * NT), BF16)
    mkp(E3, "psC_1", (128, TL))
    mkp(E3, "psC_2", (128, TL))
    mkp(E3, "psC_3", (128, TL))
    mkp(E3, "psC_4", (128, TL))
    sB1.close()
    sB2 = ExitStack()
    E4 = sB2.enter_context
    mk(E4, "w3_sb", (128, 32 * D), BF16)
    mk(E4, "gld_sb", (128, 32 * TG), BF16)
    mk(E4, "x0c_sb", (128, 2 * D), F32)
    mk(E4, "hc_sb", (128, 2 * D), F32)
    mk(E4, "x1_sb", (128, 2 * D), F32)
    mk(E4, "pT_sb", (128, 8 * TG), BF16)
    mk(E4, "u3bf_sb", (128, 2 * D), BF16)
    mk(E4, "lg_sb", (128, 2 * NE), F32)
    mk(E4, "m8_sb", (128, 2 * 8), F32)
    mk(E4, "i8_sb", (128, 2 * 8), U32)
    mk(E4, "idxf_sb", (128, 2 * 2), F32)
    mk(E4, "nm1_sb", (128, 2), F32)
    mk(E4, "t21_sb", (128, 2), F32)
    mk(E4, "den_sb", (128, 2), F32)
    mk(E4, "w1r_sb", (128, 2), F32)
    mk(E4, "w2r_sb", (128, 2), F32)
    mk(E4, "mk1_sb", (128, NE), F32)
    mk(E4, "mk2_sb", (128, NE), F32)
    mk(E4, "wfbf_sb", (128, NE), BF16)
    mkp(E4, "psD_p", (128, 8 * TG))
    mkp(E4, "psD_t", (128, 1024))
    mkp(E4, "psD_u", (128, 4 * TG))
    sB2.close()
    sC = ExitStack()
    E5 = sC.enter_context
    mk(E5, "f_sb", (128, 8 * NT), BF16)
    sC1 = ExitStack()
    E6 = sC1.enter_context
    mk(E6, "W1X_sb", (128, 4 * HID), BF16)
    mk(E6, "W1Y_sb", (128, 4 * HID), BF16)
    mk(E6, "W2X_sb", (128, 4 * D), BF16)
    mk(E6, "W2Y_sb", (128, 4 * D), BF16)
    mk(E6, "g1e_sb", (128, HID), BF16)
    mk(E6, "be1e_sb", (128, HID), BF16)
    mk(E6, "b1e_sb", (128, HID), BF16)
    mk(E6, "Sb1_sb", (128, 1), F32)
    mk(E6, "hch_sb", (128, 2 * HID), BF16)
    mk(E6, "gl_sb", (128, 2 * HID), BF16)
    mk(E6, "dg_sb", (128, 2 * 128), BF16)
    mk(E6, "glT_sb", (128, 8 * TG), BF16)
    mk(E6, "sums_sb", (128, 2), F32)
    mk(E6, "ssqE_sb", (128, 2), F32)
    mk(E6, "muE_sb", (128, 2), F32)
    mk(E6, "rstdE_sb", (128, 2), F32)
    mk(E6, "bE_sb", (128, 2), F32)
    mkp(E6, "psE_w1a", (128, HID))
    mkp(E6, "psE_w1b", (128, HID))
    mkp(E6, "psE_tr", (128, TL))
    mkp(E6, "psE_o", (128, 4 * TG))
    sC1.close()
    sC2 = ExitStack()
    E7 = sC2.enter_context
    mk(E7, "x1f_sb", (128, 2 * D), F32)
    mk(E7, "x2_sb", (128, 2 * D), F32)
    mkp(E7, "psE_f", (128, 1024))
    sC2.close()
    sC.close()
    sBC.close()
    return B


def build(core_inputs_shapes=None):
    nc = bass.Bass("TRN2", target_bir_lowering=False, detect_race_conditions=False)
    H = {}

    def din(name, shape, dt=F32):
        H[name] = nc.dram_tensor(name, list(shape), dt, kind="ExternalInput").ap()

    din("xs", (NT, D))
    din("w_in_T", (D, DFF)); din("w_xp_pad", (DI, 128)); din("w_dt_T", (DTR, DI))
    din("w_out_T", (DI, D)); din("w1_T", (D, DFF)); din("w2_T", (D, DFF))
    din("w3_T", (DFF, D)); din("W1e", (NE, D, HID)); din("W2e", (NE, HID, D))
    din("wr", (D, NE)); din("b2e", (NE, D))
    din("g1b", (NE, 128, HID)); din("be1b", (NE, 128, HID)); din("b1b", (NE, 128, HID))
    din("A_log_r", (128, 16, DS)); din("convw_r", (128, 16, 4)); din("convb_r", (128, 16))
    din("dtb_r", (128, 16)); din("D_r", (128, 16)); din("n1w_pp", (128, 8))
    din("n2w_pp", (128, 8)); din("brb", (128, NE)); din("iota16", (128, NE))
    din("ident", (128, 128)); din("onehot", (16, DS * 128))
    out_h = nc.dram_tensor("out", [NT, D], F32, kind="ExternalOutput").ap()

    def scr(name, shape, dt):
        kind = "ExternalOutput" if DEBUG_SCRATCH else "Internal"
        H[name] = nc.dram_tensor(name, list(shape), dt, kind=kind).ap()

    scr("sp_uact", (NCHUNK, 128, 16, TL), BF16)
    scr("sp_z", (NCHUNK, 128, 16, TL), BF16)
    scr("sp_dtr", (NCHUNK, 64, TL), BF16)
    scr("sp_B", (NCHUNK, 16, TL), F32)
    scr("sp_C", (NCHUNK, 16, TL), F32)
    scr("sp_h", (NCHUNK, 128, 4, D), F32)
    scr("sp_n2T", (128, 8, NT), BF16)
    scr("sp_g2", (NTG, 128, 32, TG), BF16)
    scr("sp_x1", (NTG, 128, 2, D), F32)
    if DEBUG_SCRATCH:
        scr("dbg_gld", (NTG, 128, 32 * TG), BF16)
        scr("dbg_pT", (NTG, 128, 8 * TG), BF16)

    es0 = ExitStack()
    sems = {}
    for n in ENGINES:
        sems[n + "_c"] = es0.enter_context(nc.semaphore("semc_" + n))
        for i in range(4):
            sems["%s_d%d" % (n, i)] = es0.enter_context(
                nc.semaphore("semd%d_" % i + n))
    Bf = _alloc(nc, es0)
    ctx = Ctx(nc, sems)
    for tname in TAPS:
        shape, dt = _TAPDEFS[tname]
        ctx.taps[tname] = nc.dram_tensor("tap_" + tname, list(shape), dt,
                                         kind="ExternalOutput").ap()

    with nc.Block() as block:
        for ename in ENGINES:
            fn = getattr(block, ename)

            def emit(_eng, ename=ename):
                ctx.begin(ename)
                _program(ctx, H, Bf, out_h)

            fn(emit)
    es0.close()
    return nc



def _program(ctx, H, B, out_h):
    r = ctx.run
    m_setup = _setup(ctx, H, B)
    bar = ctx.barrier()
    if "A1" in PHASES:
        _phase_a1(ctx, H, B, m_setup)
        bar = ctx.barrier()
    if "A2" in PHASES:
        _phase_a2(ctx, H, B, m_setup)
        bar = ctx.barrier()
    if "B1" in PHASES:
        _phase_b1(ctx, H, B)
        bar = ctx.barrier()
    if "B2" in PHASES:
        _phase_b2(ctx, H, B)
        bar = ctx.barrier()
    if "C" in PHASES:
        _phase_c(ctx, H, B)
        bar = ctx.barrier()
    if "FIN" in PHASES:
        _phase_fin(ctx, H, B, out_h)
    e = ctx.engs["gpsimd"]
    for slot in range(NDQ):
        if e.dq_cnt[slot]:
            for n in ENGINES:
                ctx.dep(n, (("gpsimd", slot), e.dq_cnt[slot], "d"))
    ctx.barrier()


def _setup(ctx, H, B):
    r = ctx.run
    m = {}
    for nm, dst in (("iota16", "iota16_sb"), ("brb", "brb_sb"), ("n1w_pp", "n1w_sb"),
                    ("n2w_pp", "n2w_sb"), ("convb_r", "convb_sb"), ("dtb_r", "dtb_sb"),
                    ("D_r", "D_sb")):
        m[dst] = r("sync", "dma_start", out=B[dst][:], in_=H[nm][:], dma=True)
    m["A_log"] = r("sync", "dma_start", out=B["A_sb"][:],
                   in_=H["A_log_r"].rearrange("p dc n -> p (dc n)"), dma=True)
    m["convw"] = r("sync", "dma_start", out=B["convw_sb"][:],
                   in_=H["convw_r"].rearrange("p dc t -> p (dc t)"), dma=True)
    m["ident"] = r("gpsimd", "dma_start", out=B["ident_bf"][:], in_=H["ident"][:], dma=True)
    ma1 = r("scalar", "activation", B["A_sb"][:], B["A_sb"][:], AF.Exp,
            deps=[m["A_log"]], inc=1)
    m["A"] = r("vector", "tensor_scalar", B["A_sb"][:], B["A_sb"][:], -1.0, None, OP.mult,
               deps=[ma1], inc=1)
    m["st0"] = r("gpsimd", "memset", B["state_sb"][:], 0.0, inc=1)
    m["ones"] = r("gpsimd", "memset", B["ones_sb"][:], 1.0, inc=1)
    m["hl0"] = r("gpsimd", "memset", B["halo_sb"][:], 0.0, inc=1)
    return m


def _phase_a1(ctx, H, B, ms):
    r = ctx.run
    mw = []
    for kg in range(8):
        mw.append(r("gpsimd", "dma_start",
                    out=B["w_in_sb"][:, kg * DFF:(kg + 1) * DFF],
                    in_=H["w_in_T"][kg * 128:(kg + 1) * 128, :], dma=True))
    m_wxp = r("gpsimd", "dma_start",
              out=B["w_xp_sb"][:].rearrange("p (dc m) -> p dc m", m=128),
              in_=H["w_xp_pad"].rearrange("(dc p) m -> p dc m", p=128), dma=True)
    m_fold = []
    for kg in range(8):
        m_fold.append(r("vector", "tensor_scalar",
                        B["w_in_sb"][:, kg * DFF:(kg + 1) * DFF],
                        B["w_in_sb"][:, kg * DFF:(kg + 1) * DFF],
                        B["n1w_sb"][:, kg:kg + 1], None, OP.mult,
                        deps=[mw[kg], ms["n1w_sb"]], inc=1))
    m_fold_all = m_fold[-1]

    prev = {}
    for c in range(LIMIT_CHUNKS or NCHUNK):
        buf = c % 2
        x0 = B["x0a_sb"][:, buf * 4 * D:(buf + 1) * 4 * D].rearrange("p (tt d) -> p tt d", d=D)
        m_ld = r("sync", "dma_start", out=x0,
                 in_=H["xs"][c * TL:(c + 1) * TL, :].rearrange("(tt p) d -> p tt d", p=128),
                 deps=[prev.get(("x0", buf))], dma=True)
        g0 = c * 4
        m_st = None
        m_stq = None
        for tt in range(4):
            g = g0 + tt
            m_st = r("vector", "tensor_reduce", B["sum_s"][:, g:g + 1], x0[:, tt, :],
                     AX.X, OP.add, deps=[m_ld], inc=(1 if tt == 3 else 0))
            m_stq = r("scalar", "activation", B["trash_bf"][:],
                      x0[:, tt, :], AF.Square,
                      accum_out=B["ssq_s"][:, g:g + 1], deps=[m_ld],
                      inc=(1 if tt == 3 else 0))
        ctx.drain("vector")
        ctx.autodrain = True
        sl = slice(g0, g0 + 4)
        r("vector", "tensor_scalar", B["mu0_s"][:, sl], B["sum_s"][:, sl], 1.0 / D, None,
          OP.mult, deps=[m_st, m_stq])
        r("vector", "tensor_scalar", B["sm_a"][:, sl], B["ssq_s"][:, sl], 1.0 / D, None, OP.mult)
        r("vector", "tensor_tensor", B["sm_b"][:, sl], B["mu0_s"][:, sl], B["mu0_s"][:, sl],
          OP.mult)
        r("vector", "tensor_tensor", B["sm_a"][:, sl], B["sm_a"][:, sl], B["sm_b"][:, sl],
          OP.subtract)
        m_v = r("vector", "tensor_scalar", B["sm_b"][:, sl], B["sm_a"][:, sl], EPS, None,
                OP.add, inc=1)
        ml1 = r("scalar", "activation", B["sm_b"][:, sl], B["sm_b"][:, sl], AF.Ln,
                deps=[m_v], inc=1)
        r("scalar", "activation", B["rstd0_s"][:, sl], B["sm_b"][:, sl], AF.Exp, scale=-0.5)
        m_s1 = r("scalar", "activation", B["sm_b"][:, sl], B["sm_b"][:, sl], AF.Exp,
                 inc=1)
        r("vector", "reciprocal", B["sm_b"][:, sl], B["sm_b"][:, sl], deps=[m_s1])
        r("vector", "tensor_tensor", B["sm_a"][:, sl], B["sm_a"][:, sl], B["sm_b"][:, sl],
          OP.mult)
        m_v2 = r("vector", "tensor_scalar", B["sm_a"][:, sl], B["sm_a"][:, sl], EPS, None,
                 OP.add, inc=1)
        r("scalar", "activation", B["sm_a"][:, sl], B["sm_a"][:, sl], AF.Ln, deps=[m_v2])
        m_s2 = r("scalar", "activation", B["sm_a"][:, sl], B["sm_a"][:, sl], AF.Exp,
                 scale=-0.5, inc=1)
        r("vector", "tensor_tensor", B["rs01_s"][:, sl], B["rstd0_s"][:, sl], B["sm_a"][:, sl],
          OP.mult, deps=[m_s2])
        r("vector", "scalar_tensor_tensor", B["b00_s"][:, sl], B["mu0_s"][:, sl],
          -1.0, B["rstd0_s"][:, sl], OP.mult, OP.mult)
        m_sm = r("vector", "scalar_tensor_tensor", B["b01_s"][:, sl], B["mu0_s"][:, sl],
                 -1.0, B["rs01_s"][:, sl], OP.mult, OP.mult, inc=1)
        ctx.autodrain = False
        m_cast = None
        for tt in range(4):
            g = g0 + tt
            m_cast = r("scalar", "activation",
                       B["x0bf_sb"][:, tt * D:(tt + 1) * D], x0[:, tt, :], AF.Identity,
                       scale=B["rs01_s"][:, g:g + 1], bias=B["b01_s"][:, g:g + 1],
                       deps=[m_sm, prev.get("n1T_done")], inc=1)
        m_ev = {}
        for dg in range(8):
            ps = B["psA_tr0"] if dg % 2 == 0 else B["psA_tr1"]
            m_mm = None
            for tt in range(4):
                m_mm = r("tensor", "matmul", ps[:, tt * 128:(tt + 1) * 128],
                         B["x0bf_sb"][:, tt * D + dg * 128: tt * D + (dg + 1) * 128],
                         B["ident_bf"][:],
                         deps=[m_cast, ms["ident"], m_ev.get(dg - 2)],
                         start=True, stop=True, inc=(1 if tt == 3 else 0))
            m_ev[dg] = r("scalar", "activation", B["n1T_sb"][:, dg * TL:(dg + 1) * TL],
                         ps[:], AF.Copy, deps=[m_mm], inc=1)
        m_n1T = m_ev[7]
        m_mm_ev = {}
        m_uact = {}
        for mt in range(32):
            ps = B["psA_mm0"] if mt % 2 == 0 else B["psA_mm1"]
            dep0 = [m_n1T, m_fold_all, m_mm_ev.get(mt - 2)]
            m_mm = None
            for kg in range(8):
                m_mm = r("tensor", "matmul", ps[:],
                         B["w_in_sb"][:, kg * DFF + mt * 128: kg * DFF + (mt + 1) * 128],
                         B["n1T_sb"][:, kg * TL:(kg + 1) * TL],
                         deps=(dep0 if kg == 0 else ()),
                         start=(kg == 0), stop=(kg == 7), inc=(1 if kg == 7 else 0))
            if mt < 16:
                dc = mt
                ub = dc % 2
                ust = B["ustg_sb"][:, ub * 515:(ub + 1) * 515]
                if c % 4 == 0:
                    r("vector", "memset", ust[:, 0:3], 0.0,
                      deps=[prev.get(("ustg", ub))])
                else:
                    r("vector", "tensor_copy", ust[:, 0:3],
                      B["halo_sb"][:, dc * 3:(dc + 1) * 3],
                      deps=[prev.get(("ustg", ub)), prev.get("halo_w%d" % dc)])
                m_evu = r("scalar", "activation", ust[:, 3:515], ps[:], AF.Copy,
                          deps=[m_mm, prev.get(("ustg", ub))], inc=1)
                m_mm_ev[mt] = m_evu
                m_tail = r("vector", "tensor_copy", B["halo_sb"][:, dc * 3:(dc + 1) * 3],
                           ust[:, 512:515], deps=[m_evu])
                prev["halo_w%d" % dc] = m_tail
                cb = dc % 2
                cacc = B["cacc_sb"][:, cb * TL:(cb + 1) * TL]
                ctx.drain("vector")
                r("vector", "tensor_scalar", cacc, ust[:, 0:512],
                  B["convw_sb"][:, dc * 4:dc * 4 + 1], None, OP.mult,
                  deps=[m_evu, ms["convw"], prev.get(("cacc", cb))])
                m_conv = None
                for t in range(1, 4):
                    m_conv = r("vector", "scalar_tensor_tensor", cacc, ust[:, t:512 + t],
                               B["convw_sb"][:, dc * 4 + t:dc * 4 + t + 1], cacc,
                               OP.mult, OP.add, inc=(1 if t == 3 else 0))
                prev[("ustg", ub)] = m_conv
                m_ua = r("scalar", "activation", B["uact_sb"][:, dc * TL:(dc + 1) * TL],
                         cacc, AF.Silu, bias=B["convb_sb"][:, dc:dc + 1],
                         deps=[m_conv, ms["convb_sb"], prev.get("sp_uact")], inc=1)
                prev[("cacc", cb)] = m_ua
                m_uact[dc] = m_ua
            else:
                dc = mt - 16
                m_evz = r("scalar", "activation", B["zs_sb"][:, dc * TL:(dc + 1) * TL],
                          ps[:], AF.Silu, deps=[m_mm, prev.get("sp_z")], inc=1)
                m_mm_ev[mt] = m_evz
        m_xp = None
        for k in range(16):
            m_xp = r("tensor", "matmul", B["psA_xp"][:],
                     B["w_xp_sb"][:, k * 128:(k + 1) * 128],
                     B["uact_sb"][:, k * TL:(k + 1) * TL],
                     deps=[m_uact[k], m_wxp] + ([prev.get("xp_ev")] if k == 0 else []),
                     start=(k == 0), stop=(k == 15), inc=(1 if k == 15 else 0))
        m_eB = r("scalar", "activation", B["Bst_sb"][:], B["psA_xp"][0:16, :], AF.Copy,
                 deps=[m_xp, prev.get("sp_B")], inc=1)
        m_eC = r("scalar", "activation", B["Cst_sb"][:], B["psA_xp"][32:48, :], AF.Copy,
                 deps=[prev.get("sp_C")], inc=1)
        m_eD = r("scalar", "activation", B["dtrst_sb"][:], B["psA_xp"][64:128, :], AF.Copy,
                 deps=[prev.get("sp_dtr")], inc=1)
        prev["xp_ev"] = m_eD
        m_spu = r("scalar", "dma_start", out=H["sp_uact"][c],
                  in_=B["uact_sb"][:].rearrange("p (dc t) -> p dc t", t=TL),
                  deps=[m_uact[15]], dma=True)
        m_spz = r("scalar", "dma_start", out=H["sp_z"][c],
                  in_=B["zs_sb"][:].rearrange("p (dc t) -> p dc t", t=TL),
                  deps=[m_mm_ev[31]], dma=True)
        m_spB = r("scalar", "dma_start", out=H["sp_B"][c], in_=B["Bst_sb"][:],
                  deps=[m_eB], dma=True)
        m_spC = r("scalar", "dma_start", out=H["sp_C"][c], in_=B["Cst_sb"][:],
                  deps=[m_eC], dma=True)
        m_spD = r("scalar", "dma_start", out=H["sp_dtr"][c], in_=B["dtrst_sb"][:],
                  deps=[m_eD], dma=True)
        prev["sp_uact"] = m_spu
        prev["sp_z"] = m_spz
        prev["sp_B"] = m_spB
        prev["sp_C"] = m_spC
        prev["sp_dtr"] = m_spD
        prev["n1T_done"] = m_mm_ev[31]
        prev[("x0", buf)] = m_cast
        ctx.tap("a1_x0bf_c%d" % c, B["x0bf_sb"][:], deps=[m_cast])
        ctx.tap("a1_n1T_c%d" % c, B["n1T_sb"][:], deps=[m_n1T])
        ctx.tap("a1_sum_c%d" % c, B["sum_s"][:], deps=[m_sm])
        ctx.tap("a1_mu0_c%d" % c, B["mu0_s"][:], deps=[m_sm])
        ctx.tap("a1_rstd0_c%d" % c, B["rstd0_s"][:], deps=[m_sm])
        ctx.tap("a1_var_c%d" % c, B["sm_a"][:], deps=[m_sm])
        ctx.tap("a1_ssq_c%d" % c, B["ssq_s"][:], deps=[m_sm])
        ctx.tap("a1_rs01_c%d" % c, B["rs01_s"][:], deps=[m_sm])
        ctx.tap("a1_uact_c%d" % c, B["uact_sb"][:], deps=[m_uact[15]])
        ctx.tap("a1_z_c%d" % c, B["zs_sb"][:], deps=[m_mm_ev[31]])
        ctx.tap("a1_B_c%d" % c, B["Bst_sb"][:], deps=[m_eB])
        ctx.tap("a1_dtr_c%d" % c, B["dtrst_sb"][:], deps=[m_eD])


def _phase_a2(ctx, H, B, ms):
    r = ctx.run
    ms["onehot"] = r("gpsimd", "dma_start", out=B["onehot_bf"][:], in_=H["onehot"][:],
                     dma=True)
    m_wdt = r("gpsimd", "dma_start", out=B["w_dt_sb"][:], in_=H["w_dt_T"][:], dma=True)
    m_wout = []
    for kg in range(16):
        m_wout.append(r("gpsimd", "dma_start",
                        out=B["w_out_sb"][:, kg * D:(kg + 1) * D],
                        in_=H["w_out_T"][kg * 128:(kg + 1) * 128, :], dma=True))
    prev = {}
    for c in range(LIMIT_CHUNKS or NCHUNK):
        g0 = c * 4
        batch_first = (c % 4 == 0)
        m_lu = r("sync", "dma_start",
                 out=B["uact2_sb"][:].rearrange("p (dc t) -> p dc t", t=TL),
                 in_=H["sp_uact"][c], deps=[prev.get("uact_free")], dma=True)
        m_lz = r("sync", "dma_start",
                 out=B["zs2_sb"][:].rearrange("p (dc t) -> p dc t", t=TL),
                 in_=H["sp_z"][c], deps=[prev.get("z_free")], dma=True)
        m_lx = r("sync", "dma_start",
                 out=B["x0b_sb"][:].rearrange("p (tt d) -> p tt d", d=D),
                 in_=H["xs"][c * TL:(c + 1) * TL, :].rearrange("(tt p) d -> p tt d", p=128),
                 deps=[prev.get("x0_free")], dma=True)
        m_ldt = r("sync", "dma_start", out=B["dtr2_sb"][:], in_=H["sp_dtr"][c],
                  deps=[prev.get("dtr_free")], dma=True)
        m_lB = r("gpsimd", "dma_start", out=B["Bbf_sb"][:], in_=H["sp_B"][c],
                 deps=[prev.get("Bbf_free")], dma=True)
        m_lC = r("gpsimd", "dma_start", out=B["Cbf_sb"][:], in_=H["sp_C"][c],
                 deps=[prev.get("Cbf_free")], dma=True)
        if batch_first and c > 0:
            ms["st0"] = r("gpsimd", "memset", B["state_sb"][:], 0.0,
                          deps=[prev.get("scan_all")], inc=1)
        m_bb = {}
        m_cb = {}
        m_bc_ev = prev.get("bc_ev")
        for n in range(DS):
            m_mm = r("tensor", "matmul", B["psB_bc"][:],
                     B["onehot_bf"][:, n * 128:(n + 1) * 128], B["Bbf_sb"][:],
                     deps=[m_lB, ms["onehot"], m_bc_ev], start=True, stop=True, inc=1)
            m_bc_ev = r("scalar", "activation", B["Bb_sb"][:, n * TL:(n + 1) * TL],
                        B["psB_bc"][:], AF.Copy, deps=[m_mm, prev.get("Bb_free")], inc=1)
            m_bb[n] = m_bc_ev
        for n in range(DS):
            m_mm = r("tensor", "matmul", B["psB_bc"][:],
                     B["onehot_bf"][:, n * 128:(n + 1) * 128], B["Cbf_sb"][:],
                     deps=[m_lC, m_bc_ev], start=True, stop=True, inc=1)
            m_bc_ev = r("scalar", "activation", B["Cb_sb"][:, n * TL:(n + 1) * TL],
                        B["psB_bc"][:], AF.Copy, deps=[m_mm, prev.get("Cb_free")], inc=1)
            m_cb[n] = m_bc_ev
        prev["Bbf_free"] = m_bb[DS - 1]
        prev["Cbf_free"] = m_cb[DS - 1]
        prev["bc_ev"] = m_bc_ev

        m_gate = {}
        m_sp_done = None
        m_scan_last = None
        for dc in range(16):
            db = dc % 2
            m_dmm = r("tensor", "matmul", B["psB_dt"][:],
                      B["w_dt_sb"][:, dc * 128:(dc + 1) * 128], B["dtr2_sb"][:],
                      deps=[m_ldt, m_wdt, prev.get("psdt_free")], start=True, stop=True,
                      inc=1)
            r("scalar", "activation", B["sp1_sb"][:], B["psB_dt"][:], AF.Abs,
              bias=B["dtb_sb"][:, dc:dc + 1], deps=[m_dmm, ms["dtb_sb"],
                                                    prev.get(("sp", 0))])
            r("scalar", "activation", B["sp1_sb"][:], B["sp1_sb"][:], AF.Exp, scale=-1.0)
            r("scalar", "activation", B["sp1_sb"][:], B["sp1_sb"][:], AF.Ln, bias=1.0)
            m_sp = r("scalar", "activation", B["sp2_sb"][:], B["psB_dt"][:], AF.Relu,
                     bias=B["dtb_sb"][:, dc:dc + 1], deps=[prev.get(("sp", 1))], inc=1)
            prev["psdt_free"] = m_sp
            dt_t = B["dt_sb"][:, db * TL:(db + 1) * TL]
            m_dt = r("vector", "tensor_tensor", dt_t, B["sp1_sb"][:], B["sp2_sb"][:], OP.add,
                     deps=[m_sp, prev.get(("dt", db))], inc=1)
            prev[("sp", 0)] = m_dt
            prev[("sp", 1)] = m_dt
            du_t = B["du_sb"][:, db * TL:(db + 1) * TL]
            m_du = r("vector", "tensor_tensor", du_t, dt_t, B["uact2_sb"][:, dc * TL:(dc + 1) * TL],
                     OP.mult, deps=[m_lu, prev.get(("du", db))])
            m_scan_last = None
            for grp in range(8):
                ab = grp % 2
                m_a = None
                for j in range(2):
                    n = grp * 2 + j
                    m_a = r("scalar", "activation",
                            B["aBuf_sb"][:, (ab * 2 + j) * TL:(ab * 2 + j + 1) * TL],
                            dt_t, AF.Exp, scale=B["A_sb"][:, dc * DS + n:dc * DS + n + 1],
                            deps=[m_dt, ms["A"], prev.get(("scangrp", ab))],
                            inc=(1 if j == 1 else 0))
                m_sc = None
                for j in range(2):
                    n = grp * 2 + j
                    q = n % 4
                    hb = B["hb_sb"][:, (n % 2) * TL:((n % 2) + 1) * TL]
                    bs = B["bs_sb"][:, (n % 2) * TL:((n % 2) + 1) * TL]
                    r("vector", "tensor_tensor", bs, du_t, B["Bb_sb"][:, n * TL:(n + 1) * TL],
                      OP.mult, deps=[m_du, m_bb[n], m_a])
                    r("vector", "tensor_tensor_scan", hb,
                      B["aBuf_sb"][:, (ab * 2 + j) * TL:(ab * 2 + j + 1) * TL], bs,
                      B["state_sb"][:, dc * DS + n:dc * DS + n + 1], OP.mult, OP.add,
                      deps=[ms["st0"]])
                    if n < 4:
                        m_sc = r("vector", "tensor_tensor",
                                 B["yp_sb"][:, q * TL:(q + 1) * TL], hb,
                                 B["Cb_sb"][:, n * TL:(n + 1) * TL], OP.mult,
                                 deps=[m_cb[n]], inc=(1 if j == 1 else 0), war=True)
                    else:
                        yt = B["yt_sb"][:]
                        r("vector", "tensor_tensor", yt, hb,
                          B["Cb_sb"][:, n * TL:(n + 1) * TL], OP.mult, deps=[m_cb[n]])
                        m_sc = r("vector", "tensor_tensor",
                                 B["yp_sb"][:, q * TL:(q + 1) * TL],
                                 B["yp_sb"][:, q * TL:(q + 1) * TL], yt, OP.add,
                                 inc=(1 if j == 1 else 0), war=True)
                    r("vector", "tensor_scalar",
                      B["state_sb"][:, dc * DS + n:dc * DS + n + 1],
                      hb[:, TL - 1:TL], 0.0, None, OP.add)
                prev[("scangrp", ab)] = m_sc
                m_scan_last = m_sc
            prev[("dt", db)] = m_scan_last
            prev[("du", db)] = m_scan_last
            r("vector", "tensor_tensor", B["yp_sb"][:, 0:TL], B["yp_sb"][:, 0:TL],
              B["yp_sb"][:, TL:2 * TL], OP.add)
            r("vector", "tensor_tensor", B["yp_sb"][:, 2 * TL:3 * TL],
              B["yp_sb"][:, 2 * TL:3 * TL], B["yp_sb"][:, 3 * TL:4 * TL], OP.add)
            r("vector", "tensor_tensor", B["yp_sb"][:, 0:TL], B["yp_sb"][:, 0:TL],
              B["yp_sb"][:, 2 * TL:3 * TL], OP.add)
            r("vector", "scalar_tensor_tensor", B["gt_sb"][:],
              B["uact2_sb"][:, dc * TL:(dc + 1) * TL], B["D_sb"][:, dc:dc + 1],
              B["yp_sb"][:, 0:TL], OP.mult, OP.add, deps=[ms["D_sb"]])
            m_g = r("vector", "tensor_tensor", B["uact2_sb"][:, dc * TL:(dc + 1) * TL],
                    B["gt_sb"][:], B["zs2_sb"][:, dc * TL:(dc + 1) * TL], OP.mult,
                    deps=[m_lz], inc=1)
            m_gate[dc] = m_g
        prev["scan_all"] = r("vector", "drain", inc=1)
        prev["uact_free"] = None
        prev["z_free"] = m_gate[15]
        prev["dtr_free"] = m_gate[15]
        prev["Bb_free"] = m_gate[15]
        prev["Cb_free"] = m_gate[15]
        ctx.tap("a2_gated_c%d" % c, B["uact2_sb"][:], deps=[m_gate[15]])
        m_op_ev = {}
        for mt in range(8):
            ps = B["psB_op0"] if mt % 2 == 0 else B["psB_op1"]
            m_mm = None
            for k in range(16):
                m_mm = r("tensor", "matmul", ps[:],
                         B["w_out_sb"][:, k * D + mt * 128: k * D + (mt + 1) * 128],
                         B["uact2_sb"][:, k * TL:(k + 1) * TL],
                         deps=([m_gate[k], m_wout[k]] +
                               ([m_op_ev.get(mt - 2,
                                             prev.get("psop%d_free" % (mt % 2)))]
                                if k == 0 else [])),
                         start=(k == 0), stop=(k == 15), inc=(1 if k == 15 else 0))
            m_op_ev[mt] = r("scalar", "activation", B["mT_sb"][:, mt * TL:(mt + 1) * TL],
                            ps[:], AF.Copy, deps=[m_mm, prev.get("mT_free")], inc=1)
        m_mT = m_op_ev[7]
        prev["uact_free"] = m_mT
        m_h = {}
        m_ssq2 = None
        for tt in range(4):
            ps = B["psB_tr0"] if tt % 2 == 0 else B["psB_tr1"]
            m_mm = None
            for dg in range(8):
                m_mm = r("tensor", "matmul", ps[:, dg * 128:(dg + 1) * 128],
                         B["mT_sb"][:, dg * TL + tt * 128: dg * TL + (tt + 1) * 128],
                         B["ident_bf"][:],
                         deps=[m_op_ev[dg], prev.get(("htr", tt % 2))],
                         start=True, stop=True, inc=(1 if dg == 7 else 0))
            g = g0 + tt
            r("vector", "scalar_tensor_tensor", B["h_sb"][:, tt * D:(tt + 1) * D],
              B["x0b_sb"][:, tt * D:(tt + 1) * D], B["rstd0_s"][:, g:g + 1], ps[:],
              OP.mult, OP.add, deps=[m_mm, m_lx, prev.get("h_free")])
            m_hh = r("vector", "tensor_scalar", B["h_sb"][:, tt * D:(tt + 1) * D],
                     B["h_sb"][:, tt * D:(tt + 1) * D], B["b00_s"][:, g:g + 1], None,
                     OP.add, inc=1)
            prev[("htr", tt % 2)] = m_hh
            m_h[tt] = m_hh
            g = g0 + tt
            m_ssq2 = r("scalar", "activation", B["trash_bf"][:],
                       B["h_sb"][:, tt * D:(tt + 1) * D], AF.Square,
                       accum_out=B["ssq_s"][:, g:g + 1], deps=[m_hh], inc=1)
        prev["mT_free"] = m_h[3]
        ctx.tap("a2_h_c%d" % c, B["h_sb"][:], deps=[m_h[3]])
        m_sph = r("gpsimd", "dma_start", out=H["sp_h"][c],
                  in_=B["h_sb"][:].rearrange("p (tt d) -> p tt d", d=D),
                  deps=[m_h[3]], dma=True)
        prev["h_free"] = m_sph
        sl = slice(g0, g0 + 4)
        ctx.autodrain = True
        r("vector", "tensor_scalar", B["sm_a"][:, sl], B["ssq_s"][:, sl], 1.0 / D, None,
          OP.mult, deps=[m_ssq2])
        m_v = r("vector", "tensor_scalar", B["sm_a"][:, sl], B["sm_a"][:, sl], EPS, None,
                OP.add, inc=1)
        ctx.autodrain = False
        r("scalar", "activation", B["sm_a"][:, sl], B["sm_a"][:, sl], AF.Ln, deps=[m_v])
        m_r2 = r("scalar", "activation", B["rstd2_s"][:, sl], B["sm_a"][:, sl], AF.Exp,
                 scale=-0.5, inc=1)
        m_cast = None
        for tt in range(4):
            g = g0 + tt
            m_cast = r("scalar", "activation", B["h2bf_sb"][:, tt * D:(tt + 1) * D],
                       B["h_sb"][:, tt * D:(tt + 1) * D], AF.Copy,
                       scale=B["rstd2_s"][:, g:g + 1],
                       deps=[m_r2, prev.get("h2bf_free")], inc=(1 if tt == 3 else 0))
        m_n2ev = {}
        for dg in range(8):
            ps = B["psB_op0"] if dg % 2 == 0 else B["psB_op1"]
            m_mm = None
            for tt in (3, 2, 1, 0):
                m_mm = r("tensor", "matmul", ps[:, tt * 128:(tt + 1) * 128],
                         B["h2bf_sb"][:, tt * D + dg * 128: tt * D + (dg + 1) * 128],
                         B["ident_bf"][:],
                         deps=[m_cast, m_n2ev.get(dg - 2),
                               m_op_ev[6] if dg == 0 else None,
                               m_op_ev[7] if dg == 1 else None],
                         start=True, stop=True, inc=(1 if tt == 0 else 0))
            m_n2ev[dg] = r("scalar", "activation", B["n2Tst_sb"][:, dg * TL:(dg + 1) * TL],
                           ps[:], AF.Copy, deps=[m_mm, prev.get("n2Tst_free")], inc=1)
        prev["h2bf_free"] = m_n2ev[7]
        prev["psop0_free"] = m_n2ev[6]
        prev["psop1_free"] = m_n2ev[7]
        m_spn2 = r("scalar", "dma_start",
                   out=H["sp_n2T"][:, :, c * TL:(c + 1) * TL],
                   in_=B["n2Tst_sb"][:].rearrange("p (dg t) -> p dg t", t=TL),
                   deps=[m_n2ev[7]], dma=True)
        prev["n2Tst_free"] = m_spn2
        prev["x0_free"] = m_cast


def _phase_b1(ctx, H, B):
    r = ctx.run
    m_n2 = []
    for dg in range(8):
        m_n2.append(r("sync", "dma_start", out=B["n2Ta_sb"][:, dg * NT:(dg + 1) * NT],
                      in_=H["sp_n2T"][:, dg, :], dma=True))
    prev = {}
    for mt in range(32):
        sb = mt % 2
        w1s = B["w1s_sb"][:, sb * 1024:(sb + 1) * 1024]
        w2s = B["w2s_sb"][:, sb * 1024:(sb + 1) * 1024]
        m_l1 = r("gpsimd", "dma_start",
                 out=w1s.rearrange("p (kg m) -> p kg m", m=128),
                 in_=H["w1_T"][:, mt * 128:(mt + 1) * 128].rearrange(
                     "(kg p) m -> p kg m", p=128),
                 deps=[prev.get(("w1", sb))], dma=True)
        m_l2 = r("gpsimd", "dma_start",
                 out=w2s.rearrange("p (kg m) -> p kg m", m=128),
                 in_=H["w2_T"][:, mt * 128:(mt + 1) * 128].rearrange(
                     "(kg p) m -> p kg m", p=128),
                 deps=[prev.get(("w2", sb))], dma=True)
        m_f1 = m_f2 = None
        for kg in range(8):
            m_f1 = r("vector", "tensor_scalar", w1s[:, kg * 128:(kg + 1) * 128],
                     w1s[:, kg * 128:(kg + 1) * 128], B["n2w_sb"][:, kg:kg + 1], None,
                     OP.mult, deps=[m_l1], inc=(1 if kg == 7 else 0))
            m_f2 = r("vector", "tensor_scalar", w2s[:, kg * 128:(kg + 1) * 128],
                     w2s[:, kg * 128:(kg + 1) * 128], B["n2w_sb"][:, kg:kg + 1], None,
                     OP.mult, deps=[m_l2], inc=(1 if kg == 7 else 0))
        gb = mt % 2
        gst = B["gst_sb"][:, gb * NT:(gb + 1) * NT]
        m_gd = {}
        for tg in range(8):
            ps1 = B["psC_1"] if tg % 2 == 0 else B["psC_3"]
            ps2 = B["psC_2"] if tg % 2 == 0 else B["psC_4"]
            m_mm1 = m_mm2 = None
            for kg in range(8):
                m_mm1 = r("tensor", "matmul", ps1[:],
                          w1s[:, kg * 128:(kg + 1) * 128],
                          B["n2Ta_sb"][:, kg * NT + tg * TL: kg * NT + (tg + 1) * TL],
                          deps=([m_f1, m_n2[kg]] +
                                ([prev.get(("ps", tg % 2))] if kg == 0 else [])),
                          start=(kg == 0), stop=(kg == 7), inc=(1 if kg == 7 else 0))
            for kg in range(8):
                m_mm2 = r("tensor", "matmul", ps2[:],
                          w2s[:, kg * 128:(kg + 1) * 128],
                          B["n2Ta_sb"][:, kg * NT + tg * TL: kg * NT + (tg + 1) * TL],
                          deps=([m_f2] if kg == 0 else ()),
                          start=(kg == 0), stop=(kg == 7), inc=(1 if kg == 7 else 0))
            m_s = r("scalar", "activation", B["sl_sb"][:], ps1[:], AF.Silu,
                    deps=[m_mm1, prev.get("sl_free")], inc=1)
            m_w2 = r("scalar", "activation", B["w2e_sb"][:], ps2[:], AF.Copy,
                     deps=[m_mm2, prev.get("w2e_free")], inc=1)
            gp = tg % 2
            m_g = r("vector", "tensor_tensor",
                    B["gtmp_sb"][:, gp * TL:(gp + 1) * TL],
                    B["sl_sb"][:], B["w2e_sb"][:], OP.mult,
                    deps=[m_s, m_w2, prev.get(("gtmp", gp))], inc=1)
            m_gc = r("scalar", "activation", gst[:, tg * TL:(tg + 1) * TL],
                     B["gtmp_sb"][:, gp * TL:(gp + 1) * TL], AF.Copy,
                     deps=[m_g, prev.get(("gst", gb))], inc=1)
            prev[("gtmp", gp)] = m_gc
            prev["sl_free"] = m_g
            prev["w2e_free"] = m_g
            prev[("ps", tg % 2)] = m_g
            m_gd[tg] = m_gc
        prev[("w1", sb)] = m_gd[7]
        prev[("w2", sb)] = m_gd[7]
        m_sp = None
        for tg8 in range(8):
            for half in range(2):
                m_sp = r("gpsimd", "dma_start",
                         out=H["sp_g2"][2 * tg8 + half][:, mt, :],
                         in_=gst[:, tg8 * TL + half * TG: tg8 * TL + (half + 1) * TG],
                         deps=[m_gd[tg8]], dma=True)
        prev[("gst", gb)] = m_sp


def _phase_b2(ctx, H, B):
    r = ctx.run
    m_wr = r("gpsimd", "dma_start",
             out=B["wr_sb"][:].rearrange("p (kg n) -> p kg n", n=NE),
             in_=H["wr"].rearrange("(kg p) n -> p kg n", p=128), dma=True)
    r("gpsimd", "dma_start", out=B["b2_sb"][:], in_=H["b2e"][:], dma=True)
    m_w3 = []
    for kg in range(32):
        m_w3.append(r("gpsimd", "dma_start", out=B["w3_sb"][:, kg * D:(kg + 1) * D],
                      in_=H["w3_T"][kg * 128:(kg + 1) * 128, :], dma=True))
    prev = {}
    for tg in range(LIMIT_TG or NTG):
        c = tg // 2
        tt0 = 2 * (tg % 2)
        gld = B["gld_sb"][:].rearrange("p (kg t) -> p kg t", t=TG)
        m_lg = [r("sync", "dma_start", out=gld,
                  in_=H["sp_g2"][tg], deps=[prev.get("gld_free")], dma=True)]
        m_lx = r("sync", "dma_start",
                 out=B["x0c_sb"][:].rearrange("p (tt d) -> p tt d", d=D),
                 in_=H["xs"][tg * TG:(tg + 1) * TG, :].rearrange("(tt p) d -> p tt d", p=128),
                 deps=[prev.get("x0c_free")], dma=True)
        m_lh = r("sync", "dma_start",
                 out=B["hc_sb"][:].rearrange("p (tt d) -> p tt d", d=D),
                 in_=H["sp_h"][c][:, tt0:tt0 + 2, :],
                 deps=[prev.get("hc_free")], dma=True)
        if DEBUG_SCRATCH:
            r("gpsimd", "dma_start", out=H["dbg_gld"][tg], in_=B["gld_sb"][:],
              deps=m_lg, dma=True)
        m_pev = {}
        for rnd in range(2):
            m_mmr = {}
            for mti in range(4):
                mt = 4 * rnd + mti
                dst = B["psD_p"][:, 2 * mti * TG:(2 * mti + 1) * TG]
                m_mm = None
                for kg in range(32):
                    m_mm = r("tensor", "matmul", dst,
                             B["w3_sb"][:, kg * D + mt * 128: kg * D + (mt + 1) * 128],
                             gld[:, kg, :],
                             deps=(m_lg + [m_w3[kg]] +
                                   ([prev.get("psp_free"),
                                     m_pev.get(4 * (rnd - 1) + mti)] if kg == 0 else [])),
                             start=(kg == 0), stop=(kg == 31),
                             inc=(1 if kg == 31 else 0))
                m_mmr[mti] = m_mm
            for mti in range(4):
                mt = 4 * rnd + mti
                m_pev[mt] = r("scalar", "activation",
                              B["pT_sb"][:, mt * TG:(mt + 1) * TG],
                              B["psD_p"][:, 2 * mti * TG:(2 * mti + 1) * TG], AF.Copy,
                              deps=[m_mmr[mti], prev.get("pT_free")], inc=1)
        if DEBUG_SCRATCH:
            m_dbgp = r("gpsimd", "dma_start", out=H["dbg_pT"][tg], in_=B["pT_sb"][:],
                       deps=[m_pev[mt2] for mt2 in range(8)], dma=True)
            prev["pT_free"] = m_dbgp
        m_x1 = {}
        for tt in range(2):
            g3 = 2 * tg + tt
            m_mm = None
            for dg in range(8):
                m_mm = r("tensor", "matmul", B["psD_t"][:, dg * 128:(dg + 1) * 128],
                         B["pT_sb"][:, dg * TG + tt * 128: dg * TG + (tt + 1) * 128],
                         B["ident_bf"][:],
                         deps=[m_pev[dg], prev.get("pst")],
                         start=True, stop=True, inc=(1 if dg == 7 else 0))
            r("vector", "tensor_tensor", B["x0c_sb"][:, tt * D:(tt + 1) * D],
              B["x0c_sb"][:, tt * D:(tt + 1) * D], B["hc_sb"][:, tt * D:(tt + 1) * D],
              OP.add, deps=[m_lx, m_lh])
            m_tt = r("vector", "tensor_tensor", B["x1_sb"][:, tt * D:(tt + 1) * D],
                     B["x0c_sb"][:, tt * D:(tt + 1) * D], B["psD_t"][:], OP.add,
                     deps=[m_mm, prev.get("x1_free")], inc=1)
            m_xx = r("scalar", "activation", B["trash_bf"][:],
                     B["x1_sb"][:, tt * D:(tt + 1) * D], AF.Copy,
                     accum_out=B["sum3_s"][:, g3:g3 + 1], deps=[m_tt], inc=1)
            prev["pst"] = m_xx
            ctx.tap("b2_x0h_t%d_%d" % (tg, tt), B["x0c_sb"][:, tt * D:(tt + 1) * D],
                    deps=[m_xx])
            m_x1[tt] = m_xx
            m_sq = r("scalar", "activation", B["trash_bf"][:],
                     B["x1_sb"][:, tt * D:(tt + 1) * D],
                     AF.Square, accum_out=B["ssq3_s"][:, g3:g3 + 1],
                     deps=[m_xx],
                     inc=(1 if tt == 1 else 0))
        prev["x0c_free"] = m_x1[1]
        prev["hc_free"] = m_x1[1]
        prev["pT_free"] = m_x1[1]
        ctx.tap("b2_x1_t%d" % tg, B["x1_sb"][:], deps=[m_x1[1]])
        m_spx = r("gpsimd", "dma_start", out=H["sp_x1"][tg],
                  in_=B["x1_sb"][:].rearrange("p (tt d) -> p tt d", d=D),
                  deps=[m_x1[1]], dma=True)
        ctx.drain("vector")
        ctx.autodrain = True
        sl = slice(2 * tg, 2 * tg + 2)
        r("vector", "tensor_scalar", B["sm_b"][:, sl], B["sum3_s"][:, sl], 1.0 / D, None,
          OP.mult, deps=[m_x1[1], m_sq])
        r("vector", "tensor_scalar", B["sm_a"][:, sl], B["ssq3_s"][:, sl], 1.0 / D, None,
          OP.mult)
        r("vector", "scalar_tensor_tensor", B["b03_s"][:, sl], B["sm_b"][:, sl],
          -1.0, B["sm_b"][:, sl], OP.mult, OP.mult)
        r("vector", "tensor_tensor", B["sm_a"][:, sl], B["sm_a"][:, sl], B["b03_s"][:, sl],
          OP.add)
        m_v = r("vector", "tensor_scalar", B["sm_a"][:, sl], B["sm_a"][:, sl], EPS, None,
                OP.add, inc=1)
        r("scalar", "activation", B["sm_a"][:, sl], B["sm_a"][:, sl], AF.Ln, deps=[m_v])
        m_r3 = r("scalar", "activation", B["rstd3_s"][:, sl], B["sm_a"][:, sl], AF.Exp,
                 scale=-0.5, inc=1)
        m_b3 = r("vector", "scalar_tensor_tensor", B["b03_s"][:, sl], B["sm_b"][:, sl],
                 -1.0, B["rstd3_s"][:, sl], OP.mult, OP.mult, deps=[m_r3], inc=1)
        ctx.autodrain = False
        m_cast = None
        for tt in range(2):
            g3 = 2 * tg + tt
            m_cast = r("scalar", "activation", B["u3bf_sb"][:, tt * D:(tt + 1) * D],
                       B["x1_sb"][:, tt * D:(tt + 1) * D], AF.Identity,
                       scale=B["rstd3_s"][:, g3:g3 + 1], bias=B["b03_s"][:, g3:g3 + 1],
                       deps=[m_b3, prev.get("u3bf_free")], inc=(1 if tt == 1 else 0))
        prev["x1_free"] = m_cast
        for half in range(2):
            m_mm = None
            for dgi in range(4):
                dg = half * 4 + dgi
                for tt in range(2):
                    m_mm = r("tensor", "matmul",
                             B["psD_u"][:, dgi * TG + tt * 128: dgi * TG + (tt + 1) * 128],
                             B["u3bf_sb"][:, tt * D + dg * 128: tt * D + (dg + 1) * 128],
                             B["ident_bf"][:],
                             deps=[m_cast, prev.get("psu")],
                             start=True, stop=True,
                             inc=(1 if (dgi == 3 and tt == 1) else 0))
            m_ue = None
            for dgi in range(4):
                dg = half * 4 + dgi
                m_ue = r("scalar", "activation",
                         B["u3T_sb"][:, dg * NT + tg * TG: dg * NT + (tg + 1) * TG],
                         B["psD_u"][:, dgi * TG:(dgi + 1) * TG], AF.Copy,
                         deps=[m_mm], inc=(1 if dgi == 3 else 0))
            prev["psu"] = m_ue
        prev["u3bf_free"] = prev["psu"]
        m_u3T = prev["psu"]
        m_lgmm = None
        for tt in range(2):
            for kg in range(8):
                m_lgmm = r("tensor", "matmul", B["psD_p"][:, tt * NE: tt * NE + NE],
                           B["u3T_sb"][:, kg * NT + tg * TG + tt * 128:
                                       kg * NT + tg * TG + (tt + 1) * 128],
                           B["wr_sb"][:, kg * NE:(kg + 1) * NE],
                           deps=([m_u3T, m_wr] if (tt == 0 and kg == 0) else ()),
                           start=(kg == 0), stop=(kg == 7),
                           inc=(1 if kg == 7 else 0))
        m_lgv = {}
        for tt in range(2):
            m_lgv[tt] = r("vector", "tensor_tensor", B["lg_sb"][:, tt * NE:(tt + 1) * NE],
                          B["psD_p"][:, tt * NE:tt * NE + NE], B["brb_sb"][:], OP.add,
                          deps=[m_lgmm], inc=1)
        ctx.autodrain = True
        for tt in range(2):
            g3 = 2 * tg + tt
            lg = B["lg_sb"][:, tt * NE:(tt + 1) * NE]
            r("vector", "max", B["m8_sb"][:, tt * 8:(tt + 1) * 8], lg, deps=[m_lgv[tt]])
            ctx.drain("vector")
            r("vector", "max_index", B["i8_sb"][:, tt * 8:(tt + 1) * 8],
              B["m8_sb"][:, tt * 8:(tt + 1) * 8], lg)
            ctx.drain("vector")
            r("vector", "tensor_scalar", B["idxf_sb"][:, tt * 2:(tt + 1) * 2],
              B["i8_sb"][:, tt * 8:tt * 8 + 2], 0.0, None, OP.add)
            m_vv = r("vector", "tensor_scalar", B["nm1_sb"][:, tt:tt + 1],
                     B["m8_sb"][:, tt * 8:tt * 8 + 1], -1.0, None, OP.mult, inc=1)
            m_e = r("scalar", "activation", B["t21_sb"][:, tt:tt + 1],
                    B["m8_sb"][:, tt * 8 + 1:tt * 8 + 2], AF.Exp,
                    bias=B["nm1_sb"][:, tt:tt + 1], deps=[m_vv], inc=1)
            r("vector", "tensor_scalar", B["den_sb"][:, tt:tt + 1],
              B["t21_sb"][:, tt:tt + 1], 1.0, None, OP.add, deps=[m_e])
            r("vector", "reciprocal", B["w1r_sb"][:, tt:tt + 1], B["den_sb"][:, tt:tt + 1])
            r("vector", "tensor_tensor", B["w2r_sb"][:, tt:tt + 1],
              B["t21_sb"][:, tt:tt + 1], B["w1r_sb"][:, tt:tt + 1], OP.mult)
            r("vector", "tensor_scalar", B["mk1_sb"][:], B["iota16_sb"][:],
              B["idxf_sb"][:, tt * 2:tt * 2 + 1], B["w1r_sb"][:, tt:tt + 1],
              OP.is_equal, OP.mult)
            r("vector", "tensor_scalar", B["mk2_sb"][:], B["iota16_sb"][:],
              B["idxf_sb"][:, tt * 2 + 1:tt * 2 + 2], B["w2r_sb"][:, tt:tt + 1],
              OP.is_equal, OP.mult)
            r("vector", "tensor_tensor", B["wfull_sb"][:, g3 * NE:(g3 + 1) * NE],
              B["mk1_sb"][:], B["mk2_sb"][:], OP.add)
            m_wf = r("vector", "tensor_scalar", B["wfbf_sb"][:],
                     B["wfull_sb"][:, g3 * NE:(g3 + 1) * NE], 1.0, None, OP.mult, inc=1)
            m_tr = r("tensor", "matmul", B["psD_u"][0:16, 0:128],
                     B["wfbf_sb"][:], B["ident_bf"][:],
                     deps=[m_wf, prev.get("wrt_ev"), prev.get(("psu", 0))],
                     start=True, stop=True, inc=1)
            m_wrt = r("scalar", "activation", B["wroute_sb"][:, g3 * 128:(g3 + 1) * 128],
                      B["psD_u"][0:16, 0:128], AF.Copy, deps=[m_tr], inc=1)
            prev["wrt_ev"] = m_wrt
            prev[("psu", 0)] = m_wrt
        ctx.autodrain = False
        if tg == (LIMIT_TG or NTG) - 1:
            ctx.tap("b2_u3T", B["u3T_sb"][:], deps=[prev["psu"], prev["wrt_ev"]])
        ctx.tap("b2_wfull_t%d" % tg, B["wfull_sb"][:], deps=[prev["wrt_ev"]])
        prev["gld_free"] = m_pev[7]
        prev["psp_free"] = m_lgv[1]


def _phase_c(ctx, H, B):
    r = ctx.run
    prev = {}

    def load_e(e, which):
        msd = []
        if which in ("W1", "both"):
            for half in range(2):
                dst = B["W1X_sb"] if half == 0 else B["W1Y_sb"]
                msd.append(r("gpsimd", "dma_start",
                             out=dst[:].rearrange("p (kk m) -> p kk m", m=HID),
                             in_=H["W1e"][e][half * 512:(half + 1) * 512, :].rearrange(
                                 "(kk p) m -> p kk m", p=128),
                             deps=[prev.get("W1_free")], dma=True))
        if which in ("W2", "both"):
            for half in range(2):
                dst = B["W2X_sb"] if half == 0 else B["W2Y_sb"]
                msd.append(r("gpsimd", "dma_start",
                             out=dst[:].rearrange("p (kk m) -> p kk m", m=D),
                             in_=H["W2e"][e][half * 512:(half + 1) * 512, :].rearrange(
                                 "(kk p) m -> p kk m", p=128),
                             deps=[prev.get("W2_free")], dma=True))
        return msd

    def load_vec_e(e):
        m1 = r("gpsimd", "dma_start", out=B["g1e_sb"][:], in_=H["g1b"][e],
               deps=[prev.get("g1_free")], dma=True)
        m2 = r("gpsimd", "dma_start", out=B["be1e_sb"][:], in_=H["be1b"][e],
               deps=[prev.get("be1_free")], dma=True)
        return [m1, m2]

    def part_a(e, tg, mW, mb1):
        out = {}
        for tt in range(2):
            ps = B["psE_w1a"] if tt == 0 else B["psE_w1b"]
            for kg in range(8):
                wbuf = B["W1X_sb"] if kg < 4 else B["W1Y_sb"]
                kk = kg % 4
                for nh in range(2):
                    r("tensor", "matmul", ps[:, nh * 512:(nh + 1) * 512],
                      B["u3T_sb"][:, kg * NT + tg * TG + tt * 128:
                                  kg * NT + tg * TG + (tt + 1) * 128],
                      wbuf[:, kk * HID + nh * 512: kk * HID + (nh + 1) * 512],
                      deps=(list(mW) + [prev.get(("psw1", tt)), mb1]
                            if kg == 0 and nh == 0 else ()),
                      start=(kg == 0), stop=False)
            m_mm = None
            for nh in range(2):
                m_mm = r("tensor", "matmul", ps[:, nh * 512:(nh + 1) * 512],
                         B["ones_sb"][:],
                         B["b1e_sb"][0:1, nh * 512:(nh + 1) * 512],
                         start=False, stop=True, inc=(1 if nh == 1 else 0))
            out[tt] = m_mm
        return out

    def part_evac(e, tg, mA):
        for tt in range(2):
            ps = B["psE_w1a"] if tt == 0 else B["psE_w1b"]
            hch = B["hch_sb"][:, tt * HID:(tt + 1) * HID]
            m_ev = r("scalar", "activation", hch, ps[:], AF.Copy,
                     accum_out=B["sums_sb"][:, tt:tt + 1],
                     deps=[mA[tt], prev.get(("hch", tt))], inc=1, war=True)
            prev[("psw1", tt)] = m_ev
        for tt in range(2):
            hch = B["hch_sb"][:, tt * HID:(tt + 1) * HID]
            r("scalar", "activation", B["trash_bf"][:], hch, AF.Square,
              accum_out=B["ssqE_sb"][:, tt:tt + 1], inc=0)

    def part_b(e, tg, mW, mV):
        s2 = slice(0, 2)
        r("scalar", "activation", B["muE_sb"][:, s2], B["sums_sb"][:, s2],
          AF.Copy, scale=1.0 / HID)
        m_e2 = r("scalar", "activation", B["sm_a"][:, s2], B["ssqE_sb"][:, s2],
                 AF.Copy, scale=1.0 / HID, inc=1)
        r("vector", "tensor_tensor", B["sm_b"][:, s2], B["muE_sb"][:, s2],
          B["muE_sb"][:, s2], OP.mult, deps=[m_e2])
        m_vv = r("vector", "scalar_tensor_tensor", B["sm_a"][:, s2],
                 B["sm_a"][:, s2], EPS, B["sm_b"][:, s2],
                 OP.add, OP.subtract, inc=1)
        r("scalar", "activation", B["sm_a"][:, s2], B["sm_a"][:, s2], AF.Ln,
          deps=[m_vv])
        m_rs = r("scalar", "activation", B["rstdE_sb"][:, s2], B["sm_a"][:, s2],
                 AF.Exp, scale=-0.5, inc=1)
        r("vector", "tensor_tensor", B["sm_b"][:, s2], B["muE_sb"][:, s2],
          B["rstdE_sb"][:, s2], OP.mult, deps=[m_rs])
        m_be = r("vector", "tensor_scalar", B["bE_sb"][:, s2], B["sm_b"][:, s2],
                 -1.0, None, OP.mult, inc=1)
        for tt in range(2):
            hch = B["hch_sb"][:, tt * HID:(tt + 1) * HID]
            ss = slice(tt, tt + 1)
            m_af = r("scalar", "activation", hch, hch, AF.Identity,
                     scale=B["rstdE_sb"][:, ss], bias=B["bE_sb"][:, ss],
                     deps=[m_be], inc=1)
            r("vector", "tensor_tensor", hch, hch, B["g1e_sb"][:], OP.mult,
              deps=[mV[0], m_af])
            prev[("t5", tt)] = r("vector", "tensor_tensor", hch, hch,
                                 B["be1e_sb"][:], OP.add, deps=[mV[1]], inc=1)
        m_gelu = {}
        for tt in range(2):
            hch = B["hch_sb"][:, tt * HID:(tt + 1) * HID]
            gl = B["gl_sb"][:, tt * HID:(tt + 1) * HID]
            m_gl = r("scalar", "activation", gl, hch, AF.Gelu,
                     deps=[prev[("t5", tt)]], inc=1)
            prev[("hch", tt)] = m_gl
            m_gelu[tt] = m_gl
        for tt in range(2):
            gl = B["gl_sb"][:, tt * HID:(tt + 1) * HID]
            g3 = 2 * tg + tt
            dg_t = B["dg_sb"][:, tt * 128:(tt + 1) * 128]
            m_dg = r("vector", "tensor_scalar", dg_t, B["ident_bf"][:],
                     B["wfull_sb"][:, g3 * NE + e: g3 * NE + e + 1], None, OP.mult,
                     deps=[prev.get(("dg", tt))], inc=1)
            for half in range(2):
                m_mm2 = None
                for hbi in range(4):
                    hb = half * 4 + hbi
                    m_mm2 = r("tensor", "matmul",
                              B["psE_tr"][:, hbi * 128:(hbi + 1) * 128],
                              gl[:, hb * 128:(hb + 1) * 128], dg_t,
                              deps=[m_gelu[tt], m_dg, prev.get("pstr_free")],
                              start=True, stop=True, inc=(1 if hbi == 3 else 0))
                last = (tt == 1 and half == 1)
                m_tev = r("scalar", "activation",
                          B["glT_sb"][:].rearrange(
                              "p (hb t) -> p hb t", t=TG)[:, half * 4:(half + 1) * 4,
                                                          tt * 128:(tt + 1) * 128],
                          B["psE_tr"][:].rearrange("p (hb t) -> p hb t", t=128),
                          AF.Copy, deps=[m_mm2, prev.get("glT_free")], inc=1,
                          war=(not last))
                prev["pstr_free"] = m_tev
                prev[("glT", tt, half)] = m_tev
            prev[("dg", tt)] = prev[("glT", tt, 1)]
        m_glT = prev[("glT", 1, 1)]
        m_mm3 = None
        for rnd in range(4):
            bank = rnd % 2
            for mti in range(2):
                mt = rnd * 2 + mti
                dst = B["psE_o"][:, (bank * 2 + mti) * TG:(bank * 2 + mti + 1) * TG]
                hd = ([m_glT, prev.get(("pso", bank))] + (list(mW) if rnd == 0 else [])
                      if mti == 0 else ())
                if e == 0:
                    m_mm3 = r("tensor", "matmul", dst,
                              B["b2_sb"][:, mt * 128:(mt + 1) * 128],
                              B["wroute_sb"][:, tg * TG:(tg + 1) * TG],
                              deps=hd, start=True, stop=False, inc=0)
                for kg in range(8):
                    wbuf = B["W2X_sb"] if kg < 4 else B["W2Y_sb"]
                    kk = kg % 4
                    m_mm3 = r("tensor", "matmul", dst,
                              wbuf[:, kk * D + mt * 128: kk * D + (mt + 1) * 128],
                              B["glT_sb"][:, kg * TG:(kg + 1) * TG],
                              deps=(hd if (e > 0 and kg == 0) else ()),
                              start=(kg == 0 and e != 0), stop=(kg == 7),
                              inc=(1 if (mti == 1 and kg == 7) else 0))
            m_fa = None
            for mti in range(2):
                mt = rnd * 2 + mti
                fsl = B["f_sb"][:, mt * NT + tg * TG: mt * NT + (tg + 1) * TG]
                psl = B["psE_o"][:, (bank * 2 + mti) * TG:(bank * 2 + mti + 1) * TG]
                if e == 0:
                    m_fa = r("vector", "tensor_scalar", fsl, psl, 0.0, None,
                             OP.add, deps=[m_mm3], inc=(1 if mti == 1 else 0),
                             war=True)
                else:
                    m_fa = r("vector", "tensor_tensor", fsl, psl, fsl, OP.add,
                             deps=[m_mm3], inc=(1 if mti == 1 else 0), war=True)
            prev[("pso", bank)] = m_fa
        prev["glT_free"] = m_mm3
        return m_gelu, m_mm3

    ctx.tap("c_u3T", B["u3T_sb"][:])
    for nm in ("sum3_s", "ssq3_s", "rstd3_s", "b03_s"):
        ctx.tap("c_" + nm, B[nm][:])

    eW = {0: load_e(0, "both")}
    eV = {0: load_vec_e(0)}
    eB1 = {0: r("gpsimd", "dma_start", out=B["b1e_sb"][0:1, :],
                in_=H["b1b"][0][0:1, :], dma=True)}
    ntg = LIMIT_TG or NTG
    ne = LIMIT_E or NE
    pend = None
    for e in range(ne):
        for tg in range(ntg):
            mA = part_a(e, tg, eW[e], eB1[e])
            if tg == ntg - 1:
                prev["W1_free"] = mA[1]
                prev["b1_free"] = mA[1]
                if e + 1 < ne:
                    eW[e + 1] = load_e(e + 1, "W1")
                    eB1[e + 1] = r("gpsimd", "dma_start", out=B["b1e_sb"][0:1, :],
                                   in_=H["b1b"][e + 1][0:1, :],
                                   deps=[prev["b1_free"]], dma=True)
            if pend is not None:
                pe_, ptg = pend
                m_g, m_w2 = part_b(pe_, ptg, eW[pe_], eV[pe_])
                if ptg == ntg - 1:
                    prev["W2_free"] = m_w2
                    prev["g1_free"] = prev[("t5", 1)]
                    prev["be1_free"] = prev[("t5", 1)]
                    if pe_ + 1 < ne:
                        eW[pe_ + 1] = eW[pe_ + 1] + load_e(pe_ + 1, "W2")
                        eV[pe_ + 1] = load_vec_e(pe_ + 1)
            part_evac(e, tg, mA)
            pend = (e, tg)
    part_b(pend[0], pend[1], eW[pend[0]], eV[pend[0]])


def _phase_fin(ctx, H, B, out_h):
    r = ctx.run
    prev = {}
    for tg in range(LIMIT_TG or NTG):
        m_lx = r("sync", "dma_start",
                 out=B["x1f_sb"][:].rearrange("p (tt d) -> p tt d", d=D),
                 in_=H["sp_x1"][tg], deps=[prev.get("x1f_free")], dma=True)
        m_x2 = {}
        for tt in range(2):
            m_mm = None
            for og in range(8):
                m_mm = r("tensor", "matmul", B["psE_f"][:, og * 128:(og + 1) * 128],
                         B["f_sb"][:, og * NT + tg * TG + tt * 128:
                                   og * NT + tg * TG + (tt + 1) * 128],
                         B["ident_bf"][:],
                         deps=[prev.get("psf")],
                         start=True, stop=True, inc=(1 if og == 7 else 0))
            m_x2[tt] = r("vector", "tensor_tensor", B["x2_sb"][:, tt * D:(tt + 1) * D],
                         B["x1f_sb"][:, tt * D:(tt + 1) * D], B["psE_f"][:], OP.add,
                         deps=[m_mm, m_lx, prev.get("x2_free")], inc=1)
            prev["psf"] = m_x2[tt]
        m_st = r("gpsimd", "dma_start",
                 out=out_h[tg * TG:(tg + 1) * TG, :].rearrange("(tt p) d -> p tt d", p=128),
                 in_=B["x2_sb"][:].rearrange("p (tt d) -> p tt d", d=D),
                 deps=[m_x2[1]], dma=True)
        prev["x2_free"] = m_st
        prev["x1f_free"] = m_x2[1]


_TAPDEFS = {}
for _c in range(NCHUNK):
    _TAPDEFS["a1_x0bf_c%d" % _c] = ((128, 4 * D), BF16)
    _TAPDEFS["a1_n1T_c%d" % _c] = ((128, 8 * TL), BF16)
    _TAPDEFS["a1_sum_c%d" % _c] = ((128, 32), F32)
    _TAPDEFS["a1_mu0_c%d" % _c] = ((128, 32), F32)
    _TAPDEFS["a1_rstd0_c%d" % _c] = ((128, 32), F32)
    _TAPDEFS["a1_var_c%d" % _c] = ((128, 32), F32)
    _TAPDEFS["a1_ssq_c%d" % _c] = ((128, 32), F32)
    _TAPDEFS["a1_rs01_c%d" % _c] = ((128, 32), F32)
    _TAPDEFS["a1_uact_c%d" % _c] = ((128, 16 * TL), BF16)
    _TAPDEFS["a1_z_c%d" % _c] = ((128, 16 * TL), BF16)
    _TAPDEFS["a1_B_c%d" % _c] = ((16, TL), F32)
    _TAPDEFS["a1_dtr_c%d" % _c] = ((64, TL), BF16)
    _TAPDEFS["a2_gated_c%d" % _c] = ((128, 16 * TL), BF16)
    _TAPDEFS["a2_h_c%d" % _c] = ((128, 4 * D), F32)
_TAPDEFS["b2_u3T"] = ((128, 8 * NT), BF16)
_TAPDEFS["dbg_pdt_t0"] = ((128, D), BF16)
_TAPDEFS["dbg_pdt_t1"] = ((128, D), BF16)
for _n in ("sum3_s", "ssq3_s", "rstd3_s", "b03_s"):
    _TAPDEFS["c_" + _n] = ((128, 32), F32)
_TAPDEFS["c_u3T"] = ((128, 8 * NT), BF16)
for _t in range(NTG):
    _TAPDEFS["b2_pT_t%d" % _t] = ((128, 8 * TG), BF16)
    _TAPDEFS["b2_gld_t%d" % _t] = ((128, 32 * TG), BF16)
    for _u in range(2):
        _TAPDEFS["b2_x0h_t%d_%d" % (_t, _u)] = ((128, D), F32)
    _TAPDEFS["b2_x1_t%d" % _t] = ((128, 2 * D), F32)
    _TAPDEFS["b2_wfull_t%d" % _t] = ((128, 32 * NE), F32)


def kernel(**inputs):
    global _last_results
    shared = _prep_shared(inputs)
    x = np.asarray(inputs["L0_attn_x"], dtype=np.float32)
    nc = build()
    in_maps = []
    for c in range(NCORE):
        m = dict(shared)
        m["xs"] = _asc(x[2 * c:2 * c + 2].reshape(NT, D))
        in_maps.append(m)
    res = run_bass_kernel_spmd(nc, in_maps, core_ids=list(range(NCORE)))
    _last_results = res
    out = np.concatenate([np.asarray(res.results[c]["out"]).reshape(2, L, D)
                          for c in range(NCORE)], axis=0)
    return out


# version 05 (final)
# speedup vs baseline: 1.4623x; 1.0126x over previous
import numpy as np
from contextlib import ExitStack
import concourse.bass as bass
import concourse.mybir as mybir
from concourse.bass_utils import run_bass_kernel_spmd
from concourse.alu_op_type import AluOpType as OP

F32 = mybir.dt.float32
BF16 = mybir.dt.bfloat16
U32 = mybir.dt.uint32
AF = mybir.ActivationFunctionType
AX = mybir.AxisListType

NCORE = 8
L = 2048
D = 1024
DI = 2048
DS = 16
DTR = 64
DFF = 4096
NE = 16
HID = 1024
NT = 4096
TL = 512
NCHUNK = NT // TL
TG = 256
NTG = NT // TG
EPS = 1e-5

ENGINES = ("sync", "tensor", "scalar", "vector", "gpsimd")
PHASES = ("A1", "A2", "B1", "B2", "C", "FIN")
TAPS = ()
LIMIT_CHUNKS = None
LIMIT_TG = None
DEBUG_SCRATCH = False
LIMIT_E = None
_last_results = None


NDQ = 4


class _Eng:
    __slots__ = ("name", "obj", "sem_c", "dq", "cnt_c", "dq_cnt", "dq_idx", "seen")

    def __init__(self, name, obj, sem_c, dq):
        self.name, self.obj = name, obj
        self.sem_c, self.dq = sem_c, dq
        self.reset()

    def reset(self):
        self.cnt_c = 0
        self.dq_cnt = [0] * NDQ
        self.dq_idx = 0
        self.seen = {}


class Ctx:
    def __init__(self, nc, sems):
        self.nc = nc
        self.engs = {n: _Eng(n, getattr(nc, n), sems[n + "_c"],
                             [sems["%s_d%d" % (n, i)] for i in range(NDQ)])
                     for n in ENGINES}
        self.cur = None
        self.taps = {}
        self.autodrain = False

    def begin(self, ename):
        self.cur = ename
        for e in self.engs.values():
            e.reset()

    def dep(self, ename, m, for_dma=False):
        if m is None:
            return
        e = self.engs[ename]
        dn, dv, kind = m
        if kind == "c" and dn == ename and not for_dma:
            return
        key = (dn, kind)
        if e.seen.get(key, 0) >= dv:
            return
        e.seen[key] = dv
        if self.cur == ename:
            if kind == "d":
                qn, slot = dn
                e.obj.wait_ge(self.engs[qn].dq[slot], dv)
            else:
                e.obj.wait_ge(self.engs[dn].sem_c, dv)

    def run(self, ename, meth, *args, deps=(), inc=0, dma=False, war=False, **kw):
        for m in deps:
            self.dep(ename, m, for_dma=dma)
        e = self.engs[ename]
        if self.autodrain and ename == "vector" and not dma and self.cur == ename:
            e.obj.drain()
        if dma:
            slot = e.dq_idx % NDQ
            e.dq_idx += 1
            if e.dq_cnt[slot]:
                self.dep(ename, ((ename, slot), e.dq_cnt[slot], "d"))
            if self.cur == ename:
                ins = getattr(e.obj, meth)(*args, **kw)
                ins.then_inc(e.dq[slot], 16)
            e.dq_cnt[slot] += 16
            return ((ename, slot), e.dq_cnt[slot], "d")
        if self.cur == ename:
            ins = getattr(e.obj, meth)(*args, **kw)
            if inc:
                if (not war and meth != "drain"
                        and ename in ("scalar", "vector", "gpsimd")):
                    e.obj.drain().then_inc(e.sem_c, inc)
                else:
                    ins.then_inc(e.sem_c, inc)
        e.cnt_c += inc
        if inc:
            return (ename, e.cnt_c, "c")
        return None

    def drain(self, ename):
        if self.cur == ename:
            self.engs[ename].obj.drain()

    def barrier(self):
        ms = {}
        for n in ENGINES:
            e = self.engs[n]
            for slot in range(NDQ):
                if e.dq_cnt[slot]:
                    self.dep(n, ((n, slot), e.dq_cnt[slot], "d"))
            ms[n] = self.run(n, "drain", inc=1)
        for n in ENGINES:
            for o in ENGINES:
                if o != n:
                    self.dep(n, ms[o])
        return ms

    def tap(self, name, src_ap, deps=()):
        if name in self.taps:
            return self.run("gpsimd", "dma_start", out=self.taps[name], in_=src_ap,
                            deps=deps, dma=True)
        return None


def _asc(a):
    return np.ascontiguousarray(a, dtype=np.float32)


def _prep_shared(kw):
    d = {}
    d["w_in_T"] = _asc(kw["L0_attn_in_proj_w"].T)
    xp = np.zeros((DI, 128), dtype=np.float32)
    xpw = np.asarray(kw["L0_attn_x_proj_w"], dtype=np.float32)
    xp[:, 0:16] = xpw[DTR:DTR + DS].T
    xp[:, 32:48] = xpw[DTR + DS:DTR + 2 * DS].T
    xp[:, 64:128] = xpw[0:DTR].T
    d["w_xp_pad"] = _asc(xp)
    d["w_dt_T"] = _asc(kw["L0_attn_dt_proj_w"].T)
    d["w_out_T"] = _asc(kw["L0_attn_out_proj_w"].T)
    d["w1_T"] = _asc(kw["L0_attn_w1"].T)
    d["w2_T"] = _asc(kw["L0_attn_w2"].T)
    d["w3_T"] = _asc(kw["L0_attn_w3"].T)
    d["W1e"] = _asc(kw["L0_ffn_W1"])
    d["W2e"] = _asc(kw["L0_ffn_W2"])
    d["wr"] = _asc(kw["L0_ffn_Wr"])
    d["b2e"] = _asc(kw["L0_ffn_b2"])
    for nm, src in (("g1b", "L0_ffn_g1"), ("be1b", "L0_ffn_be1"), ("b1b", "L0_ffn_b1")):
        v = np.asarray(kw[src], dtype=np.float32)
        d[nm] = _asc(np.broadcast_to(v[:, None, :], (NE, 128, HID)))
    d["A_log_r"] = _asc(np.asarray(kw["L0_attn_A_log"]).reshape(16, 128, DS)
                        .transpose(1, 0, 2))
    d["convw_r"] = _asc(np.asarray(kw["L0_attn_conv_w"]).reshape(DI, 4)
                        .reshape(16, 128, 4).transpose(1, 0, 2))
    d["convb_r"] = _asc(np.asarray(kw["L0_attn_conv_b"]).reshape(16, 128).T)
    d["dtb_r"] = _asc(np.asarray(kw["L0_attn_dt_proj_b"]).reshape(16, 128).T)
    d["D_r"] = _asc(np.asarray(kw["L0_attn_D"]).reshape(16, 128).T)
    d["n1w_pp"] = _asc(np.asarray(kw["L0_attn_norm1_w"]).reshape(8, 128).T)
    d["n2w_pp"] = _asc(np.asarray(kw["L0_attn_norm2_w"]).reshape(8, 128).T)
    d["brb"] = _asc(np.broadcast_to(np.asarray(kw["L0_ffn_br"])[None, :], (128, NE)))
    d["iota16"] = _asc(np.broadcast_to(np.arange(NE, dtype=np.float32)[None, :], (128, NE)))
    d["ident"] = _asc(np.eye(128, dtype=np.float32))
    oh = np.zeros((DS, DS * 128), dtype=np.float32)
    for n in range(DS):
        oh[n, n * 128:(n + 1) * 128] = 1.0
    d["onehot"] = _asc(oh)
    return d


def _alloc(nc, es0):
    B = {}
    E0 = es0.enter_context

    def mk(scope, name, shape, dt):
        B[name] = scope(nc.sbuf_tensor(name, list(shape), dt))

    def mkp(scope, name, shape, dt=F32):
        B[name] = scope(nc.psum_tensor(name, list(shape), dt))

    mk(E0, "ident_bf", (128, 128), BF16)
    mk(E0, "ones_sb", (1, 128), BF16)
    mk(E0, "eps_sb", (128, 1), F32)
    mk(E0, "iota16_sb", (128, NE), F32)
    mk(E0, "brb_sb", (128, NE), F32)
    mk(E0, "n1w_sb", (128, 8), F32)
    mk(E0, "n2w_sb", (128, 8), F32)
    mk(E0, "A_sb", (128, 16 * DS), F32)
    mk(E0, "convw_sb", (128, 16 * 4), F32)
    mk(E0, "convb_sb", (128, 16), F32)
    mk(E0, "dtb_sb", (128, 16), F32)
    mk(E0, "D_sb", (128, 16), F32)

    for nm in ("sum_s", "ssq_s", "mu0_s", "rstd0_s", "b00_s", "rs01_s", "b01_s",
               "rstd2_s", "sum3_s", "ssq3_s", "rstd3_s", "b03_s"):
        mk(E0, nm, (128, 32), F32)
    mk(E0, "sm_a", (128, 32), F32)
    mk(E0, "sm_b", (128, 32), F32)
    mk(E0, "trash_bf", (128, 1024), BF16)
    mk(E0, "state_sb", (128, 16 * DS), F32)
    mk(E0, "halo_sb", (128, 16 * 3), BF16)

    sA = ExitStack()
    EA = sA.enter_context
    sA1 = ExitStack()
    E1 = sA1.enter_context
    mk(E1, "w_in_sb", (128, 8 * DFF), BF16)
    mk(E1, "w_xp_sb", (128, 16 * 128), BF16)
    mk(E1, "x0a_sb", (128, 2 * 4 * D), F32)
    mk(E1, "x0bf_sb", (128, 4 * D), BF16)
    mk(E1, "n1T_sb", (128, 8 * TL), BF16)
    mk(E1, "ustg_sb", (128, 2 * 515), BF16)
    mk(E1, "cacc_sb", (128, 2 * TL), BF16)
    mk(E1, "uact_sb", (128, 16 * TL), BF16)
    mk(E1, "zs_sb", (128, 16 * TL), BF16)
    mk(E1, "Bst_sb", (16, TL), F32)
    mk(E1, "Cst_sb", (16, TL), F32)
    mk(E1, "dtrst_sb", (64, TL), BF16)
    mkp(E1, "psA_tr0", (128, TL))
    mkp(E1, "psA_tr1", (128, TL))
    mkp(E1, "psA_mm0", (128, TL))
    mkp(E1, "psA_mm1", (128, TL))
    mkp(E1, "psA_xp", (128, TL))
    sA1.close()
    sA2 = ExitStack()
    E2 = sA2.enter_context
    mk(E2, "onehot_bf", (16, DS * 128), BF16)
    mk(E2, "w_dt_sb", (64, DI), BF16)
    mk(E2, "w_out_sb", (128, 16 * D), BF16)
    mk(E2, "x0b_sb", (128, 4 * D), F32)
    mk(E2, "uact2_sb", (128, 16 * TL), BF16)
    mk(E2, "zs2_sb", (128, 16 * TL), BF16)
    mk(E2, "dtr2_sb", (64, TL), BF16)
    mk(E2, "Bbf_sb", (16, TL), BF16)
    mk(E2, "Cbf_sb", (16, TL), BF16)
    mk(E2, "Bb_sb", (128, 16 * TL), BF16)
    mk(E2, "Cb_sb", (128, 16 * TL), BF16)
    mk(E2, "sp1_sb", (128, TL), F32)
    mk(E2, "sp2_sb", (128, TL), F32)
    mk(E2, "dt_sb", (128, 2 * TL), F32)
    mk(E2, "du_sb", (128, 2 * TL), BF16)
    mk(E2, "aBuf_sb", (128, 2 * 2 * TL), F32)
    mk(E2, "bs_sb", (128, 2 * TL), BF16)
    mk(E2, "hb_sb", (128, 2 * TL), BF16)
    mk(E2, "yp_sb", (128, 4 * TL), BF16)
    mk(E2, "yt_sb", (128, TL), BF16)
    mk(E2, "gt_sb", (128, TL), BF16)
    mk(E2, "mT_sb", (128, 8 * TL), BF16)
    mk(E2, "h_sb", (128, 4 * D), F32)
    mk(E2, "h2bf_sb", (128, 4 * D), BF16)
    mk(E2, "n2Tst_sb", (128, 8 * TL), BF16)
    mkp(E2, "psB_bc", (128, TL))
    mkp(E2, "psB_dt", (128, TL))
    mkp(E2, "psB_op0", (128, TL))
    mkp(E2, "psB_op1", (128, TL))
    mkp(E2, "psB_tr0", (128, 1024))
    mkp(E2, "psB_tr1", (128, 1024))
    sA2.close()
    sA.close()

    sBC = ExitStack()
    EB = sBC.enter_context
    mk(EB, "u3T_sb", (128, 8 * NT), BF16)
    mk(EB, "wfull_sb", (128, 32 * NE), F32)
    mk(EB, "wroute_sb", (16, NT), BF16)
    mk(EB, "wr_sb", (128, 8 * NE), BF16)
    mk(EB, "b2_sb", (16, HID), BF16)
    sB1 = ExitStack()
    E3 = sB1.enter_context
    mk(E3, "n2Ta_sb", (128, 8 * NT), BF16)
    mk(E3, "w1s_sb", (128, 2 * 8 * 128), BF16)
    mk(E3, "w2s_sb", (128, 2 * 8 * 128), BF16)
    mk(E3, "sl_sb", (128, TL), BF16)
    mk(E3, "w2e_sb", (128, TL), BF16)
    mk(E3, "gtmp_sb", (128, 2 * TL), BF16)
    mk(E3, "gst_sb", (128, 2 * NT), BF16)
    mkp(E3, "psC_1", (128, TL))
    mkp(E3, "psC_2", (128, TL))
    mkp(E3, "psC_3", (128, TL))
    mkp(E3, "psC_4", (128, TL))
    sB1.close()
    sB2 = ExitStack()
    E4 = sB2.enter_context
    mk(E4, "w3_sb", (128, 32 * D), BF16)
    mk(E4, "gld_sb", (128, 32 * TG), BF16)
    mk(E4, "x0c_sb", (128, 2 * D), F32)
    mk(E4, "hc_sb", (128, 2 * D), F32)
    mk(E4, "x1_sb", (128, 2 * D), F32)
    mk(E4, "pT_sb", (128, 8 * TG), BF16)
    mk(E4, "u3bf_sb", (128, 2 * D), BF16)
    mk(E4, "lg_sb", (128, 2 * NE), F32)
    mk(E4, "m8_sb", (128, 2 * 8), F32)
    mk(E4, "i8_sb", (128, 2 * 8), U32)
    mk(E4, "idxf_sb", (128, 2 * 2), F32)
    mk(E4, "nm1_sb", (128, 2), F32)
    mk(E4, "t21_sb", (128, 2), F32)
    mk(E4, "den_sb", (128, 2), F32)
    mk(E4, "w1r_sb", (128, 2), F32)
    mk(E4, "w2r_sb", (128, 2), F32)
    mk(E4, "mk1_sb", (128, NE), F32)
    mk(E4, "mk2_sb", (128, NE), F32)
    mk(E4, "wfbf_sb", (128, NE), BF16)
    mkp(E4, "psD_p", (128, 8 * TG))
    mkp(E4, "psD_t", (128, 1024))
    mkp(E4, "psD_u", (128, 4 * TG))
    sB2.close()
    sC = ExitStack()
    E5 = sC.enter_context
    mk(E5, "f_sb", (128, 8 * NT), BF16)
    sC1 = ExitStack()
    E6 = sC1.enter_context
    mk(E6, "W1X_sb", (128, 4 * HID), BF16)
    mk(E6, "W1Y_sb", (128, 4 * HID), BF16)
    mk(E6, "W2X_sb", (128, 4 * D), BF16)
    mk(E6, "W2Y_sb", (128, 4 * D), BF16)
    mk(E6, "g1e_sb", (128, HID), BF16)
    mk(E6, "be1e_sb", (128, HID), BF16)
    mk(E6, "b1e_sb", (128, HID), BF16)
    mk(E6, "Sb1_sb", (128, 1), F32)
    mk(E6, "hch_sb", (128, 2 * HID), BF16)
    mk(E6, "gl_sb", (128, 2 * HID), BF16)
    mk(E6, "dg_sb", (128, 2 * 128), BF16)
    mk(E6, "glT_sb", (128, 8 * TG), BF16)
    mk(E6, "sums_sb", (128, 2), F32)
    mk(E6, "ssqE_sb", (128, 2), F32)
    mk(E6, "muE_sb", (128, 2), F32)
    mk(E6, "rstdE_sb", (128, 2), F32)
    mk(E6, "bE_sb", (128, 2), F32)
    mkp(E6, "psE_w1a", (128, HID))
    mkp(E6, "psE_w1b", (128, HID))
    mkp(E6, "psE_tr", (128, TL))
    mkp(E6, "psE_o", (128, 4 * TG))
    sC1.close()
    sC2 = ExitStack()
    E7 = sC2.enter_context
    mk(E7, "x1f_sb", (128, 2 * D), F32)
    mk(E7, "x2_sb", (128, 2 * D), F32)
    mkp(E7, "psE_f", (128, 1024))
    sC2.close()
    sC.close()
    sBC.close()
    return B


def build(core_inputs_shapes=None):
    nc = bass.Bass("TRN2", target_bir_lowering=False, detect_race_conditions=False)
    H = {}

    def din(name, shape, dt=F32):
        H[name] = nc.dram_tensor(name, list(shape), dt, kind="ExternalInput").ap()

    din("xs", (NT, D))
    din("w_in_T", (D, DFF)); din("w_xp_pad", (DI, 128)); din("w_dt_T", (DTR, DI))
    din("w_out_T", (DI, D)); din("w1_T", (D, DFF)); din("w2_T", (D, DFF))
    din("w3_T", (DFF, D)); din("W1e", (NE, D, HID)); din("W2e", (NE, HID, D))
    din("wr", (D, NE)); din("b2e", (NE, D))
    din("g1b", (NE, 128, HID)); din("be1b", (NE, 128, HID)); din("b1b", (NE, 128, HID))
    din("A_log_r", (128, 16, DS)); din("convw_r", (128, 16, 4)); din("convb_r", (128, 16))
    din("dtb_r", (128, 16)); din("D_r", (128, 16)); din("n1w_pp", (128, 8))
    din("n2w_pp", (128, 8)); din("brb", (128, NE)); din("iota16", (128, NE))
    din("ident", (128, 128)); din("onehot", (16, DS * 128))
    out_h = nc.dram_tensor("out", [NT, D], F32, kind="ExternalOutput").ap()

    def scr(name, shape, dt):
        kind = "ExternalOutput" if DEBUG_SCRATCH else "Internal"
        H[name] = nc.dram_tensor(name, list(shape), dt, kind=kind).ap()

    scr("sp_uact", (NCHUNK, 128, 16, TL), BF16)
    scr("sp_z", (NCHUNK, 128, 16, TL), BF16)
    scr("sp_dtr", (NCHUNK, 64, TL), BF16)
    scr("sp_B", (NCHUNK, 16, TL), F32)
    scr("sp_C", (NCHUNK, 16, TL), F32)
    scr("sp_h", (NCHUNK, 128, 4, D), F32)
    scr("sp_n2T", (128, 8, NT), BF16)
    scr("sp_g2", (NTG, 128, 32, TG), BF16)
    scr("sp_x1", (NTG, 128, 2, D), F32)
    if DEBUG_SCRATCH:
        scr("dbg_gld", (NTG, 128, 32 * TG), BF16)
        scr("dbg_pT", (NTG, 128, 8 * TG), BF16)

    es0 = ExitStack()
    sems = {}
    for n in ENGINES:
        sems[n + "_c"] = es0.enter_context(nc.semaphore("semc_" + n))
        for i in range(4):
            sems["%s_d%d" % (n, i)] = es0.enter_context(
                nc.semaphore("semd%d_" % i + n))
    Bf = _alloc(nc, es0)
    ctx = Ctx(nc, sems)
    for tname in TAPS:
        shape, dt = _TAPDEFS[tname]
        ctx.taps[tname] = nc.dram_tensor("tap_" + tname, list(shape), dt,
                                         kind="ExternalOutput").ap()

    with nc.Block() as block:
        for ename in ENGINES:
            fn = getattr(block, ename)

            def emit(_eng, ename=ename):
                ctx.begin(ename)
                _program(ctx, H, Bf, out_h)

            fn(emit)
    es0.close()
    return nc



def _program(ctx, H, B, out_h):
    r = ctx.run
    m_setup = _setup(ctx, H, B)
    bar = ctx.barrier()
    if "A1" in PHASES:
        _phase_a1(ctx, H, B, m_setup)
        bar = ctx.barrier()
    if "A2" in PHASES:
        _phase_a2(ctx, H, B, m_setup)
        bar = ctx.barrier()
    if "B1" in PHASES:
        _phase_b1(ctx, H, B)
        bar = ctx.barrier()
    if "B2" in PHASES:
        _phase_b2(ctx, H, B)
        bar = ctx.barrier()
    if "C" in PHASES:
        _phase_c(ctx, H, B)
        bar = ctx.barrier()
    if "FIN" in PHASES:
        _phase_fin(ctx, H, B, out_h)
    e = ctx.engs["gpsimd"]
    for slot in range(NDQ):
        if e.dq_cnt[slot]:
            for n in ENGINES:
                ctx.dep(n, (("gpsimd", slot), e.dq_cnt[slot], "d"))
    ctx.barrier()


def _setup(ctx, H, B):
    r = ctx.run
    m = {}
    for nm, dst in (("iota16", "iota16_sb"), ("brb", "brb_sb"), ("n1w_pp", "n1w_sb"),
                    ("n2w_pp", "n2w_sb"), ("convb_r", "convb_sb"), ("dtb_r", "dtb_sb"),
                    ("D_r", "D_sb")):
        m[dst] = r("sync", "dma_start", out=B[dst][:], in_=H[nm][:], dma=True)
    m["A_log"] = r("sync", "dma_start", out=B["A_sb"][:],
                   in_=H["A_log_r"].rearrange("p dc n -> p (dc n)"), dma=True)
    m["convw"] = r("sync", "dma_start", out=B["convw_sb"][:],
                   in_=H["convw_r"].rearrange("p dc t -> p (dc t)"), dma=True)
    m["ident"] = r("gpsimd", "dma_start", out=B["ident_bf"][:], in_=H["ident"][:], dma=True)
    ma1 = r("scalar", "activation", B["A_sb"][:], B["A_sb"][:], AF.Exp,
            deps=[m["A_log"]], inc=1)
    m["A"] = r("vector", "tensor_scalar", B["A_sb"][:], B["A_sb"][:], -1.0, None, OP.mult,
               deps=[ma1], inc=1)
    m["st0"] = r("gpsimd", "memset", B["state_sb"][:], 0.0, inc=1)
    m["ones"] = r("gpsimd", "memset", B["ones_sb"][:], 1.0, inc=1)
    m["eps"] = r("gpsimd", "memset", B["eps_sb"][:], EPS, inc=1)
    m["hl0"] = r("gpsimd", "memset", B["halo_sb"][:], 0.0, inc=1)
    return m


def _phase_a1(ctx, H, B, ms):
    r = ctx.run
    mw = []
    for kg in range(8):
        mw.append(r("gpsimd", "dma_start",
                    out=B["w_in_sb"][:, kg * DFF:(kg + 1) * DFF],
                    in_=H["w_in_T"][kg * 128:(kg + 1) * 128, :], dma=True))
    m_wxp = r("gpsimd", "dma_start",
              out=B["w_xp_sb"][:].rearrange("p (dc m) -> p dc m", m=128),
              in_=H["w_xp_pad"].rearrange("(dc p) m -> p dc m", p=128), dma=True)
    m_fold = []
    for kg in range(8):
        m_fold.append(r("vector", "tensor_scalar",
                        B["w_in_sb"][:, kg * DFF:(kg + 1) * DFF],
                        B["w_in_sb"][:, kg * DFF:(kg + 1) * DFF],
                        B["n1w_sb"][:, kg:kg + 1], None, OP.mult,
                        deps=[mw[kg], ms["n1w_sb"]], inc=1))
    m_fold_all = m_fold[-1]

    prev = {}
    for c in range(LIMIT_CHUNKS or NCHUNK):
        buf = c % 2
        x0 = B["x0a_sb"][:, buf * 4 * D:(buf + 1) * 4 * D].rearrange("p (tt d) -> p tt d", d=D)
        m_ld = r("sync", "dma_start", out=x0,
                 in_=H["xs"][c * TL:(c + 1) * TL, :].rearrange("(tt p) d -> p tt d", p=128),
                 deps=[prev.get(("x0", buf))], dma=True)
        g0 = c * 4
        m_st = None
        m_stq = None
        for tt in range(4):
            g = g0 + tt
            m_st = r("vector", "tensor_reduce", B["sum_s"][:, g:g + 1], x0[:, tt, :],
                     AX.X, OP.add, deps=[m_ld], inc=(1 if tt == 3 else 0))
            m_stq = r("scalar", "activation", B["trash_bf"][:],
                      x0[:, tt, :], AF.Square,
                      accum_out=B["ssq_s"][:, g:g + 1], deps=[m_ld],
                      inc=(1 if tt == 3 else 0))
        ctx.drain("vector")
        ctx.autodrain = True
        sl = slice(g0, g0 + 4)
        r("vector", "tensor_scalar", B["mu0_s"][:, sl], B["sum_s"][:, sl], 1.0 / D, None,
          OP.mult, deps=[m_st, m_stq])
        r("vector", "tensor_scalar", B["sm_a"][:, sl], B["ssq_s"][:, sl], 1.0 / D, None, OP.mult)
        r("vector", "tensor_tensor", B["sm_b"][:, sl], B["mu0_s"][:, sl], B["mu0_s"][:, sl],
          OP.mult)
        r("vector", "tensor_tensor", B["sm_a"][:, sl], B["sm_a"][:, sl], B["sm_b"][:, sl],
          OP.subtract)
        m_v = r("vector", "tensor_scalar", B["sm_b"][:, sl], B["sm_a"][:, sl], EPS, None,
                OP.add, inc=1)
        ml1 = r("scalar", "activation", B["sm_b"][:, sl], B["sm_b"][:, sl], AF.Ln,
                deps=[m_v], inc=1)
        r("scalar", "activation", B["rstd0_s"][:, sl], B["sm_b"][:, sl], AF.Exp, scale=-0.5)
        m_s1 = r("scalar", "activation", B["sm_b"][:, sl], B["sm_b"][:, sl], AF.Exp,
                 inc=1)
        r("vector", "reciprocal", B["sm_b"][:, sl], B["sm_b"][:, sl], deps=[m_s1])
        r("vector", "tensor_tensor", B["sm_a"][:, sl], B["sm_a"][:, sl], B["sm_b"][:, sl],
          OP.mult)
        m_v2 = r("vector", "tensor_scalar", B["sm_a"][:, sl], B["sm_a"][:, sl], EPS, None,
                 OP.add, inc=1)
        r("scalar", "activation", B["sm_a"][:, sl], B["sm_a"][:, sl], AF.Ln, deps=[m_v2])
        m_s2 = r("scalar", "activation", B["sm_a"][:, sl], B["sm_a"][:, sl], AF.Exp,
                 scale=-0.5, inc=1)
        r("vector", "tensor_tensor", B["rs01_s"][:, sl], B["rstd0_s"][:, sl], B["sm_a"][:, sl],
          OP.mult, deps=[m_s2])
        r("vector", "scalar_tensor_tensor", B["b00_s"][:, sl], B["mu0_s"][:, sl],
          -1.0, B["rstd0_s"][:, sl], OP.mult, OP.mult)
        m_sm = r("vector", "scalar_tensor_tensor", B["b01_s"][:, sl], B["mu0_s"][:, sl],
                 -1.0, B["rs01_s"][:, sl], OP.mult, OP.mult, inc=1)
        ctx.autodrain = False
        m_cast = None
        for tt in range(4):
            g = g0 + tt
            m_cast = r("scalar", "activation",
                       B["x0bf_sb"][:, tt * D:(tt + 1) * D], x0[:, tt, :], AF.Identity,
                       scale=B["rs01_s"][:, g:g + 1], bias=B["b01_s"][:, g:g + 1],
                       deps=[m_sm, prev.get("n1T_done")], inc=1)
        m_ev = {}
        for dg in range(8):
            ps = B["psA_tr0"] if dg % 2 == 0 else B["psA_tr1"]
            m_mm = None
            for tt in range(4):
                m_mm = r("tensor", "matmul", ps[:, tt * 128:(tt + 1) * 128],
                         B["x0bf_sb"][:, tt * D + dg * 128: tt * D + (dg + 1) * 128],
                         B["ident_bf"][:],
                         deps=[m_cast, ms["ident"], m_ev.get(dg - 2)],
                         start=True, stop=True, inc=(1 if tt == 3 else 0))
            m_ev[dg] = r("scalar", "activation", B["n1T_sb"][:, dg * TL:(dg + 1) * TL],
                         ps[:], AF.Copy, deps=[m_mm], inc=1)
        m_n1T = m_ev[7]
        m_mm_ev = {}
        m_uact = {}
        for mt in range(32):
            ps = B["psA_mm0"] if mt % 2 == 0 else B["psA_mm1"]
            dep0 = [m_n1T, m_fold_all, m_mm_ev.get(mt - 2)]
            m_mm = None
            for kg in range(8):
                m_mm = r("tensor", "matmul", ps[:],
                         B["w_in_sb"][:, kg * DFF + mt * 128: kg * DFF + (mt + 1) * 128],
                         B["n1T_sb"][:, kg * TL:(kg + 1) * TL],
                         deps=(dep0 if kg == 0 else ()),
                         start=(kg == 0), stop=(kg == 7), inc=(1 if kg == 7 else 0))
            if mt < 16:
                dc = mt
                ub = dc % 2
                ust = B["ustg_sb"][:, ub * 515:(ub + 1) * 515]
                if c % 4 == 0:
                    r("vector", "memset", ust[:, 0:3], 0.0,
                      deps=[prev.get(("ustg", ub))])
                else:
                    r("vector", "tensor_copy", ust[:, 0:3],
                      B["halo_sb"][:, dc * 3:(dc + 1) * 3],
                      deps=[prev.get(("ustg", ub)), prev.get("halo_w%d" % dc)])
                m_evu = r("scalar", "activation", ust[:, 3:515], ps[:], AF.Copy,
                          deps=[m_mm, prev.get(("ustg", ub))], inc=1)
                m_mm_ev[mt] = m_evu
                m_tail = r("vector", "tensor_copy", B["halo_sb"][:, dc * 3:(dc + 1) * 3],
                           ust[:, 512:515], deps=[m_evu])
                prev["halo_w%d" % dc] = m_tail
                cb = dc % 2
                cacc = B["cacc_sb"][:, cb * TL:(cb + 1) * TL]
                ctx.drain("vector")
                r("vector", "tensor_scalar", cacc, ust[:, 0:512],
                  B["convw_sb"][:, dc * 4:dc * 4 + 1], None, OP.mult,
                  deps=[m_evu, ms["convw"], prev.get(("cacc", cb))])
                m_conv = None
                for t in range(1, 4):
                    m_conv = r("vector", "scalar_tensor_tensor", cacc, ust[:, t:512 + t],
                               B["convw_sb"][:, dc * 4 + t:dc * 4 + t + 1], cacc,
                               OP.mult, OP.add, inc=(1 if t == 3 else 0))
                prev[("ustg", ub)] = m_conv
                m_ua = r("scalar", "activation", B["uact_sb"][:, dc * TL:(dc + 1) * TL],
                         cacc, AF.Silu, bias=B["convb_sb"][:, dc:dc + 1],
                         deps=[m_conv, ms["convb_sb"], prev.get("sp_uact")], inc=1)
                prev[("cacc", cb)] = m_ua
                m_uact[dc] = m_ua
            else:
                dc = mt - 16
                m_evz = r("scalar", "activation", B["zs_sb"][:, dc * TL:(dc + 1) * TL],
                          ps[:], AF.Silu, deps=[m_mm, prev.get("sp_z")], inc=1)
                m_mm_ev[mt] = m_evz
        m_xp = None
        for k in range(16):
            m_xp = r("tensor", "matmul", B["psA_xp"][:],
                     B["w_xp_sb"][:, k * 128:(k + 1) * 128],
                     B["uact_sb"][:, k * TL:(k + 1) * TL],
                     deps=[m_uact[k], m_wxp] + ([prev.get("xp_ev")] if k == 0 else []),
                     start=(k == 0), stop=(k == 15), inc=(1 if k == 15 else 0))
        m_eB = r("scalar", "activation", B["Bst_sb"][:], B["psA_xp"][0:16, :], AF.Copy,
                 deps=[m_xp, prev.get("sp_B")], inc=1)
        m_eC = r("scalar", "activation", B["Cst_sb"][:], B["psA_xp"][32:48, :], AF.Copy,
                 deps=[prev.get("sp_C")], inc=1)
        m_eD = r("scalar", "activation", B["dtrst_sb"][:], B["psA_xp"][64:128, :], AF.Copy,
                 deps=[prev.get("sp_dtr")], inc=1)
        prev["xp_ev"] = m_eD
        m_spu = r("scalar", "dma_start", out=H["sp_uact"][c],
                  in_=B["uact_sb"][:].rearrange("p (dc t) -> p dc t", t=TL),
                  deps=[m_uact[15]], dma=True)
        m_spz = r("scalar", "dma_start", out=H["sp_z"][c],
                  in_=B["zs_sb"][:].rearrange("p (dc t) -> p dc t", t=TL),
                  deps=[m_mm_ev[31]], dma=True)
        m_spB = r("scalar", "dma_start", out=H["sp_B"][c], in_=B["Bst_sb"][:],
                  deps=[m_eB], dma=True)
        m_spC = r("scalar", "dma_start", out=H["sp_C"][c], in_=B["Cst_sb"][:],
                  deps=[m_eC], dma=True)
        m_spD = r("scalar", "dma_start", out=H["sp_dtr"][c], in_=B["dtrst_sb"][:],
                  deps=[m_eD], dma=True)
        prev["sp_uact"] = m_spu
        prev["sp_z"] = m_spz
        prev["sp_B"] = m_spB
        prev["sp_C"] = m_spC
        prev["sp_dtr"] = m_spD
        prev["n1T_done"] = m_mm_ev[31]
        prev[("x0", buf)] = m_cast
        ctx.tap("a1_x0bf_c%d" % c, B["x0bf_sb"][:], deps=[m_cast])
        ctx.tap("a1_n1T_c%d" % c, B["n1T_sb"][:], deps=[m_n1T])
        ctx.tap("a1_sum_c%d" % c, B["sum_s"][:], deps=[m_sm])
        ctx.tap("a1_mu0_c%d" % c, B["mu0_s"][:], deps=[m_sm])
        ctx.tap("a1_rstd0_c%d" % c, B["rstd0_s"][:], deps=[m_sm])
        ctx.tap("a1_var_c%d" % c, B["sm_a"][:], deps=[m_sm])
        ctx.tap("a1_ssq_c%d" % c, B["ssq_s"][:], deps=[m_sm])
        ctx.tap("a1_rs01_c%d" % c, B["rs01_s"][:], deps=[m_sm])
        ctx.tap("a1_uact_c%d" % c, B["uact_sb"][:], deps=[m_uact[15]])
        ctx.tap("a1_z_c%d" % c, B["zs_sb"][:], deps=[m_mm_ev[31]])
        ctx.tap("a1_B_c%d" % c, B["Bst_sb"][:], deps=[m_eB])
        ctx.tap("a1_dtr_c%d" % c, B["dtrst_sb"][:], deps=[m_eD])


def _phase_a2(ctx, H, B, ms):
    r = ctx.run
    ms["onehot"] = r("gpsimd", "dma_start", out=B["onehot_bf"][:], in_=H["onehot"][:],
                     dma=True)
    m_wdt = r("gpsimd", "dma_start", out=B["w_dt_sb"][:], in_=H["w_dt_T"][:], dma=True)
    m_wout = []
    for kg in range(16):
        m_wout.append(r("gpsimd", "dma_start",
                        out=B["w_out_sb"][:, kg * D:(kg + 1) * D],
                        in_=H["w_out_T"][kg * 128:(kg + 1) * 128, :], dma=True))
    prev = {}
    for c in range(LIMIT_CHUNKS or NCHUNK):
        g0 = c * 4
        batch_first = (c % 4 == 0)
        m_lu = r("sync", "dma_start",
                 out=B["uact2_sb"][:].rearrange("p (dc t) -> p dc t", t=TL),
                 in_=H["sp_uact"][c], deps=[prev.get("uact_free")], dma=True)
        m_lz = r("sync", "dma_start",
                 out=B["zs2_sb"][:].rearrange("p (dc t) -> p dc t", t=TL),
                 in_=H["sp_z"][c], deps=[prev.get("z_free")], dma=True)
        m_lx = r("sync", "dma_start",
                 out=B["x0b_sb"][:].rearrange("p (tt d) -> p tt d", d=D),
                 in_=H["xs"][c * TL:(c + 1) * TL, :].rearrange("(tt p) d -> p tt d", p=128),
                 deps=[prev.get("x0_free")], dma=True)
        m_ldt = r("sync", "dma_start", out=B["dtr2_sb"][:], in_=H["sp_dtr"][c],
                  deps=[prev.get("dtr_free")], dma=True)
        m_lB = r("gpsimd", "dma_start", out=B["Bbf_sb"][:], in_=H["sp_B"][c],
                 deps=[prev.get("Bbf_free")], dma=True)
        m_lC = r("gpsimd", "dma_start", out=B["Cbf_sb"][:], in_=H["sp_C"][c],
                 deps=[prev.get("Cbf_free")], dma=True)
        if batch_first and c > 0:
            ms["st0"] = r("gpsimd", "memset", B["state_sb"][:], 0.0,
                          deps=[prev.get("scan_all")], inc=1)
        m_bb = {}
        m_cb = {}
        m_bc_ev = prev.get("bc_ev")
        for n in range(DS):
            m_mm = r("tensor", "matmul", B["psB_bc"][:],
                     B["onehot_bf"][:, n * 128:(n + 1) * 128], B["Bbf_sb"][:],
                     deps=[m_lB, ms["onehot"], m_bc_ev], start=True, stop=True, inc=1)
            m_bc_ev = r("scalar", "activation", B["Bb_sb"][:, n * TL:(n + 1) * TL],
                        B["psB_bc"][:], AF.Copy, deps=[m_mm, prev.get("Bb_free")], inc=1)
            m_bb[n] = m_bc_ev
        for n in range(DS):
            m_mm = r("tensor", "matmul", B["psB_bc"][:],
                     B["onehot_bf"][:, n * 128:(n + 1) * 128], B["Cbf_sb"][:],
                     deps=[m_lC, m_bc_ev], start=True, stop=True, inc=1)
            m_bc_ev = r("scalar", "activation", B["Cb_sb"][:, n * TL:(n + 1) * TL],
                        B["psB_bc"][:], AF.Copy, deps=[m_mm, prev.get("Cb_free")], inc=1)
            m_cb[n] = m_bc_ev
        prev["Bbf_free"] = m_bb[DS - 1]
        prev["Cbf_free"] = m_cb[DS - 1]
        prev["bc_ev"] = m_bc_ev

        m_gate = {}
        m_sp_done = None
        m_scan_last = None
        for dc in range(16):
            db = dc % 2
            m_dmm = r("tensor", "matmul", B["psB_dt"][:],
                      B["w_dt_sb"][:, dc * 128:(dc + 1) * 128], B["dtr2_sb"][:],
                      deps=[m_ldt, m_wdt, prev.get("psdt_free")], start=True, stop=True,
                      inc=1)
            r("scalar", "activation", B["sp1_sb"][:], B["psB_dt"][:], AF.Abs,
              bias=B["dtb_sb"][:, dc:dc + 1], deps=[m_dmm, ms["dtb_sb"],
                                                    prev.get(("sp", 0))])
            r("scalar", "activation", B["sp1_sb"][:], B["sp1_sb"][:], AF.Exp, scale=-1.0)
            r("scalar", "activation", B["sp1_sb"][:], B["sp1_sb"][:], AF.Ln, bias=1.0)
            m_sp = r("scalar", "activation", B["sp2_sb"][:], B["psB_dt"][:], AF.Relu,
                     bias=B["dtb_sb"][:, dc:dc + 1], deps=[prev.get(("sp", 1))], inc=1)
            prev["psdt_free"] = m_sp
            dt_t = B["dt_sb"][:, db * TL:(db + 1) * TL]
            m_dt = r("vector", "tensor_tensor", dt_t, B["sp1_sb"][:], B["sp2_sb"][:], OP.add,
                     deps=[m_sp, prev.get(("dt", db))], inc=1)
            prev[("sp", 0)] = m_dt
            prev[("sp", 1)] = m_dt
            du_t = B["du_sb"][:, db * TL:(db + 1) * TL]
            m_du = r("vector", "tensor_tensor", du_t, dt_t, B["uact2_sb"][:, dc * TL:(dc + 1) * TL],
                     OP.mult, deps=[m_lu, prev.get(("du", db))])
            m_scan_last = None
            for grp in range(8):
                ab = grp % 2
                m_a = None
                for j in range(2):
                    n = grp * 2 + j
                    m_a = r("scalar", "activation",
                            B["aBuf_sb"][:, (ab * 2 + j) * TL:(ab * 2 + j + 1) * TL],
                            dt_t, AF.Exp, scale=B["A_sb"][:, dc * DS + n:dc * DS + n + 1],
                            deps=[m_dt, ms["A"], prev.get(("scangrp", ab))],
                            inc=(1 if j == 1 else 0))
                m_sc = None
                for j in range(2):
                    n = grp * 2 + j
                    q = n % 4
                    hb = B["hb_sb"][:, (n % 2) * TL:((n % 2) + 1) * TL]
                    bs = B["bs_sb"][:, (n % 2) * TL:((n % 2) + 1) * TL]
                    r("vector", "tensor_tensor", bs, du_t, B["Bb_sb"][:, n * TL:(n + 1) * TL],
                      OP.mult, deps=[m_du, m_bb[n], m_a])
                    r("vector", "tensor_tensor_scan", hb,
                      B["aBuf_sb"][:, (ab * 2 + j) * TL:(ab * 2 + j + 1) * TL], bs,
                      B["state_sb"][:, dc * DS + n:dc * DS + n + 1], OP.mult, OP.add,
                      deps=[ms["st0"]])
                    if n < 4:
                        m_sc = r("vector", "tensor_tensor",
                                 B["yp_sb"][:, q * TL:(q + 1) * TL], hb,
                                 B["Cb_sb"][:, n * TL:(n + 1) * TL], OP.mult,
                                 deps=[m_cb[n]], inc=(1 if j == 1 else 0), war=True)
                    else:
                        yt = B["yt_sb"][:]
                        r("vector", "tensor_tensor", yt, hb,
                          B["Cb_sb"][:, n * TL:(n + 1) * TL], OP.mult, deps=[m_cb[n]])
                        m_sc = r("vector", "tensor_tensor",
                                 B["yp_sb"][:, q * TL:(q + 1) * TL],
                                 B["yp_sb"][:, q * TL:(q + 1) * TL], yt, OP.add,
                                 inc=(1 if j == 1 else 0), war=True)
                    r("vector", "tensor_scalar",
                      B["state_sb"][:, dc * DS + n:dc * DS + n + 1],
                      hb[:, TL - 1:TL], 0.0, None, OP.add)
                prev[("scangrp", ab)] = m_sc
                m_scan_last = m_sc
            prev[("dt", db)] = m_scan_last
            prev[("du", db)] = m_scan_last
            r("vector", "tensor_tensor", B["yp_sb"][:, 0:TL], B["yp_sb"][:, 0:TL],
              B["yp_sb"][:, TL:2 * TL], OP.add)
            r("vector", "tensor_tensor", B["yp_sb"][:, 2 * TL:3 * TL],
              B["yp_sb"][:, 2 * TL:3 * TL], B["yp_sb"][:, 3 * TL:4 * TL], OP.add)
            r("vector", "tensor_tensor", B["yp_sb"][:, 0:TL], B["yp_sb"][:, 0:TL],
              B["yp_sb"][:, 2 * TL:3 * TL], OP.add)
            r("vector", "scalar_tensor_tensor", B["gt_sb"][:],
              B["uact2_sb"][:, dc * TL:(dc + 1) * TL], B["D_sb"][:, dc:dc + 1],
              B["yp_sb"][:, 0:TL], OP.mult, OP.add, deps=[ms["D_sb"]])
            m_g = r("vector", "tensor_tensor", B["uact2_sb"][:, dc * TL:(dc + 1) * TL],
                    B["gt_sb"][:], B["zs2_sb"][:, dc * TL:(dc + 1) * TL], OP.mult,
                    deps=[m_lz], inc=1)
            m_gate[dc] = m_g
        prev["scan_all"] = r("vector", "drain", inc=1)
        prev["uact_free"] = None
        prev["z_free"] = m_gate[15]
        prev["dtr_free"] = m_gate[15]
        prev["Bb_free"] = m_gate[15]
        prev["Cb_free"] = m_gate[15]
        ctx.tap("a2_gated_c%d" % c, B["uact2_sb"][:], deps=[m_gate[15]])
        m_op_ev = {}
        for mt in range(8):
            ps = B["psB_op0"] if mt % 2 == 0 else B["psB_op1"]
            m_mm = None
            for k in range(16):
                m_mm = r("tensor", "matmul", ps[:],
                         B["w_out_sb"][:, k * D + mt * 128: k * D + (mt + 1) * 128],
                         B["uact2_sb"][:, k * TL:(k + 1) * TL],
                         deps=([m_gate[k], m_wout[k]] +
                               ([m_op_ev.get(mt - 2,
                                             prev.get("psop%d_free" % (mt % 2)))]
                                if k == 0 else [])),
                         start=(k == 0), stop=(k == 15), inc=(1 if k == 15 else 0))
            m_op_ev[mt] = r("scalar", "activation", B["mT_sb"][:, mt * TL:(mt + 1) * TL],
                            ps[:], AF.Copy, deps=[m_mm, prev.get("mT_free")], inc=1)
        m_mT = m_op_ev[7]
        prev["uact_free"] = m_mT
        m_h = {}
        m_ssq2 = None
        for tt in range(4):
            ps = B["psB_tr0"] if tt % 2 == 0 else B["psB_tr1"]
            m_mm = None
            for dg in range(8):
                m_mm = r("tensor", "matmul", ps[:, dg * 128:(dg + 1) * 128],
                         B["mT_sb"][:, dg * TL + tt * 128: dg * TL + (tt + 1) * 128],
                         B["ident_bf"][:],
                         deps=[m_op_ev[dg], prev.get(("htr", tt % 2))],
                         start=True, stop=True, inc=(1 if dg == 7 else 0))
            g = g0 + tt
            r("vector", "scalar_tensor_tensor", B["h_sb"][:, tt * D:(tt + 1) * D],
              B["x0b_sb"][:, tt * D:(tt + 1) * D], B["rstd0_s"][:, g:g + 1], ps[:],
              OP.mult, OP.add, deps=[m_mm, m_lx, prev.get("h_free")])
            m_hh = r("vector", "tensor_scalar", B["h_sb"][:, tt * D:(tt + 1) * D],
                     B["h_sb"][:, tt * D:(tt + 1) * D], B["b00_s"][:, g:g + 1], None,
                     OP.add, inc=1)
            prev[("htr", tt % 2)] = m_hh
            m_h[tt] = m_hh
            g = g0 + tt
            m_ssq2 = r("scalar", "activation", B["trash_bf"][:],
                       B["h_sb"][:, tt * D:(tt + 1) * D], AF.Square,
                       accum_out=B["ssq_s"][:, g:g + 1], deps=[m_hh], inc=1)
        prev["mT_free"] = m_h[3]
        ctx.tap("a2_h_c%d" % c, B["h_sb"][:], deps=[m_h[3]])
        m_sph = r("gpsimd", "dma_start", out=H["sp_h"][c],
                  in_=B["h_sb"][:].rearrange("p (tt d) -> p tt d", d=D),
                  deps=[m_h[3]], dma=True)
        prev["h_free"] = m_sph
        sl = slice(g0, g0 + 4)
        ctx.autodrain = True
        r("vector", "tensor_scalar", B["sm_a"][:, sl], B["ssq_s"][:, sl], 1.0 / D, None,
          OP.mult, deps=[m_ssq2])
        m_v = r("vector", "tensor_scalar", B["sm_a"][:, sl], B["sm_a"][:, sl], EPS, None,
                OP.add, inc=1)
        ctx.autodrain = False
        r("scalar", "activation", B["sm_a"][:, sl], B["sm_a"][:, sl], AF.Ln, deps=[m_v])
        m_r2 = r("scalar", "activation", B["rstd2_s"][:, sl], B["sm_a"][:, sl], AF.Exp,
                 scale=-0.5, inc=1)
        m_cast = None
        for tt in range(4):
            g = g0 + tt
            m_cast = r("scalar", "activation", B["h2bf_sb"][:, tt * D:(tt + 1) * D],
                       B["h_sb"][:, tt * D:(tt + 1) * D], AF.Copy,
                       scale=B["rstd2_s"][:, g:g + 1],
                       deps=[m_r2, prev.get("h2bf_free")], inc=(1 if tt == 3 else 0))
        m_n2ev = {}
        for dg in range(8):
            ps = B["psB_op0"] if dg % 2 == 0 else B["psB_op1"]
            m_mm = None
            for tt in (3, 2, 1, 0):
                m_mm = r("tensor", "matmul", ps[:, tt * 128:(tt + 1) * 128],
                         B["h2bf_sb"][:, tt * D + dg * 128: tt * D + (dg + 1) * 128],
                         B["ident_bf"][:],
                         deps=[m_cast, m_n2ev.get(dg - 2),
                               m_op_ev[6] if dg == 0 else None,
                               m_op_ev[7] if dg == 1 else None],
                         start=True, stop=True, inc=(1 if tt == 0 else 0))
            m_n2ev[dg] = r("scalar", "activation", B["n2Tst_sb"][:, dg * TL:(dg + 1) * TL],
                           ps[:], AF.Copy, deps=[m_mm, prev.get("n2Tst_free")], inc=1)
        prev["h2bf_free"] = m_n2ev[7]
        prev["psop0_free"] = m_n2ev[6]
        prev["psop1_free"] = m_n2ev[7]
        m_spn2 = r("scalar", "dma_start",
                   out=H["sp_n2T"][:, :, c * TL:(c + 1) * TL],
                   in_=B["n2Tst_sb"][:].rearrange("p (dg t) -> p dg t", t=TL),
                   deps=[m_n2ev[7]], dma=True)
        prev["n2Tst_free"] = m_spn2
        prev["x0_free"] = m_cast


def _phase_b1(ctx, H, B):
    r = ctx.run
    m_n2 = []
    for dg in range(8):
        m_n2.append(r("sync", "dma_start", out=B["n2Ta_sb"][:, dg * NT:(dg + 1) * NT],
                      in_=H["sp_n2T"][:, dg, :], dma=True))
    prev = {}
    for mt in range(32):
        sb = mt % 2
        w1s = B["w1s_sb"][:, sb * 1024:(sb + 1) * 1024]
        w2s = B["w2s_sb"][:, sb * 1024:(sb + 1) * 1024]
        m_l1 = r("gpsimd", "dma_start",
                 out=w1s.rearrange("p (kg m) -> p kg m", m=128),
                 in_=H["w1_T"][:, mt * 128:(mt + 1) * 128].rearrange(
                     "(kg p) m -> p kg m", p=128),
                 deps=[prev.get(("w1", sb))], dma=True)
        m_l2 = r("gpsimd", "dma_start",
                 out=w2s.rearrange("p (kg m) -> p kg m", m=128),
                 in_=H["w2_T"][:, mt * 128:(mt + 1) * 128].rearrange(
                     "(kg p) m -> p kg m", p=128),
                 deps=[prev.get(("w2", sb))], dma=True)
        m_f1 = m_f2 = None
        for kg in range(8):
            m_f1 = r("vector", "tensor_scalar", w1s[:, kg * 128:(kg + 1) * 128],
                     w1s[:, kg * 128:(kg + 1) * 128], B["n2w_sb"][:, kg:kg + 1], None,
                     OP.mult, deps=[m_l1], inc=(1 if kg == 7 else 0))
            m_f2 = r("vector", "tensor_scalar", w2s[:, kg * 128:(kg + 1) * 128],
                     w2s[:, kg * 128:(kg + 1) * 128], B["n2w_sb"][:, kg:kg + 1], None,
                     OP.mult, deps=[m_l2], inc=(1 if kg == 7 else 0))
        gb = mt % 2
        gst = B["gst_sb"][:, gb * NT:(gb + 1) * NT]
        m_gd = {}
        for tg in range(8):
            ps1 = B["psC_1"] if tg % 2 == 0 else B["psC_3"]
            ps2 = B["psC_2"] if tg % 2 == 0 else B["psC_4"]
            m_mm1 = m_mm2 = None
            for kg in range(8):
                m_mm1 = r("tensor", "matmul", ps1[:],
                          w1s[:, kg * 128:(kg + 1) * 128],
                          B["n2Ta_sb"][:, kg * NT + tg * TL: kg * NT + (tg + 1) * TL],
                          deps=([m_f1, m_n2[kg]] +
                                ([prev.get(("ps", tg % 2))] if kg == 0 else [])),
                          start=(kg == 0), stop=(kg == 7), inc=(1 if kg == 7 else 0))
            for kg in range(8):
                m_mm2 = r("tensor", "matmul", ps2[:],
                          w2s[:, kg * 128:(kg + 1) * 128],
                          B["n2Ta_sb"][:, kg * NT + tg * TL: kg * NT + (tg + 1) * TL],
                          deps=([m_f2] if kg == 0 else ()),
                          start=(kg == 0), stop=(kg == 7), inc=(1 if kg == 7 else 0))
            m_s = r("scalar", "activation", B["sl_sb"][:], ps1[:], AF.Silu,
                    deps=[m_mm1, prev.get("sl_free")], inc=1)
            m_w2 = r("scalar", "activation", B["w2e_sb"][:], ps2[:], AF.Copy,
                     deps=[m_mm2, prev.get("w2e_free")], inc=1)
            gp = tg % 2
            m_g = r("vector", "tensor_tensor",
                    B["gtmp_sb"][:, gp * TL:(gp + 1) * TL],
                    B["sl_sb"][:], B["w2e_sb"][:], OP.mult,
                    deps=[m_s, m_w2, prev.get(("gtmp", gp))], inc=1)
            m_gc = r("scalar", "activation", gst[:, tg * TL:(tg + 1) * TL],
                     B["gtmp_sb"][:, gp * TL:(gp + 1) * TL], AF.Copy,
                     deps=[m_g, prev.get(("gst", gb))], inc=1)
            prev[("gtmp", gp)] = m_gc
            prev["sl_free"] = m_g
            prev["w2e_free"] = m_g
            prev[("ps", tg % 2)] = m_g
            m_gd[tg] = m_gc
        prev[("w1", sb)] = m_gd[7]
        prev[("w2", sb)] = m_gd[7]
        m_sp = None
        for tg8 in range(8):
            for half in range(2):
                m_sp = r("gpsimd", "dma_start",
                         out=H["sp_g2"][2 * tg8 + half][:, mt, :],
                         in_=gst[:, tg8 * TL + half * TG: tg8 * TL + (half + 1) * TG],
                         deps=[m_gd[tg8]], dma=True)
        prev[("gst", gb)] = m_sp


def _phase_b2(ctx, H, B):
    r = ctx.run
    m_wr = r("gpsimd", "dma_start",
             out=B["wr_sb"][:].rearrange("p (kg n) -> p kg n", n=NE),
             in_=H["wr"].rearrange("(kg p) n -> p kg n", p=128), dma=True)
    r("gpsimd", "dma_start", out=B["b2_sb"][:], in_=H["b2e"][:], dma=True)
    m_w3 = []
    for kg in range(32):
        m_w3.append(r("gpsimd", "dma_start", out=B["w3_sb"][:, kg * D:(kg + 1) * D],
                      in_=H["w3_T"][kg * 128:(kg + 1) * 128, :], dma=True))
    prev = {}
    for tg in range(LIMIT_TG or NTG):
        c = tg // 2
        tt0 = 2 * (tg % 2)
        gld = B["gld_sb"][:].rearrange("p (kg t) -> p kg t", t=TG)
        m_lg = [r("sync", "dma_start", out=gld,
                  in_=H["sp_g2"][tg], deps=[prev.get("gld_free")], dma=True)]
        m_lx = r("sync", "dma_start",
                 out=B["x0c_sb"][:].rearrange("p (tt d) -> p tt d", d=D),
                 in_=H["xs"][tg * TG:(tg + 1) * TG, :].rearrange("(tt p) d -> p tt d", p=128),
                 deps=[prev.get("x0c_free")], dma=True)
        m_lh = r("sync", "dma_start",
                 out=B["hc_sb"][:].rearrange("p (tt d) -> p tt d", d=D),
                 in_=H["sp_h"][c][:, tt0:tt0 + 2, :],
                 deps=[prev.get("hc_free")], dma=True)
        if DEBUG_SCRATCH:
            r("gpsimd", "dma_start", out=H["dbg_gld"][tg], in_=B["gld_sb"][:],
              deps=m_lg, dma=True)
        m_pev = {}
        for rnd in range(2):
            m_mmr = {}
            for mti in range(4):
                mt = 4 * rnd + mti
                dst = B["psD_p"][:, 2 * mti * TG:(2 * mti + 1) * TG]
                m_mm = None
                for kg in range(32):
                    m_mm = r("tensor", "matmul", dst,
                             B["w3_sb"][:, kg * D + mt * 128: kg * D + (mt + 1) * 128],
                             gld[:, kg, :],
                             deps=(m_lg + [m_w3[kg]] +
                                   ([prev.get("psp_free"),
                                     m_pev.get(4 * (rnd - 1) + mti)] if kg == 0 else [])),
                             start=(kg == 0), stop=(kg == 31),
                             inc=(1 if kg == 31 else 0))
                m_mmr[mti] = m_mm
            for mti in range(4):
                mt = 4 * rnd + mti
                m_pev[mt] = r("scalar", "activation",
                              B["pT_sb"][:, mt * TG:(mt + 1) * TG],
                              B["psD_p"][:, 2 * mti * TG:(2 * mti + 1) * TG], AF.Copy,
                              deps=[m_mmr[mti], prev.get("pT_free")], inc=1)
        if DEBUG_SCRATCH:
            m_dbgp = r("gpsimd", "dma_start", out=H["dbg_pT"][tg], in_=B["pT_sb"][:],
                       deps=[m_pev[mt2] for mt2 in range(8)], dma=True)
            prev["pT_free"] = m_dbgp
        m_x1 = {}
        for tt in range(2):
            g3 = 2 * tg + tt
            m_mm = None
            for dg in range(8):
                m_mm = r("tensor", "matmul", B["psD_t"][:, dg * 128:(dg + 1) * 128],
                         B["pT_sb"][:, dg * TG + tt * 128: dg * TG + (tt + 1) * 128],
                         B["ident_bf"][:],
                         deps=[m_pev[dg], prev.get("pst")],
                         start=True, stop=True, inc=(1 if dg == 7 else 0))
            r("vector", "tensor_tensor", B["x0c_sb"][:, tt * D:(tt + 1) * D],
              B["x0c_sb"][:, tt * D:(tt + 1) * D], B["hc_sb"][:, tt * D:(tt + 1) * D],
              OP.add, deps=[m_lx, m_lh])
            m_tt = r("vector", "tensor_tensor", B["x1_sb"][:, tt * D:(tt + 1) * D],
                     B["x0c_sb"][:, tt * D:(tt + 1) * D], B["psD_t"][:], OP.add,
                     deps=[m_mm, prev.get("x1_free")], inc=1)
            m_xx = r("scalar", "activation", B["trash_bf"][:],
                     B["x1_sb"][:, tt * D:(tt + 1) * D], AF.Copy,
                     accum_out=B["sum3_s"][:, g3:g3 + 1], deps=[m_tt], inc=1)
            prev["pst"] = m_xx
            ctx.tap("b2_x0h_t%d_%d" % (tg, tt), B["x0c_sb"][:, tt * D:(tt + 1) * D],
                    deps=[m_xx])
            m_x1[tt] = m_xx
            m_sq = r("scalar", "activation", B["trash_bf"][:],
                     B["x1_sb"][:, tt * D:(tt + 1) * D],
                     AF.Square, accum_out=B["ssq3_s"][:, g3:g3 + 1],
                     deps=[m_xx],
                     inc=(1 if tt == 1 else 0))
        prev["x0c_free"] = m_x1[1]
        prev["hc_free"] = m_x1[1]
        prev["pT_free"] = m_x1[1]
        ctx.tap("b2_x1_t%d" % tg, B["x1_sb"][:], deps=[m_x1[1]])
        m_spx = r("gpsimd", "dma_start", out=H["sp_x1"][tg],
                  in_=B["x1_sb"][:].rearrange("p (tt d) -> p tt d", d=D),
                  deps=[m_x1[1]], dma=True)
        ctx.drain("vector")
        ctx.autodrain = True
        sl = slice(2 * tg, 2 * tg + 2)
        r("vector", "tensor_scalar", B["sm_b"][:, sl], B["sum3_s"][:, sl], 1.0 / D, None,
          OP.mult, deps=[m_x1[1], m_sq])
        r("vector", "tensor_scalar", B["sm_a"][:, sl], B["ssq3_s"][:, sl], 1.0 / D, None,
          OP.mult)
        r("vector", "scalar_tensor_tensor", B["b03_s"][:, sl], B["sm_b"][:, sl],
          -1.0, B["sm_b"][:, sl], OP.mult, OP.mult)
        r("vector", "tensor_tensor", B["sm_a"][:, sl], B["sm_a"][:, sl], B["b03_s"][:, sl],
          OP.add)
        m_v = r("vector", "tensor_scalar", B["sm_a"][:, sl], B["sm_a"][:, sl], EPS, None,
                OP.add, inc=1)
        r("scalar", "activation", B["sm_a"][:, sl], B["sm_a"][:, sl], AF.Ln, deps=[m_v])
        m_r3 = r("scalar", "activation", B["rstd3_s"][:, sl], B["sm_a"][:, sl], AF.Exp,
                 scale=-0.5, inc=1)
        m_b3 = r("vector", "scalar_tensor_tensor", B["b03_s"][:, sl], B["sm_b"][:, sl],
                 -1.0, B["rstd3_s"][:, sl], OP.mult, OP.mult, deps=[m_r3], inc=1)
        ctx.autodrain = False
        m_cast = None
        for tt in range(2):
            g3 = 2 * tg + tt
            m_cast = r("scalar", "activation", B["u3bf_sb"][:, tt * D:(tt + 1) * D],
                       B["x1_sb"][:, tt * D:(tt + 1) * D], AF.Identity,
                       scale=B["rstd3_s"][:, g3:g3 + 1], bias=B["b03_s"][:, g3:g3 + 1],
                       deps=[m_b3, prev.get("u3bf_free")], inc=(1 if tt == 1 else 0))
        prev["x1_free"] = m_cast
        for half in range(2):
            m_mm = None
            for dgi in range(4):
                dg = half * 4 + dgi
                for tt in range(2):
                    m_mm = r("tensor", "matmul",
                             B["psD_u"][:, dgi * TG + tt * 128: dgi * TG + (tt + 1) * 128],
                             B["u3bf_sb"][:, tt * D + dg * 128: tt * D + (dg + 1) * 128],
                             B["ident_bf"][:],
                             deps=[m_cast, prev.get("psu")],
                             start=True, stop=True,
                             inc=(1 if (dgi == 3 and tt == 1) else 0))
            m_ue = None
            for dgi in range(4):
                dg = half * 4 + dgi
                m_ue = r("scalar", "activation",
                         B["u3T_sb"][:, dg * NT + tg * TG: dg * NT + (tg + 1) * TG],
                         B["psD_u"][:, dgi * TG:(dgi + 1) * TG], AF.Copy,
                         deps=[m_mm], inc=(1 if dgi == 3 else 0))
            prev["psu"] = m_ue
        prev["u3bf_free"] = prev["psu"]
        m_u3T = prev["psu"]
        m_lgmm = None
        for tt in range(2):
            for kg in range(8):
                m_lgmm = r("tensor", "matmul", B["psD_p"][:, tt * NE: tt * NE + NE],
                           B["u3T_sb"][:, kg * NT + tg * TG + tt * 128:
                                       kg * NT + tg * TG + (tt + 1) * 128],
                           B["wr_sb"][:, kg * NE:(kg + 1) * NE],
                           deps=([m_u3T, m_wr] if (tt == 0 and kg == 0) else ()),
                           start=(kg == 0), stop=(kg == 7),
                           inc=(1 if kg == 7 else 0))
        m_lgv = {}
        for tt in range(2):
            m_lgv[tt] = r("vector", "tensor_tensor", B["lg_sb"][:, tt * NE:(tt + 1) * NE],
                          B["psD_p"][:, tt * NE:tt * NE + NE], B["brb_sb"][:], OP.add,
                          deps=[m_lgmm], inc=1)
        ctx.autodrain = True
        for tt in range(2):
            g3 = 2 * tg + tt
            lg = B["lg_sb"][:, tt * NE:(tt + 1) * NE]
            r("vector", "max", B["m8_sb"][:, tt * 8:(tt + 1) * 8], lg, deps=[m_lgv[tt]])
            ctx.drain("vector")
            r("vector", "max_index", B["i8_sb"][:, tt * 8:(tt + 1) * 8],
              B["m8_sb"][:, tt * 8:(tt + 1) * 8], lg)
            ctx.drain("vector")
            r("vector", "tensor_scalar", B["idxf_sb"][:, tt * 2:(tt + 1) * 2],
              B["i8_sb"][:, tt * 8:tt * 8 + 2], 0.0, None, OP.add)
            m_vv = r("vector", "tensor_scalar", B["nm1_sb"][:, tt:tt + 1],
                     B["m8_sb"][:, tt * 8:tt * 8 + 1], -1.0, None, OP.mult, inc=1)
            m_e = r("scalar", "activation", B["t21_sb"][:, tt:tt + 1],
                    B["m8_sb"][:, tt * 8 + 1:tt * 8 + 2], AF.Exp,
                    bias=B["nm1_sb"][:, tt:tt + 1], deps=[m_vv], inc=1)
            r("vector", "tensor_scalar", B["den_sb"][:, tt:tt + 1],
              B["t21_sb"][:, tt:tt + 1], 1.0, None, OP.add, deps=[m_e])
            r("vector", "reciprocal", B["w1r_sb"][:, tt:tt + 1], B["den_sb"][:, tt:tt + 1])
            r("vector", "tensor_tensor", B["w2r_sb"][:, tt:tt + 1],
              B["t21_sb"][:, tt:tt + 1], B["w1r_sb"][:, tt:tt + 1], OP.mult)
            r("vector", "tensor_scalar", B["mk1_sb"][:], B["iota16_sb"][:],
              B["idxf_sb"][:, tt * 2:tt * 2 + 1], B["w1r_sb"][:, tt:tt + 1],
              OP.is_equal, OP.mult)
            r("vector", "tensor_scalar", B["mk2_sb"][:], B["iota16_sb"][:],
              B["idxf_sb"][:, tt * 2 + 1:tt * 2 + 2], B["w2r_sb"][:, tt:tt + 1],
              OP.is_equal, OP.mult)
            r("vector", "tensor_tensor", B["wfull_sb"][:, g3 * NE:(g3 + 1) * NE],
              B["mk1_sb"][:], B["mk2_sb"][:], OP.add)
            m_wf = r("vector", "tensor_scalar", B["wfbf_sb"][:],
                     B["wfull_sb"][:, g3 * NE:(g3 + 1) * NE], 1.0, None, OP.mult, inc=1)
            m_tr = r("tensor", "matmul", B["psD_u"][0:16, 0:128],
                     B["wfbf_sb"][:], B["ident_bf"][:],
                     deps=[m_wf, prev.get("wrt_ev"), prev.get(("psu", 0))],
                     start=True, stop=True, inc=1)
            m_wrt = r("scalar", "activation", B["wroute_sb"][:, g3 * 128:(g3 + 1) * 128],
                      B["psD_u"][0:16, 0:128], AF.Copy, deps=[m_tr], inc=1)
            prev["wrt_ev"] = m_wrt
            prev[("psu", 0)] = m_wrt
        ctx.autodrain = False
        if tg == (LIMIT_TG or NTG) - 1:
            ctx.tap("b2_u3T", B["u3T_sb"][:], deps=[prev["psu"], prev["wrt_ev"]])
        ctx.tap("b2_wfull_t%d" % tg, B["wfull_sb"][:], deps=[prev["wrt_ev"]])
        prev["gld_free"] = m_pev[7]
        prev["psp_free"] = m_lgv[1]


def _phase_c(ctx, H, B):
    r = ctx.run
    prev = {}

    def load_e(e, which):
        msd = []
        if which in ("W1", "both"):
            for half in range(2):
                dst = B["W1X_sb"] if half == 0 else B["W1Y_sb"]
                msd.append(r("gpsimd", "dma_start",
                             out=dst[:].rearrange("p (kk m) -> p kk m", m=HID),
                             in_=H["W1e"][e][half * 512:(half + 1) * 512, :].rearrange(
                                 "(kk p) m -> p kk m", p=128),
                             deps=[prev.get("W1_free")], dma=True))
        if which in ("W2", "both"):
            for half in range(2):
                dst = B["W2X_sb"] if half == 0 else B["W2Y_sb"]
                msd.append(r("gpsimd", "dma_start",
                             out=dst[:].rearrange("p (kk m) -> p kk m", m=D),
                             in_=H["W2e"][e][half * 512:(half + 1) * 512, :].rearrange(
                                 "(kk p) m -> p kk m", p=128),
                             deps=[prev.get("W2_free")], dma=True))
        return msd

    def load_vec_e(e):
        m1 = r("gpsimd", "dma_start", out=B["g1e_sb"][:], in_=H["g1b"][e],
               deps=[prev.get("g1_free")], dma=True)
        m2 = r("gpsimd", "dma_start", out=B["be1e_sb"][:], in_=H["be1b"][e],
               deps=[prev.get("be1_free")], dma=True)
        return [m1, m2]

    def part_a(e, tg, mW, mb1):
        out = {}
        for tt in range(2):
            ps = B["psE_w1a"] if tt == 0 else B["psE_w1b"]
            for kg in range(8):
                wbuf = B["W1X_sb"] if kg < 4 else B["W1Y_sb"]
                kk = kg % 4
                for nh in range(2):
                    r("tensor", "matmul", ps[:, nh * 512:(nh + 1) * 512],
                      B["u3T_sb"][:, kg * NT + tg * TG + tt * 128:
                                  kg * NT + tg * TG + (tt + 1) * 128],
                      wbuf[:, kk * HID + nh * 512: kk * HID + (nh + 1) * 512],
                      deps=(list(mW) + [prev.get(("psw1", tt)), mb1]
                            if kg == 0 and nh == 0 else ()),
                      start=(kg == 0), stop=False)
            m_mm = None
            for nh in range(2):
                m_mm = r("tensor", "matmul", ps[:, nh * 512:(nh + 1) * 512],
                         B["ones_sb"][:],
                         B["b1e_sb"][0:1, nh * 512:(nh + 1) * 512],
                         start=False, stop=True, inc=(1 if nh == 1 else 0))
            out[tt] = m_mm
        return out

    def part_evac(e, tg, mA):
        for tt in range(2):
            ps = B["psE_w1a"] if tt == 0 else B["psE_w1b"]
            hch = B["hch_sb"][:, tt * HID:(tt + 1) * HID]
            m_ev = r("scalar", "activation", hch, ps[:], AF.Copy,
                     accum_out=B["sums_sb"][:, tt:tt + 1],
                     deps=[mA[tt], prev.get(("hch", tt))], inc=1, war=True)
            prev[("psw1", tt)] = m_ev
        for tt in range(2):
            hch = B["hch_sb"][:, tt * HID:(tt + 1) * HID]
            r("scalar", "activation", B["trash_bf"][:], hch, AF.Square,
              accum_out=B["ssqE_sb"][:, tt:tt + 1], inc=0)

    def part_b(e, tg, mW, mV):
        for tt in range(2):
            ss = slice(tt, tt + 1)
            r("scalar", "activation", B["muE_sb"][:, ss], B["sums_sb"][:, ss],
              AF.Copy, scale=1.0 / HID)
            r("scalar", "activation", B["sm_b"][:, ss], B["muE_sb"][:, ss],
              AF.Square)
            r("scalar", "activation", B["sm_b"][:, ss], B["sm_b"][:, ss],
              AF.Identity, scale=-1.0, bias=B["eps_sb"][:])
            r("scalar", "activation", B["sm_a"][:, ss], B["ssqE_sb"][:, ss],
              AF.Ln, scale=1.0 / HID, bias=B["sm_b"][:, ss])
            r("scalar", "activation", B["rstdE_sb"][:, ss], B["sm_a"][:, ss],
              AF.Exp, scale=-0.5)
            r("scalar", "activation", B["sm_b"][:, ss], B["muE_sb"][:, ss],
              AF.Identity, scale=B["rstdE_sb"][:, ss])
            r("scalar", "activation", B["bE_sb"][:, ss], B["sm_b"][:, ss],
              AF.Identity, scale=-1.0)
        for tt in range(2):
            hch = B["hch_sb"][:, tt * HID:(tt + 1) * HID]
            ss = slice(tt, tt + 1)
            m_af = r("scalar", "activation", hch, hch, AF.Identity,
                     scale=B["rstdE_sb"][:, ss], bias=B["bE_sb"][:, ss], inc=1)
            r("vector", "tensor_tensor", hch, hch, B["g1e_sb"][:], OP.mult,
              deps=[mV[0], m_af])
            prev[("t5", tt)] = r("vector", "tensor_tensor", hch, hch,
                                 B["be1e_sb"][:], OP.add, deps=[mV[1]], inc=1)
        m_gelu = {}
        for tt in range(2):
            hch = B["hch_sb"][:, tt * HID:(tt + 1) * HID]
            gl = B["gl_sb"][:, tt * HID:(tt + 1) * HID]
            m_gl = r("scalar", "activation", gl, hch, AF.Gelu,
                     deps=[prev[("t5", tt)]], inc=1)
            prev[("hch", tt)] = m_gl
            m_gelu[tt] = m_gl
        for tt in range(2):
            gl = B["gl_sb"][:, tt * HID:(tt + 1) * HID]
            g3 = 2 * tg + tt
            dg_t = B["dg_sb"][:, tt * 128:(tt + 1) * 128]
            m_dg = r("vector", "tensor_scalar", dg_t, B["ident_bf"][:],
                     B["wfull_sb"][:, g3 * NE + e: g3 * NE + e + 1], None, OP.mult,
                     deps=[prev.get(("dg", tt))], inc=1)
            for half in range(2):
                m_mm2 = None
                for hbi in range(4):
                    hb = half * 4 + hbi
                    m_mm2 = r("tensor", "matmul",
                              B["psE_tr"][:, hbi * 128:(hbi + 1) * 128],
                              gl[:, hb * 128:(hb + 1) * 128], dg_t,
                              deps=[m_gelu[tt], m_dg, prev.get("pstr_free")],
                              start=True, stop=True, inc=(1 if hbi == 3 else 0))
                last = (tt == 1 and half == 1)
                m_tev = r("vector", "tensor_scalar",
                          B["glT_sb"][:].rearrange(
                              "p (hb t) -> p hb t", t=TG)[:, half * 4:(half + 1) * 4,
                                                          tt * 128:(tt + 1) * 128],
                          B["psE_tr"][:].rearrange("p (hb t) -> p hb t", t=128),
                          0.0, None, OP.add,
                          deps=[m_mm2, prev.get("glT_free")], inc=1,
                          war=(not last))
                prev["pstr_free"] = m_tev
                prev[("glT", tt, half)] = m_tev
            prev[("dg", tt)] = prev[("glT", tt, 1)]
        m_glT = prev[("glT", 1, 1)]
        m_mm3 = None
        for rnd in range(4):
            bank = rnd % 2
            for mti in range(2):
                mt = rnd * 2 + mti
                dst = B["psE_o"][:, (bank * 2 + mti) * TG:(bank * 2 + mti + 1) * TG]
                hd = ([m_glT, prev.get(("pso", bank))] + (list(mW) if rnd == 0 else [])
                      if mti == 0 else ())
                if e == 0:
                    m_mm3 = r("tensor", "matmul", dst,
                              B["b2_sb"][:, mt * 128:(mt + 1) * 128],
                              B["wroute_sb"][:, tg * TG:(tg + 1) * TG],
                              deps=hd, start=True, stop=False, inc=0)
                for kg in range(8):
                    wbuf = B["W2X_sb"] if kg < 4 else B["W2Y_sb"]
                    kk = kg % 4
                    m_mm3 = r("tensor", "matmul", dst,
                              wbuf[:, kk * D + mt * 128: kk * D + (mt + 1) * 128],
                              B["glT_sb"][:, kg * TG:(kg + 1) * TG],
                              deps=(hd if (e > 0 and kg == 0) else ()),
                              start=(kg == 0 and e != 0), stop=(kg == 7),
                              inc=(1 if (mti == 1 and kg == 7) else 0))
            m_fa = None
            for mti in range(2):
                mt = rnd * 2 + mti
                fsl = B["f_sb"][:, mt * NT + tg * TG: mt * NT + (tg + 1) * TG]
                psl = B["psE_o"][:, (bank * 2 + mti) * TG:(bank * 2 + mti + 1) * TG]
                if e == 0:
                    m_fa = r("vector", "tensor_scalar", fsl, psl, 0.0, None,
                             OP.add, deps=[m_mm3], inc=(1 if mti == 1 else 0),
                             war=True)
                else:
                    m_fa = r("vector", "tensor_tensor", fsl, psl, fsl, OP.add,
                             deps=[m_mm3], inc=(1 if mti == 1 else 0), war=True)
            prev[("pso", bank)] = m_fa
        prev["glT_free"] = m_mm3
        return m_gelu, m_mm3

    ctx.tap("c_u3T", B["u3T_sb"][:])
    for nm in ("sum3_s", "ssq3_s", "rstd3_s", "b03_s"):
        ctx.tap("c_" + nm, B[nm][:])

    eW = {0: load_e(0, "both")}
    eV = {0: load_vec_e(0)}
    eB1 = {0: r("gpsimd", "dma_start", out=B["b1e_sb"][0:1, :],
                in_=H["b1b"][0][0:1, :], dma=True)}
    ntg = LIMIT_TG or NTG
    ne = LIMIT_E or NE
    pend = None
    for e in range(ne):
        for tg in range(ntg):
            mA = part_a(e, tg, eW[e], eB1[e])
            if tg == ntg - 1:
                prev["W1_free"] = mA[1]
                prev["b1_free"] = mA[1]
                if e + 1 < ne:
                    eW[e + 1] = load_e(e + 1, "W1")
                    eB1[e + 1] = r("gpsimd", "dma_start", out=B["b1e_sb"][0:1, :],
                                   in_=H["b1b"][e + 1][0:1, :],
                                   deps=[prev["b1_free"]], dma=True)
            if pend is not None:
                pe_, ptg = pend
                m_g, m_w2 = part_b(pe_, ptg, eW[pe_], eV[pe_])
                if ptg == ntg - 1:
                    prev["W2_free"] = m_w2
                    prev["g1_free"] = prev[("t5", 1)]
                    prev["be1_free"] = prev[("t5", 1)]
                    if pe_ + 1 < ne:
                        eW[pe_ + 1] = eW[pe_ + 1] + load_e(pe_ + 1, "W2")
                        eV[pe_ + 1] = load_vec_e(pe_ + 1)
            part_evac(e, tg, mA)
            pend = (e, tg)
    part_b(pend[0], pend[1], eW[pend[0]], eV[pend[0]])


def _phase_fin(ctx, H, B, out_h):
    r = ctx.run
    prev = {}
    for tg in range(LIMIT_TG or NTG):
        m_lx = r("sync", "dma_start",
                 out=B["x1f_sb"][:].rearrange("p (tt d) -> p tt d", d=D),
                 in_=H["sp_x1"][tg], deps=[prev.get("x1f_free")], dma=True)
        m_x2 = {}
        for tt in range(2):
            m_mm = None
            for og in range(8):
                m_mm = r("tensor", "matmul", B["psE_f"][:, og * 128:(og + 1) * 128],
                         B["f_sb"][:, og * NT + tg * TG + tt * 128:
                                   og * NT + tg * TG + (tt + 1) * 128],
                         B["ident_bf"][:],
                         deps=[prev.get("psf")],
                         start=True, stop=True, inc=(1 if og == 7 else 0))
            m_x2[tt] = r("vector", "tensor_tensor", B["x2_sb"][:, tt * D:(tt + 1) * D],
                         B["x1f_sb"][:, tt * D:(tt + 1) * D], B["psE_f"][:], OP.add,
                         deps=[m_mm, m_lx, prev.get("x2_free")], inc=1)
            prev["psf"] = m_x2[tt]
        m_st = r("gpsimd", "dma_start",
                 out=out_h[tg * TG:(tg + 1) * TG, :].rearrange("(tt p) d -> p tt d", p=128),
                 in_=B["x2_sb"][:].rearrange("p (tt d) -> p tt d", d=D),
                 deps=[m_x2[1]], dma=True)
        prev["x2_free"] = m_st
        prev["x1f_free"] = m_x2[1]


_TAPDEFS = {}
for _c in range(NCHUNK):
    _TAPDEFS["a1_x0bf_c%d" % _c] = ((128, 4 * D), BF16)
    _TAPDEFS["a1_n1T_c%d" % _c] = ((128, 8 * TL), BF16)
    _TAPDEFS["a1_sum_c%d" % _c] = ((128, 32), F32)
    _TAPDEFS["a1_mu0_c%d" % _c] = ((128, 32), F32)
    _TAPDEFS["a1_rstd0_c%d" % _c] = ((128, 32), F32)
    _TAPDEFS["a1_var_c%d" % _c] = ((128, 32), F32)
    _TAPDEFS["a1_ssq_c%d" % _c] = ((128, 32), F32)
    _TAPDEFS["a1_rs01_c%d" % _c] = ((128, 32), F32)
    _TAPDEFS["a1_uact_c%d" % _c] = ((128, 16 * TL), BF16)
    _TAPDEFS["a1_z_c%d" % _c] = ((128, 16 * TL), BF16)
    _TAPDEFS["a1_B_c%d" % _c] = ((16, TL), F32)
    _TAPDEFS["a1_dtr_c%d" % _c] = ((64, TL), BF16)
    _TAPDEFS["a2_gated_c%d" % _c] = ((128, 16 * TL), BF16)
    _TAPDEFS["a2_h_c%d" % _c] = ((128, 4 * D), F32)
_TAPDEFS["b2_u3T"] = ((128, 8 * NT), BF16)
_TAPDEFS["dbg_pdt_t0"] = ((128, D), BF16)
_TAPDEFS["dbg_pdt_t1"] = ((128, D), BF16)
for _n in ("sum3_s", "ssq3_s", "rstd3_s", "b03_s"):
    _TAPDEFS["c_" + _n] = ((128, 32), F32)
_TAPDEFS["c_u3T"] = ((128, 8 * NT), BF16)
for _t in range(NTG):
    _TAPDEFS["b2_pT_t%d" % _t] = ((128, 8 * TG), BF16)
    _TAPDEFS["b2_gld_t%d" % _t] = ((128, 32 * TG), BF16)
    for _u in range(2):
        _TAPDEFS["b2_x0h_t%d_%d" % (_t, _u)] = ((128, D), F32)
    _TAPDEFS["b2_x1_t%d" % _t] = ((128, 2 * D), F32)
    _TAPDEFS["b2_wfull_t%d" % _t] = ((128, 32 * NE), F32)


def kernel(**inputs):
    global _last_results
    shared = _prep_shared(inputs)
    x = np.asarray(inputs["L0_attn_x"], dtype=np.float32)
    nc = build()
    in_maps = []
    for c in range(NCORE):
        m = dict(shared)
        m["xs"] = _asc(x[2 * c:2 * c + 2].reshape(NT, D))
        in_maps.append(m)
    res = run_bass_kernel_spmd(nc, in_maps, core_ids=list(range(NCORE)))
    _last_results = res
    out = np.concatenate([np.asarray(res.results[c]["out"]).reshape(2, L, D)
                          for c in range(NCORE)], axis=0)
    return out
